# version 38
# baseline (speedup 1.0000x reference)
import contextlib

import numpy as np
import ml_dtypes

import concourse.bass as bass
import concourse.mybir as mybir
from concourse.bass_isa import ReduceOp
from concourse.bass_utils import run_bass_kernel_spmd

F32 = mybir.dt.float32
BF16 = mybir.dt.bfloat16
I32 = mybir.dt.int32
I16 = mybir.dt.int16
I8 = mybir.dt.int8
AF = mybir.ActivationFunctionType
OP = mybir.AluOpType
AX = mybir.AxisListType.X

B, T, H, S, DH = 4, 4096, 1024, 16, 4096
EXT = T // 2 + 4
EXTP = 2176
EPS = 1e-5
SAFE_ROUND = False

NT = (EXT + 127) // 128
TILES = [(i * 128, min(EXT, (i + 1) * 128)) for i in range(NT)]
AW = 256
AGROUPS = [(i * AW, (i + 1) * AW) for i in range(7)] + [(7 * AW, EXT)]
AWM = EXT - 7 * AW
GW = 256
FG = [(a, min(a + GW, EXT)) for a in range(0, EXT, GW)]

ENGINES = ("SP", "PE", "DVE", "ACT", "GP")
_ENG_ATTR = {"SP": "sync", "PE": "tensor", "DVE": "vector", "ACT": "scalar",
             "GP": "gpsimd"}


class Prog:

    def __init__(self, nc):
        self.nc = nc
        self.ops = []

    def op(self, eng, fn, reads=(), writes=(), dma=False, group=None):
        self.ops.append(dict(eng=eng, fn=fn, reads=tuple(reads),
                             writes=tuple(writes), dma=dma, group=group,
                             inc=None, sem=None, barrier=False))

    def barrier(self):
        self.ops.append(dict(eng="SP", fn=None, reads=(), writes=(), dma=False,
                             group=None, inc=None, sem=None, barrier=True))

    def finalize_and_emit(self):
        nc, ops = self.nc, self.ops
        last_w, readers = {}, {}
        deps_l = []
        last_per_eng = {}
        last_dma_sem = {}
        dma_i = 0
        NDMA_ = 48
        last_barrier = None
        for i, o in enumerate(ops):
            deps = set()
            if o["barrier"]:
                deps.update(v for v in last_per_eng.values() if v is not None)
                deps.update(v for v in last_dma_sem.values() if v is not None)
                if last_barrier is not None:
                    deps.add(last_barrier)
                deps.discard(i)
                deps_l.append(deps)
                last_barrier = i
                last_per_eng = {}
                last_dma_sem = {}
                continue
            if last_barrier is not None:
                deps.add(last_barrier)
            if o["dma"]:
                last_dma_sem[dma_i % NDMA_] = i
                dma_i += 1
            last_per_eng[(o["eng"], o["dma"])] = i
            for r in o["reads"]:
                if r in last_w:
                    deps.add(last_w[r][0])
            for w in o["writes"]:
                if w in last_w:
                    wi, wg = last_w[w]
                    if not (o["group"] is not None and o["group"] == wg):
                        deps.add(wi)
                deps.update(readers.get(w, ()))
            deps.discard(i)
            deps_l.append(deps)
            for w in o["writes"]:
                last_w[w] = (i, o["group"])
                readers[w] = []
            for r in o["reads"]:
                readers.setdefault(r, []).append(i)

        need_inc = [False] * len(ops)
        for ds in deps_l:
            for d in ds:
                need_inc[d] = True

        es = contextlib.ExitStack()
        chain = {e: es.enter_context(nc.semaphore(f"ch_{e}")) for e in ENGINES}
        NDMA = 48
        dpool = [es.enter_context(nc.semaphore(f"dma{i}")) for i in range(NDMA)]
        cnt = {e: 0 for e in ENGINES}
        dcnt = [0] * NDMA
        dix = 0
        for i, o in enumerate(ops):
            if o["dma"]:
                j = dix % NDMA
                dix += 1
                o["pre_wait"] = (dpool[j], dcnt[j]) if dcnt[j] else None
                dcnt[j] += 16
                o["sem"] = (f"D{j}", dpool[j])
                o["inc"] = dcnt[j]
            elif need_inc[i]:
                cnt[o["eng"]] += 1
                o["sem"] = ("C" + o["eng"], chain[o["eng"]])
                o["inc"] = cnt[o["eng"]]

        know = [None] * len(ops)
        know_is = [None] * len(ops)
        eng_last = {e: None for e in ENGINES}
        waits_l = [None] * len(ops)
        for i, o in enumerate(ops):
            k = (dict(know_is[eng_last[o["eng"]]])
                 if eng_last[o["eng"]] is not None else {})
            waits = []
            for d in sorted(deps_l[i]):
                do = ops[d]
                skey, shandle = do["sem"]
                val = do["inc"]
                if k.get(skey, 0) < val:
                    waits.append((shandle, val))
                    k[skey] = val
                    for kk, vv in know[d].items():
                        if k.get(kk, 0) < vv:
                            k[kk] = vv
            if o.get("pre_wait") is not None:
                sh, val = o["pre_wait"]
                skey = o["sem"][0]
                if k.get(skey, 0) < val:
                    waits.append((sh, val))
                    k[skey] = val
            if o["sem"] is not None:
                know[i] = dict(k)
                skey = o["sem"][0]
                know[i][skey] = max(know[i].get(skey, 0), o["inc"])
            else:
                know[i] = k
            know_is[i] = k
            eng_last[o["eng"]] = i
            waits_l[i] = waits

        with nc.Block() as block:
            for eng in ENGINES:
                idxs = [i for i, o in enumerate(ops) if o["eng"] == eng]
                if not idxs and eng != "SP":
                    continue

                def body(handle, eng=eng, idxs=idxs):
                    for i in idxs:
                        o = ops[i]
                        for sh, val in waits_l[i]:
                            handle.wait_ge(sh, val)
                        instr = o["fn"]() if o["fn"] is not None else handle.nop()
                        if o["sem"] is not None:
                            instr.then_inc(o["sem"][1], 16 if o["dma"] else 1)
                    if eng == "SP":
                        for e2 in ENGINES:
                            if cnt.get(e2, 0):
                                handle.wait_ge(chain[e2], cnt[e2])
                        for j in range(NDMA):
                            if dcnt[j]:
                                handle.wait_ge(dpool[j], dcnt[j])

                getattr(block, _ENG_ATTR[eng])(body)
        self._sem_stack = es
        return len(ops)


def build_nc(debug=False):
    nc = bass.Bass("TRN2", target_bir_lowering=False, debug=False)
    p = Prog(nc)
    V, A, PE, GP, SY = nc.vector, nc.scalar, nc.tensor, nc.gpsimd, nc.sync
    QENG = {"SP": SY, "ACT": A, "GP": GP}

    def din(name, shape, dt=F32):
        return nc.dram_tensor(name, shape, dt, kind="ExternalInput").ap()

    x_tm = din("x_tm", [EXT, H])
    x_fm = din("x_fm", [H, EXT])
    xg_fm = din("xg_fm", [H, T])
    W_in = din("W_in", [H, H]); W_D = din("W_D", [H, H]); W_out = din("W_out", [H, H])
    W_in_T = din("W_in_T", [H, H]); W_B_T = din("W_B_T", [H, H])
    B_mat = din("B_mat", [H, S]); C_mat = din("C_mat", [S, H])
    W_ig = din("W_ig", [H, S]); W_fg = din("W_fg", [H, S])
    b_in_cols = din("b_in_cols", [128, 8]); b_B_cols = din("b_B_cols", [128, 8])
    b_out_cols = din("b_out_cols", [128, 8])
    b_ig_col = din("b_ig_col", [S, 1]); b_fg_col = din("b_fg_col", [S, 1])
    gamma_b = din("gamma_b", [128, H]); beta_b = din("beta_b", [128, H])
    bD_row = din("bD_row", [1, H])
    convw = din("convw", [128, 3 * 32])
    convb = din("convb", [128, 32])
    w1t_f = din("w1t_f", [H, DH])
    w2t_f = din("w2t_f", [DH, H])
    qmask = din("qmask", [128, NT])
    hsel0 = din("hsel0", [S, 1]); hsel1 = din("hsel1", [S, 1])
    ones1 = din("ones1", [1, 128])
    ones_col = din("ones_col", [128, 1])
    id_f32 = din("id_f32", [128, 128])
    onesrow_ext = din("onesrow_ext", [1, EXT], BF16)
    zrow_ext = din("zrow_ext", [1, EXT], BF16)
    out_tm = nc.dram_tensor("out_tm", [T // 2, H], F32, kind="ExternalOutput").ap()
    w1q_hbm = nc.dram_tensor("w1q_hbm", [H, DH], BF16, kind="Internal").ap()
    w2q_hbm = nc.dram_tensor("w2q_hbm", [DH, H], BF16, kind="Internal").ap()
    r1_hbm = nc.dram_tensor("r1_hbm", [EXT, H], F32, kind="Internal").ap()
    r1q_hbm = nc.dram_tensor("r1q_hbm", [EXTP, H], BF16, kind="Internal").ap()
    dbg = {}
    if debug:
        for nm, shape, dt in [("d_gates", [48, T], F32), ("d_hs", [S, EXT], F32),
                              ("d_xp", [128, 8 * 512], F32),
                              ("d_y", [128, 8 * 512], F32),
                              ("d_z", [128, 8 * 512], F32),
                              ("d_r1", [EXT, H], F32),
                              ("d_h", [128, 32 * (GW + 2)], F32),
                              ("d_wg", [128, 8 * 48], F32),
                              ("d_dq", [1, EXT], F32)]:
            dbg[nm] = nc.dram_tensor(nm, shape, dt, kind="ExternalOutput").ap()

    root = contextlib.ExitStack()

    def SB(name, shape, dt=F32, stack=None):
        return (stack or root).enter_context(nc.sbuf_tensor(name, shape, dt))

    def PSUM(name, shape, dt=F32, stack=None):
        return (stack or root).enter_context(nc.psum_tensor(name, shape, dt))

    def load(dst_ap, src_ap, region, q="SP", reads=()):
        p.op(q, lambda d=dst_ap, s=src_ap, e=QENG[q]: e.dma_start(out=d, in_=s),
             reads=reads, writes=(region,), dma=True)

    def store(dst_ap, src_ap, reads, region, q="ACT"):
        p.op(q, lambda d=dst_ap, s=src_ap, e=QENG[q]: e.dma_start(out=d, in_=s),
             reads=reads, writes=(region,), dma=True)

    def load_fm(dst2d, width, src_slice, region, q="SP", stride=None):
        st = stride if stride is not None else width
        nch = src_slice.shape[0] // 128
        d3 = dst2d[:, 0:nch * st].rearrange("p (c t) -> p c t",
                                            c=nch)[:, :, 0:width]
        s3 = src_slice.rearrange("(c p) t -> p c t", p=128)
        p.op(q, lambda d=d3, s=s3, e=QENG[q]: e.dma_start(out=d, in_=s),
             reads=(), writes=(region,), dma=True)

    def mm(group, out_ap, lhsT_ap, rhs_ap, reads, wr, start, stop):
        p.op("PE", lambda o=out_ap, l=lhsT_ap, r=rhs_ap, st=start, sp=stop:
             PE.matmul(o, l, r, start=st, stop=sp),
             reads=reads, writes=(wr,), group=group)

    stAB = contextlib.ExitStack()
    s_qmask = SB("s_qmask", [128, NT])
    s_hsel0 = SB("s_hsel0", [S, 1]); s_hsel1 = SB("s_hsel1", [S, 1])
    s_ones1 = SB("s_ones1", [1, 128])
    s_onesc = SB("s_onesc", [128, 1])
    s_idf = SB("s_idf", [128, 128])
    s_binc = SB("s_binc", [128, 8]); s_boutc = SB("s_boutc", [128, 8])
    s_w1mean = SB("s_w1mean", [128, 1]); s_w2mean = SB("s_w2mean", [128, 1])
    dqcols = SB("dqcols", [128, NT])
    s_convw = SB("s_convw", [128, 3 * 32]); s_convb = SB("s_convb", [128, 32])
    s_gamb = SB("s_gamb", [128, H], stack=stAB)
    s_betb = SB("s_betb", [128, H], stack=stAB)
    hsA_hi = SB("hsA_hi", [S + 1, EXT], BF16, stack=stAB)
    hsA_lo = SB("hsA_lo", [S + 1, EXT], BF16, stack=stAB)
    CA_hi = SB("CA_hi", [S + 1, H], BF16, stack=stAB)
    CA_lo = SB("CA_lo", [S + 1, H], BF16, stack=stAB)

    TW = 1024 if SAFE_ROUND else 2048
    NTB = 4

    def alloc_tern(tstack):
        twfs = tuple(SB(f"twf{i}", [128, TW], stack=tstack) for i in range(NTB))
        _tu = tuple(SB(f"tu{i}", [128, TW], stack=tstack) for i in range(2))
        tus = (_tu[0], _tu[1], _tu[0], _tu[1])
        _tk16 = tuple(SB(f"tk16_{i}", [128, TW], I16, stack=tstack)
                      for i in range(2))
        tk16s = (_tk16[0], _tk16[1], _tk16[0], _tk16[1])
        _tb = tuple(SB(f"tb{i}", [128, TW], BF16, stack=tstack) for i in range(2))
        tbs = (_tb[0], _tb[1], _tb[0], _tb[1])
        if SAFE_ROUND:
            tk = SB("tk", [128, TW], I32, stack=tstack)
            trr = SB("trr", [128, TW], stack=tstack)
            tcp = SB("tcp", [128, TW], stack=tstack)
        else:
            tk = trr = tcp = None
        tcol = SB("tcol", [128, 10], stack=tstack)
        return twfs, tus, tk16s, tbs, tk, trr, tcp, tcol

    for dst, src, rg in [(s_gamb, gamma_b, "s_gamb"), (s_betb, beta_b, "s_betb"),
                         (s_qmask, qmask, "s_qmask"), (s_hsel0, hsel0, "s_hsel0"),
                         (s_hsel1, hsel1, "s_hsel1"), (s_ones1, ones1, "s_ones1"), (s_onesc, ones_col, "s_onesc"),
                         (s_idf, id_f32, "s_idf"), (s_binc, b_in_cols, "s_binc"),
                         (s_boutc, b_out_cols, "s_boutc"),
                         (s_convw, convw, "s_convw"), (s_convb, convb, "s_convb")]:
        load(dst[:], src[:], rg)

    stA = contextlib.ExitStack()
    g80 = SB("g80", [80, T], stack=stA)
    hs_buf = SB("hs_buf", [S, T + 4], stack=stA)
    GM = 80
    G = SB("G", [128, 8 * GM], stack=stA)
    WGf = SB("WGf", [128, 8 * GM], stack=stA)
    WG_hi = SB("WG_hi", [128, 8 * GM], BF16, stack=stA)
    WG_lo = SB("WG_lo", [128, 8 * GM], BF16, stack=stA)
    bG = SB("bG", [GM, 1], stack=stA)
    wtrs = (SB("wtr0", [128, H], stack=stA),
            SB("wtr1", [128, H], stack=stA))
    bmt = SB("bmt", [128, 8 * S], stack=stA)
    wigt = SB("wigt", [128, 8 * S], stack=stA)
    wfgt = SB("wfgt", [128, 8 * S], stack=stA)
    bic = SB("bic", [128, 8], stack=stA)
    bbc = SB("bbc", [128, 8], stack=stA)
    ps_pre = PSUM("ps_pre", [128, 1024], stack=stA)
    ps_g = PSUM("ps_g", [GM, 512], stack=stA)

    load_fm(bmt, S, B_mat[:], "bmt")
    load(bic[:], b_in_cols[:], "bic")
    load(bbc[:], b_B_cols[:], "bbc")
    load_fm(wigt, S, W_ig[:], "wigt")
    load_fm(wfgt, S, W_fg[:], "wfgt")

    p.op("DVE", lambda: V.memset(G[:, :], 0.0), writes=("Ginit",))
    for fc in range(8):
        wtr, wr_ = wtrs[fc % 2], f"wtr{fc % 2}"
        po = (fc % 2) * 512
        load_fm(wtr, 128, W_B_T[:, fc * 128:(fc + 1) * 128], wr_)
        for j in range(8):
            mm(f"P{fc}", ps_pre[:, po:po + 16], wtr[:, j * 128:(j + 1) * 128],
               bmt[:, j * S:(j + 1) * S], reads=(wr_, "bmt"),
               wr=f"psPre{fc % 2}", start=(j == 0), stop=(j == 7))
        p.op("DVE", lambda o=G[:, fc * GM:fc * GM + 16],
             i=ps_pre[:, po:po + 16], fc=fc:
             V.tensor_copy(o, i), reads=(f"psPre{fc % 2}", "Ginit"),
             writes=(f"G{fc}",))
        p.op("DVE", lambda o=G[:, fc * GM + 32:fc * GM + 48],
             i=wigt[:, fc * S:(fc + 1) * S]: V.tensor_copy(o, i),
             reads=("wigt", "Ginit"), writes=(f"G{fc}",))
        p.op("DVE", lambda o=G[:, fc * GM + 64:fc * GM + 80],
             i=wfgt[:, fc * S:(fc + 1) * S]: V.tensor_copy(o, i),
             reads=("wfgt", "Ginit"), writes=(f"G{fc}",))

    for fin in range(8):
        wtr, wr_ = wtrs[fin % 2], f"wtr{fin % 2}"
        po = (fin % 2) * 512
        load_fm(wtr, 128, W_in_T[:, fin * 128:(fin + 1) * 128], wr_)
        for fc in range(8):
            mm(f"WG{fin}", ps_pre[:, po:po + GM],
               wtr[:, fc * 128:(fc + 1) * 128],
               G[:, fc * GM:(fc + 1) * GM], reads=(wr_, f"G{fc}"),
               wr=f"psPre{fin % 2}", start=(fc == 0), stop=(fc == 7))
        p.op("DVE", lambda o=WGf[:, fin * GM:(fin + 1) * GM],
             i=ps_pre[:, po:po + GM], fin=fin:
             V.tensor_copy(o, i), reads=(f"psPre{fin % 2}",),
             writes=(f"WGf{fin}",))
        p.op("ACT", lambda o=WG_hi[:, fin * GM:(fin + 1) * GM],
             i=WGf[:, fin * GM:(fin + 1) * GM]: A.activation(o, i, AF.Copy),
             reads=(f"WGf{fin}",), writes=(f"WGhi{fin}",))
        p.op("DVE", lambda o=WG_lo[:, fin * GM:(fin + 1) * GM],
             a=WGf[:, fin * GM:(fin + 1) * GM],
             b=WG_hi[:, fin * GM:(fin + 1) * GM]:
             V.tensor_sub(o, a, b), reads=(f"WGf{fin}", f"WGhi{fin}"),
             writes=(f"WGlo{fin}",))
    if debug:
        store(dbg["d_wg"][:], WGf[:, 0:8 * 48], tuple(f"WGf{i}" for i in range(8)),
              "d_wg")

    t1 = SB("t1", [128, 8], stack=stA)
    for j in range(8):
        wtr, wr_ = wtrs[j % 2], f"wtr{j % 2}"
        po = (j % 2) * 512
        load_fm(wtr, 128, W_B_T[:, j * 128:(j + 1) * 128], wr_)
        for fc in range(8):
            mm(f"t1{j}", ps_pre[:, po:po + 1], wtr[:, fc * 128:(fc + 1) * 128],
               bic[:, fc:fc + 1], reads=(wr_, "bic"),
               wr=f"psPre{j % 2}", start=(fc == 0), stop=(fc == 7))
        p.op("DVE", lambda o=t1[:, j:j + 1], i=ps_pre[:, po:po + 1],
             b=bbc[:, j:j + 1], j=j:
             V.tensor_add(o, i, b), reads=(f"psPre{j % 2}", "bbc"),
             writes=("t1",))
    for j in range(8):
        mm("bP", ps_pre[0:16, 0:1], bmt[:, j * S:(j + 1) * S], t1[:, j:j + 1],
           reads=("bmt", "t1"), wr="psPre0", start=(j == 0), stop=(j == 7))
    btmp = SB("btmp", [S, 3], stack=stA)
    p.op("DVE", lambda: V.tensor_copy(btmp[:, 0:1], ps_pre[0:16, 0:1]),
         reads=("psPre0",), writes=("btmp0",))
    for which, wt, wrg, bcol, col in (("ig", wigt, "wigt", b_ig_col, 1),
                                      ("fg", wfgt, "wfgt", b_fg_col, 2)):
        for fc in range(8):
            mm(f"b{which}", ps_pre[0:16, 1:2], wt[:, fc * S:(fc + 1) * S],
               bic[:, fc:fc + 1], reads=(wrg, "bic"),
               wr="psPre0", start=(fc == 0), stop=(fc == 7))
        bcs = SB(f"bc_{which}", [S, 1], stack=stA)
        load(bcs[:], bcol[:], f"bc_{which}")
        p.op("DVE", lambda o=btmp[:, col:col + 1], i=ps_pre[0:16, 1:2], b=bcs[:]:
             V.tensor_add(o, i, b), reads=("psPre0", f"bc_{which}"),
             writes=(f"btmp{col}",))
    store(bG[0:16, :], btmp[:, 0:1], ("btmp0",), "bG")
    store(bG[32:48, :], btmp[:, 1:2], ("btmp1",), "bG")
    store(bG[64:80, :], btmp[:, 2:3], ("btmp2",), "bG")

    ctmp = SB("ctmp", [S, H], stack=stA)
    bdtmp = SB("bdtmp", [1, H], stack=stA)
    load(ctmp[:], C_mat[:], "ctmp")
    load(bdtmp[:], bD_row[:], "bdtmp")
    p.op("ACT", lambda: A.activation(CA_hi[0:S, :], ctmp[:], AF.Copy),
         reads=("ctmp",), writes=("CAhi",))
    p.op("DVE", lambda: V.tensor_sub(CA_lo[0:S, :], ctmp[:], CA_hi[0:S, :]),
         reads=("ctmp", "CAhi"), writes=("CAlo",))
    bdh = SB("bdh", [1, H], BF16, stack=stA)
    bdl = SB("bdl", [1, H], BF16, stack=stA)
    p.op("ACT", lambda: A.activation(bdh[:], bdtmp[:], AF.Copy),
         reads=("bdtmp",), writes=("bdh",))
    p.op("DVE", lambda: V.tensor_sub(bdl[:], bdtmp[:], bdh[:]),
         reads=("bdtmp", "bdh"), writes=("bdl",))
    store(CA_hi[S:S + 1, :], bdh[:], ("bdh",), "CAhi", q="SP")
    store(CA_lo[S:S + 1, :], bdl[:], ("bdl",), "CAlo", q="SP")

    twfs, tus, tk16s, tbs, tk, trr, tcp, tcol = alloc_tern(stA)

    def tern_slices(nrows, width):
        return [(rc, c0, min(width, c0 + TW))
                for rc in range(nrows) for c0 in range(0, width, TW)]

    def tern_pass1(src, nrows, width, mean_out, mean_rg, tg,
                   ps_hbuf, ps_rg, scol):
        stag = f"tcol{scol}"
        slices = tern_slices(nrows, width)
        for i, (rc, c0, c1) in enumerate(slices):
            wv = c1 - c0
            pr = i % NTB
            twf, tu = twfs[pr], tus[pr]
            load(twf[:, 0:wv], src[rc * 128:(rc + 1) * 128, c0:c1], f"twf{pr}",
                 q=("SP", "ACT")[i % 2])
            if i % 2 == 0:
                p.op("ACT", lambda twf=twf, tu=tu, e=wv: A.activation(
                    tu[:, 0:e], twf[:, 0:e], AF.Abs, accum_out=tcol[:, 0:1]),
                     reads=(f"twf{pr}",), writes=(f"tu{pr % 2}", "tcol0"))
                acc, arg = "tcol0", 0
            else:
                p.op("DVE", lambda twf=twf, e=wv: V.tensor_reduce(
                    tcol[:, 3:4].rearrange("p (t o) -> p t o", o=1),
                    twf[:, 0:e].rearrange("p (t e) -> p t e", t=1),
                    AX, OP.add, apply_absolute_value=True),
                     reads=(f"twf{pr}",), writes=("tcol3",))
                acc, arg = "tcol3", 3
            ac = 6 + (i % 4)
            if i < 4:
                p.op("DVE", lambda ac=ac, a=arg: V.tensor_copy(
                    tcol[:, ac:ac + 1], tcol[:, a:a + 1]),
                     reads=(acc,), writes=(f"tcol{ac}",))
            else:
                p.op("DVE", lambda a=arg, ac=ac: V.tensor_add(
                    tcol[:, ac:ac + 1], tcol[:, ac:ac + 1], tcol[:, a:a + 1]),
                     reads=(acc, f"tcol{ac}"), writes=(f"tcol{ac}",))
            yield
        nun = len(slices)
        p.op("DVE", lambda: V.tensor_add(tcol[:, 6:7], tcol[:, 6:7],
                                         tcol[:, 7:8]),
             reads=("tcol6", "tcol7"), writes=("tcol6",))
        if nun > 2:
            p.op("DVE", lambda: V.tensor_add(tcol[:, 8:9], tcol[:, 8:9],
                                             tcol[:, 9:10]),
                 reads=("tcol8", "tcol9"), writes=("tcol8",))
            p.op("DVE", lambda: V.tensor_add(tcol[:, 6:7], tcol[:, 6:7],
                                             tcol[:, 8:9]),
                 reads=("tcol6", "tcol8"), writes=("tcol6",))
        p.op("DVE", lambda: V.tensor_scalar(tcol[:, 1:2], tcol[:, 6:7],
                                            1.0 / (nrows * 128 * width), None,
                                            OP.mult),
             reads=("tcol6",), writes=("tcol1",))
        mm("parsum" + tg, ps_hbuf[0:1, 0:1], s_onesc[:], tcol[:, 1:2],
           reads=("s_onesc", "tcol1"), wr=ps_rg, start=True, stop=True)
        p.op("DVE", lambda: V.tensor_copy(tcol[0:1, 1:2], ps_hbuf[0:1, 0:1]),
             reads=(ps_rg,), writes=("tcol1",))
        mm("parbc" + tg, ps_hbuf[0:128, 0:1], s_ones1[:], tcol[0:1, 1:2],
           reads=("s_ones1", "tcol1"), wr=ps_rg, start=True, stop=True)
        p.op("DVE", lambda: V.tensor_copy(tcol[:, 1:2], ps_hbuf[0:128, 0:1]),
             reads=(ps_rg,), writes=("tcol1",))
        p.op("DVE", lambda: V.tensor_scalar_max(mean_out[:], tcol[:, 1:2], 1e-5),
             reads=("tcol1",), writes=(mean_rg,))
        p.op("DVE", lambda mo=mean_out, sc=scol: V.reciprocal(
            tcol[:, sc:sc + 1], mo[:]),
             reads=(mean_rg,), writes=(stag,))
        yield

    def tern_pass2(src, dst_hbm, nrows, width, tg, scol):
        stag = f"tcol{scol}"
        slices = tern_slices(nrows, width)
        for i, (rc, c0, c1) in enumerate(slices):
            wv = c1 - c0
            pr = i % NTB
            twf, tu, tk16, tb = twfs[pr], tus[pr], tk16s[pr], tbs[pr]
            load(twf[:, 0:wv], src[rc * 128:(rc + 1) * 128, c0:c1], f"twf{pr}",
                 q=("SP", "ACT")[i % 2])
            if SAFE_ROUND:
                p.op("DVE", lambda twf=twf, tu=tu, e=wv: V.tensor_scalar(
                    tu[:, 0:e], twf[:, 0:e], tcol[:, scol:scol + 1], None,
                    OP.mult),
                     reads=(f"twf{pr}", stag), writes=(f"tu{pr % 2}",))
                p.op("DVE", lambda tu=tu, e=wv: V.tensor_copy(tk[:, 0:e],
                                                              tu[:, 0:e]),
                     reads=(f"tu{pr % 2}",), writes=("tk",))
                p.op("DVE", lambda tu=tu, e=wv: V.tensor_sub(trr[:, 0:e],
                                                             tu[:, 0:e],
                                                             tk[:, 0:e]),
                     reads=(f"tu{pr % 2}", "tk"), writes=("trr",))
                p.op("DVE", lambda e=wv: V.tensor_scalar(tcp[:, 0:e], trr[:, 0:e],
                                                         0.5, None, OP.is_gt),
                     reads=("trr",), writes=("tcp",))
                p.op("DVE", lambda tu=tu, e=wv: V.scalar_tensor_tensor(
                    tu[:, 0:e], trr[:, 0:e], -0.5, tcp[:, 0:e], OP.is_lt,
                    OP.subtract),
                     reads=("trr", "tcp"), writes=(f"tu{pr % 2}",))
                p.op("DVE", lambda tu=tu, e=wv: V.tensor_sub(tu[:, 0:e],
                                                             tk[:, 0:e],
                                                             tu[:, 0:e]),
                     reads=(f"tu{pr % 2}", "tk"), writes=(f"tu{pr % 2}",))
                p.op("DVE", lambda tu=tu, tb=tb, e=wv: V.tensor_scalar(
                    tb[:, 0:e], tu[:, 0:e], 1.0, -1.0, OP.min, OP.max),
                     reads=(f"tu{pr % 2}",), writes=(f"tb{pr % 2}",))
            elif i % 2 == 0:
                p.op("ACT", lambda twf=twf, tu=tu, e=wv: A.activation(
                    tu[:, 0:e], twf[:, 0:e], AF.Identity,
                    scale=tcol[:, scol:scol + 1]),
                     reads=(f"twf{pr}", stag), writes=(f"tu{pr % 2}",))
                p.op("DVE", lambda tu=tu, tk16=tk16, e=wv: V.tensor_scalar(
                    tk16[:, 0:e], tu[:, 0:e], 1.49, -1.49, OP.min, OP.max),
                     reads=(f"tu{pr % 2}",), writes=(f"tk16{pr % 2}",))
                p.op("ACT", lambda tk16=tk16, tb=tb, e=wv: A.activation(
                    tb[:, 0:e], tk16[:, 0:e], AF.Copy),
                     reads=(f"tk16{pr % 2}",), writes=(f"tb{pr % 2}",))
            else:
                p.op("DVE", lambda twf=twf, tu=tu, e=wv: V.tensor_scalar(
                    tu[:, 0:e], twf[:, 0:e], tcol[:, scol:scol + 1], 1.49,
                    OP.mult, OP.min),
                     reads=(f"twf{pr}", stag), writes=(f"tu{pr % 2}",))
                p.op("DVE", lambda tu=tu, tk16=tk16, e=wv: V.tensor_scalar(
                    tk16[:, 0:e], tu[:, 0:e], -1.49, None, OP.max),
                     reads=(f"tu{pr % 2}",), writes=(f"tk16{pr % 2}",))
                p.op("ACT", lambda tk16=tk16, tb=tb, e=wv: A.activation(
                    tb[:, 0:e], tk16[:, 0:e], AF.Copy),
                     reads=(f"tk16{pr % 2}",), writes=(f"tb{pr % 2}",))
            store(dst_hbm[rc * 128:(rc + 1) * 128, c0:c1], tb[:, 0:wv],
                  (f"tb{pr % 2}",), f"q{tg}{rc}_{c0}")
            yield

    def tern_p1_all(ps_hbuf, ps_rg):
        yield from tern_pass1(w1t_f, 8, DH, s_w1mean, "s_w1mean", "w1",
                              ps_hbuf, ps_rg, 2)
        yield from tern_pass1(w2t_f, 32, H, s_w2mean, "s_w2mean", "w2",
                              ps_hbuf, ps_rg, 4)

    def tern_p2_all():
        yield from tern_pass2(w1t_f, w1q_hbm, 8, DH, "w1", 2)
        yield from tern_pass2(w2t_f, w2q_hbm, 32, H, "w2", 4)

    def tern_all():
        yield from tern_p1_all(ps_pre, "psPre0")
        yield from tern_p2_all()

    tern_p1 = tern_all()

    def pump_p1(n):
        for _ in range(n):
            try:
                next(tern_p1)
            except StopIteration:
                break

    GSL = 256
    xgfs = (SB("xgf0", [128, 8 * GSL], stack=stA),
            SB("xgf1", [128, 8 * GSL], stack=stA))
    xghs = (SB("xgh0", [128, 8 * GSL], BF16, stack=stA),
            SB("xgh1", [128, 8 * GSL], BF16, stack=stA))
    xgls = (SB("xgl0", [128, 8 * GSL], BF16, stack=stA),
            SB("xgl1", [128, 8 * GSL], BF16, stack=stA))
    for sl in range(T // GSL):
        c0 = sl * GSL
        pr = sl % 2
        xgf, xgh, xgl = xgfs[pr], xghs[pr], xgls[pr]
        go = pr * GSL
        load_fm(xgf, GSL, xg_fm[:, c0:c0 + GSL], f"xgf{pr}")
        p.op("ACT", lambda xgf=xgf, xgh=xgh: A.activation(xgh[:], xgf[:],
                                                          AF.Copy),
             reads=(f"xgf{pr}",), writes=(f"xgh{pr}",))
        p.op("DVE", lambda xgf=xgf, xgh=xgh, xgl=xgl:
             V.tensor_sub(xgl[:], xgf[:], xgh[:]),
             reads=(f"xgf{pr}", f"xgh{pr}"), writes=(f"xgl{pr}",))
        for c in range(8):
            sh = WG_hi[:, c * GM:(c + 1) * GM]
            slo = WG_lo[:, c * GM:(c + 1) * GM]
            mh = xgh[:, c * GSL:(c + 1) * GSL]
            ml = xgl[:, c * GSL:(c + 1) * GSL]
            mm(f"g{sl}", ps_g[:, go:go + GSL], sh, mh,
               reads=(f"WGhi{c}", f"xgh{pr}"),
               wr=f"psG{pr}", start=(c == 0), stop=False)
            mm(f"g{sl}", ps_g[:, go:go + GSL], sh, ml,
               reads=(f"WGhi{c}", f"xgl{pr}"),
               wr=f"psG{pr}", start=False, stop=False)
            mm(f"g{sl}", ps_g[:, go:go + GSL], slo, mh,
               reads=(f"WGlo{c}", f"xgh{pr}"),
               wr=f"psG{pr}", start=False, stop=(c == 7))
        p.op("ACT", lambda o=g80[0:16, c0:c0 + GSL], go=go:
             A.activation(o, ps_g[0:16, go:go + GSL], AF.Identity,
                          bias=bG[0:16, :]),
             reads=(f"psG{pr}", "bG"), writes=(f"gB{sl % 8}",))
        p.op("ACT", lambda o=g80[32:48, c0:c0 + GSL], go=go:
             A.activation(o, ps_g[32:48, go:go + GSL], AF.Sigmoid,
                          bias=bG[32:48, :]),
             reads=(f"psG{pr}", "bG"), writes=(f"gI{sl % 8}",))
        p.op("ACT", lambda o=g80[64:80, c0:c0 + GSL], go=go:
             A.activation(o, ps_g[64:80, go:go + GSL], AF.Sigmoid,
                          bias=bG[64:80, :]),
             reads=(f"psG{pr}", "bG"), writes=(f"gF{sl % 8}",))
        pump_p1(9)
    gr = tuple(f"gB{i}" for i in range(8)) + tuple(f"gI{i}" for i in range(8)) + tuple(f"gF{i}" for i in range(8))
    if debug:
        store(dbg["d_gates"][0:16, :], g80[0:16, :], gr, "d_gates")
        store(dbg["d_gates"][16:32, :], g80[32:48, :], gr, "d_gates")
        store(dbg["d_gates"][32:48, :], g80[64:80, :], gr, "d_gates")

    hs_e = SB("hs_e", [S, EXT], stack=stA)
    TH = T // 2
    uu = g80[0:16, :]
    store(xgfs[0][0:16, :TH], g80[32:48, 0:TH], gr, "igc", q="SP")
    store(xgfs[1][0:16, :TH], g80[32:48, TH:T], gr, "igc2", q="SP")
    p.op("DVE", lambda: V.tensor_mul(uu[:, 0:TH], xgfs[0][0:16, :TH],
                                     g80[0:16, 0:TH]),
         reads=gr + ("igc",), writes=("uu",) + tuple(f"gB{i}" for i in range(8)))
    p.op("DVE", lambda: V.tensor_mul(uu[:, TH:T], xgfs[1][0:16, :TH],
                                     g80[0:16, TH:T]),
         reads=gr + ("igc2", "uu"), writes=("uu",))
    p.op("DVE", lambda: V.memset(hs_buf[:, 0:2], 0.0), writes=("hs",))
    p.op("DVE", lambda: V.memset(hs_buf[:, T + 2:T + 4], 0.0), writes=("hs",))
    store(xgfs[0][0:16, :TH], g80[64:80, 0:TH], gr + ("uu", "igc"), "fgc",
          q="SP")
    store(xgfs[1][0:16, :TH], g80[64:80, TH:T], gr + ("uu", "igc2"), "fgc2",
          q="SP")
    p.op("DVE", lambda: V.tensor_tensor_scan(hs_buf[:, 2:TH + 2],
                                             xgfs[0][0:16, :TH],
                                             uu[:, 0:TH], 0.0,
                                             OP.mult, OP.add),
         reads=gr + ("uu", "fgc"), writes=("hs",))
    p.op("DVE", lambda: V.tensor_tensor_scan(hs_buf[:, TH + 2:T + 2],
                                             xgfs[1][0:16, :TH],
                                             uu[:, TH:T],
                                             hs_buf[:, TH + 1:TH + 2],
                                             OP.mult, OP.add),
         reads=gr + ("uu", "fgc2", "hs"), writes=("hs",))
    p.op("DVE", lambda: V.tensor_scalar(hs_e[:], hs_buf[:, 0:EXT], s_hsel0[:],
                                        None, OP.mult),
         reads=("hs", "s_hsel0"), writes=("hs_e",))
    p.op("DVE", lambda: V.scalar_tensor_tensor(hs_e[:], hs_buf[:, 2048:2048 + EXT],
                                               s_hsel1[:], hs_e[:], OP.mult, OP.add),
         reads=("hs", "s_hsel1", "hs_e"), writes=("hs_e",))
    if debug:
        store(dbg["d_hs"][:], hs_e[:], ("hs_e",), "d_hs")
    p.op("ACT", lambda: A.activation(hsA_hi[0:S, :], hs_e[:], AF.Copy),
         reads=("hs_e",), writes=("hsAhi",))
    p.op("DVE", lambda: V.tensor_sub(hsA_lo[0:S, :], hs_e[:], hsA_hi[0:S, :]),
         reads=("hs_e", "hsAhi"), writes=("hsAlo",))
    load(hsA_hi[S:S + 1, :], onesrow_ext[:], "hsAhi")
    load(hsA_lo[S:S + 1, :], zrow_ext[:], "hsAlo")

    pump_p1(10 ** 9)

    stA.close()
    p.barrier()

    stB = contextlib.ExitStack()
    Wq = {}
    for nm in ("Win", "WD", "Wout"):
        Wq[nm] = (SB(nm + "_hi", [128, 8 * H], BF16, stack=stB),
                  SB(nm + "_lo", [128, 8 * H], BF16, stack=stB))
    xtm = SB("xtm", [128, H], stack=stB)
    xb0 = SB("xb0", [128, 8 * AWM], stack=stB)
    xb1 = SB("xb1", [128, 8 * AWM], stack=stB)
    zS2 = (SB("zS0", [128, 8 * AWM], stack=stB),
           SB("zS1", [128, 8 * AWM], stack=stB))
    xp_f = SB("xp_f", [128, 8 * AWM], stack=stB)
    yf_c = SB("yf_c", [128, AWM], stack=stB)
    wi = 0
    for nm, src in (("Win", W_in), ("WD", W_D), ("Wout", W_out)):
        hi, lo = Wq[nm]
        for c in range(8):
            pr = wi % 2
            wi += 1
            wtmp = (xb0, xb1)[pr][:, 0:H]
            load(wtmp, src[c * 128:(c + 1) * 128, :], f"xb{pr}")
            if pr == 0:
                p.op("ACT", lambda o=hi[:, c * H:(c + 1) * H], a=wtmp:
                     A.activation(o, a, AF.Copy),
                     reads=(f"xb{pr}",), writes=(f"{nm}hi{c}",))
            else:
                p.op("DVE", lambda o=hi[:, c * H:(c + 1) * H], a=wtmp:
                     V.tensor_copy(o, a),
                     reads=(f"xb{pr}",), writes=(f"{nm}hi{c}",))
            p.op("GP", lambda o=lo[:, c * H:(c + 1) * H], a=wtmp,
                 b=hi[:, c * H:(c + 1) * H]: GP.tensor_sub(o, a, b),
                 reads=(f"xb{pr}", f"{nm}hi{c}"), writes=(f"{nm}lo{c}",))

    pA_h = SB("pA_h", [128, 8 * AWM], BF16, stack=stB)
    pA_l = SB("pA_l", [128, 8 * AWM], BF16, stack=stB)
    pB_h = SB("pB_h", [128, 8 * AWM], BF16, stack=stB)
    pB_l = SB("pB_l", [128, 8 * AWM], BF16, stack=stB)
    z_tm = SB("z_tm", [128, H], stack=stB)
    zsq = SB("zsq", [128, H], stack=stB)
    s_s = z_tm
    sc_t = SB("sc_t", [128, H], stack=stB)
    r1t = z_tm
    q_i = SB("q_i", [128, H], I16, stack=stB)
    q_b = SB("q_b", [128, H], BF16, stack=stB)
    colz = SB("colz", [128, 16], stack=stB)
    if SAFE_ROUND:
        rr_t = SB("rr_t", [128, H], stack=stB)
        cp_t = SB("cp_t", [128, H], stack=stB)
    ps_a = PSUM("ps_a", [128, 1024], stack=stB)
    ps_tr = PSUM("ps_tr", [128, 1024], stack=stB)

    lgen = [None]

    def lpump(n):
        if lgen[0] is None:
            return
        for _ in range(n):
            try:
                next(lgen[0])
            except StopIteration:
                lgen[0] = None
                break

    def attn_mm(gi):
        g0, g1 = AGROUPS[gi]
        w = g1 - g0
        tg = f"A{gi}"
        zS = zS2[gi % 2]
        ZR = f"zS{gi % 2}"
        xb = (xb0, xb1)[gi % 2]
        xbr = f"xb{gi % 2}"
        load_fm(xb, w, x_fm[:, g0:g1], xbr, stride=AWM)
        for k in range(8):
            p.op("ACT", lambda k=k: A.activation(
                pA_h[:, k * AWM:k * AWM + w], xb[:, k * AWM:k * AWM + w],
                AF.Copy),
                 reads=(xbr,), writes=(f"pAh{k}",))
            p.op("DVE", lambda k=k: V.tensor_sub(
                pA_l[:, k * AWM:k * AWM + w], xb[:, k * AWM:k * AWM + w],
                pA_h[:, k * AWM:k * AWM + w]),
                 reads=(xbr, f"pAh{k}"), writes=(f"pAl{k}",))

        for m in range(8):
            od = ps_a[0:128, (m % 2) * 512:(m % 2) * 512 + w]
            for k in range(8):
                sh = Wq["Win"][0][:, k * H + m * 128:k * H + m * 128 + 128]
                sl = Wq["Win"][1][:, k * H + m * 128:k * H + m * 128 + 128]
                mh = pA_h[:, k * AWM:k * AWM + w]
                ml = pA_l[:, k * AWM:k * AWM + w]
                mm(tg + f"xp{m}", od, sh, mh, reads=(f"Winhi{k}", f"pAh{k}"),
                   wr=f"psA{m % 2}", start=(k == 0), stop=False)
                mm(tg + f"xp{m}", od, sh, ml, reads=(f"Winhi{k}", f"pAl{k}"),
                   wr=f"psA{m % 2}", start=False, stop=False)
                mm(tg + f"xp{m}", od, sl, mh, reads=(f"Winlo{k}", f"pAh{k}"),
                   wr=f"psA{m % 2}", start=False, stop=(k == 7))
            p.op("DVE", lambda o=xp_f[:, m * AWM:m * AWM + w], i=od,
                 bc=s_binc[:, m:m + 1], mm_=m:
                 V.tensor_scalar(o, i, bc, None, OP.add),
                 reads=(f"psA{m % 2}", "s_binc"), writes=(f"xpf{m}",))
            p.op("ACT", lambda m=m: A.activation(
                pB_h[:, m * AWM:m * AWM + w], xp_f[:, m * AWM:m * AWM + w],
                AF.Copy),
                 reads=(f"xpf{m}",), writes=(f"pBh{m}",))
            p.op("DVE", lambda m=m: V.tensor_sub(
                pB_l[:, m * AWM:m * AWM + w], xp_f[:, m * AWM:m * AWM + w],
                pB_h[:, m * AWM:m * AWM + w]),
                 reads=(f"xpf{m}", f"pBh{m}"), writes=(f"pBl{m}",))
            lpump(1)
        if debug and gi == 0:
            for m in range(8):
                store(dbg["d_xp"][:, m * w:(m + 1) * w],
                      xp_f[:, m * AWM:m * AWM + w], (f"xpf{m}",), "d_xp")

        for m in range(8):
            od = ps_a[0:128, (m % 2) * 512:(m % 2) * 512 + w]
            for k in range(8):
                sh = Wq["WD"][0][:, k * H + m * 128:k * H + m * 128 + 128]
                sl = Wq["WD"][1][:, k * H + m * 128:k * H + m * 128 + 128]
                mh = pB_h[:, k * AWM:k * AWM + w]
                ml = pB_l[:, k * AWM:k * AWM + w]
                mm(tg + f"y{m}", od, sh, mh, reads=(f"WDhi{k}", f"pBh{k}"),
                   wr=f"psA{m % 2}", start=(k == 0), stop=False)
                mm(tg + f"y{m}", od, sh, ml, reads=(f"WDhi{k}", f"pBl{k}"),
                   wr=f"psA{m % 2}", start=False, stop=False)
                mm(tg + f"y{m}", od, sl, mh, reads=(f"WDlo{k}", f"pBh{k}"),
                   wr=f"psA{m % 2}", start=False, stop=False)
            mm(tg + f"y{m}", od, CA_hi[:, m * 128:(m + 1) * 128],
               hsA_hi[:, g0:g1], reads=("CAhi", "hsAhi"),
               wr=f"psA{m % 2}", start=False, stop=False)
            mm(tg + f"y{m}", od, CA_hi[:, m * 128:(m + 1) * 128],
               hsA_lo[:, g0:g1], reads=("CAhi", "hsAlo"),
               wr=f"psA{m % 2}", start=False, stop=False)
            mm(tg + f"y{m}", od, CA_lo[:, m * 128:(m + 1) * 128],
               hsA_hi[:, g0:g1], reads=("CAlo", "hsAhi"),
               wr=f"psA{m % 2}", start=False, stop=True)
            p.op("ACT", lambda i=od, mm_=m: A.activation(yf_c[:, 0:w], i, AF.Tanh),
                 reads=(f"psA{m % 2}",), writes=("yf_c",))
            if debug and gi == 0:
                store(dbg["d_y"][:, m * w:(m + 1) * w], yf_c[:, 0:w],
                      ("yf_c",), "d_y")
            p.op("ACT", lambda m=m: A.activation(
                pA_h[:, m * AWM:m * AWM + w], yf_c[:, 0:w], AF.Copy),
                 reads=("yf_c",), writes=(f"pAh{m}",))
            p.op("DVE", lambda m=m: V.tensor_sub(
                pA_l[:, m * AWM:m * AWM + w], yf_c[:, 0:w],
                pA_h[:, m * AWM:m * AWM + w]),
                 reads=("yf_c", f"pAh{m}"), writes=(f"pAl{m}",))

        for m in range(8):
            od = ps_a[0:128, (m % 2) * 512:(m % 2) * 512 + w]
            for k in range(8):
                sh = Wq["Wout"][0][:, k * H + m * 128:k * H + m * 128 + 128]
                sl = Wq["Wout"][1][:, k * H + m * 128:k * H + m * 128 + 128]
                mh = pA_h[:, k * AWM:k * AWM + w]
                ml = pA_l[:, k * AWM:k * AWM + w]
                mm(tg + f"z{m}", od, sh, mh, reads=(f"Wouthi{k}", f"pAh{k}"),
                   wr=f"psA{m % 2}", start=(k == 0), stop=False)
                mm(tg + f"z{m}", od, sh, ml, reads=(f"Wouthi{k}", f"pAl{k}"),
                   wr=f"psA{m % 2}", start=False, stop=False)
                mm(tg + f"z{m}", od, sl, mh, reads=(f"Woutlo{k}", f"pAh{k}"),
                   wr=f"psA{m % 2}", start=False, stop=(k == 7))
            p.op("DVE", lambda o=zS[:, m * AWM:m * AWM + w], i=od,
                 x=xp_f[:, m * AWM:m * AWM + w], bc=s_boutc[:, m:m + 1], mm_=m:
                 V.scalar_tensor_tensor(o, i, bc, x, OP.add, OP.add),
                 reads=(f"psA{m % 2}", f"xpf{m}", "s_boutc"), writes=(ZR,))
            lpump(1)
        if debug and gi == 0:
            for m in range(8):
                store(dbg["d_z"][:, m * w:(m + 1) * w],
                      zS[:, m * AWM:m * AWM + w], (ZR,), "d_z")

    def attn_ladder(gi):
        g0, g1 = AGROUPS[gi]
        w = g1 - g0
        tg = f"A{gi}"
        zS = zS2[gi % 2]
        ZR = f"zS{gi % 2}"
        nsub = (w + 127) // 128
        for si in range(nsub):
            u0 = g0 + si * 128
            u1 = min(g1, u0 + 128)
            uw = u1 - u0
            ti = u0 // 128
            sg = f"{tg}s{si}"
            for m_ in range(8):
                p.op("PE", lambda o=ps_tr[0:uw, m_ * 128:(m_ + 1) * 128],
                     i=zS[:, m_ * AWM + si * 128:m_ * AWM + si * 128 + uw]:
                     PE.transpose(o, i, s_idf[:]),
                     reads=(ZR, "s_idf"), writes=("psTR",), group=sg + "tr")
            p.op("DVE", lambda e=uw: V.tensor_scalar(
                z_tm[0:e, :], ps_tr[0:e, :], 1.0, None, OP.mult, OP.add,
                accum_out=colz[0:e, 0:1]),
                 reads=("psTR",), writes=("z_tm", "colz"))
            p.op("ACT", lambda e=uw: A.activation(zsq[0:e, :], z_tm[0:e, :],
                                                  AF.Square,
                                                  accum_out=colz[0:e, 1:2]),
                 reads=("z_tm",), writes=("zsq", "colz"))
            p.op("DVE", lambda e=uw: V.tensor_scalar(colz[0:e, 2:3],
                                                     colz[0:e, 0:1], 1.0 / H,
                                                     None, OP.mult),
                 reads=("colz",), writes=("colz",))
            p.op("DVE", lambda e=uw: V.tensor_mul(colz[0:e, 3:4], colz[0:e, 2:3],
                                                  colz[0:e, 2:3]),
                 reads=("colz",), writes=("colz",))
            p.op("DVE", lambda e=uw: V.scalar_tensor_tensor(
                colz[0:e, 4:5], colz[0:e, 1:2], 1.0 / H, colz[0:e, 3:4],
                OP.mult, OP.subtract),
                 reads=("colz",), writes=("colz",))
            p.op("DVE", lambda e=uw: V.tensor_scalar(colz[0:e, 4:5],
                                                     colz[0:e, 4:5], EPS, None,
                                                     OP.add),
                 reads=("colz",), writes=("colz",))
            p.op("ACT", lambda e=uw: A.activation(colz[0:e, 5:6], colz[0:e, 4:5],
                                                  AF.Sqrt),
                 reads=("colz",), writes=("colz",))
            p.op("DVE", lambda e=uw: V.reciprocal(colz[0:e, 6:7], colz[0:e, 5:6]),
                 reads=("colz",), writes=("colz",))
            yield
            p.op("DVE", lambda e=uw: V.tensor_scalar(sc_t[0:e, :], z_tm[0:e, :],
                                                     colz[0:e, 2:3],
                                                     colz[0:e, 6:7],
                                                     OP.subtract, OP.mult),
                 reads=("z_tm", "colz"), writes=("sc_t",))
            load(xtm[0:uw, :], x_tm[u0:u1, :], "xtm", q="ACT")
            p.op("GP", lambda e=uw: GP.tensor_mul(sc_t[0:e, :], sc_t[0:e, :],
                                                  s_gamb[0:e, :]),
                 reads=("sc_t", "s_gamb"), writes=("sc_t",))
            p.op("GP", lambda e=uw: GP.tensor_add(xtm[0:e, :], xtm[0:e, :],
                                                  s_betb[0:e, :]),
                 reads=("xtm", "s_betb"), writes=("xtm",))
            p.op("DVE", lambda e=uw: V.tensor_add(s_s[0:e, :], sc_t[0:e, :],
                                                  xtm[0:e, :]),
                 reads=("sc_t", "xtm"), writes=("z_tm",))
            yield
            p.op("DVE", lambda e=uw: V.tensor_reduce(colz[0:e, 7:8], s_s[0:e, :],
                                                     AX, OP.add),
                 reads=("z_tm",), writes=("colz",))
            p.op("ACT", lambda e=uw: A.activation(zsq[0:e, :], s_s[0:e, :],
                                                  AF.Square,
                                                  accum_out=colz[0:e, 8:9]),
                 reads=("z_tm",), writes=("zsq", "colz"))
            p.op("DVE", lambda e=uw: V.tensor_scalar(colz[0:e, 9:10],
                                                     colz[0:e, 7:8], 1.0 / H,
                                                     None, OP.mult),
                 reads=("colz",), writes=("colz",))
            p.op("DVE", lambda e=uw: V.tensor_mul(colz[0:e, 10:11],
                                                  colz[0:e, 9:10],
                                                  colz[0:e, 9:10]),
                 reads=("colz",), writes=("colz",))
            p.op("DVE", lambda e=uw: V.scalar_tensor_tensor(
                colz[0:e, 10:11], colz[0:e, 8:9], 1.0 / H, colz[0:e, 10:11],
                OP.mult, OP.subtract),
                 reads=("colz",), writes=("colz",))
            p.op("DVE", lambda e=uw: V.tensor_scalar(colz[0:e, 10:11],
                                                     colz[0:e, 10:11], EPS,
                                                     None, OP.add),
                 reads=("colz",), writes=("colz",))
            p.op("ACT", lambda e=uw: A.activation(colz[0:e, 11:12],
                                                  colz[0:e, 10:11], AF.Sqrt),
                 reads=("colz",), writes=("colz",))
            p.op("DVE", lambda e=uw: V.reciprocal(colz[0:e, 12:13],
                                                  colz[0:e, 11:12]),
                 reads=("colz",), writes=("colz",))
            yield
            p.op("DVE", lambda e=uw: V.tensor_scalar(sc_t[0:e, :], s_s[0:e, :],
                                                     colz[0:e, 9:10], None,
                                                     OP.subtract),
                 reads=("z_tm", "colz"), writes=("sc_t",))
            p.op("DVE", lambda e=uw: V.tensor_reduce(colz[0:e, 13:14],
                                                     sc_t[0:e, :], AX, OP.max,
                                                     apply_absolute_value=True),
                 reads=("sc_t",), writes=("colz",))
            p.op("DVE", lambda e=uw: V.tensor_mul(colz[0:e, 13:14],
                                                  colz[0:e, 13:14],
                                                  colz[0:e, 12:13]),
                 reads=("colz",), writes=("colz",))
            p.op("DVE", lambda e=uw: V.tensor_scalar_max(colz[0:e, 13:14],
                                                         colz[0:e, 13:14], 1e-5),
                 reads=("colz",), writes=("colz",))
            p.op("DVE", lambda e=uw: V.tensor_scalar(colz[0:e, 14:15],
                                                     colz[0:e, 13:14],
                                                     1.0 / 127.0, None,
                                                     OP.mult),
                 reads=("colz",), writes=("colz",))
            p.op("DVE", lambda e=uw: V.reciprocal(colz[0:e, 15:16],
                                                  colz[0:e, 13:14]),
                 reads=("colz",), writes=("colz",))
            p.op("DVE", lambda e=uw: V.tensor_mul(colz[0:e, 15:16],
                                                  colz[0:e, 15:16],
                                                  colz[0:e, 12:13]),
                 reads=("colz",), writes=("colz",))
            p.op("DVE", lambda e=uw, t=ti: V.tensor_scalar(
                colz[0:e, 15:16], colz[0:e, 15:16], 127.0,
                s_qmask[0:e, t:t + 1], OP.mult, OP.mult),
                 reads=("colz", "s_qmask"), writes=("colz",))
            p.op("DVE", lambda e=uw: V.tensor_scalar(r1t[0:e, :], sc_t[0:e, :],
                                                     colz[0:e, 12:13], None,
                                                     OP.mult),
                 reads=("sc_t", "colz"), writes=("z_tm",))
            store(r1_hbm[u0:u1, :], r1t[0:uw, :], ("z_tm",), f"r1_{ti}")
            if debug:
                store(dbg["d_r1"][u0:u1, :], r1t[0:uw, :], ("z_tm",), "d_r1x")
            yield
            p.op("DVE", lambda e=uw: V.tensor_scalar(q_i[0:e, :], sc_t[0:e, :],
                                                     colz[0:e, 15:16], None,
                                                     OP.mult),
                 reads=("sc_t", "colz"), writes=("q_i",))
            if SAFE_ROUND:
                p.op("DVE", lambda e=uw: V.tensor_scalar(zsq[0:e, :],
                                                         sc_t[0:e, :],
                                                         colz[0:e, 15:16],
                                                         None, OP.mult),
                     reads=("sc_t", "colz"), writes=("zsq",))
                p.op("DVE", lambda e=uw: V.tensor_sub(rr_t[0:e, :], zsq[0:e, :],
                                                      q_i[0:e, :]),
                     reads=("zsq", "q_i"), writes=("rr_t",))
                p.op("DVE", lambda e=uw: V.tensor_scalar(cp_t[0:e, :],
                                                         rr_t[0:e, :], 0.5,
                                                         None, OP.is_gt),
                     reads=("rr_t",), writes=("cp_t",))
                p.op("DVE", lambda e=uw: V.scalar_tensor_tensor(
                    rr_t[0:e, :], rr_t[0:e, :], -0.5, cp_t[0:e, :], OP.is_lt,
                    OP.subtract),
                     reads=("rr_t", "cp_t"), writes=("rr_t",))
                p.op("DVE", lambda e=uw: V.tensor_sub(q_b[0:e, :], q_i[0:e, :],
                                                      rr_t[0:e, :]),
                     reads=("q_i", "rr_t"), writes=("q_b",))
            else:
                p.op("ACT", lambda e=uw: A.activation(q_b[0:e, :], q_i[0:e, :],
                                                      AF.Copy),
                     reads=("q_i",), writes=("q_b",))
            store(r1q_hbm[u0:u1, :], q_b[0:uw, :], ("q_b",), f"r1q_{ti}")
            p.op("DVE", lambda e=uw, t=ti: V.tensor_copy(dqcols[0:e, t:t + 1],
                                                         colz[0:e, 14:15]),
                 reads=("colz",), writes=("dqcols",))
            yield

    for gi in range(len(AGROUPS)):
        attn_mm(gi)
        lpump(10 ** 9)
        lgen[0] = attn_ladder(gi)
    lpump(10 ** 9)

    p.op("DVE", lambda: V.memset(q_b[:], 0.0), reads=(), writes=("q_b",))
    store(r1q_hbm[EXT:EXTP, :], q_b[0:EXTP - EXT, :], ("q_b",), "r1qpad")

    stB.close()
    stAB.close()
    p.barrier()

    stC = contextlib.ExitStack()
    w1q = SB("w1q", [128, 8 * DH], BF16, stack=stC)
    w2q = SB("w2q", [128, 32 * H], BF16, stack=stC)
    for c in range(8):
        load(w1q[:, c * DH:(c + 1) * DH], w1q_hbm[c * 128:(c + 1) * 128, :],
             f"w1q{c}", reads=(f"qw1{c}_0", f"qw1{c}_2048"))
    def load_w2q():
        for c in range(32):
            load(w2q[:, c * H:(c + 1) * H], w2q_hbm[c * 128:(c + 1) * 128, :],
                 f"w2q{c}", reads=(f"qw2{c}_0",))

    hbuf = SB("hbuf", [128, 32 * (GW + 2)], stack=stC)
    RQW = 512
    r1qg = SB("r1qg", [128, 8 * RQW], BF16, stack=stC)
    s2 = PSUM("ps_s2", [128, H], stack=stC)
    o_t = SB("o_t", [128, H], stack=stC)
    s2q = o_t
    r1g = o_t
    dqs = SB("dqs", [128, GW + 2], stack=stC)
    s2xs = SB("s2xs", [128, GW], stack=stC)
    pmax = SB("pmax", [128, GW], stack=stC)
    colfs = (SB("colfA", [128, 10], stack=stC),
             SB("colfB", [128, 10], stack=stC))
    rowf = SB("rowf", [1, GW + 2], stack=stC)
    if SAFE_ROUND:
        _sa = SB("sc_a0", [128, GW], stack=stC)
        _sd = SB("sc_d0", [128, GW], stack=stC)
        sc_as, sc_ds = (_sa, _sa), (_sd, _sd)

        def satag(m):
            return "sca0"

        def sdtag(m):
            return "scd0"
    else:
        sc_as = (SB("sc_a0", [128, GW], stack=stC),
                 SB("sc_a1", [128, GW], stack=stC))
        sc_ds = (SB("sc_d0", [128, GW], stack=stC),
                 SB("sc_d1", [128, GW], stack=stC))

        def satag(m):
            return f"sca{m % 2}"

        def sdtag(m):
            return f"scd{m % 2}"
    hq_i = SB("hq_i", [128, 32 * GW], I8, stack=stC)
    hq_b = SB("hq_b", [128, 32 * GW], BF16, stack=stC)
    if SAFE_ROUND:
        hrr = SB("hrr", [128, GW], stack=stC)
        hcp = SB("hcp", [128, GW], stack=stC)
    ps_m1 = PSUM("ps_m1", [128, 1024], stack=stC)
    ps_bc = PSUM("ps_bc", [128, 512], stack=stC)
    ps_o = PSUM("ps_o", [128, 1024], stack=stC)

    HB = "hb"
    HQ = "hq_b"

    def ffn_pars(gi):
        f0, f1 = FG[gi]
        fw = f1 - f0
        a0 = max(0, f0 - 1)
        a1 = min(EXT, f1 + 1)
        return f0, f1, fw, a0, a1, a1 - a0

    def ffn_conv_a(gi, m_):
        f0, f1, fw, a0, a1, aw = ffn_pars(gi)
        c_lo = max(f0, 1)
        c_hi = min(f1, EXT - 1)
        ce = c_hi - c_lo
        cs = c_lo - a0
        hm = hbuf[:, m_ * (GW + 2):m_ * (GW + 2) + aw]
        p.op("ACT", lambda h=hm, e=ce, c0=cs, mm_=m_:
             A.activation(sc_as[mm_ % 2][:, 0:e], h[:, c0:c0 + e],
                          AF.Identity, bias=s_convb[:, mm_:mm_ + 1],
                          scale=s_convw[:, 32 + mm_:33 + mm_]),
             reads=(HB + str(m_), "s_convw", "s_convb"),
             writes=(satag(m_),))
        p.op("ACT", lambda h=hm, e=ce, c0=cs, mm_=m_:
             A.activation(sc_ds[mm_ % 2][:, 0:e], h[:, c0 + 1:c0 + 1 + e],
                          AF.Identity, scale=s_convw[:, 64 + mm_:65 + mm_]),
             reads=(HB + str(m_), "s_convw"), writes=(sdtag(m_),))

    def ffn_conv_cd(gi, m_):
        f0, f1, fw, a0, a1, aw = ffn_pars(gi)
        c_lo = max(f0, 1)
        c_hi = min(f1, EXT - 1)
        ce = c_hi - c_lo
        cs = c_lo - a0
        hm = hbuf[:, m_ * (GW + 2):m_ * (GW + 2) + aw]
        p.op("DVE", lambda h=hm, e=ce, c0=cs, mm_=m_:
             V.scalar_tensor_tensor(sc_as[mm_ % 2][:, 0:e],
                                    h[:, c0 - 1:c0 - 1 + e],
                                    s_convw[:, mm_:mm_ + 1],
                                    sc_as[mm_ % 2][:, 0:e],
                                    OP.mult, OP.add),
             reads=(HB + str(m_), "s_convw", satag(m_)),
             writes=(satag(m_),))
        p.op("DVE", lambda h=hm, e=ce, c0=cs, mm_=m_:
             V.tensor_add(sc_ds[mm_ % 2][:, 0:e], sc_ds[mm_ % 2][:, 0:e],
                          sc_as[mm_ % 2][:, 0:e]),
             reads=(satag(m_), sdtag(m_)), writes=(sdtag(m_),))
        p.op("ACT", lambda h=hm, e=ce, c0=cs, mm_=m_:
             A.activation(h[:, c0:c0 + e], sc_ds[mm_ % 2][:, 0:e], AF.Silu),
             reads=(sdtag(m_),), writes=(HB + str(m_),))

    def ffn_conv_chunk(gi, m_):
        ffn_conv_a(gi, m_)
        ffn_conv_cd(gi, m_)

    def ffn_quant_chunk(gi, m_):
        colf = colfs[gi % 2]
        f0, f1, fw, a0, a1, aw = ffn_pars(gi)
        co = f0 - a0
        hm = hbuf[:, m_ * (GW + 2) + co:m_ * (GW + 2) + co + fw]
        if SAFE_ROUND:
            p.op("DVE", lambda h=hm, e=fw, mm_=m_:
                 V.scalar_tensor_tensor(sc_as[mm_ % 2][:, 0:e], h, 1.0,
                                        s2xs[:, 0:e], OP.mult, OP.mult),
                 reads=(HB + str(m_), "s2xs"), writes=(satag(m_),))
            p.op("DVE", lambda e=fw, mm_=m_: V.tensor_copy(
                hq_i[:, mm_ * GW:mm_ * GW + e], sc_as[mm_ % 2][:, 0:e]),
                 reads=(satag(m_),), writes=("hq_i",))
            p.op("DVE", lambda e=fw, mm_=m_: V.tensor_sub(
                hrr[:, 0:e], sc_as[mm_ % 2][:, 0:e],
                hq_i[:, mm_ * GW:mm_ * GW + e]),
                 reads=(satag(m_), "hq_i"), writes=("hrr",))
            p.op("DVE", lambda e=fw: V.tensor_scalar(hcp[:, 0:e], hrr[:, 0:e],
                                                     0.5, None, OP.is_gt),
                 reads=("hrr",), writes=("hcp",))
            p.op("DVE", lambda e=fw: V.scalar_tensor_tensor(
                hrr[:, 0:e], hrr[:, 0:e], -0.5, hcp[:, 0:e], OP.is_lt,
                OP.subtract),
                 reads=("hrr", "hcp"), writes=("hrr",))
            p.op("DVE", lambda e=fw, mm_=m_: V.tensor_sub(
                hq_b[:, mm_ * GW:mm_ * GW + e],
                hq_i[:, mm_ * GW:mm_ * GW + e], hrr[:, 0:e]),
                 reads=("hq_i", "hrr"), writes=(HQ,))
        else:
            p.op("DVE", lambda h=hm, e=fw, mm_=m_:
                 V.scalar_tensor_tensor(hq_i[:, mm_ * GW:mm_ * GW + e], h,
                                        1.0, s2xs[:, 0:e], OP.mult, OP.mult),
                 reads=(HB + str(m_), "s2xs"), writes=("hq_i",))

    def ffn_hq_cast(gi):
        if not SAFE_ROUND:
            p.op("ACT", lambda: A.activation(hq_b[:, :], hq_i[:, :], AF.Copy),
                 reads=("hq_i",), writes=(HQ,))

    def ffn_A1(gi, qprev, m2g=None):
        real = gi < len(FG)
        colf = colfs[gi % 2]
        CF = f"colf{gi % 2}"
        f0, f1, fw, a0, a1, aw = ffn_pars(min(gi, len(FG) - 1))
        tg = f"F{gi}"
        al0 = (a0 // 128) * 128
        al1 = min(EXTP, ((a1 + 127) // 128) * 128)
        alw = al1 - al0
        off = a0 - al0
        r1q_reads = tuple(f"r1q_{t}" for t in range(al0 // 128,
                                                    min(NT, (al1 + 127) // 128)))
        if al1 > EXT:
            r1q_reads = r1q_reads + ("r1qpad",)
        for c in range(8 if real else 0):
            p.op("SP", lambda d=r1qg[:, c * RQW:c * RQW + alw],
                 s=r1q_hbm[al0:al1, c * 128:(c + 1) * 128]:
                 SY.dma_start(out=d, in_=s, transpose=True),
                 reads=r1q_reads, writes=(f"r1qg{c}",), dma=True)
        for t in range(al0 // 128, min(NT, (a1 + 127) // 128) if real else 0):
            tl0, tl1 = TILES[t]
            uw_t = tl1 - tl0
            p.op("PE", lambda t=t, o0=tl0 - al0, e_=uw_t: PE.transpose(
                ps_bc[0:1, o0:o0 + e_], dqcols[0:e_, t:t + 1],
                s_idf[0:e_, 0:e_]),
                 reads=("dqcols", "s_idf"), writes=("psBC",), group=tg + "dqt")
        if real:
            p.op("DVE", lambda o0=a0 - al0: V.tensor_scalar(
                rowf[:, 0:aw], ps_bc[0:1, o0:o0 + aw], s_w1mean[0:1, :], None,
                OP.mult),
                 reads=("psBC", "s_w1mean"), writes=("rowf",))
            p.op("PE", lambda: PE.matmul(ps_bc[0:128, 0:aw], s_ones1[:],
                                         rowf[:, 0:aw], start=True, stop=True),
                 reads=("s_ones1", "rowf"), writes=("psBC",), group=tg + "bc")
            p.op("DVE", lambda: V.tensor_copy(dqs[:, 0:aw], ps_bc[0:128, 0:aw]),
                 reads=("psBC",), writes=("dqs",))
        inline_conv = not (debug and gi == 1)
        m2q = []
        m2_nsub = 0
        if m2g is not None:
            f0m, f1m = FG[m2g]
            m2_nsub = (f1m - f0m + 127) // 128
            for si in range(m2_nsub):
                m2q.extend(ffn_m2_mm_list(m2g, si))
            ffn_r1g_load(m2g, 0)
        m2i = 0
        if SAFE_ROUND:
            m2pump = (len(m2q) + 31) // 32 if m2q else 0
            for m_ in range(33):
                if m_ < 32:
                    if qprev is not None:
                        ffn_quant_chunk(qprev, m_)
                    for _ in range(m2pump):
                        if m2i < len(m2q):
                            if m2i == 64 and m2_nsub > 1:
                                for _x in ffn_m2_tail(m2g, 0):
                                    pass
                                ffn_r1g_load(m2g, 1)
                            m2q[m2i]()
                            m2i += 1
                    od = ps_m1[0:128, (m_ % 2) * 512:(m_ % 2) * 512 + aw]
                    for k in range(8):
                        mm(tg + f"m{m_}", od,
                           w1q[:, k * DH + m_ * 128:k * DH + m_ * 128 + 128],
                           r1qg[:, k * RQW + off:k * RQW + off + aw],
                           reads=(f"w1q{k}", f"r1qg{k}"),
                           wr=f"psM{m_ % 2}", start=(k == 0), stop=(k == 7))
                    p.op("DVE",
                         lambda o=hbuf[:, m_ * (GW + 2):m_ * (GW + 2) + aw],
                         i=od, d=dqs[:, 0:aw], mm_=m_:
                         V.scalar_tensor_tensor(o, i, 1.0, d, OP.mult, OP.mult),
                         reads=(f"psM{m_ % 2}", "dqs"), writes=(HB + str(m_),))
                    if inline_conv:
                        ffn_conv_a(gi, m_)
                        ffn_conv_cd(gi, m_)
            if m2g is not None:
                for _x in ffn_m2_tail(m2g, m2_nsub - 1):
                    pass
            if qprev is not None:
                ffn_hq_cast(qprev)
            if not inline_conv:
                store(dbg["d_h"][:], hbuf[:],
                      tuple(HB + str(i) for i in range(32)), "d_hx")
                for m_ in range(32):
                    ffn_conv_chunk(gi, m_)
            return

        a2hold = [ffn_A2(qprev) if qprev is not None else None]
        tails = []
        on_tail_done = []

        def pump_tails(n):
            for tgen in tails[:]:
                for _ in range(n):
                    try:
                        next(tgen)
                    except StopIteration:
                        tails.remove(tgen)
                        if on_tail_done:
                            on_tail_done.pop(0)()
                        break
        PRE = 9
        NIT = 33 + PRE
        m2pump = (len(m2q) + NIT - 2) // (NIT - 1) if m2q else 0
        for it in range(NIT):
            m_ = it - PRE
            if a2hold[0] is not None:
                for _ in range(4):
                    try:
                        next(a2hold[0])
                    except StopIteration:
                        a2hold[0] = None
                        break
            q_m = it - PRE + 1
            if qprev is not None and 0 <= q_m < 32:
                ffn_quant_chunk(qprev, q_m)
            for _ in range(m2pump):
                if m2i < len(m2q):
                    if m2i == 64 and m2_nsub > 1:
                        t0 = ffn_m2_tail(m2g, 0)
                        try:
                            next(t0)
                            tails.append(t0)
                            on_tail_done.append(
                                lambda: ffn_r1g_load(m2g, 1))
                        except StopIteration:
                            ffn_r1g_load(m2g, 1)
                    m2q[m2i]()
                    m2i += 1
                    if m2i == len(m2q):
                        t1 = ffn_m2_tail(m2g, m2_nsub - 1)
                        try:
                            next(t1)
                            tails.append(t1)
                        except StopIteration:
                            pass
            pump_tails(2)
            if real and 0 <= m_ < 32:
                od = ps_m1[0:128, (m_ % 2) * 512:(m_ % 2) * 512 + aw]
                for k in range(8):
                    mm(tg + f"m{m_}", od,
                       w1q[:, k * DH + m_ * 128:k * DH + m_ * 128 + 128],
                       r1qg[:, k * RQW + off:k * RQW + off + aw],
                       reads=(f"w1q{k}", f"r1qg{k}"),
                       wr=f"psM{m_ % 2}", start=(k == 0), stop=(k == 7))
                p.op("DVE", lambda o=hbuf[:, m_ * (GW + 2):m_ * (GW + 2) + aw],
                     i=od, d=dqs[:, 0:aw], mm_=m_:
                     V.scalar_tensor_tensor(o, i, 1.0, d, OP.mult, OP.mult),
                     reads=(f"psM{m_ % 2}", "dqs"), writes=(HB + str(m_),))
                if inline_conv:
                    ffn_conv_a(gi, m_)
            if real and inline_conv and m_ >= 1:
                ffn_conv_cd(gi, m_ - 1)
        while a2hold[0] is not None:
            try:
                next(a2hold[0])
            except StopIteration:
                a2hold[0] = None
        pump_tails(10 ** 9)
        if qprev is not None:
            ffn_hq_cast(qprev)
        if not inline_conv:
            store(dbg["d_h"][:], hbuf[:],
                  tuple(HB + str(i) for i in range(32)), "d_hx")
            for m_ in range(32):
                ffn_conv_chunk(gi, m_)

    def ffn_A2(gi):
        colf = colfs[gi % 2]
        CF = f"colf{gi % 2}"
        f0, f1, fw, a0, a1, aw = ffn_pars(gi)
        tg = f"F{gi}"
        co = f0 - a0
        p.op("DVE", lambda e=fw, c0=co: V.tensor_reduce(
            pmax[:, 0:e].rearrange("p (t o) -> p t o", o=1),
            hbuf[:, :].rearrange("p (c t) -> p t c", c=32)[:, c0:c0 + e, :],
            AX, OP.max, apply_absolute_value=True),
             reads=tuple(HB + str(i) for i in range(32)), writes=("pmax",))
        yield
        nsub = (fw + 127) // 128
        for si in range(nsub):
            u0 = f0 + si * 128
            u1 = min(f1, u0 + 128)
            uw = u1 - u0
            sg = tg + f"s{si}"
            p.op("PE", lambda e=uw, s0=si * 128: PE.transpose(
                ps_bc[0:e, 0:128], pmax[:, s0:s0 + e], s_idf[:]),
                 reads=("pmax", "s_idf"), writes=("psBC",), group=sg + "tp")
            yield
            p.op("DVE", lambda e=uw: V.tensor_reduce(colf[0:e, 0:1],
                                                     ps_bc[0:e, 0:128], AX,
                                                     OP.max),
                 reads=("psBC",), writes=(CF,))
            yield
            p.op("DVE", lambda e=uw: V.tensor_scalar_max(colf[0:e, 0:1],
                                                         colf[0:e, 0:1], 1e-5),
                 reads=(CF,), writes=(CF,))
            yield
            p.op("DVE", lambda e=uw: V.reciprocal(colf[0:e, 1:2], colf[0:e, 0:1]),
                 reads=(CF,), writes=(CF,))
            yield
            p.op("DVE", lambda e=uw: V.tensor_scalar(colf[0:e, 2:3],
                                                     colf[0:e, 1:2], 127.0,
                                                     None, OP.mult),
                 reads=(CF,), writes=(CF,))
            yield
            p.op("DVE", lambda e=uw, si=si: V.tensor_scalar(
                colf[0:e, 8 + si:9 + si], colf[0:e, 0:1], s_w2mean[0:e, :],
                1.0 / 127.0, OP.mult, OP.mult),
                 reads=(CF, "s_w2mean"), writes=(CF,))
            yield
            p.op("PE", lambda e=uw: PE.transpose(ps_bc[0:1, 0:e],
                                                 colf[0:e, 2:3],
                                                 s_idf[0:e, 0:e]),
                 reads=(CF, "s_idf"), writes=("psBC",), group=sg + "t2")
            yield
            p.op("DVE", lambda e=uw, s0=si * 128: V.tensor_copy(
                rowf[:, s0:s0 + e], ps_bc[0:1, 0:e]),
                 reads=("psBC",), writes=("rowf",))
            yield
        p.op("PE", lambda e=fw: PE.matmul(ps_bc[0:128, 0:e], s_ones1[:],
                                          rowf[:, 0:e], start=True, stop=True),
             reads=("s_ones1", "rowf"), writes=("psBC",), group=tg + "b2")
        yield
        p.op("DVE", lambda e=fw: V.tensor_copy(s2xs[:, 0:e], ps_bc[0:128, 0:e]),
             reads=("psBC",), writes=("s2xs",))
        yield

    def ffn_m2_sub(gi, si):
        f0, f1 = FG[gi]
        u0 = f0 + si * 128
        u1 = min(f1, u0 + 128)
        uw = u1 - u0
        out0 = max(2, u0)
        out1 = min(EXT - 2, u1)
        return u0, u1, uw, out0, out1

    def ffn_m2_mm_list(gi, si):
        tg = f"F{gi}"
        u0, u1, uw, out0, out1 = ffn_m2_sub(gi, si)
        if out1 <= out0:
            return []
        sg = tg + f"S{si}"
        ems = []
        for nh in range(2):
            od = ps_o[0:uw, nh * 512:(nh + 1) * 512]
            for c in range(32):
                sta = hq_b[:, c * GW + si * 128:c * GW + si * 128 + uw]
                mv = w2q[:, c * H + nh * 512:c * H + nh * 512 + 512]
                ems.append(lambda od=od, sta=sta, mv=mv, c=c, nh=nh:
                           mm(sg + f"o{nh}", od, sta, mv,
                              reads=(HQ, f"w2q{c}"), wr=f"psO{nh}",
                              start=(c == 0), stop=(c == 31)))
        return ems

    def ffn_r1g_load(gi, si):
        u0, u1, uw, out0, out1 = ffn_m2_sub(gi, si)
        if out1 <= out0:
            return
        load(r1g[0:uw, :], r1_hbm[u0:u1, :], "o_t", q="ACT",
             reads=(f"r1_{u0 // 128}",))

    def ffn_m2_tail(gi, si):
        colf = colfs[gi % 2]
        CF = f"colf{gi % 2}"
        u0, u1, uw, out0, out1 = ffn_m2_sub(gi, si)
        if out1 <= out0:
            return
        if True:
            p.op("DVE", lambda e=uw, si=si: V.scalar_tensor_tensor(
                s2[0:e, :], ps_o[0:e, :], colf[0:e, 8 + si:9 + si], r1g[0:e, :],
                OP.mult, OP.add, accum_out=colf[0:e, 4:5]),
                 reads=("psO0", "psO1", CF, "o_t"), writes=("s2", CF))
            yield
            p.op("ACT", lambda e=uw: A.activation(s2q[0:e, :], s2[0:e, :],
                                                  AF.Square,
                                                  accum_out=colf[0:e, 5:6]),
                 reads=("s2",), writes=("o_t", CF))
            yield
            p.op("DVE", lambda e=uw: V.tensor_scalar(colf[0:e, 4:5],
                                                     colf[0:e, 4:5], 1.0 / H,
                                                     None, OP.mult),
                 reads=(CF,), writes=(CF,))
            yield
            p.op("DVE", lambda e=uw: V.tensor_mul(colf[0:e, 6:7], colf[0:e, 4:5],
                                                  colf[0:e, 4:5]),
                 reads=(CF,), writes=(CF,))
            yield
            p.op("DVE", lambda e=uw: V.scalar_tensor_tensor(
                colf[0:e, 6:7], colf[0:e, 5:6], 1.0 / H, colf[0:e, 6:7],
                OP.mult, OP.subtract),
                 reads=(CF,), writes=(CF,))
            yield
            p.op("DVE", lambda e=uw: V.tensor_scalar(colf[0:e, 6:7],
                                                     colf[0:e, 6:7], EPS, None,
                                                     OP.add),
                 reads=(CF,), writes=(CF,))
            yield
            p.op("ACT", lambda e=uw: A.activation(colf[0:e, 7:8], colf[0:e, 6:7],
                                                  AF.Sqrt),
                 reads=(CF,), writes=(CF,))
            yield
            p.op("DVE", lambda e=uw: V.reciprocal(colf[0:e, 7:8], colf[0:e, 7:8]),
                 reads=(CF,), writes=(CF,))
            yield
            p.op("DVE", lambda e=uw: V.tensor_scalar(o_t[0:e, :], s2[0:e, :],
                                                     colf[0:e, 4:5],
                                                     colf[0:e, 7:8],
                                                     OP.subtract, OP.mult),
                 reads=("s2", CF), writes=("o_t",))
            yield
            store(out_tm[out0 - 2:out1 - 2, :], o_t[out0 - u0:out1 - u0, :],
                  ("o_t",), f"out{u0}")

    def ffn_m2_out(gi):
        f0, f1 = FG[gi]
        nsub = (f1 - f0 + 127) // 128
        for si in range(nsub):
            for em in ffn_m2_mm_list(gi, si):
                em()
            ffn_r1g_load(gi, si)
            for _ in ffn_m2_tail(gi, si):
                pass

    NG = len(FG)
    load_w2q_once = [load_w2q]
    NLOOP = NG if SAFE_ROUND else NG + 2
    for gi in range(NLOOP):
        qprev = gi - 1 if (gi >= 1 and gi - 1 < NG) else None
        m2g = gi - 2 if (gi >= 2 and not SAFE_ROUND) else None
        ffn_A1(gi, qprev, m2g)
        if load_w2q_once:
            load_w2q_once.pop()()
        if SAFE_ROUND:
            if gi >= 1:
                ffn_m2_out(gi - 1)
            for _ in ffn_A2(gi):
                pass
    if SAFE_ROUND:
        for m_ in range(32):
            ffn_quant_chunk(NG - 1, m_)
        ffn_hq_cast(NG - 1)
        ffn_m2_out(NG - 1)

    stC.close()
    root.close()
    n = p.finalize_and_emit()
    return nc, n, dbg


_NC_CACHE = {}


def _get_nc(debug=False):
    key = (debug, SAFE_ROUND)
    if key not in _NC_CACHE:
        _NC_CACHE[key] = build_nc(debug)
    return _NC_CACHE[key]


def make_in_maps(inputs, debug=False):
    f32 = np.float32
    K_EXT = EXT
    x = np.asarray(inputs["L0_attn_x"], f32)
    W_in = np.ascontiguousarray(inputs["L0_attn_W_in"], f32)
    W_B = np.ascontiguousarray(inputs["L0_attn_W_B"], f32)
    W_D = np.ascontiguousarray(inputs["L0_attn_W_D"], f32)
    W_out = np.ascontiguousarray(inputs["L0_attn_W_out"], f32)
    B_mat = np.ascontiguousarray(inputs["L0_attn_B_mat"], f32)
    C_mat = np.ascontiguousarray(inputs["L0_attn_C_mat"], f32)
    W_ig = np.ascontiguousarray(inputs["L0_attn_W_ig"], f32)
    W_fg = np.ascontiguousarray(inputs["L0_attn_W_fg"], f32)
    b_in = np.asarray(inputs["L0_attn_b_in"], f32)
    b_B = np.asarray(inputs["L0_attn_b_B"], f32)
    b_D = np.asarray(inputs["L0_attn_b_D"], f32)
    b_out = np.asarray(inputs["L0_attn_b_out"], f32)
    b_ig = np.asarray(inputs["L0_attn_b_ig"], f32)
    b_fg = np.asarray(inputs["L0_attn_b_fg"], f32)
    gamma = np.asarray(inputs["L0_attn_gamma"], f32)
    beta = np.asarray(inputs["L0_attn_beta"], f32)
    w1 = np.asarray(inputs["L0_ffn_w1"], f32)
    conv_w = np.asarray(inputs["L0_ffn_conv_w"], f32)
    conv_b = np.asarray(inputs["L0_ffn_conv_b"], f32)
    w2 = np.asarray(inputs["L0_ffn_w2"], f32)

    shared = {
        "W_in": W_in, "W_D": W_D, "W_out": W_out,
        "W_in_T": np.ascontiguousarray(W_in.T),
        "W_B_T": np.ascontiguousarray(W_B.T),
        "B_mat": B_mat, "C_mat": C_mat, "W_ig": W_ig, "W_fg": W_fg,
        "b_in_cols": np.ascontiguousarray(b_in.reshape(8, 128).T),
        "b_B_cols": np.ascontiguousarray(b_B.reshape(8, 128).T),
        "b_out_cols": np.ascontiguousarray(b_out.reshape(8, 128).T),
        "b_ig_col": np.ascontiguousarray(b_ig.reshape(S, 1)),
        "b_fg_col": np.ascontiguousarray(b_fg.reshape(S, 1)),
        "gamma_b": np.ascontiguousarray(np.broadcast_to(gamma[None, :], (128, H))),
        "beta_b": np.ascontiguousarray(np.broadcast_to(beta[None, :], (128, H))),
        "bD_row": np.ascontiguousarray(b_D.reshape(1, H)),
        "convw": np.ascontiguousarray(
            conv_w[:, 0, :].reshape(32, 128, 3).transpose(1, 2, 0).reshape(128, 96)),
        "convb": np.ascontiguousarray(conv_b.reshape(32, 128).T),
        "w1t_f": np.ascontiguousarray(w1.T),
        "w2t_f": np.ascontiguousarray(w2.T),
        "ones1": np.ones((1, 128), f32),
        "ones_col": np.ones((128, 1), f32),
        "id_f32": np.eye(128, dtype=f32),
        "onesrow_ext": np.ones((1, K_EXT), ml_dtypes.bfloat16),
        "zrow_ext": np.zeros((1, K_EXT), ml_dtypes.bfloat16),
    }
    in_maps = []
    for c in range(8):
        b, half = c // 2, c % 2
        s = half * 2048
        lo, hi = s - 2, s + 2048 + 2
        xe = np.zeros((EXT, H), f32)
        a, bb = max(0, lo), min(T, hi)
        xe[a - lo:bb - lo] = x[b, a:bb]
        qm = np.zeros((EXTP,), f32)
        qm[a - lo:bb - lo] = 1.0
        m = dict(shared)
        m["x_tm"] = xe
        m["x_fm"] = np.ascontiguousarray(xe.T)
        m["xg_fm"] = np.ascontiguousarray(x[b].T)
        m["qmask"] = np.ascontiguousarray(qm.reshape(NT, 128).T)
        m["hsel0"] = np.full((S, 1), 1.0 - half, f32)
        m["hsel1"] = np.full((S, 1), float(half), f32)
        in_maps.append(m)
    return in_maps


def kernel(**inputs):
    nc, _, _ = _get_nc(False)
    in_maps = make_in_maps(inputs)
    res = run_bass_kernel_spmd(nc, in_maps, core_ids=list(range(8)))
    out = np.empty((B, T, H), np.float32)
    for c in range(8):
        b, half = c // 2, c % 2
        out[b, half * 2048:(half + 1) * 2048] = res.results[c]["out_tm"]
    return out


# version 39
# speedup vs baseline: 1.0880x; 1.0880x over previous
import contextlib

import numpy as np
import ml_dtypes

import concourse.bass as bass
import concourse.mybir as mybir
from concourse.bass_isa import ReduceOp
from concourse.bass_utils import run_bass_kernel_spmd

F32 = mybir.dt.float32
BF16 = mybir.dt.bfloat16
I32 = mybir.dt.int32
I16 = mybir.dt.int16
I8 = mybir.dt.int8
AF = mybir.ActivationFunctionType
OP = mybir.AluOpType
AX = mybir.AxisListType.X

B, T, H, S, DH = 4, 4096, 1024, 16, 4096
EXT = T // 2 + 4
EXTP = 2176
EPS = 1e-5
SAFE_ROUND = False

NT = (EXT + 127) // 128
TILES = [(i * 128, min(EXT, (i + 1) * 128)) for i in range(NT)]
AW = 256
AGROUPS = [(i * AW, (i + 1) * AW) for i in range(7)] + [(7 * AW, EXT)]
AWM = EXT - 7 * AW
GW = 256
FG = [(a, min(a + GW, EXT)) for a in range(0, EXT, GW)]

ENGINES = ("SP", "PE", "DVE", "ACT", "GP")
_ENG_ATTR = {"SP": "sync", "PE": "tensor", "DVE": "vector", "ACT": "scalar",
             "GP": "gpsimd"}


class Prog:

    def __init__(self, nc):
        self.nc = nc
        self.ops = []

    def op(self, eng, fn, reads=(), writes=(), dma=False, group=None):
        self.ops.append(dict(eng=eng, fn=fn, reads=tuple(reads),
                             writes=tuple(writes), dma=dma, group=group,
                             inc=None, sem=None, barrier=False))

    def barrier(self):
        self.ops.append(dict(eng="SP", fn=None, reads=(), writes=(), dma=False,
                             group=None, inc=None, sem=None, barrier=True))

    def finalize_and_emit(self):
        nc, ops = self.nc, self.ops
        last_w, readers = {}, {}
        deps_l = []
        last_per_eng = {}
        last_dma_sem = {}
        dma_i = 0
        NDMA_ = 48
        last_barrier = None
        for i, o in enumerate(ops):
            deps = set()
            if o["barrier"]:
                deps.update(v for v in last_per_eng.values() if v is not None)
                deps.update(v for v in last_dma_sem.values() if v is not None)
                if last_barrier is not None:
                    deps.add(last_barrier)
                deps.discard(i)
                deps_l.append(deps)
                last_barrier = i
                last_per_eng = {}
                last_dma_sem = {}
                continue
            if last_barrier is not None:
                deps.add(last_barrier)
            if o["dma"]:
                last_dma_sem[dma_i % NDMA_] = i
                dma_i += 1
            last_per_eng[(o["eng"], o["dma"])] = i
            for r in o["reads"]:
                if r in last_w:
                    deps.add(last_w[r][0])
            for w in o["writes"]:
                if w in last_w:
                    wi, wg = last_w[w]
                    if not (o["group"] is not None and o["group"] == wg):
                        deps.add(wi)
                deps.update(readers.get(w, ()))
            deps.discard(i)
            deps_l.append(deps)
            for w in o["writes"]:
                last_w[w] = (i, o["group"])
                readers[w] = []
            for r in o["reads"]:
                readers.setdefault(r, []).append(i)

        need_inc = [False] * len(ops)
        for ds in deps_l:
            for d in ds:
                need_inc[d] = True

        es = contextlib.ExitStack()
        chain = {e: es.enter_context(nc.semaphore(f"ch_{e}")) for e in ENGINES}
        NDMA = 48
        dpool = [es.enter_context(nc.semaphore(f"dma{i}")) for i in range(NDMA)]
        cnt = {e: 0 for e in ENGINES}
        dcnt = [0] * NDMA
        dix = 0
        for i, o in enumerate(ops):
            if o["dma"]:
                j = dix % NDMA
                dix += 1
                o["pre_wait"] = (dpool[j], dcnt[j]) if dcnt[j] else None
                dcnt[j] += 16
                o["sem"] = (f"D{j}", dpool[j])
                o["inc"] = dcnt[j]
            elif need_inc[i]:
                cnt[o["eng"]] += 1
                o["sem"] = ("C" + o["eng"], chain[o["eng"]])
                o["inc"] = cnt[o["eng"]]

        know = [None] * len(ops)
        know_is = [None] * len(ops)
        eng_last = {e: None for e in ENGINES}
        waits_l = [None] * len(ops)
        for i, o in enumerate(ops):
            k = (dict(know_is[eng_last[o["eng"]]])
                 if eng_last[o["eng"]] is not None else {})
            waits = []
            for d in sorted(deps_l[i]):
                do = ops[d]
                skey, shandle = do["sem"]
                val = do["inc"]
                if k.get(skey, 0) < val:
                    waits.append((shandle, val))
                    k[skey] = val
                    for kk, vv in know[d].items():
                        if k.get(kk, 0) < vv:
                            k[kk] = vv
            if o.get("pre_wait") is not None:
                sh, val = o["pre_wait"]
                skey = o["sem"][0]
                if k.get(skey, 0) < val:
                    waits.append((sh, val))
                    k[skey] = val
            if o["sem"] is not None:
                know[i] = dict(k)
                skey = o["sem"][0]
                know[i][skey] = max(know[i].get(skey, 0), o["inc"])
            else:
                know[i] = k
            know_is[i] = k
            eng_last[o["eng"]] = i
            waits_l[i] = waits

        with nc.Block() as block:
            for eng in ENGINES:
                idxs = [i for i, o in enumerate(ops) if o["eng"] == eng]
                if not idxs and eng != "SP":
                    continue

                def body(handle, eng=eng, idxs=idxs):
                    for i in idxs:
                        o = ops[i]
                        for sh, val in waits_l[i]:
                            handle.wait_ge(sh, val)
                        instr = o["fn"]() if o["fn"] is not None else handle.nop()
                        if o["sem"] is not None:
                            instr.then_inc(o["sem"][1], 16 if o["dma"] else 1)
                    if eng == "SP":
                        for e2 in ENGINES:
                            if cnt.get(e2, 0):
                                handle.wait_ge(chain[e2], cnt[e2])
                        for j in range(NDMA):
                            if dcnt[j]:
                                handle.wait_ge(dpool[j], dcnt[j])

                getattr(block, _ENG_ATTR[eng])(body)
        self._sem_stack = es
        return len(ops)


def build_nc(debug=False):
    nc = bass.Bass("TRN2", target_bir_lowering=False, debug=False)
    p = Prog(nc)
    V, A, PE, GP, SY = nc.vector, nc.scalar, nc.tensor, nc.gpsimd, nc.sync
    QENG = {"SP": SY, "ACT": A, "GP": GP}

    def din(name, shape, dt=F32):
        return nc.dram_tensor(name, shape, dt, kind="ExternalInput").ap()

    x_tm = din("x_tm", [EXT, H])
    x_fm = din("x_fm", [H, EXT])
    xg_fm = din("xg_fm", [H, T])
    W_in = din("W_in", [H, H]); W_D = din("W_D", [H, H]); W_out = din("W_out", [H, H])
    W_in_T = din("W_in_T", [H, H]); W_B_T = din("W_B_T", [H, H])
    B_mat = din("B_mat", [H, S]); C_mat = din("C_mat", [S, H])
    W_ig = din("W_ig", [H, S]); W_fg = din("W_fg", [H, S])
    b_in_cols = din("b_in_cols", [128, 8]); b_B_cols = din("b_B_cols", [128, 8])
    b_out_cols = din("b_out_cols", [128, 8])
    b_ig_col = din("b_ig_col", [S, 1]); b_fg_col = din("b_fg_col", [S, 1])
    gamma_b = din("gamma_b", [128, H]); beta_b = din("beta_b", [128, H])
    bD_row = din("bD_row", [1, H])
    convw = din("convw", [128, 3 * 32])
    convb = din("convb", [128, 32])
    w1t_f = din("w1t_f", [H, DH])
    w2t_f = din("w2t_f", [DH, H])
    qmask = din("qmask", [128, NT])
    hsel0 = din("hsel0", [S, 1]); hsel1 = din("hsel1", [S, 1])
    ones1 = din("ones1", [1, 128])
    ones_col = din("ones_col", [128, 1])
    id_f32 = din("id_f32", [128, 128])
    onesrow_ext = din("onesrow_ext", [1, EXT], BF16)
    zrow_ext = din("zrow_ext", [1, EXT], BF16)
    out_tm = nc.dram_tensor("out_tm", [T // 2, H], F32, kind="ExternalOutput").ap()
    w1q_hbm = nc.dram_tensor("w1q_hbm", [H, DH], BF16, kind="Internal").ap()
    w2q_hbm = nc.dram_tensor("w2q_hbm", [DH, H], BF16, kind="Internal").ap()
    r1_hbm = nc.dram_tensor("r1_hbm", [EXT, H], F32, kind="Internal").ap()
    r1q_hbm = nc.dram_tensor("r1q_hbm", [EXTP, H], BF16, kind="Internal").ap()
    dbg = {}
    if debug:
        for nm, shape, dt in [("d_gates", [48, T], F32), ("d_hs", [S, EXT], F32),
                              ("d_xp", [128, 8 * 512], F32),
                              ("d_y", [128, 8 * 512], F32),
                              ("d_z", [128, 8 * 512], F32),
                              ("d_r1", [EXT, H], F32),
                              ("d_h", [128, 32 * (GW + 2)], F32),
                              ("d_wg", [128, 8 * 48], F32),
                              ("d_dq", [1, EXT], F32)]:
            dbg[nm] = nc.dram_tensor(nm, shape, dt, kind="ExternalOutput").ap()

    root = contextlib.ExitStack()

    def SB(name, shape, dt=F32, stack=None):
        return (stack or root).enter_context(nc.sbuf_tensor(name, shape, dt))

    def PSUM(name, shape, dt=F32, stack=None):
        return (stack or root).enter_context(nc.psum_tensor(name, shape, dt))

    def load(dst_ap, src_ap, region, q="SP", reads=()):
        p.op(q, lambda d=dst_ap, s=src_ap, e=QENG[q]: e.dma_start(out=d, in_=s),
             reads=reads, writes=(region,), dma=True)

    def store(dst_ap, src_ap, reads, region, q="ACT"):
        p.op(q, lambda d=dst_ap, s=src_ap, e=QENG[q]: e.dma_start(out=d, in_=s),
             reads=reads, writes=(region,), dma=True)

    def load_fm(dst2d, width, src_slice, region, q="SP", stride=None):
        st = stride if stride is not None else width
        nch = src_slice.shape[0] // 128
        d3 = dst2d[:, 0:nch * st].rearrange("p (c t) -> p c t",
                                            c=nch)[:, :, 0:width]
        s3 = src_slice.rearrange("(c p) t -> p c t", p=128)
        p.op(q, lambda d=d3, s=s3, e=QENG[q]: e.dma_start(out=d, in_=s),
             reads=(), writes=(region,), dma=True)

    def mm(group, out_ap, lhsT_ap, rhs_ap, reads, wr, start, stop):
        p.op("PE", lambda o=out_ap, l=lhsT_ap, r=rhs_ap, st=start, sp=stop:
             PE.matmul(o, l, r, start=st, stop=sp),
             reads=reads, writes=(wr,), group=group)

    stAB = contextlib.ExitStack()
    s_qmask = SB("s_qmask", [128, NT])
    s_hsel0 = SB("s_hsel0", [S, 1]); s_hsel1 = SB("s_hsel1", [S, 1])
    s_ones1 = SB("s_ones1", [1, 128])
    s_onesc = SB("s_onesc", [128, 1])
    s_idf = SB("s_idf", [128, 128])
    s_binc = SB("s_binc", [128, 8]); s_boutc = SB("s_boutc", [128, 8])
    s_w1mean = SB("s_w1mean", [128, 1]); s_w2mean = SB("s_w2mean", [128, 1])
    dqcols = SB("dqcols", [128, NT])
    s_convw = SB("s_convw", [128, 3 * 32]); s_convb = SB("s_convb", [128, 32])
    s_gamb = SB("s_gamb", [128, H], stack=stAB)
    s_betb = SB("s_betb", [128, H], stack=stAB)
    hsA_hi = SB("hsA_hi", [S + 1, EXT], BF16, stack=stAB)
    hsA_lo = SB("hsA_lo", [S + 1, EXT], BF16, stack=stAB)
    CA_hi = SB("CA_hi", [S + 1, H], BF16, stack=stAB)
    CA_lo = SB("CA_lo", [S + 1, H], BF16, stack=stAB)

    TW = 1024 if SAFE_ROUND else 2048
    NTB = 4

    def alloc_tern(tstack):
        twfs = tuple(SB(f"twf{i}", [128, TW], stack=tstack) for i in range(NTB))
        _tu = tuple(SB(f"tu{i}", [128, TW], stack=tstack) for i in range(2))
        tus = (_tu[0], _tu[1], _tu[0], _tu[1])
        _tk16 = tuple(SB(f"tk16_{i}", [128, TW], I16, stack=tstack)
                      for i in range(2))
        tk16s = (_tk16[0], _tk16[1], _tk16[0], _tk16[1])
        _tb = tuple(SB(f"tb{i}", [128, TW], BF16, stack=tstack) for i in range(2))
        tbs = (_tb[0], _tb[1], _tb[0], _tb[1])
        if SAFE_ROUND:
            tk = SB("tk", [128, TW], I32, stack=tstack)
            trr = SB("trr", [128, TW], stack=tstack)
            tcp = SB("tcp", [128, TW], stack=tstack)
        else:
            tk = trr = tcp = None
        tcol = SB("tcol", [128, 10], stack=tstack)
        return twfs, tus, tk16s, tbs, tk, trr, tcp, tcol

    for dst, src, rg in [(s_gamb, gamma_b, "s_gamb"), (s_betb, beta_b, "s_betb"),
                         (s_qmask, qmask, "s_qmask"), (s_hsel0, hsel0, "s_hsel0"),
                         (s_hsel1, hsel1, "s_hsel1"), (s_ones1, ones1, "s_ones1"), (s_onesc, ones_col, "s_onesc"),
                         (s_idf, id_f32, "s_idf"), (s_binc, b_in_cols, "s_binc"),
                         (s_boutc, b_out_cols, "s_boutc"),
                         (s_convw, convw, "s_convw"), (s_convb, convb, "s_convb")]:
        load(dst[:], src[:], rg)

    stA = contextlib.ExitStack()
    g80 = SB("g80", [80, T], stack=stA)
    hs_buf = SB("hs_buf", [S, T + 4], stack=stA)
    GM = 80
    G = SB("G", [128, 8 * GM], stack=stA)
    WGf = SB("WGf", [128, 8 * GM], stack=stA)
    WG_hi = SB("WG_hi", [128, 8 * GM], BF16, stack=stA)
    WG_lo = SB("WG_lo", [128, 8 * GM], BF16, stack=stA)
    bG = SB("bG", [GM, 1], stack=stA)
    wtrs = (SB("wtr0", [128, H], stack=stA),
            SB("wtr1", [128, H], stack=stA))
    bmt = SB("bmt", [128, 8 * S], stack=stA)
    wigt = SB("wigt", [128, 8 * S], stack=stA)
    wfgt = SB("wfgt", [128, 8 * S], stack=stA)
    bic = SB("bic", [128, 8], stack=stA)
    bbc = SB("bbc", [128, 8], stack=stA)
    ps_pre = PSUM("ps_pre", [128, 1024], stack=stA)
    ps_g = PSUM("ps_g", [GM, 512], stack=stA)

    load_fm(bmt, S, B_mat[:], "bmt")
    load(bic[:], b_in_cols[:], "bic")
    load(bbc[:], b_B_cols[:], "bbc")
    load_fm(wigt, S, W_ig[:], "wigt")
    load_fm(wfgt, S, W_fg[:], "wfgt")

    p.op("DVE", lambda: V.memset(G[:, :], 0.0), writes=("Ginit",))
    for fc in range(8):
        wtr, wr_ = wtrs[fc % 2], f"wtr{fc % 2}"
        po = (fc % 2) * 512
        load_fm(wtr, 128, W_B_T[:, fc * 128:(fc + 1) * 128], wr_)
        for j in range(8):
            mm(f"P{fc}", ps_pre[:, po:po + 16], wtr[:, j * 128:(j + 1) * 128],
               bmt[:, j * S:(j + 1) * S], reads=(wr_, "bmt"),
               wr=f"psPre{fc % 2}", start=(j == 0), stop=(j == 7))
        p.op("DVE", lambda o=G[:, fc * GM:fc * GM + 16],
             i=ps_pre[:, po:po + 16], fc=fc:
             V.tensor_copy(o, i), reads=(f"psPre{fc % 2}", "Ginit"),
             writes=(f"G{fc}",))
        p.op("DVE", lambda o=G[:, fc * GM + 32:fc * GM + 48],
             i=wigt[:, fc * S:(fc + 1) * S]: V.tensor_copy(o, i),
             reads=("wigt", "Ginit"), writes=(f"G{fc}",))
        p.op("DVE", lambda o=G[:, fc * GM + 64:fc * GM + 80],
             i=wfgt[:, fc * S:(fc + 1) * S]: V.tensor_copy(o, i),
             reads=("wfgt", "Ginit"), writes=(f"G{fc}",))

    for fin in range(8):
        wtr, wr_ = wtrs[fin % 2], f"wtr{fin % 2}"
        po = (fin % 2) * 512
        load_fm(wtr, 128, W_in_T[:, fin * 128:(fin + 1) * 128], wr_)
        for fc in range(8):
            mm(f"WG{fin}", ps_pre[:, po:po + GM],
               wtr[:, fc * 128:(fc + 1) * 128],
               G[:, fc * GM:(fc + 1) * GM], reads=(wr_, f"G{fc}"),
               wr=f"psPre{fin % 2}", start=(fc == 0), stop=(fc == 7))
        p.op("DVE", lambda o=WGf[:, fin * GM:(fin + 1) * GM],
             i=ps_pre[:, po:po + GM], fin=fin:
             V.tensor_copy(o, i), reads=(f"psPre{fin % 2}",),
             writes=(f"WGf{fin}",))
        p.op("ACT", lambda o=WG_hi[:, fin * GM:(fin + 1) * GM],
             i=WGf[:, fin * GM:(fin + 1) * GM]: A.activation(o, i, AF.Copy),
             reads=(f"WGf{fin}",), writes=(f"WGhi{fin}",))
        p.op("DVE", lambda o=WG_lo[:, fin * GM:(fin + 1) * GM],
             a=WGf[:, fin * GM:(fin + 1) * GM],
             b=WG_hi[:, fin * GM:(fin + 1) * GM]:
             V.tensor_sub(o, a, b), reads=(f"WGf{fin}", f"WGhi{fin}"),
             writes=(f"WGlo{fin}",))
    if debug:
        store(dbg["d_wg"][:], WGf[:, 0:8 * 48], tuple(f"WGf{i}" for i in range(8)),
              "d_wg")

    t1 = SB("t1", [128, 8], stack=stA)
    for j in range(8):
        wtr, wr_ = wtrs[j % 2], f"wtr{j % 2}"
        po = (j % 2) * 512
        load_fm(wtr, 128, W_B_T[:, j * 128:(j + 1) * 128], wr_)
        for fc in range(8):
            mm(f"t1{j}", ps_pre[:, po:po + 1], wtr[:, fc * 128:(fc + 1) * 128],
               bic[:, fc:fc + 1], reads=(wr_, "bic"),
               wr=f"psPre{j % 2}", start=(fc == 0), stop=(fc == 7))
        p.op("DVE", lambda o=t1[:, j:j + 1], i=ps_pre[:, po:po + 1],
             b=bbc[:, j:j + 1], j=j:
             V.tensor_add(o, i, b), reads=(f"psPre{j % 2}", "bbc"),
             writes=("t1",))
    for j in range(8):
        mm("bP", ps_pre[0:16, 0:1], bmt[:, j * S:(j + 1) * S], t1[:, j:j + 1],
           reads=("bmt", "t1"), wr="psPre0", start=(j == 0), stop=(j == 7))
    btmp = SB("btmp", [S, 3], stack=stA)
    p.op("DVE", lambda: V.tensor_copy(btmp[:, 0:1], ps_pre[0:16, 0:1]),
         reads=("psPre0",), writes=("btmp0",))
    for which, wt, wrg, bcol, col in (("ig", wigt, "wigt", b_ig_col, 1),
                                      ("fg", wfgt, "wfgt", b_fg_col, 2)):
        for fc in range(8):
            mm(f"b{which}", ps_pre[0:16, 1:2], wt[:, fc * S:(fc + 1) * S],
               bic[:, fc:fc + 1], reads=(wrg, "bic"),
               wr="psPre0", start=(fc == 0), stop=(fc == 7))
        bcs = SB(f"bc_{which}", [S, 1], stack=stA)
        load(bcs[:], bcol[:], f"bc_{which}")
        p.op("DVE", lambda o=btmp[:, col:col + 1], i=ps_pre[0:16, 1:2], b=bcs[:]:
             V.tensor_add(o, i, b), reads=("psPre0", f"bc_{which}"),
             writes=(f"btmp{col}",))
    store(bG[0:16, :], btmp[:, 0:1], ("btmp0",), "bG")
    store(bG[32:48, :], btmp[:, 1:2], ("btmp1",), "bG")
    store(bG[64:80, :], btmp[:, 2:3], ("btmp2",), "bG")

    ctmp = SB("ctmp", [S, H], stack=stA)
    bdtmp = SB("bdtmp", [1, H], stack=stA)
    load(ctmp[:], C_mat[:], "ctmp")
    load(bdtmp[:], bD_row[:], "bdtmp")
    p.op("ACT", lambda: A.activation(CA_hi[0:S, :], ctmp[:], AF.Copy),
         reads=("ctmp",), writes=("CAhi",))
    p.op("DVE", lambda: V.tensor_sub(CA_lo[0:S, :], ctmp[:], CA_hi[0:S, :]),
         reads=("ctmp", "CAhi"), writes=("CAlo",))
    bdh = SB("bdh", [1, H], BF16, stack=stA)
    bdl = SB("bdl", [1, H], BF16, stack=stA)
    p.op("ACT", lambda: A.activation(bdh[:], bdtmp[:], AF.Copy),
         reads=("bdtmp",), writes=("bdh",))
    p.op("DVE", lambda: V.tensor_sub(bdl[:], bdtmp[:], bdh[:]),
         reads=("bdtmp", "bdh"), writes=("bdl",))
    store(CA_hi[S:S + 1, :], bdh[:], ("bdh",), "CAhi", q="SP")
    store(CA_lo[S:S + 1, :], bdl[:], ("bdl",), "CAlo", q="SP")

    twfs, tus, tk16s, tbs, tk, trr, tcp, tcol = alloc_tern(stA)

    def tern_slices(nrows, width):
        return [(rc, c0, min(width, c0 + TW))
                for rc in range(nrows) for c0 in range(0, width, TW)]

    def tern_pass1(src, nrows, width, mean_out, mean_rg, tg,
                   ps_hbuf, ps_rg, scol):
        stag = f"tcol{scol}"
        slices = tern_slices(nrows, width)
        for i, (rc, c0, c1) in enumerate(slices):
            wv = c1 - c0
            pr = i % NTB
            twf, tu = twfs[pr], tus[pr]
            load(twf[:, 0:wv], src[rc * 128:(rc + 1) * 128, c0:c1], f"twf{pr}")
            if i % 2 == 0:
                p.op("ACT", lambda twf=twf, tu=tu, e=wv: A.activation(
                    tu[:, 0:e], twf[:, 0:e], AF.Abs, accum_out=tcol[:, 0:1]),
                     reads=(f"twf{pr}",), writes=(f"tu{pr % 2}", "tcol0"))
                acc, arg = "tcol0", 0
            else:
                p.op("DVE", lambda twf=twf, e=wv: V.tensor_reduce(
                    tcol[:, 3:4].rearrange("p (t o) -> p t o", o=1),
                    twf[:, 0:e].rearrange("p (t e) -> p t e", t=1),
                    AX, OP.add, apply_absolute_value=True),
                     reads=(f"twf{pr}",), writes=("tcol3",))
                acc, arg = "tcol3", 3
            ac = 6 + (i % 4)
            if i < 4:
                p.op("DVE", lambda ac=ac, a=arg: V.tensor_copy(
                    tcol[:, ac:ac + 1], tcol[:, a:a + 1]),
                     reads=(acc,), writes=(f"tcol{ac}",))
            else:
                p.op("DVE", lambda a=arg, ac=ac: V.tensor_add(
                    tcol[:, ac:ac + 1], tcol[:, ac:ac + 1], tcol[:, a:a + 1]),
                     reads=(acc, f"tcol{ac}"), writes=(f"tcol{ac}",))
            yield
        nun = len(slices)
        p.op("DVE", lambda: V.tensor_add(tcol[:, 6:7], tcol[:, 6:7],
                                         tcol[:, 7:8]),
             reads=("tcol6", "tcol7"), writes=("tcol6",))
        if nun > 2:
            p.op("DVE", lambda: V.tensor_add(tcol[:, 8:9], tcol[:, 8:9],
                                             tcol[:, 9:10]),
                 reads=("tcol8", "tcol9"), writes=("tcol8",))
            p.op("DVE", lambda: V.tensor_add(tcol[:, 6:7], tcol[:, 6:7],
                                             tcol[:, 8:9]),
                 reads=("tcol6", "tcol8"), writes=("tcol6",))
        p.op("DVE", lambda: V.tensor_scalar(tcol[:, 1:2], tcol[:, 6:7],
                                            1.0 / (nrows * 128 * width), None,
                                            OP.mult),
             reads=("tcol6",), writes=("tcol1",))
        mm("parsum" + tg, ps_hbuf[0:1, 0:1], s_onesc[:], tcol[:, 1:2],
           reads=("s_onesc", "tcol1"), wr=ps_rg, start=True, stop=True)
        p.op("DVE", lambda: V.tensor_copy(tcol[0:1, 1:2], ps_hbuf[0:1, 0:1]),
             reads=(ps_rg,), writes=("tcol1",))
        mm("parbc" + tg, ps_hbuf[0:128, 0:1], s_ones1[:], tcol[0:1, 1:2],
           reads=("s_ones1", "tcol1"), wr=ps_rg, start=True, stop=True)
        p.op("DVE", lambda: V.tensor_copy(tcol[:, 1:2], ps_hbuf[0:128, 0:1]),
             reads=(ps_rg,), writes=("tcol1",))
        p.op("DVE", lambda: V.tensor_scalar_max(mean_out[:], tcol[:, 1:2], 1e-5),
             reads=("tcol1",), writes=(mean_rg,))
        p.op("DVE", lambda mo=mean_out, sc=scol: V.reciprocal(
            tcol[:, sc:sc + 1], mo[:]),
             reads=(mean_rg,), writes=(stag,))
        yield

    def tern_pass2(src, dst_hbm, nrows, width, tg, scol):
        stag = f"tcol{scol}"
        slices = tern_slices(nrows, width)
        for i, (rc, c0, c1) in enumerate(slices):
            wv = c1 - c0
            pr = i % NTB
            twf, tu, tk16, tb = twfs[pr], tus[pr], tk16s[pr], tbs[pr]
            load(twf[:, 0:wv], src[rc * 128:(rc + 1) * 128, c0:c1], f"twf{pr}")
            if SAFE_ROUND:
                p.op("DVE", lambda twf=twf, tu=tu, e=wv: V.tensor_scalar(
                    tu[:, 0:e], twf[:, 0:e], tcol[:, scol:scol + 1], None,
                    OP.mult),
                     reads=(f"twf{pr}", stag), writes=(f"tu{pr % 2}",))
                p.op("DVE", lambda tu=tu, e=wv: V.tensor_copy(tk[:, 0:e],
                                                              tu[:, 0:e]),
                     reads=(f"tu{pr % 2}",), writes=("tk",))
                p.op("DVE", lambda tu=tu, e=wv: V.tensor_sub(trr[:, 0:e],
                                                             tu[:, 0:e],
                                                             tk[:, 0:e]),
                     reads=(f"tu{pr % 2}", "tk"), writes=("trr",))
                p.op("DVE", lambda e=wv: V.tensor_scalar(tcp[:, 0:e], trr[:, 0:e],
                                                         0.5, None, OP.is_gt),
                     reads=("trr",), writes=("tcp",))
                p.op("DVE", lambda tu=tu, e=wv: V.scalar_tensor_tensor(
                    tu[:, 0:e], trr[:, 0:e], -0.5, tcp[:, 0:e], OP.is_lt,
                    OP.subtract),
                     reads=("trr", "tcp"), writes=(f"tu{pr % 2}",))
                p.op("DVE", lambda tu=tu, e=wv: V.tensor_sub(tu[:, 0:e],
                                                             tk[:, 0:e],
                                                             tu[:, 0:e]),
                     reads=(f"tu{pr % 2}", "tk"), writes=(f"tu{pr % 2}",))
                p.op("DVE", lambda tu=tu, tb=tb, e=wv: V.tensor_scalar(
                    tb[:, 0:e], tu[:, 0:e], 1.0, -1.0, OP.min, OP.max),
                     reads=(f"tu{pr % 2}",), writes=(f"tb{pr % 2}",))
            elif i % 2 == 0:
                p.op("ACT", lambda twf=twf, tu=tu, e=wv: A.activation(
                    tu[:, 0:e], twf[:, 0:e], AF.Identity,
                    scale=tcol[:, scol:scol + 1]),
                     reads=(f"twf{pr}", stag), writes=(f"tu{pr % 2}",))
                p.op("DVE", lambda tu=tu, tk16=tk16, e=wv: V.tensor_scalar(
                    tk16[:, 0:e], tu[:, 0:e], 1.49, -1.49, OP.min, OP.max),
                     reads=(f"tu{pr % 2}",), writes=(f"tk16{pr % 2}",))
                p.op("ACT", lambda tk16=tk16, tb=tb, e=wv: A.activation(
                    tb[:, 0:e], tk16[:, 0:e], AF.Copy),
                     reads=(f"tk16{pr % 2}",), writes=(f"tb{pr % 2}",))
            else:
                p.op("DVE", lambda twf=twf, tu=tu, e=wv: V.tensor_scalar(
                    tu[:, 0:e], twf[:, 0:e], tcol[:, scol:scol + 1], 1.49,
                    OP.mult, OP.min),
                     reads=(f"twf{pr}", stag), writes=(f"tu{pr % 2}",))
                p.op("DVE", lambda tu=tu, tk16=tk16, e=wv: V.tensor_scalar(
                    tk16[:, 0:e], tu[:, 0:e], -1.49, None, OP.max),
                     reads=(f"tu{pr % 2}",), writes=(f"tk16{pr % 2}",))
                p.op("ACT", lambda tk16=tk16, tb=tb, e=wv: A.activation(
                    tb[:, 0:e], tk16[:, 0:e], AF.Copy),
                     reads=(f"tk16{pr % 2}",), writes=(f"tb{pr % 2}",))
            store(dst_hbm[rc * 128:(rc + 1) * 128, c0:c1], tb[:, 0:wv],
                  (f"tb{pr % 2}",), f"q{tg}{rc}_{c0}")
            yield

    def tern_p1_all(ps_hbuf, ps_rg):
        yield from tern_pass1(w1t_f, 8, DH, s_w1mean, "s_w1mean", "w1",
                              ps_hbuf, ps_rg, 2)
        yield from tern_pass1(w2t_f, 32, H, s_w2mean, "s_w2mean", "w2",
                              ps_hbuf, ps_rg, 4)

    def tern_p2_all():
        yield from tern_pass2(w1t_f, w1q_hbm, 8, DH, "w1", 2)
        yield from tern_pass2(w2t_f, w2q_hbm, 32, H, "w2", 4)

    def tern_all():
        yield from tern_p1_all(ps_pre, "psPre0")
        yield from tern_p2_all()

    tern_p1 = tern_all()

    def pump_p1(n):
        for _ in range(n):
            try:
                next(tern_p1)
            except StopIteration:
                break

    GSL = 256
    xgfs = (SB("xgf0", [128, 8 * GSL], stack=stA),
            SB("xgf1", [128, 8 * GSL], stack=stA))
    xghs = (SB("xgh0", [128, 8 * GSL], BF16, stack=stA),
            SB("xgh1", [128, 8 * GSL], BF16, stack=stA))
    xgls = (SB("xgl0", [128, 8 * GSL], BF16, stack=stA),
            SB("xgl1", [128, 8 * GSL], BF16, stack=stA))
    for sl in range(T // GSL):
        c0 = sl * GSL
        pr = sl % 2
        xgf, xgh, xgl = xgfs[pr], xghs[pr], xgls[pr]
        go = pr * GSL
        load_fm(xgf, GSL, xg_fm[:, c0:c0 + GSL], f"xgf{pr}")
        p.op("ACT", lambda xgf=xgf, xgh=xgh: A.activation(xgh[:], xgf[:],
                                                          AF.Copy),
             reads=(f"xgf{pr}",), writes=(f"xgh{pr}",))
        p.op("DVE", lambda xgf=xgf, xgh=xgh, xgl=xgl:
             V.tensor_sub(xgl[:], xgf[:], xgh[:]),
             reads=(f"xgf{pr}", f"xgh{pr}"), writes=(f"xgl{pr}",))
        for c in range(8):
            sh = WG_hi[:, c * GM:(c + 1) * GM]
            slo = WG_lo[:, c * GM:(c + 1) * GM]
            mh = xgh[:, c * GSL:(c + 1) * GSL]
            ml = xgl[:, c * GSL:(c + 1) * GSL]
            mm(f"g{sl}", ps_g[:, go:go + GSL], sh, mh,
               reads=(f"WGhi{c}", f"xgh{pr}"),
               wr=f"psG{pr}", start=(c == 0), stop=False)
            mm(f"g{sl}", ps_g[:, go:go + GSL], sh, ml,
               reads=(f"WGhi{c}", f"xgl{pr}"),
               wr=f"psG{pr}", start=False, stop=False)
            mm(f"g{sl}", ps_g[:, go:go + GSL], slo, mh,
               reads=(f"WGlo{c}", f"xgh{pr}"),
               wr=f"psG{pr}", start=False, stop=(c == 7))
        p.op("ACT", lambda o=g80[0:16, c0:c0 + GSL], go=go:
             A.activation(o, ps_g[0:16, go:go + GSL], AF.Identity,
                          bias=bG[0:16, :]),
             reads=(f"psG{pr}", "bG"), writes=(f"gB{sl % 8}",))
        p.op("ACT", lambda o=g80[32:48, c0:c0 + GSL], go=go:
             A.activation(o, ps_g[32:48, go:go + GSL], AF.Sigmoid,
                          bias=bG[32:48, :]),
             reads=(f"psG{pr}", "bG"), writes=(f"gI{sl % 8}",))
        p.op("ACT", lambda o=g80[64:80, c0:c0 + GSL], go=go:
             A.activation(o, ps_g[64:80, go:go + GSL], AF.Sigmoid,
                          bias=bG[64:80, :]),
             reads=(f"psG{pr}", "bG"), writes=(f"gF{sl % 8}",))
        pump_p1(9)
    gr = tuple(f"gB{i}" for i in range(8)) + tuple(f"gI{i}" for i in range(8)) + tuple(f"gF{i}" for i in range(8))
    if debug:
        store(dbg["d_gates"][0:16, :], g80[0:16, :], gr, "d_gates")
        store(dbg["d_gates"][16:32, :], g80[32:48, :], gr, "d_gates")
        store(dbg["d_gates"][32:48, :], g80[64:80, :], gr, "d_gates")

    hs_e = SB("hs_e", [S, EXT], stack=stA)
    TH = T // 2
    uu = g80[0:16, :]
    store(xgfs[0][0:16, :TH], g80[32:48, 0:TH], gr, "igc", q="SP")
    store(xgfs[1][0:16, :TH], g80[32:48, TH:T], gr, "igc2", q="SP")
    p.op("DVE", lambda: V.tensor_mul(uu[:, 0:TH], xgfs[0][0:16, :TH],
                                     g80[0:16, 0:TH]),
         reads=gr + ("igc",), writes=("uu",) + tuple(f"gB{i}" for i in range(8)))
    p.op("DVE", lambda: V.tensor_mul(uu[:, TH:T], xgfs[1][0:16, :TH],
                                     g80[0:16, TH:T]),
         reads=gr + ("igc2", "uu"), writes=("uu",))
    p.op("DVE", lambda: V.memset(hs_buf[:, 0:2], 0.0), writes=("hs",))
    p.op("DVE", lambda: V.memset(hs_buf[:, T + 2:T + 4], 0.0), writes=("hs",))
    store(xgfs[0][0:16, :TH], g80[64:80, 0:TH], gr + ("uu", "igc"), "fgc",
          q="SP")
    store(xgfs[1][0:16, :TH], g80[64:80, TH:T], gr + ("uu", "igc2"), "fgc2",
          q="SP")
    p.op("DVE", lambda: V.tensor_tensor_scan(hs_buf[:, 2:TH + 2],
                                             xgfs[0][0:16, :TH],
                                             uu[:, 0:TH], 0.0,
                                             OP.mult, OP.add),
         reads=gr + ("uu", "fgc"), writes=("hs",))
    p.op("DVE", lambda: V.tensor_tensor_scan(hs_buf[:, TH + 2:T + 2],
                                             xgfs[1][0:16, :TH],
                                             uu[:, TH:T],
                                             hs_buf[:, TH + 1:TH + 2],
                                             OP.mult, OP.add),
         reads=gr + ("uu", "fgc2", "hs"), writes=("hs",))
    p.op("DVE", lambda: V.tensor_scalar(hs_e[:], hs_buf[:, 0:EXT], s_hsel0[:],
                                        None, OP.mult),
         reads=("hs", "s_hsel0"), writes=("hs_e",))
    p.op("DVE", lambda: V.scalar_tensor_tensor(hs_e[:], hs_buf[:, 2048:2048 + EXT],
                                               s_hsel1[:], hs_e[:], OP.mult, OP.add),
         reads=("hs", "s_hsel1", "hs_e"), writes=("hs_e",))
    if debug:
        store(dbg["d_hs"][:], hs_e[:], ("hs_e",), "d_hs")
    p.op("ACT", lambda: A.activation(hsA_hi[0:S, :], hs_e[:], AF.Copy),
         reads=("hs_e",), writes=("hsAhi",))
    p.op("DVE", lambda: V.tensor_sub(hsA_lo[0:S, :], hs_e[:], hsA_hi[0:S, :]),
         reads=("hs_e", "hsAhi"), writes=("hsAlo",))
    load(hsA_hi[S:S + 1, :], onesrow_ext[:], "hsAhi")
    load(hsA_lo[S:S + 1, :], zrow_ext[:], "hsAlo")

    pump_p1(10 ** 9)

    stA.close()
    p.barrier()

    stB = contextlib.ExitStack()
    Wq = {}
    for nm in ("Win", "WD", "Wout"):
        Wq[nm] = (SB(nm + "_hi", [128, 8 * H], BF16, stack=stB),
                  SB(nm + "_lo", [128, 8 * H], BF16, stack=stB))
    xtm = SB("xtm", [128, H], stack=stB)
    xb0 = SB("xb0", [128, 8 * AWM], stack=stB)
    xb1 = SB("xb1", [128, 8 * AWM], stack=stB)
    zS2 = (SB("zS0", [128, 8 * AWM], stack=stB),
           SB("zS1", [128, 8 * AWM], stack=stB))
    xp_f = SB("xp_f", [128, 8 * AWM], stack=stB)
    yf_c = SB("yf_c", [128, AWM], stack=stB)
    wi = 0
    for nm, src in (("Win", W_in), ("WD", W_D), ("Wout", W_out)):
        hi, lo = Wq[nm]
        for c in range(8):
            pr = wi % 2
            wi += 1
            wtmp = (xb0, xb1)[pr][:, 0:H]
            load(wtmp, src[c * 128:(c + 1) * 128, :], f"xb{pr}")
            if pr == 0:
                p.op("ACT", lambda o=hi[:, c * H:(c + 1) * H], a=wtmp:
                     A.activation(o, a, AF.Copy),
                     reads=(f"xb{pr}",), writes=(f"{nm}hi{c}",))
            else:
                p.op("DVE", lambda o=hi[:, c * H:(c + 1) * H], a=wtmp:
                     V.tensor_copy(o, a),
                     reads=(f"xb{pr}",), writes=(f"{nm}hi{c}",))
            p.op("GP", lambda o=lo[:, c * H:(c + 1) * H], a=wtmp,
                 b=hi[:, c * H:(c + 1) * H]: GP.tensor_sub(o, a, b),
                 reads=(f"xb{pr}", f"{nm}hi{c}"), writes=(f"{nm}lo{c}",))

    pA_h = SB("pA_h", [128, 8 * AWM], BF16, stack=stB)
    pA_l = SB("pA_l", [128, 8 * AWM], BF16, stack=stB)
    pB_h = SB("pB_h", [128, 8 * AWM], BF16, stack=stB)
    pB_l = SB("pB_l", [128, 8 * AWM], BF16, stack=stB)
    z_tm = SB("z_tm", [128, H], stack=stB)
    zsq = SB("zsq", [128, H], stack=stB)
    s_s = z_tm
    sc_t = SB("sc_t", [128, H], stack=stB)
    r1t = z_tm
    q_i = SB("q_i", [128, H], I16, stack=stB)
    q_b = SB("q_b", [128, H], BF16, stack=stB)
    colz = SB("colz", [128, 16], stack=stB)
    if SAFE_ROUND:
        rr_t = SB("rr_t", [128, H], stack=stB)
        cp_t = SB("cp_t", [128, H], stack=stB)
    ps_a = PSUM("ps_a", [128, 2048], stack=stB)
    ps_tr = PSUM("ps_tr", [128, 1024], stack=stB)

    lgen = [None]

    def lpump(n):
        if lgen[0] is None:
            return
        for _ in range(n):
            try:
                next(lgen[0])
            except StopIteration:
                lgen[0] = None
                break

    def attn_mm(gi):
        g0, g1 = AGROUPS[gi]
        w = g1 - g0
        tg = f"A{gi}"
        zS = zS2[gi % 2]
        ZR = f"zS{gi % 2}"
        xb = (xb0, xb1)[gi % 2]
        xbr = f"xb{gi % 2}"
        load_fm(xb, w, x_fm[:, g0:g1], xbr, stride=AWM)
        for k in range(8):
            p.op("ACT", lambda k=k: A.activation(
                pA_h[:, k * AWM:k * AWM + w], xb[:, k * AWM:k * AWM + w],
                AF.Copy),
                 reads=(xbr,), writes=(f"pAh{k}",))
            p.op("DVE", lambda k=k: V.tensor_sub(
                pA_l[:, k * AWM:k * AWM + w], xb[:, k * AWM:k * AWM + w],
                pA_h[:, k * AWM:k * AWM + w]),
                 reads=(xbr, f"pAh{k}"), writes=(f"pAl{k}",))

        for m in range(8):
            od = ps_a[0:128, (m % 4) * 512:(m % 4) * 512 + w]
            for k in range(8):
                sh = Wq["Win"][0][:, k * H + m * 128:k * H + m * 128 + 128]
                sl = Wq["Win"][1][:, k * H + m * 128:k * H + m * 128 + 128]
                mh = pA_h[:, k * AWM:k * AWM + w]
                ml = pA_l[:, k * AWM:k * AWM + w]
                mm(tg + f"xp{m}", od, sh, mh, reads=(f"Winhi{k}", f"pAh{k}"),
                   wr=f"psA{m % 4}", start=(k == 0), stop=False)
                mm(tg + f"xp{m}", od, sh, ml, reads=(f"Winhi{k}", f"pAl{k}"),
                   wr=f"psA{m % 4}", start=False, stop=False)
                mm(tg + f"xp{m}", od, sl, mh, reads=(f"Winlo{k}", f"pAh{k}"),
                   wr=f"psA{m % 4}", start=False, stop=(k == 7))
            p.op("DVE", lambda o=xp_f[:, m * AWM:m * AWM + w], i=od,
                 bc=s_binc[:, m:m + 1], mm_=m:
                 V.tensor_scalar(o, i, bc, None, OP.add),
                 reads=(f"psA{m % 4}", "s_binc"), writes=(f"xpf{m}",))
            p.op("ACT", lambda m=m: A.activation(
                pB_h[:, m * AWM:m * AWM + w], xp_f[:, m * AWM:m * AWM + w],
                AF.Copy),
                 reads=(f"xpf{m}",), writes=(f"pBh{m}",))
            p.op("DVE", lambda m=m: V.tensor_sub(
                pB_l[:, m * AWM:m * AWM + w], xp_f[:, m * AWM:m * AWM + w],
                pB_h[:, m * AWM:m * AWM + w]),
                 reads=(f"xpf{m}", f"pBh{m}"), writes=(f"pBl{m}",))
            lpump(1)
        if debug and gi == 0:
            for m in range(8):
                store(dbg["d_xp"][:, m * w:(m + 1) * w],
                      xp_f[:, m * AWM:m * AWM + w], (f"xpf{m}",), "d_xp")

        for m in range(8):
            od = ps_a[0:128, (m % 4) * 512:(m % 4) * 512 + w]
            for k in range(8):
                sh = Wq["WD"][0][:, k * H + m * 128:k * H + m * 128 + 128]
                sl = Wq["WD"][1][:, k * H + m * 128:k * H + m * 128 + 128]
                mh = pB_h[:, k * AWM:k * AWM + w]
                ml = pB_l[:, k * AWM:k * AWM + w]
                mm(tg + f"y{m}", od, sh, mh, reads=(f"WDhi{k}", f"pBh{k}"),
                   wr=f"psA{m % 4}", start=(k == 0), stop=False)
                mm(tg + f"y{m}", od, sh, ml, reads=(f"WDhi{k}", f"pBl{k}"),
                   wr=f"psA{m % 4}", start=False, stop=False)
                mm(tg + f"y{m}", od, sl, mh, reads=(f"WDlo{k}", f"pBh{k}"),
                   wr=f"psA{m % 4}", start=False, stop=False)
            mm(tg + f"y{m}", od, CA_hi[:, m * 128:(m + 1) * 128],
               hsA_hi[:, g0:g1], reads=("CAhi", "hsAhi"),
               wr=f"psA{m % 4}", start=False, stop=False)
            mm(tg + f"y{m}", od, CA_hi[:, m * 128:(m + 1) * 128],
               hsA_lo[:, g0:g1], reads=("CAhi", "hsAlo"),
               wr=f"psA{m % 4}", start=False, stop=False)
            mm(tg + f"y{m}", od, CA_lo[:, m * 128:(m + 1) * 128],
               hsA_hi[:, g0:g1], reads=("CAlo", "hsAhi"),
               wr=f"psA{m % 4}", start=False, stop=True)
            p.op("ACT", lambda i=od, mm_=m: A.activation(yf_c[:, 0:w], i, AF.Tanh),
                 reads=(f"psA{m % 4}",), writes=("yf_c",))
            if debug and gi == 0:
                store(dbg["d_y"][:, m * w:(m + 1) * w], yf_c[:, 0:w],
                      ("yf_c",), "d_y")
            p.op("ACT", lambda m=m: A.activation(
                pA_h[:, m * AWM:m * AWM + w], yf_c[:, 0:w], AF.Copy),
                 reads=("yf_c",), writes=(f"pAh{m}",))
            p.op("DVE", lambda m=m: V.tensor_sub(
                pA_l[:, m * AWM:m * AWM + w], yf_c[:, 0:w],
                pA_h[:, m * AWM:m * AWM + w]),
                 reads=("yf_c", f"pAh{m}"), writes=(f"pAl{m}",))

        for m in range(8):
            od = ps_a[0:128, (m % 4) * 512:(m % 4) * 512 + w]
            for k in range(8):
                sh = Wq["Wout"][0][:, k * H + m * 128:k * H + m * 128 + 128]
                sl = Wq["Wout"][1][:, k * H + m * 128:k * H + m * 128 + 128]
                mh = pA_h[:, k * AWM:k * AWM + w]
                ml = pA_l[:, k * AWM:k * AWM + w]
                mm(tg + f"z{m}", od, sh, mh, reads=(f"Wouthi{k}", f"pAh{k}"),
                   wr=f"psA{m % 4}", start=(k == 0), stop=False)
                mm(tg + f"z{m}", od, sh, ml, reads=(f"Wouthi{k}", f"pAl{k}"),
                   wr=f"psA{m % 4}", start=False, stop=False)
                mm(tg + f"z{m}", od, sl, mh, reads=(f"Woutlo{k}", f"pAh{k}"),
                   wr=f"psA{m % 4}", start=False, stop=(k == 7))
            p.op("DVE", lambda o=zS[:, m * AWM:m * AWM + w], i=od,
                 x=xp_f[:, m * AWM:m * AWM + w], bc=s_boutc[:, m:m + 1], mm_=m:
                 V.scalar_tensor_tensor(o, i, bc, x, OP.add, OP.add),
                 reads=(f"psA{m % 4}", f"xpf{m}", "s_boutc"), writes=(ZR,))
            lpump(1)
        if debug and gi == 0:
            for m in range(8):
                store(dbg["d_z"][:, m * w:(m + 1) * w],
                      zS[:, m * AWM:m * AWM + w], (ZR,), "d_z")

    def attn_ladder(gi):
        g0, g1 = AGROUPS[gi]
        w = g1 - g0
        tg = f"A{gi}"
        zS = zS2[gi % 2]
        ZR = f"zS{gi % 2}"
        nsub = (w + 127) // 128
        for si in range(nsub):
            u0 = g0 + si * 128
            u1 = min(g1, u0 + 128)
            uw = u1 - u0
            ti = u0 // 128
            sg = f"{tg}s{si}"
            for m_ in range(8):
                p.op("PE", lambda o=ps_tr[0:uw, m_ * 128:(m_ + 1) * 128],
                     i=zS[:, m_ * AWM + si * 128:m_ * AWM + si * 128 + uw]:
                     PE.transpose(o, i, s_idf[:]),
                     reads=(ZR, "s_idf"), writes=("psTR",), group=sg + "tr")
            p.op("DVE", lambda e=uw: V.tensor_scalar(
                z_tm[0:e, :], ps_tr[0:e, :], 1.0, None, OP.mult, OP.add,
                accum_out=colz[0:e, 0:1]),
                 reads=("psTR",), writes=("z_tm", "colz"))
            p.op("ACT", lambda e=uw: A.activation(zsq[0:e, :], z_tm[0:e, :],
                                                  AF.Square,
                                                  accum_out=colz[0:e, 1:2]),
                 reads=("z_tm",), writes=("zsq", "colz"))
            p.op("DVE", lambda e=uw: V.tensor_scalar(colz[0:e, 2:3],
                                                     colz[0:e, 0:1], 1.0 / H,
                                                     None, OP.mult),
                 reads=("colz",), writes=("colz",))
            p.op("DVE", lambda e=uw: V.tensor_mul(colz[0:e, 3:4], colz[0:e, 2:3],
                                                  colz[0:e, 2:3]),
                 reads=("colz",), writes=("colz",))
            p.op("DVE", lambda e=uw: V.scalar_tensor_tensor(
                colz[0:e, 4:5], colz[0:e, 1:2], 1.0 / H, colz[0:e, 3:4],
                OP.mult, OP.subtract),
                 reads=("colz",), writes=("colz",))
            p.op("DVE", lambda e=uw: V.tensor_scalar(colz[0:e, 4:5],
                                                     colz[0:e, 4:5], EPS, None,
                                                     OP.add),
                 reads=("colz",), writes=("colz",))
            p.op("ACT", lambda e=uw: A.activation(colz[0:e, 5:6], colz[0:e, 4:5],
                                                  AF.Sqrt),
                 reads=("colz",), writes=("colz",))
            p.op("DVE", lambda e=uw: V.reciprocal(colz[0:e, 6:7], colz[0:e, 5:6]),
                 reads=("colz",), writes=("colz",))
            yield
            p.op("DVE", lambda e=uw: V.tensor_scalar(sc_t[0:e, :], z_tm[0:e, :],
                                                     colz[0:e, 2:3],
                                                     colz[0:e, 6:7],
                                                     OP.subtract, OP.mult),
                 reads=("z_tm", "colz"), writes=("sc_t",))
            load(xtm[0:uw, :], x_tm[u0:u1, :], "xtm", q="ACT")
            p.op("GP", lambda e=uw: GP.tensor_mul(sc_t[0:e, :], sc_t[0:e, :],
                                                  s_gamb[0:e, :]),
                 reads=("sc_t", "s_gamb"), writes=("sc_t",))
            p.op("GP", lambda e=uw: GP.tensor_add(xtm[0:e, :], xtm[0:e, :],
                                                  s_betb[0:e, :]),
                 reads=("xtm", "s_betb"), writes=("xtm",))
            p.op("DVE", lambda e=uw: V.tensor_add(s_s[0:e, :], sc_t[0:e, :],
                                                  xtm[0:e, :]),
                 reads=("sc_t", "xtm"), writes=("z_tm",))
            yield
            p.op("DVE", lambda e=uw: V.tensor_reduce(colz[0:e, 7:8], s_s[0:e, :],
                                                     AX, OP.add),
                 reads=("z_tm",), writes=("colz",))
            p.op("ACT", lambda e=uw: A.activation(zsq[0:e, :], s_s[0:e, :],
                                                  AF.Square,
                                                  accum_out=colz[0:e, 8:9]),
                 reads=("z_tm",), writes=("zsq", "colz"))
            p.op("DVE", lambda e=uw: V.tensor_scalar(colz[0:e, 9:10],
                                                     colz[0:e, 7:8], 1.0 / H,
                                                     None, OP.mult),
                 reads=("colz",), writes=("colz",))
            p.op("DVE", lambda e=uw: V.tensor_mul(colz[0:e, 10:11],
                                                  colz[0:e, 9:10],
                                                  colz[0:e, 9:10]),
                 reads=("colz",), writes=("colz",))
            p.op("DVE", lambda e=uw: V.scalar_tensor_tensor(
                colz[0:e, 10:11], colz[0:e, 8:9], 1.0 / H, colz[0:e, 10:11],
                OP.mult, OP.subtract),
                 reads=("colz",), writes=("colz",))
            p.op("DVE", lambda e=uw: V.tensor_scalar(colz[0:e, 10:11],
                                                     colz[0:e, 10:11], EPS,
                                                     None, OP.add),
                 reads=("colz",), writes=("colz",))
            p.op("ACT", lambda e=uw: A.activation(colz[0:e, 11:12],
                                                  colz[0:e, 10:11], AF.Sqrt),
                 reads=("colz",), writes=("colz",))
            p.op("DVE", lambda e=uw: V.reciprocal(colz[0:e, 12:13],
                                                  colz[0:e, 11:12]),
                 reads=("colz",), writes=("colz",))
            yield
            p.op("DVE", lambda e=uw: V.tensor_scalar(sc_t[0:e, :], s_s[0:e, :],
                                                     colz[0:e, 9:10], None,
                                                     OP.subtract),
                 reads=("z_tm", "colz"), writes=("sc_t",))
            p.op("DVE", lambda e=uw: V.tensor_reduce(colz[0:e, 13:14],
                                                     sc_t[0:e, :], AX, OP.max,
                                                     apply_absolute_value=True),
                 reads=("sc_t",), writes=("colz",))
            p.op("DVE", lambda e=uw: V.tensor_mul(colz[0:e, 13:14],
                                                  colz[0:e, 13:14],
                                                  colz[0:e, 12:13]),
                 reads=("colz",), writes=("colz",))
            p.op("DVE", lambda e=uw: V.tensor_scalar_max(colz[0:e, 13:14],
                                                         colz[0:e, 13:14], 1e-5),
                 reads=("colz",), writes=("colz",))
            p.op("DVE", lambda e=uw: V.tensor_scalar(colz[0:e, 14:15],
                                                     colz[0:e, 13:14],
                                                     1.0 / 127.0, None,
                                                     OP.mult),
                 reads=("colz",), writes=("colz",))
            p.op("DVE", lambda e=uw: V.reciprocal(colz[0:e, 15:16],
                                                  colz[0:e, 13:14]),
                 reads=("colz",), writes=("colz",))
            p.op("DVE", lambda e=uw: V.tensor_mul(colz[0:e, 15:16],
                                                  colz[0:e, 15:16],
                                                  colz[0:e, 12:13]),
                 reads=("colz",), writes=("colz",))
            p.op("DVE", lambda e=uw, t=ti: V.tensor_scalar(
                colz[0:e, 15:16], colz[0:e, 15:16], 127.0,
                s_qmask[0:e, t:t + 1], OP.mult, OP.mult),
                 reads=("colz", "s_qmask"), writes=("colz",))
            p.op("DVE", lambda e=uw: V.tensor_scalar(r1t[0:e, :], sc_t[0:e, :],
                                                     colz[0:e, 12:13], None,
                                                     OP.mult),
                 reads=("sc_t", "colz"), writes=("z_tm",))
            store(r1_hbm[u0:u1, :], r1t[0:uw, :], ("z_tm",), f"r1_{ti}")
            if debug:
                store(dbg["d_r1"][u0:u1, :], r1t[0:uw, :], ("z_tm",), "d_r1x")
            yield
            p.op("DVE", lambda e=uw: V.tensor_scalar(q_i[0:e, :], sc_t[0:e, :],
                                                     colz[0:e, 15:16], None,
                                                     OP.mult),
                 reads=("sc_t", "colz"), writes=("q_i",))
            if SAFE_ROUND:
                p.op("DVE", lambda e=uw: V.tensor_scalar(zsq[0:e, :],
                                                         sc_t[0:e, :],
                                                         colz[0:e, 15:16],
                                                         None, OP.mult),
                     reads=("sc_t", "colz"), writes=("zsq",))
                p.op("DVE", lambda e=uw: V.tensor_sub(rr_t[0:e, :], zsq[0:e, :],
                                                      q_i[0:e, :]),
                     reads=("zsq", "q_i"), writes=("rr_t",))
                p.op("DVE", lambda e=uw: V.tensor_scalar(cp_t[0:e, :],
                                                         rr_t[0:e, :], 0.5,
                                                         None, OP.is_gt),
                     reads=("rr_t",), writes=("cp_t",))
                p.op("DVE", lambda e=uw: V.scalar_tensor_tensor(
                    rr_t[0:e, :], rr_t[0:e, :], -0.5, cp_t[0:e, :], OP.is_lt,
                    OP.subtract),
                     reads=("rr_t", "cp_t"), writes=("rr_t",))
                p.op("DVE", lambda e=uw: V.tensor_sub(q_b[0:e, :], q_i[0:e, :],
                                                      rr_t[0:e, :]),
                     reads=("q_i", "rr_t"), writes=("q_b",))
            else:
                p.op("ACT", lambda e=uw: A.activation(q_b[0:e, :], q_i[0:e, :],
                                                      AF.Copy),
                     reads=("q_i",), writes=("q_b",))
            store(r1q_hbm[u0:u1, :], q_b[0:uw, :], ("q_b",), f"r1q_{ti}")
            p.op("DVE", lambda e=uw, t=ti: V.tensor_copy(dqcols[0:e, t:t + 1],
                                                         colz[0:e, 14:15]),
                 reads=("colz",), writes=("dqcols",))
            yield

    for gi in range(len(AGROUPS)):
        attn_mm(gi)
        lpump(10 ** 9)
        lgen[0] = attn_ladder(gi)
    lpump(10 ** 9)

    p.op("DVE", lambda: V.memset(q_b[:], 0.0), reads=(), writes=("q_b",))
    store(r1q_hbm[EXT:EXTP, :], q_b[0:EXTP - EXT, :], ("q_b",), "r1qpad")

    stB.close()
    stAB.close()
    p.barrier()

    stC = contextlib.ExitStack()
    w1q = SB("w1q", [128, 8 * DH], BF16, stack=stC)
    w2q = SB("w2q", [128, 32 * H], BF16, stack=stC)
    for c in range(8):
        load(w1q[:, c * DH:(c + 1) * DH], w1q_hbm[c * 128:(c + 1) * 128, :],
             f"w1q{c}", reads=(f"qw1{c}_0", f"qw1{c}_2048"))
    def load_w2q():
        for c in range(32):
            load(w2q[:, c * H:(c + 1) * H], w2q_hbm[c * 128:(c + 1) * 128, :],
                 f"w2q{c}", reads=(f"qw2{c}_0",))

    hbuf = SB("hbuf", [128, 32 * (GW + 2)], stack=stC)
    RQW = 512
    r1qg = SB("r1qg", [128, 8 * RQW], BF16, stack=stC)
    s2 = PSUM("ps_s2", [128, H], stack=stC)
    o_t = SB("o_t", [128, H], stack=stC)
    s2q = o_t
    r1g = o_t
    dqs = SB("dqs", [128, GW + 2], stack=stC)
    s2xs = SB("s2xs", [128, GW], stack=stC)
    pmax = SB("pmax", [128, GW], stack=stC)
    colfs = (SB("colfA", [128, 10], stack=stC),
             SB("colfB", [128, 10], stack=stC))
    rowf = SB("rowf", [1, GW + 2], stack=stC)
    if SAFE_ROUND:
        _sa = SB("sc_a0", [128, GW], stack=stC)
        _sd = SB("sc_d0", [128, GW], stack=stC)
        sc_as, sc_ds = (_sa, _sa), (_sd, _sd)

        def satag(m):
            return "sca0"

        def sdtag(m):
            return "scd0"
    else:
        sc_as = (SB("sc_a0", [128, GW], stack=stC),
                 SB("sc_a1", [128, GW], stack=stC))
        sc_ds = (SB("sc_d0", [128, GW], stack=stC),
                 SB("sc_d1", [128, GW], stack=stC))

        def satag(m):
            return f"sca{m % 2}"

        def sdtag(m):
            return f"scd{m % 2}"
    hq_i = SB("hq_i", [128, 32 * GW], I8, stack=stC)
    hq_b = SB("hq_b", [128, 32 * GW], BF16, stack=stC)
    if SAFE_ROUND:
        hrr = SB("hrr", [128, GW], stack=stC)
        hcp = SB("hcp", [128, GW], stack=stC)
    ps_m1 = PSUM("ps_m1", [128, 1024], stack=stC)
    ps_bc = PSUM("ps_bc", [128, 512], stack=stC)
    ps_o = PSUM("ps_o", [128, 1024], stack=stC)

    HB = "hb"
    HQ = "hq_b"

    def ffn_pars(gi):
        f0, f1 = FG[gi]
        fw = f1 - f0
        a0 = max(0, f0 - 1)
        a1 = min(EXT, f1 + 1)
        return f0, f1, fw, a0, a1, a1 - a0

    def ffn_conv_a(gi, m_):
        f0, f1, fw, a0, a1, aw = ffn_pars(gi)
        c_lo = max(f0, 1)
        c_hi = min(f1, EXT - 1)
        ce = c_hi - c_lo
        cs = c_lo - a0
        hm = hbuf[:, m_ * (GW + 2):m_ * (GW + 2) + aw]
        p.op("ACT", lambda h=hm, e=ce, c0=cs, mm_=m_:
             A.activation(sc_as[mm_ % 2][:, 0:e], h[:, c0:c0 + e],
                          AF.Identity, bias=s_convb[:, mm_:mm_ + 1],
                          scale=s_convw[:, 32 + mm_:33 + mm_]),
             reads=(HB + str(m_), "s_convw", "s_convb"),
             writes=(satag(m_),))
        p.op("ACT", lambda h=hm, e=ce, c0=cs, mm_=m_:
             A.activation(sc_ds[mm_ % 2][:, 0:e], h[:, c0 + 1:c0 + 1 + e],
                          AF.Identity, scale=s_convw[:, 64 + mm_:65 + mm_]),
             reads=(HB + str(m_), "s_convw"), writes=(sdtag(m_),))

    def ffn_conv_cd(gi, m_):
        f0, f1, fw, a0, a1, aw = ffn_pars(gi)
        c_lo = max(f0, 1)
        c_hi = min(f1, EXT - 1)
        ce = c_hi - c_lo
        cs = c_lo - a0
        hm = hbuf[:, m_ * (GW + 2):m_ * (GW + 2) + aw]
        p.op("DVE", lambda h=hm, e=ce, c0=cs, mm_=m_:
             V.scalar_tensor_tensor(sc_as[mm_ % 2][:, 0:e],
                                    h[:, c0 - 1:c0 - 1 + e],
                                    s_convw[:, mm_:mm_ + 1],
                                    sc_as[mm_ % 2][:, 0:e],
                                    OP.mult, OP.add),
             reads=(HB + str(m_), "s_convw", satag(m_)),
             writes=(satag(m_),))
        p.op("DVE", lambda h=hm, e=ce, c0=cs, mm_=m_:
             V.tensor_add(sc_ds[mm_ % 2][:, 0:e], sc_ds[mm_ % 2][:, 0:e],
                          sc_as[mm_ % 2][:, 0:e]),
             reads=(satag(m_), sdtag(m_)), writes=(sdtag(m_),))
        p.op("ACT", lambda h=hm, e=ce, c0=cs, mm_=m_:
             A.activation(h[:, c0:c0 + e], sc_ds[mm_ % 2][:, 0:e], AF.Silu),
             reads=(sdtag(m_),), writes=(HB + str(m_),))

    def ffn_conv_chunk(gi, m_):
        ffn_conv_a(gi, m_)
        ffn_conv_cd(gi, m_)

    def ffn_quant_chunk(gi, m_):
        colf = colfs[gi % 2]
        f0, f1, fw, a0, a1, aw = ffn_pars(gi)
        co = f0 - a0
        hm = hbuf[:, m_ * (GW + 2) + co:m_ * (GW + 2) + co + fw]
        if SAFE_ROUND:
            p.op("DVE", lambda h=hm, e=fw, mm_=m_:
                 V.scalar_tensor_tensor(sc_as[mm_ % 2][:, 0:e], h, 1.0,
                                        s2xs[:, 0:e], OP.mult, OP.mult),
                 reads=(HB + str(m_), "s2xs"), writes=(satag(m_),))
            p.op("DVE", lambda e=fw, mm_=m_: V.tensor_copy(
                hq_i[:, mm_ * GW:mm_ * GW + e], sc_as[mm_ % 2][:, 0:e]),
                 reads=(satag(m_),), writes=("hq_i",))
            p.op("DVE", lambda e=fw, mm_=m_: V.tensor_sub(
                hrr[:, 0:e], sc_as[mm_ % 2][:, 0:e],
                hq_i[:, mm_ * GW:mm_ * GW + e]),
                 reads=(satag(m_), "hq_i"), writes=("hrr",))
            p.op("DVE", lambda e=fw: V.tensor_scalar(hcp[:, 0:e], hrr[:, 0:e],
                                                     0.5, None, OP.is_gt),
                 reads=("hrr",), writes=("hcp",))
            p.op("DVE", lambda e=fw: V.scalar_tensor_tensor(
                hrr[:, 0:e], hrr[:, 0:e], -0.5, hcp[:, 0:e], OP.is_lt,
                OP.subtract),
                 reads=("hrr", "hcp"), writes=("hrr",))
            p.op("DVE", lambda e=fw, mm_=m_: V.tensor_sub(
                hq_b[:, mm_ * GW:mm_ * GW + e],
                hq_i[:, mm_ * GW:mm_ * GW + e], hrr[:, 0:e]),
                 reads=("hq_i", "hrr"), writes=(HQ,))
        else:
            p.op("DVE", lambda h=hm, e=fw, mm_=m_:
                 V.scalar_tensor_tensor(hq_i[:, mm_ * GW:mm_ * GW + e], h,
                                        1.0, s2xs[:, 0:e], OP.mult, OP.mult),
                 reads=(HB + str(m_), "s2xs"), writes=("hq_i",))

    def ffn_hq_cast(gi):
        if not SAFE_ROUND:
            p.op("ACT", lambda: A.activation(hq_b[:, :], hq_i[:, :], AF.Copy),
                 reads=("hq_i",), writes=(HQ,))

    def ffn_A1(gi, qprev, m2g=None):
        real = gi < len(FG)
        colf = colfs[gi % 2]
        CF = f"colf{gi % 2}"
        f0, f1, fw, a0, a1, aw = ffn_pars(min(gi, len(FG) - 1))
        tg = f"F{gi}"
        al0 = (a0 // 128) * 128
        al1 = min(EXTP, ((a1 + 127) // 128) * 128)
        alw = al1 - al0
        off = a0 - al0
        r1q_reads = tuple(f"r1q_{t}" for t in range(al0 // 128,
                                                    min(NT, (al1 + 127) // 128)))
        if al1 > EXT:
            r1q_reads = r1q_reads + ("r1qpad",)
        for c in range(8 if real else 0):
            p.op("SP", lambda d=r1qg[:, c * RQW:c * RQW + alw],
                 s=r1q_hbm[al0:al1, c * 128:(c + 1) * 128]:
                 SY.dma_start(out=d, in_=s, transpose=True),
                 reads=r1q_reads, writes=(f"r1qg{c}",), dma=True)
        for t in range(al0 // 128, min(NT, (a1 + 127) // 128) if real else 0):
            tl0, tl1 = TILES[t]
            uw_t = tl1 - tl0
            p.op("PE", lambda t=t, o0=tl0 - al0, e_=uw_t: PE.transpose(
                ps_bc[0:1, o0:o0 + e_], dqcols[0:e_, t:t + 1],
                s_idf[0:e_, 0:e_]),
                 reads=("dqcols", "s_idf"), writes=("psBC",), group=tg + "dqt")
        if real:
            p.op("DVE", lambda o0=a0 - al0: V.tensor_scalar(
                rowf[:, 0:aw], ps_bc[0:1, o0:o0 + aw], s_w1mean[0:1, :], None,
                OP.mult),
                 reads=("psBC", "s_w1mean"), writes=("rowf",))
            p.op("PE", lambda: PE.matmul(ps_bc[0:128, 0:aw], s_ones1[:],
                                         rowf[:, 0:aw], start=True, stop=True),
                 reads=("s_ones1", "rowf"), writes=("psBC",), group=tg + "bc")
            p.op("DVE", lambda: V.tensor_copy(dqs[:, 0:aw], ps_bc[0:128, 0:aw]),
                 reads=("psBC",), writes=("dqs",))
        inline_conv = not (debug and gi == 1)
        m2q = []
        m2_nsub = 0
        if m2g is not None:
            f0m, f1m = FG[m2g]
            m2_nsub = (f1m - f0m + 127) // 128
            for si in range(m2_nsub):
                m2q.extend(ffn_m2_mm_list(m2g, si))
            ffn_r1g_load(m2g, 0)
        m2i = 0
        if SAFE_ROUND:
            m2pump = (len(m2q) + 31) // 32 if m2q else 0
            for m_ in range(33):
                if m_ < 32:
                    if qprev is not None:
                        ffn_quant_chunk(qprev, m_)
                    for _ in range(m2pump):
                        if m2i < len(m2q):
                            if m2i == 64 and m2_nsub > 1:
                                for _x in ffn_m2_tail(m2g, 0):
                                    pass
                                ffn_r1g_load(m2g, 1)
                            m2q[m2i]()
                            m2i += 1
                    od = ps_m1[0:128, (m_ % 2) * 512:(m_ % 2) * 512 + aw]
                    for k in range(8):
                        mm(tg + f"m{m_}", od,
                           w1q[:, k * DH + m_ * 128:k * DH + m_ * 128 + 128],
                           r1qg[:, k * RQW + off:k * RQW + off + aw],
                           reads=(f"w1q{k}", f"r1qg{k}"),
                           wr=f"psM{m_ % 2}", start=(k == 0), stop=(k == 7))
                    p.op("DVE",
                         lambda o=hbuf[:, m_ * (GW + 2):m_ * (GW + 2) + aw],
                         i=od, d=dqs[:, 0:aw], mm_=m_:
                         V.scalar_tensor_tensor(o, i, 1.0, d, OP.mult, OP.mult),
                         reads=(f"psM{m_ % 2}", "dqs"), writes=(HB + str(m_),))
                    if inline_conv:
                        ffn_conv_a(gi, m_)
                        ffn_conv_cd(gi, m_)
            if m2g is not None:
                for _x in ffn_m2_tail(m2g, m2_nsub - 1):
                    pass
            if qprev is not None:
                ffn_hq_cast(qprev)
            if not inline_conv:
                store(dbg["d_h"][:], hbuf[:],
                      tuple(HB + str(i) for i in range(32)), "d_hx")
                for m_ in range(32):
                    ffn_conv_chunk(gi, m_)
            return

        a2hold = [ffn_A2(qprev) if qprev is not None else None]
        tails = []
        on_tail_done = []

        def pump_tails(n):
            for tgen in tails[:]:
                for _ in range(n):
                    try:
                        next(tgen)
                    except StopIteration:
                        tails.remove(tgen)
                        if on_tail_done:
                            on_tail_done.pop(0)()
                        break
        PRE = 9
        NIT = 33 + PRE
        m2pump = (len(m2q) + NIT - 2) // (NIT - 1) if m2q else 0
        for it in range(NIT):
            m_ = it - PRE
            if a2hold[0] is not None:
                for _ in range(4):
                    try:
                        next(a2hold[0])
                    except StopIteration:
                        a2hold[0] = None
                        break
            q_m = it - PRE + 1
            if qprev is not None and 0 <= q_m < 32:
                ffn_quant_chunk(qprev, q_m)
            for _ in range(m2pump):
                if m2i < len(m2q):
                    if m2i == 64 and m2_nsub > 1:
                        t0 = ffn_m2_tail(m2g, 0)
                        try:
                            next(t0)
                            tails.append(t0)
                            on_tail_done.append(
                                lambda: ffn_r1g_load(m2g, 1))
                        except StopIteration:
                            ffn_r1g_load(m2g, 1)
                    m2q[m2i]()
                    m2i += 1
                    if m2i == len(m2q):
                        t1 = ffn_m2_tail(m2g, m2_nsub - 1)
                        try:
                            next(t1)
                            tails.append(t1)
                        except StopIteration:
                            pass
            pump_tails(2)
            if real and 0 <= m_ < 32:
                od = ps_m1[0:128, (m_ % 2) * 512:(m_ % 2) * 512 + aw]
                for k in range(8):
                    mm(tg + f"m{m_}", od,
                       w1q[:, k * DH + m_ * 128:k * DH + m_ * 128 + 128],
                       r1qg[:, k * RQW + off:k * RQW + off + aw],
                       reads=(f"w1q{k}", f"r1qg{k}"),
                       wr=f"psM{m_ % 2}", start=(k == 0), stop=(k == 7))
                p.op("DVE", lambda o=hbuf[:, m_ * (GW + 2):m_ * (GW + 2) + aw],
                     i=od, d=dqs[:, 0:aw], mm_=m_:
                     V.scalar_tensor_tensor(o, i, 1.0, d, OP.mult, OP.mult),
                     reads=(f"psM{m_ % 2}", "dqs"), writes=(HB + str(m_),))
                if inline_conv:
                    ffn_conv_a(gi, m_)
            if real and inline_conv and m_ >= 1:
                ffn_conv_cd(gi, m_ - 1)
        while a2hold[0] is not None:
            try:
                next(a2hold[0])
            except StopIteration:
                a2hold[0] = None
        pump_tails(10 ** 9)
        if qprev is not None:
            ffn_hq_cast(qprev)
        if not inline_conv:
            store(dbg["d_h"][:], hbuf[:],
                  tuple(HB + str(i) for i in range(32)), "d_hx")
            for m_ in range(32):
                ffn_conv_chunk(gi, m_)

    def ffn_A2(gi):
        colf = colfs[gi % 2]
        CF = f"colf{gi % 2}"
        f0, f1, fw, a0, a1, aw = ffn_pars(gi)
        tg = f"F{gi}"
        co = f0 - a0
        p.op("DVE", lambda e=fw, c0=co: V.tensor_reduce(
            pmax[:, 0:e].rearrange("p (t o) -> p t o", o=1),
            hbuf[:, :].rearrange("p (c t) -> p t c", c=32)[:, c0:c0 + e, :],
            AX, OP.max, apply_absolute_value=True),
             reads=tuple(HB + str(i) for i in range(32)), writes=("pmax",))
        yield
        nsub = (fw + 127) // 128
        for si in range(nsub):
            u0 = f0 + si * 128
            u1 = min(f1, u0 + 128)
            uw = u1 - u0
            sg = tg + f"s{si}"
            p.op("PE", lambda e=uw, s0=si * 128: PE.transpose(
                ps_bc[0:e, 0:128], pmax[:, s0:s0 + e], s_idf[:]),
                 reads=("pmax", "s_idf"), writes=("psBC",), group=sg + "tp")
            yield
            p.op("DVE", lambda e=uw: V.tensor_reduce(colf[0:e, 0:1],
                                                     ps_bc[0:e, 0:128], AX,
                                                     OP.max),
                 reads=("psBC",), writes=(CF,))
            yield
            p.op("DVE", lambda e=uw: V.tensor_scalar_max(colf[0:e, 0:1],
                                                         colf[0:e, 0:1], 1e-5),
                 reads=(CF,), writes=(CF,))
            yield
            p.op("DVE", lambda e=uw: V.reciprocal(colf[0:e, 1:2], colf[0:e, 0:1]),
                 reads=(CF,), writes=(CF,))
            yield
            p.op("DVE", lambda e=uw: V.tensor_scalar(colf[0:e, 2:3],
                                                     colf[0:e, 1:2], 127.0,
                                                     None, OP.mult),
                 reads=(CF,), writes=(CF,))
            yield
            p.op("DVE", lambda e=uw, si=si: V.tensor_scalar(
                colf[0:e, 8 + si:9 + si], colf[0:e, 0:1], s_w2mean[0:e, :],
                1.0 / 127.0, OP.mult, OP.mult),
                 reads=(CF, "s_w2mean"), writes=(CF,))
            yield
            p.op("PE", lambda e=uw: PE.transpose(ps_bc[0:1, 0:e],
                                                 colf[0:e, 2:3],
                                                 s_idf[0:e, 0:e]),
                 reads=(CF, "s_idf"), writes=("psBC",), group=sg + "t2")
            yield
            p.op("DVE", lambda e=uw, s0=si * 128: V.tensor_copy(
                rowf[:, s0:s0 + e], ps_bc[0:1, 0:e]),
                 reads=("psBC",), writes=("rowf",))
            yield
        p.op("PE", lambda e=fw: PE.matmul(ps_bc[0:128, 0:e], s_ones1[:],
                                          rowf[:, 0:e], start=True, stop=True),
             reads=("s_ones1", "rowf"), writes=("psBC",), group=tg + "b2")
        yield
        p.op("DVE", lambda e=fw: V.tensor_copy(s2xs[:, 0:e], ps_bc[0:128, 0:e]),
             reads=("psBC",), writes=("s2xs",))
        yield

    def ffn_m2_sub(gi, si):
        f0, f1 = FG[gi]
        u0 = f0 + si * 128
        u1 = min(f1, u0 + 128)
        uw = u1 - u0
        out0 = max(2, u0)
        out1 = min(EXT - 2, u1)
        return u0, u1, uw, out0, out1

    def ffn_m2_mm_list(gi, si):
        tg = f"F{gi}"
        u0, u1, uw, out0, out1 = ffn_m2_sub(gi, si)
        if out1 <= out0:
            return []
        sg = tg + f"S{si}"
        ems = []
        for nh in range(2):
            od = ps_o[0:uw, nh * 512:(nh + 1) * 512]
            for c in range(32):
                sta = hq_b[:, c * GW + si * 128:c * GW + si * 128 + uw]
                mv = w2q[:, c * H + nh * 512:c * H + nh * 512 + 512]
                ems.append(lambda od=od, sta=sta, mv=mv, c=c, nh=nh:
                           mm(sg + f"o{nh}", od, sta, mv,
                              reads=(HQ, f"w2q{c}"), wr=f"psO{nh}",
                              start=(c == 0), stop=(c == 31)))
        return ems

    def ffn_r1g_load(gi, si):
        u0, u1, uw, out0, out1 = ffn_m2_sub(gi, si)
        if out1 <= out0:
            return
        load(r1g[0:uw, :], r1_hbm[u0:u1, :], "o_t", q="ACT",
             reads=(f"r1_{u0 // 128}",))

    def ffn_m2_tail(gi, si):
        colf = colfs[gi % 2]
        CF = f"colf{gi % 2}"
        u0, u1, uw, out0, out1 = ffn_m2_sub(gi, si)
        if out1 <= out0:
            return
        if True:
            p.op("DVE", lambda e=uw, si=si: V.scalar_tensor_tensor(
                s2[0:e, :], ps_o[0:e, :], colf[0:e, 8 + si:9 + si], r1g[0:e, :],
                OP.mult, OP.add, accum_out=colf[0:e, 4:5]),
                 reads=("psO0", "psO1", CF, "o_t"), writes=("s2", CF))
            yield
            p.op("ACT", lambda e=uw: A.activation(s2q[0:e, :], s2[0:e, :],
                                                  AF.Square,
                                                  accum_out=colf[0:e, 5:6]),
                 reads=("s2",), writes=("o_t", CF))
            yield
            p.op("DVE", lambda e=uw: V.tensor_scalar(colf[0:e, 4:5],
                                                     colf[0:e, 4:5], 1.0 / H,
                                                     None, OP.mult),
                 reads=(CF,), writes=(CF,))
            yield
            p.op("DVE", lambda e=uw: V.tensor_mul(colf[0:e, 6:7], colf[0:e, 4:5],
                                                  colf[0:e, 4:5]),
                 reads=(CF,), writes=(CF,))
            yield
            p.op("DVE", lambda e=uw: V.scalar_tensor_tensor(
                colf[0:e, 6:7], colf[0:e, 5:6], 1.0 / H, colf[0:e, 6:7],
                OP.mult, OP.subtract),
                 reads=(CF,), writes=(CF,))
            yield
            p.op("DVE", lambda e=uw: V.tensor_scalar(colf[0:e, 6:7],
                                                     colf[0:e, 6:7], EPS, None,
                                                     OP.add),
                 reads=(CF,), writes=(CF,))
            yield
            p.op("ACT", lambda e=uw: A.activation(colf[0:e, 7:8], colf[0:e, 6:7],
                                                  AF.Sqrt),
                 reads=(CF,), writes=(CF,))
            yield
            p.op("DVE", lambda e=uw: V.reciprocal(colf[0:e, 7:8], colf[0:e, 7:8]),
                 reads=(CF,), writes=(CF,))
            yield
            p.op("DVE", lambda e=uw: V.tensor_scalar(o_t[0:e, :], s2[0:e, :],
                                                     colf[0:e, 4:5],
                                                     colf[0:e, 7:8],
                                                     OP.subtract, OP.mult),
                 reads=("s2", CF), writes=("o_t",))
            yield
            store(out_tm[out0 - 2:out1 - 2, :], o_t[out0 - u0:out1 - u0, :],
                  ("o_t",), f"out{u0}")

    def ffn_m2_out(gi):
        f0, f1 = FG[gi]
        nsub = (f1 - f0 + 127) // 128
        for si in range(nsub):
            for em in ffn_m2_mm_list(gi, si):
                em()
            ffn_r1g_load(gi, si)
            for _ in ffn_m2_tail(gi, si):
                pass

    NG = len(FG)
    load_w2q_once = [load_w2q]
    NLOOP = NG if SAFE_ROUND else NG + 2
    for gi in range(NLOOP):
        qprev = gi - 1 if (gi >= 1 and gi - 1 < NG) else None
        m2g = gi - 2 if (gi >= 2 and not SAFE_ROUND) else None
        ffn_A1(gi, qprev, m2g)
        if load_w2q_once:
            load_w2q_once.pop()()
        if SAFE_ROUND:
            if gi >= 1:
                ffn_m2_out(gi - 1)
            for _ in ffn_A2(gi):
                pass
    if SAFE_ROUND:
        for m_ in range(32):
            ffn_quant_chunk(NG - 1, m_)
        ffn_hq_cast(NG - 1)
        ffn_m2_out(NG - 1)

    stC.close()
    root.close()
    n = p.finalize_and_emit()
    return nc, n, dbg


_NC_CACHE = {}


def _get_nc(debug=False):
    key = (debug, SAFE_ROUND)
    if key not in _NC_CACHE:
        _NC_CACHE[key] = build_nc(debug)
    return _NC_CACHE[key]


def make_in_maps(inputs, debug=False):
    f32 = np.float32
    K_EXT = EXT
    x = np.asarray(inputs["L0_attn_x"], f32)
    W_in = np.ascontiguousarray(inputs["L0_attn_W_in"], f32)
    W_B = np.ascontiguousarray(inputs["L0_attn_W_B"], f32)
    W_D = np.ascontiguousarray(inputs["L0_attn_W_D"], f32)
    W_out = np.ascontiguousarray(inputs["L0_attn_W_out"], f32)
    B_mat = np.ascontiguousarray(inputs["L0_attn_B_mat"], f32)
    C_mat = np.ascontiguousarray(inputs["L0_attn_C_mat"], f32)
    W_ig = np.ascontiguousarray(inputs["L0_attn_W_ig"], f32)
    W_fg = np.ascontiguousarray(inputs["L0_attn_W_fg"], f32)
    b_in = np.asarray(inputs["L0_attn_b_in"], f32)
    b_B = np.asarray(inputs["L0_attn_b_B"], f32)
    b_D = np.asarray(inputs["L0_attn_b_D"], f32)
    b_out = np.asarray(inputs["L0_attn_b_out"], f32)
    b_ig = np.asarray(inputs["L0_attn_b_ig"], f32)
    b_fg = np.asarray(inputs["L0_attn_b_fg"], f32)
    gamma = np.asarray(inputs["L0_attn_gamma"], f32)
    beta = np.asarray(inputs["L0_attn_beta"], f32)
    w1 = np.asarray(inputs["L0_ffn_w1"], f32)
    conv_w = np.asarray(inputs["L0_ffn_conv_w"], f32)
    conv_b = np.asarray(inputs["L0_ffn_conv_b"], f32)
    w2 = np.asarray(inputs["L0_ffn_w2"], f32)

    shared = {
        "W_in": W_in, "W_D": W_D, "W_out": W_out,
        "W_in_T": np.ascontiguousarray(W_in.T),
        "W_B_T": np.ascontiguousarray(W_B.T),
        "B_mat": B_mat, "C_mat": C_mat, "W_ig": W_ig, "W_fg": W_fg,
        "b_in_cols": np.ascontiguousarray(b_in.reshape(8, 128).T),
        "b_B_cols": np.ascontiguousarray(b_B.reshape(8, 128).T),
        "b_out_cols": np.ascontiguousarray(b_out.reshape(8, 128).T),
        "b_ig_col": np.ascontiguousarray(b_ig.reshape(S, 1)),
        "b_fg_col": np.ascontiguousarray(b_fg.reshape(S, 1)),
        "gamma_b": np.ascontiguousarray(np.broadcast_to(gamma[None, :], (128, H))),
        "beta_b": np.ascontiguousarray(np.broadcast_to(beta[None, :], (128, H))),
        "bD_row": np.ascontiguousarray(b_D.reshape(1, H)),
        "convw": np.ascontiguousarray(
            conv_w[:, 0, :].reshape(32, 128, 3).transpose(1, 2, 0).reshape(128, 96)),
        "convb": np.ascontiguousarray(conv_b.reshape(32, 128).T),
        "w1t_f": np.ascontiguousarray(w1.T),
        "w2t_f": np.ascontiguousarray(w2.T),
        "ones1": np.ones((1, 128), f32),
        "ones_col": np.ones((128, 1), f32),
        "id_f32": np.eye(128, dtype=f32),
        "onesrow_ext": np.ones((1, K_EXT), ml_dtypes.bfloat16),
        "zrow_ext": np.zeros((1, K_EXT), ml_dtypes.bfloat16),
    }
    in_maps = []
    for c in range(8):
        b, half = c // 2, c % 2
        s = half * 2048
        lo, hi = s - 2, s + 2048 + 2
        xe = np.zeros((EXT, H), f32)
        a, bb = max(0, lo), min(T, hi)
        xe[a - lo:bb - lo] = x[b, a:bb]
        qm = np.zeros((EXTP,), f32)
        qm[a - lo:bb - lo] = 1.0
        m = dict(shared)
        m["x_tm"] = xe
        m["x_fm"] = np.ascontiguousarray(xe.T)
        m["xg_fm"] = np.ascontiguousarray(x[b].T)
        m["qmask"] = np.ascontiguousarray(qm.reshape(NT, 128).T)
        m["hsel0"] = np.full((S, 1), 1.0 - half, f32)
        m["hsel1"] = np.full((S, 1), float(half), f32)
        in_maps.append(m)
    return in_maps


def kernel(**inputs):
    nc, _, _ = _get_nc(False)
    in_maps = make_in_maps(inputs)
    res = run_bass_kernel_spmd(nc, in_maps, core_ids=list(range(8)))
    out = np.empty((B, T, H), np.float32)
    for c in range(8):
        b, half = c // 2, c % 2
        out[b, half * 2048:(half + 1) * 2048] = res.results[c]["out_tm"]
    return out


# version 40
# speedup vs baseline: 1.0887x; 1.0007x over previous
import contextlib

import numpy as np
import ml_dtypes

import concourse.bass as bass
import concourse.mybir as mybir
from concourse.bass_isa import ReduceOp
from concourse.bass_utils import run_bass_kernel_spmd

F32 = mybir.dt.float32
BF16 = mybir.dt.bfloat16
I32 = mybir.dt.int32
I16 = mybir.dt.int16
I8 = mybir.dt.int8
AF = mybir.ActivationFunctionType
OP = mybir.AluOpType
AX = mybir.AxisListType.X

B, T, H, S, DH = 4, 4096, 1024, 16, 4096
EXT = T // 2 + 4
EXTP = 2176
EPS = 1e-5
SAFE_ROUND = False

NT = (EXT + 127) // 128
TILES = [(i * 128, min(EXT, (i + 1) * 128)) for i in range(NT)]
AW = 256
AGROUPS = [(i * AW, (i + 1) * AW) for i in range(7)] + [(7 * AW, EXT)]
AWM = EXT - 7 * AW
GW = 256
FG = [(a, min(a + GW, EXT)) for a in range(0, EXT, GW)]

ENGINES = ("SP", "PE", "DVE", "ACT", "GP")
_ENG_ATTR = {"SP": "sync", "PE": "tensor", "DVE": "vector", "ACT": "scalar",
             "GP": "gpsimd"}


class Prog:

    def __init__(self, nc):
        self.nc = nc
        self.ops = []

    def op(self, eng, fn, reads=(), writes=(), dma=False, group=None):
        self.ops.append(dict(eng=eng, fn=fn, reads=tuple(reads),
                             writes=tuple(writes), dma=dma, group=group,
                             inc=None, sem=None, barrier=False))

    def barrier(self):
        self.ops.append(dict(eng="SP", fn=None, reads=(), writes=(), dma=False,
                             group=None, inc=None, sem=None, barrier=True))

    def finalize_and_emit(self):
        nc, ops = self.nc, self.ops
        last_w, readers = {}, {}
        deps_l = []
        last_per_eng = {}
        last_dma_sem = {}
        dma_i = 0
        NDMA_ = 48
        last_barrier = None
        for i, o in enumerate(ops):
            deps = set()
            if o["barrier"]:
                deps.update(v for v in last_per_eng.values() if v is not None)
                deps.update(v for v in last_dma_sem.values() if v is not None)
                if last_barrier is not None:
                    deps.add(last_barrier)
                deps.discard(i)
                deps_l.append(deps)
                last_barrier = i
                last_per_eng = {}
                last_dma_sem = {}
                continue
            if last_barrier is not None:
                deps.add(last_barrier)
            if o["dma"]:
                last_dma_sem[dma_i % NDMA_] = i
                dma_i += 1
            last_per_eng[(o["eng"], o["dma"])] = i
            for r in o["reads"]:
                if r in last_w:
                    deps.add(last_w[r][0])
            for w in o["writes"]:
                if w in last_w:
                    wi, wg = last_w[w]
                    if not (o["group"] is not None and o["group"] == wg):
                        deps.add(wi)
                deps.update(readers.get(w, ()))
            deps.discard(i)
            deps_l.append(deps)
            for w in o["writes"]:
                last_w[w] = (i, o["group"])
                readers[w] = []
            for r in o["reads"]:
                readers.setdefault(r, []).append(i)

        need_inc = [False] * len(ops)
        for ds in deps_l:
            for d in ds:
                need_inc[d] = True

        es = contextlib.ExitStack()
        chain = {e: es.enter_context(nc.semaphore(f"ch_{e}")) for e in ENGINES}
        NDMA = 48
        dpool = [es.enter_context(nc.semaphore(f"dma{i}")) for i in range(NDMA)]
        cnt = {e: 0 for e in ENGINES}
        dcnt = [0] * NDMA
        dix = 0
        for i, o in enumerate(ops):
            if o["dma"]:
                j = dix % NDMA
                dix += 1
                o["pre_wait"] = (dpool[j], dcnt[j]) if dcnt[j] else None
                dcnt[j] += 16
                o["sem"] = (f"D{j}", dpool[j])
                o["inc"] = dcnt[j]
            elif need_inc[i]:
                cnt[o["eng"]] += 1
                o["sem"] = ("C" + o["eng"], chain[o["eng"]])
                o["inc"] = cnt[o["eng"]]

        know = [None] * len(ops)
        know_is = [None] * len(ops)
        eng_last = {e: None for e in ENGINES}
        waits_l = [None] * len(ops)
        for i, o in enumerate(ops):
            k = (dict(know_is[eng_last[o["eng"]]])
                 if eng_last[o["eng"]] is not None else {})
            waits = []
            for d in sorted(deps_l[i]):
                do = ops[d]
                skey, shandle = do["sem"]
                val = do["inc"]
                if k.get(skey, 0) < val:
                    waits.append((shandle, val))
                    k[skey] = val
                    for kk, vv in know[d].items():
                        if k.get(kk, 0) < vv:
                            k[kk] = vv
            if o.get("pre_wait") is not None:
                sh, val = o["pre_wait"]
                skey = o["sem"][0]
                if k.get(skey, 0) < val:
                    waits.append((sh, val))
                    k[skey] = val
            if o["sem"] is not None:
                know[i] = dict(k)
                skey = o["sem"][0]
                know[i][skey] = max(know[i].get(skey, 0), o["inc"])
            else:
                know[i] = k
            know_is[i] = k
            eng_last[o["eng"]] = i
            waits_l[i] = waits

        with nc.Block() as block:
            for eng in ENGINES:
                idxs = [i for i, o in enumerate(ops) if o["eng"] == eng]
                if not idxs and eng != "SP":
                    continue

                def body(handle, eng=eng, idxs=idxs):
                    for i in idxs:
                        o = ops[i]
                        for sh, val in waits_l[i]:
                            handle.wait_ge(sh, val)
                        instr = o["fn"]() if o["fn"] is not None else handle.nop()
                        if o["sem"] is not None:
                            instr.then_inc(o["sem"][1], 16 if o["dma"] else 1)
                    if eng == "SP":
                        for e2 in ENGINES:
                            if cnt.get(e2, 0):
                                handle.wait_ge(chain[e2], cnt[e2])
                        for j in range(NDMA):
                            if dcnt[j]:
                                handle.wait_ge(dpool[j], dcnt[j])

                getattr(block, _ENG_ATTR[eng])(body)
        self._sem_stack = es
        return len(ops)


def build_nc(debug=False):
    nc = bass.Bass("TRN2", target_bir_lowering=False, debug=False)
    p = Prog(nc)
    V, A, PE, GP, SY = nc.vector, nc.scalar, nc.tensor, nc.gpsimd, nc.sync
    QENG = {"SP": SY, "ACT": A, "GP": GP}

    def din(name, shape, dt=F32):
        return nc.dram_tensor(name, shape, dt, kind="ExternalInput").ap()

    x_tm = din("x_tm", [EXT, H])
    x_fm = din("x_fm", [H, EXT])
    xg_fm = din("xg_fm", [H, T])
    W_in = din("W_in", [H, H]); W_D = din("W_D", [H, H]); W_out = din("W_out", [H, H])
    W_in_T = din("W_in_T", [H, H]); W_B_T = din("W_B_T", [H, H])
    B_mat = din("B_mat", [H, S]); C_mat = din("C_mat", [S, H])
    W_ig = din("W_ig", [H, S]); W_fg = din("W_fg", [H, S])
    b_in_cols = din("b_in_cols", [128, 8]); b_B_cols = din("b_B_cols", [128, 8])
    b_out_cols = din("b_out_cols", [128, 8])
    b_ig_col = din("b_ig_col", [S, 1]); b_fg_col = din("b_fg_col", [S, 1])
    gamma_b = din("gamma_b", [128, H]); beta_b = din("beta_b", [128, H])
    bD_row = din("bD_row", [1, H])
    convw = din("convw", [128, 3 * 32])
    convb = din("convb", [128, 32])
    w1t_f = din("w1t_f", [H, DH])
    w2t_f = din("w2t_f", [DH, H])
    qmask = din("qmask", [128, NT])
    hsel0 = din("hsel0", [S, 1]); hsel1 = din("hsel1", [S, 1])
    ones1 = din("ones1", [1, 128])
    ones_col = din("ones_col", [128, 1])
    id_f32 = din("id_f32", [128, 128])
    onesrow_ext = din("onesrow_ext", [1, EXT], BF16)
    zrow_ext = din("zrow_ext", [1, EXT], BF16)
    out_tm = nc.dram_tensor("out_tm", [T // 2, H], F32, kind="ExternalOutput").ap()
    w1q_hbm = nc.dram_tensor("w1q_hbm", [H, DH], BF16, kind="Internal").ap()
    w2q_hbm = nc.dram_tensor("w2q_hbm", [DH, H], BF16, kind="Internal").ap()
    r1_hbm = nc.dram_tensor("r1_hbm", [EXT, H], F32, kind="Internal").ap()
    r1q_hbm = nc.dram_tensor("r1q_hbm", [EXTP, H], BF16, kind="Internal").ap()
    dbg = {}
    if debug:
        for nm, shape, dt in [("d_gates", [48, T], F32), ("d_hs", [S, EXT], F32),
                              ("d_xp", [128, 8 * 512], F32),
                              ("d_y", [128, 8 * 512], F32),
                              ("d_z", [128, 8 * 512], F32),
                              ("d_r1", [EXT, H], F32),
                              ("d_h", [128, 32 * (GW + 2)], F32),
                              ("d_wg", [128, 8 * 48], F32),
                              ("d_dq", [1, EXT], F32)]:
            dbg[nm] = nc.dram_tensor(nm, shape, dt, kind="ExternalOutput").ap()

    root = contextlib.ExitStack()

    def SB(name, shape, dt=F32, stack=None):
        return (stack or root).enter_context(nc.sbuf_tensor(name, shape, dt))

    def PSUM(name, shape, dt=F32, stack=None):
        return (stack or root).enter_context(nc.psum_tensor(name, shape, dt))

    def load(dst_ap, src_ap, region, q="SP", reads=()):
        p.op(q, lambda d=dst_ap, s=src_ap, e=QENG[q]: e.dma_start(out=d, in_=s),
             reads=reads, writes=(region,), dma=True)

    def store(dst_ap, src_ap, reads, region, q="ACT"):
        p.op(q, lambda d=dst_ap, s=src_ap, e=QENG[q]: e.dma_start(out=d, in_=s),
             reads=reads, writes=(region,), dma=True)

    def load_fm(dst2d, width, src_slice, region, q="SP", stride=None):
        st = stride if stride is not None else width
        nch = src_slice.shape[0] // 128
        d3 = dst2d[:, 0:nch * st].rearrange("p (c t) -> p c t",
                                            c=nch)[:, :, 0:width]
        s3 = src_slice.rearrange("(c p) t -> p c t", p=128)
        p.op(q, lambda d=d3, s=s3, e=QENG[q]: e.dma_start(out=d, in_=s),
             reads=(), writes=(region,), dma=True)

    def mm(group, out_ap, lhsT_ap, rhs_ap, reads, wr, start, stop):
        p.op("PE", lambda o=out_ap, l=lhsT_ap, r=rhs_ap, st=start, sp=stop:
             PE.matmul(o, l, r, start=st, stop=sp),
             reads=reads, writes=(wr,), group=group)

    stAB = contextlib.ExitStack()
    s_qmask = SB("s_qmask", [128, NT])
    s_hsel0 = SB("s_hsel0", [S, 1]); s_hsel1 = SB("s_hsel1", [S, 1])
    s_ones1 = SB("s_ones1", [1, 128])
    s_onesc = SB("s_onesc", [128, 1])
    s_idf = SB("s_idf", [128, 128])
    s_binc = SB("s_binc", [128, 8]); s_boutc = SB("s_boutc", [128, 8])
    s_w1mean = SB("s_w1mean", [128, 1]); s_w2mean = SB("s_w2mean", [128, 1])
    dqcols = SB("dqcols", [128, NT])
    s_convw = SB("s_convw", [128, 3 * 32]); s_convb = SB("s_convb", [128, 32])
    s_gamb = SB("s_gamb", [128, H], stack=stAB)
    s_betb = SB("s_betb", [128, H], stack=stAB)
    hsA_hi = SB("hsA_hi", [S + 1, EXT], BF16, stack=stAB)
    hsA_lo = SB("hsA_lo", [S + 1, EXT], BF16, stack=stAB)
    CA_hi = SB("CA_hi", [S + 1, H], BF16, stack=stAB)
    CA_lo = SB("CA_lo", [S + 1, H], BF16, stack=stAB)

    TW = 1024 if SAFE_ROUND else 2048
    NTB = 4

    def alloc_tern(tstack):
        twfs = tuple(SB(f"twf{i}", [128, TW], stack=tstack) for i in range(NTB))
        _tu = tuple(SB(f"tu{i}", [128, TW], stack=tstack) for i in range(2))
        tus = (_tu[0], _tu[1], _tu[0], _tu[1])
        _tk16 = tuple(SB(f"tk16_{i}", [128, TW], I16, stack=tstack)
                      for i in range(2))
        tk16s = (_tk16[0], _tk16[1], _tk16[0], _tk16[1])
        _tb = tuple(SB(f"tb{i}", [128, TW], BF16, stack=tstack) for i in range(2))
        tbs = (_tb[0], _tb[1], _tb[0], _tb[1])
        if SAFE_ROUND:
            tk = SB("tk", [128, TW], I32, stack=tstack)
            trr = SB("trr", [128, TW], stack=tstack)
            tcp = SB("tcp", [128, TW], stack=tstack)
        else:
            tk = trr = tcp = None
        tcol = SB("tcol", [128, 10], stack=tstack)
        return twfs, tus, tk16s, tbs, tk, trr, tcp, tcol

    for dst, src, rg in [(s_gamb, gamma_b, "s_gamb"), (s_betb, beta_b, "s_betb"),
                         (s_qmask, qmask, "s_qmask"), (s_hsel0, hsel0, "s_hsel0"),
                         (s_hsel1, hsel1, "s_hsel1"), (s_ones1, ones1, "s_ones1"), (s_onesc, ones_col, "s_onesc"),
                         (s_idf, id_f32, "s_idf"), (s_binc, b_in_cols, "s_binc"),
                         (s_boutc, b_out_cols, "s_boutc"),
                         (s_convw, convw, "s_convw"), (s_convb, convb, "s_convb")]:
        load(dst[:], src[:], rg)

    stA = contextlib.ExitStack()
    g80 = SB("g80", [80, T], stack=stA)
    hs_buf = SB("hs_buf", [S, T + 4], stack=stA)
    GM = 80
    G = SB("G", [128, 8 * GM], stack=stA)
    WGf = SB("WGf", [128, 8 * GM], stack=stA)
    WG_hi = SB("WG_hi", [128, 8 * GM], BF16, stack=stA)
    WG_lo = SB("WG_lo", [128, 8 * GM], BF16, stack=stA)
    bG = SB("bG", [GM, 1], stack=stA)
    wtrs = (SB("wtr0", [128, H], stack=stA),
            SB("wtr1", [128, H], stack=stA))
    bmt = SB("bmt", [128, 8 * S], stack=stA)
    wigt = SB("wigt", [128, 8 * S], stack=stA)
    wfgt = SB("wfgt", [128, 8 * S], stack=stA)
    bic = SB("bic", [128, 8], stack=stA)
    bbc = SB("bbc", [128, 8], stack=stA)
    ps_pre = PSUM("ps_pre", [128, 1024], stack=stA)
    ps_g = PSUM("ps_g", [GM, 512], stack=stA)

    load_fm(bmt, S, B_mat[:], "bmt")
    load(bic[:], b_in_cols[:], "bic")
    load(bbc[:], b_B_cols[:], "bbc")
    load_fm(wigt, S, W_ig[:], "wigt")
    load_fm(wfgt, S, W_fg[:], "wfgt")

    p.op("DVE", lambda: V.memset(G[:, :], 0.0), writes=("Ginit",))
    for fc in range(8):
        wtr, wr_ = wtrs[fc % 2], f"wtr{fc % 2}"
        po = (fc % 2) * 512
        load_fm(wtr, 128, W_B_T[:, fc * 128:(fc + 1) * 128], wr_)
        for j in range(8):
            mm(f"P{fc}", ps_pre[:, po:po + 16], wtr[:, j * 128:(j + 1) * 128],
               bmt[:, j * S:(j + 1) * S], reads=(wr_, "bmt"),
               wr=f"psPre{fc % 2}", start=(j == 0), stop=(j == 7))
        p.op("DVE", lambda o=G[:, fc * GM:fc * GM + 16],
             i=ps_pre[:, po:po + 16], fc=fc:
             V.tensor_copy(o, i), reads=(f"psPre{fc % 2}", "Ginit"),
             writes=(f"G{fc}",))
        p.op("DVE", lambda o=G[:, fc * GM + 32:fc * GM + 48],
             i=wigt[:, fc * S:(fc + 1) * S]: V.tensor_copy(o, i),
             reads=("wigt", "Ginit"), writes=(f"G{fc}",))
        p.op("DVE", lambda o=G[:, fc * GM + 64:fc * GM + 80],
             i=wfgt[:, fc * S:(fc + 1) * S]: V.tensor_copy(o, i),
             reads=("wfgt", "Ginit"), writes=(f"G{fc}",))

    for fin in range(8):
        wtr, wr_ = wtrs[fin % 2], f"wtr{fin % 2}"
        po = (fin % 2) * 512
        load_fm(wtr, 128, W_in_T[:, fin * 128:(fin + 1) * 128], wr_)
        for fc in range(8):
            mm(f"WG{fin}", ps_pre[:, po:po + GM],
               wtr[:, fc * 128:(fc + 1) * 128],
               G[:, fc * GM:(fc + 1) * GM], reads=(wr_, f"G{fc}"),
               wr=f"psPre{fin % 2}", start=(fc == 0), stop=(fc == 7))
        p.op("DVE", lambda o=WGf[:, fin * GM:(fin + 1) * GM],
             i=ps_pre[:, po:po + GM], fin=fin:
             V.tensor_copy(o, i), reads=(f"psPre{fin % 2}",),
             writes=(f"WGf{fin}",))
        p.op("ACT", lambda o=WG_hi[:, fin * GM:(fin + 1) * GM],
             i=WGf[:, fin * GM:(fin + 1) * GM]: A.activation(o, i, AF.Copy),
             reads=(f"WGf{fin}",), writes=(f"WGhi{fin}",))
        p.op("DVE", lambda o=WG_lo[:, fin * GM:(fin + 1) * GM],
             a=WGf[:, fin * GM:(fin + 1) * GM],
             b=WG_hi[:, fin * GM:(fin + 1) * GM]:
             V.tensor_sub(o, a, b), reads=(f"WGf{fin}", f"WGhi{fin}"),
             writes=(f"WGlo{fin}",))
    if debug:
        store(dbg["d_wg"][:], WGf[:, 0:8 * 48], tuple(f"WGf{i}" for i in range(8)),
              "d_wg")

    t1 = SB("t1", [128, 8], stack=stA)
    for j in range(8):
        wtr, wr_ = wtrs[j % 2], f"wtr{j % 2}"
        po = (j % 2) * 512
        load_fm(wtr, 128, W_B_T[:, j * 128:(j + 1) * 128], wr_)
        for fc in range(8):
            mm(f"t1{j}", ps_pre[:, po:po + 1], wtr[:, fc * 128:(fc + 1) * 128],
               bic[:, fc:fc + 1], reads=(wr_, "bic"),
               wr=f"psPre{j % 2}", start=(fc == 0), stop=(fc == 7))
        p.op("DVE", lambda o=t1[:, j:j + 1], i=ps_pre[:, po:po + 1],
             b=bbc[:, j:j + 1], j=j:
             V.tensor_add(o, i, b), reads=(f"psPre{j % 2}", "bbc"),
             writes=("t1",))
    for j in range(8):
        mm("bP", ps_pre[0:16, 0:1], bmt[:, j * S:(j + 1) * S], t1[:, j:j + 1],
           reads=("bmt", "t1"), wr="psPre0", start=(j == 0), stop=(j == 7))
    btmp = SB("btmp", [S, 3], stack=stA)
    p.op("DVE", lambda: V.tensor_copy(btmp[:, 0:1], ps_pre[0:16, 0:1]),
         reads=("psPre0",), writes=("btmp0",))
    for which, wt, wrg, bcol, col in (("ig", wigt, "wigt", b_ig_col, 1),
                                      ("fg", wfgt, "wfgt", b_fg_col, 2)):
        for fc in range(8):
            mm(f"b{which}", ps_pre[0:16, 1:2], wt[:, fc * S:(fc + 1) * S],
               bic[:, fc:fc + 1], reads=(wrg, "bic"),
               wr="psPre0", start=(fc == 0), stop=(fc == 7))
        bcs = SB(f"bc_{which}", [S, 1], stack=stA)
        load(bcs[:], bcol[:], f"bc_{which}")
        p.op("DVE", lambda o=btmp[:, col:col + 1], i=ps_pre[0:16, 1:2], b=bcs[:]:
             V.tensor_add(o, i, b), reads=("psPre0", f"bc_{which}"),
             writes=(f"btmp{col}",))
    store(bG[0:16, :], btmp[:, 0:1], ("btmp0",), "bG")
    store(bG[32:48, :], btmp[:, 1:2], ("btmp1",), "bG")
    store(bG[64:80, :], btmp[:, 2:3], ("btmp2",), "bG")

    ctmp = SB("ctmp", [S, H], stack=stA)
    bdtmp = SB("bdtmp", [1, H], stack=stA)
    load(ctmp[:], C_mat[:], "ctmp")
    load(bdtmp[:], bD_row[:], "bdtmp")
    p.op("ACT", lambda: A.activation(CA_hi[0:S, :], ctmp[:], AF.Copy),
         reads=("ctmp",), writes=("CAhi",))
    p.op("DVE", lambda: V.tensor_sub(CA_lo[0:S, :], ctmp[:], CA_hi[0:S, :]),
         reads=("ctmp", "CAhi"), writes=("CAlo",))
    bdh = SB("bdh", [1, H], BF16, stack=stA)
    bdl = SB("bdl", [1, H], BF16, stack=stA)
    p.op("ACT", lambda: A.activation(bdh[:], bdtmp[:], AF.Copy),
         reads=("bdtmp",), writes=("bdh",))
    p.op("DVE", lambda: V.tensor_sub(bdl[:], bdtmp[:], bdh[:]),
         reads=("bdtmp", "bdh"), writes=("bdl",))
    store(CA_hi[S:S + 1, :], bdh[:], ("bdh",), "CAhi", q="SP")
    store(CA_lo[S:S + 1, :], bdl[:], ("bdl",), "CAlo", q="SP")

    twfs, tus, tk16s, tbs, tk, trr, tcp, tcol = alloc_tern(stA)

    def tern_slices(nrows, width):
        return [(rc, c0, min(width, c0 + TW))
                for rc in range(nrows) for c0 in range(0, width, TW)]

    def tern_pass1(src, nrows, width, mean_out, mean_rg, tg,
                   ps_hbuf, ps_rg, scol):
        stag = f"tcol{scol}"
        slices = tern_slices(nrows, width)
        for i, (rc, c0, c1) in enumerate(slices):
            wv = c1 - c0
            pr = i % NTB
            twf, tu = twfs[pr], tus[pr]
            load(twf[:, 0:wv], src[rc * 128:(rc + 1) * 128, c0:c1], f"twf{pr}")
            if i % 2 == 0:
                p.op("ACT", lambda twf=twf, tu=tu, e=wv: A.activation(
                    tu[:, 0:e], twf[:, 0:e], AF.Abs, accum_out=tcol[:, 0:1]),
                     reads=(f"twf{pr}",), writes=(f"tu{pr % 2}", "tcol0"))
                acc, arg = "tcol0", 0
            else:
                p.op("DVE", lambda twf=twf, e=wv: V.tensor_reduce(
                    tcol[:, 3:4].rearrange("p (t o) -> p t o", o=1),
                    twf[:, 0:e].rearrange("p (t e) -> p t e", t=1),
                    AX, OP.add, apply_absolute_value=True),
                     reads=(f"twf{pr}",), writes=("tcol3",))
                acc, arg = "tcol3", 3
            ac = 6 + (i % 4)
            if i < 4:
                p.op("DVE", lambda ac=ac, a=arg: V.tensor_copy(
                    tcol[:, ac:ac + 1], tcol[:, a:a + 1]),
                     reads=(acc,), writes=(f"tcol{ac}",))
            else:
                p.op("DVE", lambda a=arg, ac=ac: V.tensor_add(
                    tcol[:, ac:ac + 1], tcol[:, ac:ac + 1], tcol[:, a:a + 1]),
                     reads=(acc, f"tcol{ac}"), writes=(f"tcol{ac}",))
            yield
        nun = len(slices)
        p.op("DVE", lambda: V.tensor_add(tcol[:, 6:7], tcol[:, 6:7],
                                         tcol[:, 7:8]),
             reads=("tcol6", "tcol7"), writes=("tcol6",))
        if nun > 2:
            p.op("DVE", lambda: V.tensor_add(tcol[:, 8:9], tcol[:, 8:9],
                                             tcol[:, 9:10]),
                 reads=("tcol8", "tcol9"), writes=("tcol8",))
            p.op("DVE", lambda: V.tensor_add(tcol[:, 6:7], tcol[:, 6:7],
                                             tcol[:, 8:9]),
                 reads=("tcol6", "tcol8"), writes=("tcol6",))
        p.op("DVE", lambda: V.tensor_scalar(tcol[:, 1:2], tcol[:, 6:7],
                                            1.0 / (nrows * 128 * width), None,
                                            OP.mult),
             reads=("tcol6",), writes=("tcol1",))
        mm("parsum" + tg, ps_hbuf[0:1, 0:1], s_onesc[:], tcol[:, 1:2],
           reads=("s_onesc", "tcol1"), wr=ps_rg, start=True, stop=True)
        p.op("DVE", lambda: V.tensor_copy(tcol[0:1, 1:2], ps_hbuf[0:1, 0:1]),
             reads=(ps_rg,), writes=("tcol1",))
        mm("parbc" + tg, ps_hbuf[0:128, 0:1], s_ones1[:], tcol[0:1, 1:2],
           reads=("s_ones1", "tcol1"), wr=ps_rg, start=True, stop=True)
        p.op("DVE", lambda: V.tensor_copy(tcol[:, 1:2], ps_hbuf[0:128, 0:1]),
             reads=(ps_rg,), writes=("tcol1",))
        p.op("DVE", lambda: V.tensor_scalar_max(mean_out[:], tcol[:, 1:2], 1e-5),
             reads=("tcol1",), writes=(mean_rg,))
        p.op("DVE", lambda mo=mean_out, sc=scol: V.reciprocal(
            tcol[:, sc:sc + 1], mo[:]),
             reads=(mean_rg,), writes=(stag,))
        yield

    def tern_pass2(src, dst_hbm, nrows, width, tg, scol):
        stag = f"tcol{scol}"
        slices = tern_slices(nrows, width)
        for i, (rc, c0, c1) in enumerate(slices):
            wv = c1 - c0
            pr = i % NTB
            twf, tu, tk16, tb = twfs[pr], tus[pr], tk16s[pr], tbs[pr]
            load(twf[:, 0:wv], src[rc * 128:(rc + 1) * 128, c0:c1], f"twf{pr}")
            if SAFE_ROUND:
                p.op("DVE", lambda twf=twf, tu=tu, e=wv: V.tensor_scalar(
                    tu[:, 0:e], twf[:, 0:e], tcol[:, scol:scol + 1], None,
                    OP.mult),
                     reads=(f"twf{pr}", stag), writes=(f"tu{pr % 2}",))
                p.op("DVE", lambda tu=tu, e=wv: V.tensor_copy(tk[:, 0:e],
                                                              tu[:, 0:e]),
                     reads=(f"tu{pr % 2}",), writes=("tk",))
                p.op("DVE", lambda tu=tu, e=wv: V.tensor_sub(trr[:, 0:e],
                                                             tu[:, 0:e],
                                                             tk[:, 0:e]),
                     reads=(f"tu{pr % 2}", "tk"), writes=("trr",))
                p.op("DVE", lambda e=wv: V.tensor_scalar(tcp[:, 0:e], trr[:, 0:e],
                                                         0.5, None, OP.is_gt),
                     reads=("trr",), writes=("tcp",))
                p.op("DVE", lambda tu=tu, e=wv: V.scalar_tensor_tensor(
                    tu[:, 0:e], trr[:, 0:e], -0.5, tcp[:, 0:e], OP.is_lt,
                    OP.subtract),
                     reads=("trr", "tcp"), writes=(f"tu{pr % 2}",))
                p.op("DVE", lambda tu=tu, e=wv: V.tensor_sub(tu[:, 0:e],
                                                             tk[:, 0:e],
                                                             tu[:, 0:e]),
                     reads=(f"tu{pr % 2}", "tk"), writes=(f"tu{pr % 2}",))
                p.op("DVE", lambda tu=tu, tb=tb, e=wv: V.tensor_scalar(
                    tb[:, 0:e], tu[:, 0:e], 1.0, -1.0, OP.min, OP.max),
                     reads=(f"tu{pr % 2}",), writes=(f"tb{pr % 2}",))
            elif i % 2 == 0:
                p.op("ACT", lambda twf=twf, tu=tu, e=wv: A.activation(
                    tu[:, 0:e], twf[:, 0:e], AF.Identity,
                    scale=tcol[:, scol:scol + 1]),
                     reads=(f"twf{pr}", stag), writes=(f"tu{pr % 2}",))
                p.op("DVE", lambda tu=tu, tk16=tk16, e=wv: V.tensor_scalar(
                    tk16[:, 0:e], tu[:, 0:e], 1.49, -1.49, OP.min, OP.max),
                     reads=(f"tu{pr % 2}",), writes=(f"tk16{pr % 2}",))
                p.op("ACT", lambda tk16=tk16, tb=tb, e=wv: A.activation(
                    tb[:, 0:e], tk16[:, 0:e], AF.Copy),
                     reads=(f"tk16{pr % 2}",), writes=(f"tb{pr % 2}",))
            else:
                p.op("DVE", lambda twf=twf, tu=tu, e=wv: V.tensor_scalar(
                    tu[:, 0:e], twf[:, 0:e], tcol[:, scol:scol + 1], 1.49,
                    OP.mult, OP.min),
                     reads=(f"twf{pr}", stag), writes=(f"tu{pr % 2}",))
                p.op("DVE", lambda tu=tu, tk16=tk16, e=wv: V.tensor_scalar(
                    tk16[:, 0:e], tu[:, 0:e], -1.49, None, OP.max),
                     reads=(f"tu{pr % 2}",), writes=(f"tk16{pr % 2}",))
                p.op("ACT", lambda tk16=tk16, tb=tb, e=wv: A.activation(
                    tb[:, 0:e], tk16[:, 0:e], AF.Copy),
                     reads=(f"tk16{pr % 2}",), writes=(f"tb{pr % 2}",))
            store(dst_hbm[rc * 128:(rc + 1) * 128, c0:c1], tb[:, 0:wv],
                  (f"tb{pr % 2}",), f"q{tg}{rc}_{c0}")
            yield

    def tern_p1_all(ps_hbuf, ps_rg):
        yield from tern_pass1(w1t_f, 8, DH, s_w1mean, "s_w1mean", "w1",
                              ps_hbuf, ps_rg, 2)
        yield from tern_pass1(w2t_f, 32, H, s_w2mean, "s_w2mean", "w2",
                              ps_hbuf, ps_rg, 4)

    def tern_p2_all():
        yield from tern_pass2(w1t_f, w1q_hbm, 8, DH, "w1", 2)
        yield from tern_pass2(w2t_f, w2q_hbm, 32, H, "w2", 4)

    def tern_all():
        yield from tern_p1_all(ps_pre, "psPre0")
        yield from tern_p2_all()

    tern_p1 = tern_all()

    def pump_p1(n):
        for _ in range(n):
            try:
                next(tern_p1)
            except StopIteration:
                break

    GSL = 256
    xgfs = (SB("xgf0", [128, 8 * GSL], stack=stA),
            SB("xgf1", [128, 8 * GSL], stack=stA))
    xghs = (SB("xgh0", [128, 8 * GSL], BF16, stack=stA),
            SB("xgh1", [128, 8 * GSL], BF16, stack=stA))
    xgls = (SB("xgl0", [128, 8 * GSL], BF16, stack=stA),
            SB("xgl1", [128, 8 * GSL], BF16, stack=stA))
    for sl in range(T // GSL):
        c0 = sl * GSL
        pr = sl % 2
        xgf, xgh, xgl = xgfs[pr], xghs[pr], xgls[pr]
        go = pr * GSL
        load_fm(xgf, GSL, xg_fm[:, c0:c0 + GSL], f"xgf{pr}")
        p.op("ACT", lambda xgf=xgf, xgh=xgh: A.activation(xgh[:], xgf[:],
                                                          AF.Copy),
             reads=(f"xgf{pr}",), writes=(f"xgh{pr}",))
        p.op("DVE", lambda xgf=xgf, xgh=xgh, xgl=xgl:
             V.tensor_sub(xgl[:], xgf[:], xgh[:]),
             reads=(f"xgf{pr}", f"xgh{pr}"), writes=(f"xgl{pr}",))
        for c in range(8):
            sh = WG_hi[:, c * GM:(c + 1) * GM]
            slo = WG_lo[:, c * GM:(c + 1) * GM]
            mh = xgh[:, c * GSL:(c + 1) * GSL]
            ml = xgl[:, c * GSL:(c + 1) * GSL]
            mm(f"g{sl}", ps_g[:, go:go + GSL], sh, mh,
               reads=(f"WGhi{c}", f"xgh{pr}"),
               wr=f"psG{pr}", start=(c == 0), stop=False)
            mm(f"g{sl}", ps_g[:, go:go + GSL], sh, ml,
               reads=(f"WGhi{c}", f"xgl{pr}"),
               wr=f"psG{pr}", start=False, stop=False)
            mm(f"g{sl}", ps_g[:, go:go + GSL], slo, mh,
               reads=(f"WGlo{c}", f"xgh{pr}"),
               wr=f"psG{pr}", start=False, stop=(c == 7))
        p.op("ACT", lambda o=g80[0:16, c0:c0 + GSL], go=go:
             A.activation(o, ps_g[0:16, go:go + GSL], AF.Identity,
                          bias=bG[0:16, :]),
             reads=(f"psG{pr}", "bG"), writes=(f"gB{sl % 8}",))
        p.op("ACT", lambda o=g80[32:48, c0:c0 + GSL], go=go:
             A.activation(o, ps_g[32:48, go:go + GSL], AF.Sigmoid,
                          bias=bG[32:48, :]),
             reads=(f"psG{pr}", "bG"), writes=(f"gI{sl % 8}",))
        p.op("ACT", lambda o=g80[64:80, c0:c0 + GSL], go=go:
             A.activation(o, ps_g[64:80, go:go + GSL], AF.Sigmoid,
                          bias=bG[64:80, :]),
             reads=(f"psG{pr}", "bG"), writes=(f"gF{sl % 8}",))
        pump_p1(9)
    gr = tuple(f"gB{i}" for i in range(8)) + tuple(f"gI{i}" for i in range(8)) + tuple(f"gF{i}" for i in range(8))
    if debug:
        store(dbg["d_gates"][0:16, :], g80[0:16, :], gr, "d_gates")
        store(dbg["d_gates"][16:32, :], g80[32:48, :], gr, "d_gates")
        store(dbg["d_gates"][32:48, :], g80[64:80, :], gr, "d_gates")

    hs_e = SB("hs_e", [S, EXT], stack=stA)
    TH = T // 2
    uu = g80[0:16, :]
    store(xgfs[0][0:16, :TH], g80[32:48, 0:TH], gr, "igc", q="SP")
    store(xgfs[1][0:16, :TH], g80[32:48, TH:T], gr, "igc2", q="SP")
    p.op("DVE", lambda: V.tensor_mul(uu[:, 0:TH], xgfs[0][0:16, :TH],
                                     g80[0:16, 0:TH]),
         reads=gr + ("igc",), writes=("uu",) + tuple(f"gB{i}" for i in range(8)))
    p.op("DVE", lambda: V.tensor_mul(uu[:, TH:T], xgfs[1][0:16, :TH],
                                     g80[0:16, TH:T]),
         reads=gr + ("igc2", "uu"), writes=("uu",))
    p.op("DVE", lambda: V.memset(hs_buf[:, 0:2], 0.0), writes=("hs",))
    p.op("DVE", lambda: V.memset(hs_buf[:, T + 2:T + 4], 0.0), writes=("hs",))
    store(xgfs[0][0:16, :TH], g80[64:80, 0:TH], gr + ("uu", "igc"), "fgc",
          q="SP")
    store(xgfs[1][0:16, :TH], g80[64:80, TH:T], gr + ("uu", "igc2"), "fgc2",
          q="SP")
    p.op("DVE", lambda: V.tensor_tensor_scan(hs_buf[:, 2:TH + 2],
                                             xgfs[0][0:16, :TH],
                                             uu[:, 0:TH], 0.0,
                                             OP.mult, OP.add),
         reads=gr + ("uu", "fgc"), writes=("hs",))
    p.op("DVE", lambda: V.tensor_tensor_scan(hs_buf[:, TH + 2:T + 2],
                                             xgfs[1][0:16, :TH],
                                             uu[:, TH:T],
                                             hs_buf[:, TH + 1:TH + 2],
                                             OP.mult, OP.add),
         reads=gr + ("uu", "fgc2", "hs"), writes=("hs",))
    p.op("DVE", lambda: V.tensor_scalar(hs_e[:], hs_buf[:, 0:EXT], s_hsel0[:],
                                        None, OP.mult),
         reads=("hs", "s_hsel0"), writes=("hs_e",))
    p.op("DVE", lambda: V.scalar_tensor_tensor(hs_e[:], hs_buf[:, 2048:2048 + EXT],
                                               s_hsel1[:], hs_e[:], OP.mult, OP.add),
         reads=("hs", "s_hsel1", "hs_e"), writes=("hs_e",))
    if debug:
        store(dbg["d_hs"][:], hs_e[:], ("hs_e",), "d_hs")
    p.op("ACT", lambda: A.activation(hsA_hi[0:S, :], hs_e[:], AF.Copy),
         reads=("hs_e",), writes=("hsAhi",))
    p.op("DVE", lambda: V.tensor_sub(hsA_lo[0:S, :], hs_e[:], hsA_hi[0:S, :]),
         reads=("hs_e", "hsAhi"), writes=("hsAlo",))
    load(hsA_hi[S:S + 1, :], onesrow_ext[:], "hsAhi")
    load(hsA_lo[S:S + 1, :], zrow_ext[:], "hsAlo")

    pump_p1(10 ** 9)

    stA.close()
    p.barrier()

    stB = contextlib.ExitStack()
    Wq = {}
    for nm in ("Win", "WD", "Wout"):
        Wq[nm] = (SB(nm + "_hi", [128, 8 * H], BF16, stack=stB),
                  SB(nm + "_lo", [128, 8 * H], BF16, stack=stB))
    xtm = SB("xtm", [128, H], stack=stB)
    xb0 = SB("xb0", [128, 8 * AWM], stack=stB)
    xb1 = SB("xb1", [128, 8 * AWM], stack=stB)
    zS2 = (SB("zS0", [128, 8 * AWM], stack=stB),
           SB("zS1", [128, 8 * AWM], stack=stB))
    xp_f = SB("xp_f", [128, 8 * AWM], stack=stB)
    yf_c = SB("yf_c", [128, AWM], stack=stB)
    wi = 0
    for nm, src in (("Win", W_in), ("WD", W_D), ("Wout", W_out)):
        hi, lo = Wq[nm]
        for c in range(8):
            pr = wi % 2
            wi += 1
            wtmp = (xb0, xb1)[pr][:, 0:H]
            load(wtmp, src[c * 128:(c + 1) * 128, :], f"xb{pr}")
            if pr == 0:
                p.op("ACT", lambda o=hi[:, c * H:(c + 1) * H], a=wtmp:
                     A.activation(o, a, AF.Copy),
                     reads=(f"xb{pr}",), writes=(f"{nm}hi{c}",))
            else:
                p.op("DVE", lambda o=hi[:, c * H:(c + 1) * H], a=wtmp:
                     V.tensor_copy(o, a),
                     reads=(f"xb{pr}",), writes=(f"{nm}hi{c}",))
            p.op("GP", lambda o=lo[:, c * H:(c + 1) * H], a=wtmp,
                 b=hi[:, c * H:(c + 1) * H]: GP.tensor_sub(o, a, b),
                 reads=(f"xb{pr}", f"{nm}hi{c}"), writes=(f"{nm}lo{c}",))

    pA_h = SB("pA_h", [128, 8 * AWM], BF16, stack=stB)
    pA_l = SB("pA_l", [128, 8 * AWM], BF16, stack=stB)
    pB_h = SB("pB_h", [128, 8 * AWM], BF16, stack=stB)
    pB_l = SB("pB_l", [128, 8 * AWM], BF16, stack=stB)
    z_tm = SB("z_tm", [128, H], stack=stB)
    zsq = SB("zsq", [128, H], stack=stB)
    s_s = z_tm
    sc_t = SB("sc_t", [128, H], stack=stB)
    r1t = z_tm
    q_i = SB("q_i", [128, H], I16, stack=stB)
    q_b = SB("q_b", [128, H], BF16, stack=stB)
    colz = SB("colz", [128, 16], stack=stB)
    if SAFE_ROUND:
        rr_t = SB("rr_t", [128, H], stack=stB)
        cp_t = SB("cp_t", [128, H], stack=stB)
    ps_a = PSUM("ps_a", [128, 2048], stack=stB)
    ps_tr = PSUM("ps_tr", [128, 1024], stack=stB)

    lgen = [None]

    def lpump(n):
        if lgen[0] is None:
            return
        for _ in range(n):
            try:
                next(lgen[0])
            except StopIteration:
                lgen[0] = None
                break

    def attn_mm(gi):
        g0, g1 = AGROUPS[gi]
        w = g1 - g0
        tg = f"A{gi}"
        zS = zS2[gi % 2]
        ZR = f"zS{gi % 2}"
        xb = (xb0, xb1)[gi % 2]
        xbr = f"xb{gi % 2}"
        load_fm(xb, w, x_fm[:, g0:g1], xbr, stride=AWM)
        for k in range(8):
            p.op("ACT", lambda k=k: A.activation(
                pA_h[:, k * AWM:k * AWM + w], xb[:, k * AWM:k * AWM + w],
                AF.Copy),
                 reads=(xbr,), writes=(f"pAh{k}",))
            p.op("DVE", lambda k=k: V.tensor_sub(
                pA_l[:, k * AWM:k * AWM + w], xb[:, k * AWM:k * AWM + w],
                pA_h[:, k * AWM:k * AWM + w]),
                 reads=(xbr, f"pAh{k}"), writes=(f"pAl{k}",))

        for m in range(8):
            od = ps_a[0:128, (m % 4) * 512:(m % 4) * 512 + w]
            for k in range(8):
                sh = Wq["Win"][0][:, k * H + m * 128:k * H + m * 128 + 128]
                sl = Wq["Win"][1][:, k * H + m * 128:k * H + m * 128 + 128]
                mh = pA_h[:, k * AWM:k * AWM + w]
                ml = pA_l[:, k * AWM:k * AWM + w]
                mm(tg + f"xp{m}", od, sh, mh, reads=(f"Winhi{k}", f"pAh{k}"),
                   wr=f"psA{m % 4}", start=(k == 0), stop=False)
                mm(tg + f"xp{m}", od, sh, ml, reads=(f"Winhi{k}", f"pAl{k}"),
                   wr=f"psA{m % 4}", start=False, stop=False)
                mm(tg + f"xp{m}", od, sl, mh, reads=(f"Winlo{k}", f"pAh{k}"),
                   wr=f"psA{m % 4}", start=False, stop=(k == 7))
            p.op("DVE", lambda o=xp_f[:, m * AWM:m * AWM + w], i=od,
                 bc=s_binc[:, m:m + 1], mm_=m:
                 V.tensor_scalar(o, i, bc, None, OP.add),
                 reads=(f"psA{m % 4}", "s_binc"), writes=(f"xpf{m}",))
            p.op("ACT", lambda m=m: A.activation(
                pB_h[:, m * AWM:m * AWM + w], xp_f[:, m * AWM:m * AWM + w],
                AF.Copy),
                 reads=(f"xpf{m}",), writes=(f"pBh{m}",))
            p.op("DVE", lambda m=m: V.tensor_sub(
                pB_l[:, m * AWM:m * AWM + w], xp_f[:, m * AWM:m * AWM + w],
                pB_h[:, m * AWM:m * AWM + w]),
                 reads=(f"xpf{m}", f"pBh{m}"), writes=(f"pBl{m}",))
            lpump(1)
        if debug and gi == 0:
            for m in range(8):
                store(dbg["d_xp"][:, m * w:(m + 1) * w],
                      xp_f[:, m * AWM:m * AWM + w], (f"xpf{m}",), "d_xp")

        for m in range(8):
            od = ps_a[0:128, (m % 4) * 512:(m % 4) * 512 + w]
            for k in range(8):
                sh = Wq["WD"][0][:, k * H + m * 128:k * H + m * 128 + 128]
                sl = Wq["WD"][1][:, k * H + m * 128:k * H + m * 128 + 128]
                mh = pB_h[:, k * AWM:k * AWM + w]
                ml = pB_l[:, k * AWM:k * AWM + w]
                mm(tg + f"y{m}", od, sh, mh, reads=(f"WDhi{k}", f"pBh{k}"),
                   wr=f"psA{m % 4}", start=(k == 0), stop=False)
                mm(tg + f"y{m}", od, sh, ml, reads=(f"WDhi{k}", f"pBl{k}"),
                   wr=f"psA{m % 4}", start=False, stop=False)
                mm(tg + f"y{m}", od, sl, mh, reads=(f"WDlo{k}", f"pBh{k}"),
                   wr=f"psA{m % 4}", start=False, stop=False)
            mm(tg + f"y{m}", od, CA_hi[:, m * 128:(m + 1) * 128],
               hsA_hi[:, g0:g1], reads=("CAhi", "hsAhi"),
               wr=f"psA{m % 4}", start=False, stop=False)
            mm(tg + f"y{m}", od, CA_hi[:, m * 128:(m + 1) * 128],
               hsA_lo[:, g0:g1], reads=("CAhi", "hsAlo"),
               wr=f"psA{m % 4}", start=False, stop=False)
            mm(tg + f"y{m}", od, CA_lo[:, m * 128:(m + 1) * 128],
               hsA_hi[:, g0:g1], reads=("CAlo", "hsAhi"),
               wr=f"psA{m % 4}", start=False, stop=True)
            p.op("ACT", lambda i=od, mm_=m: A.activation(yf_c[:, 0:w], i, AF.Tanh),
                 reads=(f"psA{m % 4}",), writes=("yf_c",))
            if debug and gi == 0:
                store(dbg["d_y"][:, m * w:(m + 1) * w], yf_c[:, 0:w],
                      ("yf_c",), "d_y")
            p.op("ACT", lambda m=m: A.activation(
                pA_h[:, m * AWM:m * AWM + w], yf_c[:, 0:w], AF.Copy),
                 reads=("yf_c",), writes=(f"pAh{m}",))
            p.op("DVE", lambda m=m: V.tensor_sub(
                pA_l[:, m * AWM:m * AWM + w], yf_c[:, 0:w],
                pA_h[:, m * AWM:m * AWM + w]),
                 reads=("yf_c", f"pAh{m}"), writes=(f"pAl{m}",))

        for m in range(8):
            od = ps_a[0:128, (m % 4) * 512:(m % 4) * 512 + w]
            for k in range(8):
                sh = Wq["Wout"][0][:, k * H + m * 128:k * H + m * 128 + 128]
                sl = Wq["Wout"][1][:, k * H + m * 128:k * H + m * 128 + 128]
                mh = pA_h[:, k * AWM:k * AWM + w]
                ml = pA_l[:, k * AWM:k * AWM + w]
                mm(tg + f"z{m}", od, sh, mh, reads=(f"Wouthi{k}", f"pAh{k}"),
                   wr=f"psA{m % 4}", start=(k == 0), stop=False)
                mm(tg + f"z{m}", od, sh, ml, reads=(f"Wouthi{k}", f"pAl{k}"),
                   wr=f"psA{m % 4}", start=False, stop=False)
                mm(tg + f"z{m}", od, sl, mh, reads=(f"Woutlo{k}", f"pAh{k}"),
                   wr=f"psA{m % 4}", start=False, stop=(k == 7))
            p.op("DVE", lambda o=zS[:, m * AWM:m * AWM + w], i=od,
                 x=xp_f[:, m * AWM:m * AWM + w], bc=s_boutc[:, m:m + 1], mm_=m:
                 V.scalar_tensor_tensor(o, i, bc, x, OP.add, OP.add),
                 reads=(f"psA{m % 4}", f"xpf{m}", "s_boutc"), writes=(ZR,))
            lpump(1)
        if debug and gi == 0:
            for m in range(8):
                store(dbg["d_z"][:, m * w:(m + 1) * w],
                      zS[:, m * AWM:m * AWM + w], (ZR,), "d_z")

    def attn_ladder(gi):
        g0, g1 = AGROUPS[gi]
        w = g1 - g0
        tg = f"A{gi}"
        zS = zS2[gi % 2]
        ZR = f"zS{gi % 2}"
        nsub = (w + 127) // 128
        for si in range(nsub):
            u0 = g0 + si * 128
            u1 = min(g1, u0 + 128)
            uw = u1 - u0
            ti = u0 // 128
            sg = f"{tg}s{si}"
            for m_ in range(8):
                p.op("PE", lambda o=ps_tr[0:uw, m_ * 128:(m_ + 1) * 128],
                     i=zS[:, m_ * AWM + si * 128:m_ * AWM + si * 128 + uw]:
                     PE.transpose(o, i, s_idf[:]),
                     reads=(ZR, "s_idf"), writes=("psTR",), group=sg + "tr")
            p.op("DVE", lambda e=uw: V.tensor_scalar(
                z_tm[0:e, :], ps_tr[0:e, :], 1.0, None, OP.mult, OP.add,
                accum_out=colz[0:e, 0:1]),
                 reads=("psTR",), writes=("z_tm", "colz"))
            p.op("ACT", lambda e=uw: A.activation(zsq[0:e, :], z_tm[0:e, :],
                                                  AF.Square,
                                                  accum_out=colz[0:e, 1:2]),
                 reads=("z_tm",), writes=("zsq", "colz"))
            p.op("DVE", lambda e=uw: V.tensor_scalar(colz[0:e, 2:3],
                                                     colz[0:e, 0:1], 1.0 / H,
                                                     None, OP.mult),
                 reads=("colz",), writes=("colz",))
            p.op("DVE", lambda e=uw: V.tensor_mul(colz[0:e, 3:4], colz[0:e, 2:3],
                                                  colz[0:e, 2:3]),
                 reads=("colz",), writes=("colz",))
            p.op("DVE", lambda e=uw: V.scalar_tensor_tensor(
                colz[0:e, 4:5], colz[0:e, 1:2], 1.0 / H, colz[0:e, 3:4],
                OP.mult, OP.subtract),
                 reads=("colz",), writes=("colz",))
            p.op("DVE", lambda e=uw: V.tensor_scalar(colz[0:e, 4:5],
                                                     colz[0:e, 4:5], EPS, None,
                                                     OP.add),
                 reads=("colz",), writes=("colz",))
            p.op("ACT", lambda e=uw: A.activation(colz[0:e, 5:6], colz[0:e, 4:5],
                                                  AF.Sqrt),
                 reads=("colz",), writes=("colz",))
            p.op("DVE", lambda e=uw: V.reciprocal(colz[0:e, 6:7], colz[0:e, 5:6]),
                 reads=("colz",), writes=("colz",))
            yield
            p.op("DVE", lambda e=uw: V.tensor_scalar(sc_t[0:e, :], z_tm[0:e, :],
                                                     colz[0:e, 2:3],
                                                     colz[0:e, 6:7],
                                                     OP.subtract, OP.mult),
                 reads=("z_tm", "colz"), writes=("sc_t",))
            load(xtm[0:uw, :], x_tm[u0:u1, :], "xtm", q="ACT")
            p.op("GP", lambda e=uw: GP.tensor_mul(sc_t[0:e, :], sc_t[0:e, :],
                                                  s_gamb[0:e, :]),
                 reads=("sc_t", "s_gamb"), writes=("sc_t",))
            p.op("GP", lambda e=uw: GP.tensor_add(xtm[0:e, :], xtm[0:e, :],
                                                  s_betb[0:e, :]),
                 reads=("xtm", "s_betb"), writes=("xtm",))
            p.op("DVE", lambda e=uw: V.tensor_add(s_s[0:e, :], sc_t[0:e, :],
                                                  xtm[0:e, :]),
                 reads=("sc_t", "xtm"), writes=("z_tm",))
            yield
            p.op("DVE", lambda e=uw: V.tensor_reduce(colz[0:e, 7:8], s_s[0:e, :],
                                                     AX, OP.add),
                 reads=("z_tm",), writes=("colz",))
            p.op("ACT", lambda e=uw: A.activation(zsq[0:e, :], s_s[0:e, :],
                                                  AF.Square,
                                                  accum_out=colz[0:e, 8:9]),
                 reads=("z_tm",), writes=("zsq", "colz"))
            p.op("DVE", lambda e=uw: V.tensor_scalar(colz[0:e, 9:10],
                                                     colz[0:e, 7:8], 1.0 / H,
                                                     None, OP.mult),
                 reads=("colz",), writes=("colz",))
            p.op("DVE", lambda e=uw: V.tensor_mul(colz[0:e, 10:11],
                                                  colz[0:e, 9:10],
                                                  colz[0:e, 9:10]),
                 reads=("colz",), writes=("colz",))
            p.op("DVE", lambda e=uw: V.scalar_tensor_tensor(
                colz[0:e, 10:11], colz[0:e, 8:9], 1.0 / H, colz[0:e, 10:11],
                OP.mult, OP.subtract),
                 reads=("colz",), writes=("colz",))
            p.op("DVE", lambda e=uw: V.tensor_scalar(colz[0:e, 10:11],
                                                     colz[0:e, 10:11], EPS,
                                                     None, OP.add),
                 reads=("colz",), writes=("colz",))
            p.op("ACT", lambda e=uw: A.activation(colz[0:e, 11:12],
                                                  colz[0:e, 10:11], AF.Sqrt),
                 reads=("colz",), writes=("colz",))
            p.op("DVE", lambda e=uw: V.reciprocal(colz[0:e, 12:13],
                                                  colz[0:e, 11:12]),
                 reads=("colz",), writes=("colz",))
            yield
            p.op("DVE", lambda e=uw: V.tensor_scalar(sc_t[0:e, :], s_s[0:e, :],
                                                     colz[0:e, 9:10], None,
                                                     OP.subtract),
                 reads=("z_tm", "colz"), writes=("sc_t",))
            p.op("DVE", lambda e=uw: V.tensor_reduce(colz[0:e, 13:14],
                                                     sc_t[0:e, :], AX, OP.max,
                                                     apply_absolute_value=True),
                 reads=("sc_t",), writes=("colz",))
            p.op("DVE", lambda e=uw: V.tensor_mul(colz[0:e, 13:14],
                                                  colz[0:e, 13:14],
                                                  colz[0:e, 12:13]),
                 reads=("colz",), writes=("colz",))
            p.op("DVE", lambda e=uw: V.tensor_scalar_max(colz[0:e, 13:14],
                                                         colz[0:e, 13:14], 1e-5),
                 reads=("colz",), writes=("colz",))
            p.op("DVE", lambda e=uw: V.tensor_scalar(colz[0:e, 14:15],
                                                     colz[0:e, 13:14],
                                                     1.0 / 127.0, None,
                                                     OP.mult),
                 reads=("colz",), writes=("colz",))
            p.op("DVE", lambda e=uw: V.reciprocal(colz[0:e, 15:16],
                                                  colz[0:e, 13:14]),
                 reads=("colz",), writes=("colz",))
            p.op("DVE", lambda e=uw: V.tensor_mul(colz[0:e, 15:16],
                                                  colz[0:e, 15:16],
                                                  colz[0:e, 12:13]),
                 reads=("colz",), writes=("colz",))
            p.op("DVE", lambda e=uw, t=ti: V.tensor_scalar(
                colz[0:e, 15:16], colz[0:e, 15:16], 127.0,
                s_qmask[0:e, t:t + 1], OP.mult, OP.mult),
                 reads=("colz", "s_qmask"), writes=("colz",))
            p.op("DVE", lambda e=uw: V.tensor_scalar(r1t[0:e, :], sc_t[0:e, :],
                                                     colz[0:e, 12:13], None,
                                                     OP.mult),
                 reads=("sc_t", "colz"), writes=("z_tm",))
            store(r1_hbm[u0:u1, :], r1t[0:uw, :], ("z_tm",), f"r1_{ti}")
            if debug:
                store(dbg["d_r1"][u0:u1, :], r1t[0:uw, :], ("z_tm",), "d_r1x")
            yield
            p.op("DVE", lambda e=uw: V.tensor_scalar(q_i[0:e, :], sc_t[0:e, :],
                                                     colz[0:e, 15:16], None,
                                                     OP.mult),
                 reads=("sc_t", "colz"), writes=("q_i",))
            if SAFE_ROUND:
                p.op("DVE", lambda e=uw: V.tensor_scalar(zsq[0:e, :],
                                                         sc_t[0:e, :],
                                                         colz[0:e, 15:16],
                                                         None, OP.mult),
                     reads=("sc_t", "colz"), writes=("zsq",))
                p.op("DVE", lambda e=uw: V.tensor_sub(rr_t[0:e, :], zsq[0:e, :],
                                                      q_i[0:e, :]),
                     reads=("zsq", "q_i"), writes=("rr_t",))
                p.op("DVE", lambda e=uw: V.tensor_scalar(cp_t[0:e, :],
                                                         rr_t[0:e, :], 0.5,
                                                         None, OP.is_gt),
                     reads=("rr_t",), writes=("cp_t",))
                p.op("DVE", lambda e=uw: V.scalar_tensor_tensor(
                    rr_t[0:e, :], rr_t[0:e, :], -0.5, cp_t[0:e, :], OP.is_lt,
                    OP.subtract),
                     reads=("rr_t", "cp_t"), writes=("rr_t",))
                p.op("DVE", lambda e=uw: V.tensor_sub(q_b[0:e, :], q_i[0:e, :],
                                                      rr_t[0:e, :]),
                     reads=("q_i", "rr_t"), writes=("q_b",))
            else:
                p.op("ACT", lambda e=uw: A.activation(q_b[0:e, :], q_i[0:e, :],
                                                      AF.Copy),
                     reads=("q_i",), writes=("q_b",))
            store(r1q_hbm[u0:u1, :], q_b[0:uw, :], ("q_b",), f"r1q_{ti}")
            p.op("DVE", lambda e=uw, t=ti: V.tensor_copy(dqcols[0:e, t:t + 1],
                                                         colz[0:e, 14:15]),
                 reads=("colz",), writes=("dqcols",))
            yield

    for gi in range(len(AGROUPS)):
        attn_mm(gi)
        lpump(10 ** 9)
        lgen[0] = attn_ladder(gi)
    lpump(10 ** 9)

    p.op("DVE", lambda: V.memset(q_b[:], 0.0), reads=(), writes=("q_b",))
    store(r1q_hbm[EXT:EXTP, :], q_b[0:EXTP - EXT, :], ("q_b",), "r1qpad")

    stB.close()
    stAB.close()
    p.barrier()

    stC = contextlib.ExitStack()
    w1q = SB("w1q", [128, 8 * DH], BF16, stack=stC)
    w2q = SB("w2q", [128, 32 * H], BF16, stack=stC)
    for c in range(8):
        load(w1q[:, c * DH:(c + 1) * DH], w1q_hbm[c * 128:(c + 1) * 128, :],
             f"w1q{c}", reads=(f"qw1{c}_0", f"qw1{c}_2048"))
    def load_w2q():
        for c in range(32):
            load(w2q[:, c * H:(c + 1) * H], w2q_hbm[c * 128:(c + 1) * 128, :],
                 f"w2q{c}", reads=(f"qw2{c}_0",))

    hbuf = SB("hbuf", [128, 32 * (GW + 2)], stack=stC)
    RQW = 512
    r1qg = SB("r1qg", [128, 8 * RQW], BF16, stack=stC)
    s2 = PSUM("ps_s2", [128, H], stack=stC)
    o_t = SB("o_t", [128, H], stack=stC)
    s2q = o_t
    r1g = o_t
    dqs = SB("dqs", [128, GW + 2], stack=stC)
    s2xs = SB("s2xs", [128, GW], stack=stC)
    pmax = SB("pmax", [128, GW], stack=stC)
    colfs = (SB("colfA", [128, 10], stack=stC),
             SB("colfB", [128, 10], stack=stC))
    rowf = SB("rowf", [1, GW + 2], stack=stC)
    if SAFE_ROUND:
        _sa = SB("sc_a0", [128, GW], stack=stC)
        _sd = SB("sc_d0", [128, GW], stack=stC)
        sc_as, sc_ds = (_sa, _sa), (_sd, _sd)

        def satag(m):
            return "sca0"

        def sdtag(m):
            return "scd0"
    else:
        sc_as = (SB("sc_a0", [128, GW], stack=stC),
                 SB("sc_a1", [128, GW], stack=stC))
        sc_ds = (SB("sc_d0", [128, GW], stack=stC),
                 SB("sc_d1", [128, GW], stack=stC))

        def satag(m):
            return f"sca{m % 2}"

        def sdtag(m):
            return f"scd{m % 2}"
    hq_i = SB("hq_i", [128, 32 * GW], I8, stack=stC)
    hq_b = SB("hq_b", [128, 32 * GW], BF16, stack=stC)
    if SAFE_ROUND:
        hrr = SB("hrr", [128, GW], stack=stC)
        hcp = SB("hcp", [128, GW], stack=stC)
    ps_m1 = PSUM("ps_m1", [128, 1536], stack=stC)
    ps_bc = PSUM("ps_bc", [128, 512], stack=stC)
    ps_o = PSUM("ps_o", [128, 1024], stack=stC)

    HB = "hb"
    HQ = "hq_b"

    def ffn_pars(gi):
        f0, f1 = FG[gi]
        fw = f1 - f0
        a0 = max(0, f0 - 1)
        a1 = min(EXT, f1 + 1)
        return f0, f1, fw, a0, a1, a1 - a0

    def ffn_conv_a(gi, m_):
        f0, f1, fw, a0, a1, aw = ffn_pars(gi)
        c_lo = max(f0, 1)
        c_hi = min(f1, EXT - 1)
        ce = c_hi - c_lo
        cs = c_lo - a0
        hm = hbuf[:, m_ * (GW + 2):m_ * (GW + 2) + aw]
        p.op("ACT", lambda h=hm, e=ce, c0=cs, mm_=m_:
             A.activation(sc_as[mm_ % 2][:, 0:e], h[:, c0:c0 + e],
                          AF.Identity, bias=s_convb[:, mm_:mm_ + 1],
                          scale=s_convw[:, 32 + mm_:33 + mm_]),
             reads=(HB + str(m_), "s_convw", "s_convb"),
             writes=(satag(m_),))
        p.op("ACT", lambda h=hm, e=ce, c0=cs, mm_=m_:
             A.activation(sc_ds[mm_ % 2][:, 0:e], h[:, c0 + 1:c0 + 1 + e],
                          AF.Identity, scale=s_convw[:, 64 + mm_:65 + mm_]),
             reads=(HB + str(m_), "s_convw"), writes=(sdtag(m_),))

    def ffn_conv_cd(gi, m_):
        f0, f1, fw, a0, a1, aw = ffn_pars(gi)
        c_lo = max(f0, 1)
        c_hi = min(f1, EXT - 1)
        ce = c_hi - c_lo
        cs = c_lo - a0
        hm = hbuf[:, m_ * (GW + 2):m_ * (GW + 2) + aw]
        p.op("DVE", lambda h=hm, e=ce, c0=cs, mm_=m_:
             V.scalar_tensor_tensor(sc_as[mm_ % 2][:, 0:e],
                                    h[:, c0 - 1:c0 - 1 + e],
                                    s_convw[:, mm_:mm_ + 1],
                                    sc_as[mm_ % 2][:, 0:e],
                                    OP.mult, OP.add),
             reads=(HB + str(m_), "s_convw", satag(m_)),
             writes=(satag(m_),))
        p.op("DVE", lambda h=hm, e=ce, c0=cs, mm_=m_:
             V.tensor_add(sc_ds[mm_ % 2][:, 0:e], sc_ds[mm_ % 2][:, 0:e],
                          sc_as[mm_ % 2][:, 0:e]),
             reads=(satag(m_), sdtag(m_)), writes=(sdtag(m_),))
        p.op("ACT", lambda h=hm, e=ce, c0=cs, mm_=m_:
             A.activation(h[:, c0:c0 + e], sc_ds[mm_ % 2][:, 0:e], AF.Silu),
             reads=(sdtag(m_),), writes=(HB + str(m_),))

    def ffn_conv_chunk(gi, m_):
        ffn_conv_a(gi, m_)
        ffn_conv_cd(gi, m_)

    def ffn_quant_chunk(gi, m_):
        colf = colfs[gi % 2]
        f0, f1, fw, a0, a1, aw = ffn_pars(gi)
        co = f0 - a0
        hm = hbuf[:, m_ * (GW + 2) + co:m_ * (GW + 2) + co + fw]
        if SAFE_ROUND:
            p.op("DVE", lambda h=hm, e=fw, mm_=m_:
                 V.scalar_tensor_tensor(sc_as[mm_ % 2][:, 0:e], h, 1.0,
                                        s2xs[:, 0:e], OP.mult, OP.mult),
                 reads=(HB + str(m_), "s2xs"), writes=(satag(m_),))
            p.op("DVE", lambda e=fw, mm_=m_: V.tensor_copy(
                hq_i[:, mm_ * GW:mm_ * GW + e], sc_as[mm_ % 2][:, 0:e]),
                 reads=(satag(m_),), writes=("hq_i",))
            p.op("DVE", lambda e=fw, mm_=m_: V.tensor_sub(
                hrr[:, 0:e], sc_as[mm_ % 2][:, 0:e],
                hq_i[:, mm_ * GW:mm_ * GW + e]),
                 reads=(satag(m_), "hq_i"), writes=("hrr",))
            p.op("DVE", lambda e=fw: V.tensor_scalar(hcp[:, 0:e], hrr[:, 0:e],
                                                     0.5, None, OP.is_gt),
                 reads=("hrr",), writes=("hcp",))
            p.op("DVE", lambda e=fw: V.scalar_tensor_tensor(
                hrr[:, 0:e], hrr[:, 0:e], -0.5, hcp[:, 0:e], OP.is_lt,
                OP.subtract),
                 reads=("hrr", "hcp"), writes=("hrr",))
            p.op("DVE", lambda e=fw, mm_=m_: V.tensor_sub(
                hq_b[:, mm_ * GW:mm_ * GW + e],
                hq_i[:, mm_ * GW:mm_ * GW + e], hrr[:, 0:e]),
                 reads=("hq_i", "hrr"), writes=(HQ,))
        else:
            p.op("DVE", lambda h=hm, e=fw, mm_=m_:
                 V.scalar_tensor_tensor(hq_i[:, mm_ * GW:mm_ * GW + e], h,
                                        1.0, s2xs[:, 0:e], OP.mult, OP.mult),
                 reads=(HB + str(m_), "s2xs"), writes=("hq_i",))

    def ffn_hq_cast(gi):
        if not SAFE_ROUND:
            p.op("ACT", lambda: A.activation(hq_b[:, :], hq_i[:, :], AF.Copy),
                 reads=("hq_i",), writes=(HQ,))

    def ffn_A1(gi, qprev, m2g=None):
        real = gi < len(FG)
        colf = colfs[gi % 2]
        CF = f"colf{gi % 2}"
        f0, f1, fw, a0, a1, aw = ffn_pars(min(gi, len(FG) - 1))
        tg = f"F{gi}"
        al0 = (a0 // 128) * 128
        al1 = min(EXTP, ((a1 + 127) // 128) * 128)
        alw = al1 - al0
        off = a0 - al0
        r1q_reads = tuple(f"r1q_{t}" for t in range(al0 // 128,
                                                    min(NT, (al1 + 127) // 128)))
        if al1 > EXT:
            r1q_reads = r1q_reads + ("r1qpad",)
        for c in range(8 if real else 0):
            p.op("SP", lambda d=r1qg[:, c * RQW:c * RQW + alw],
                 s=r1q_hbm[al0:al1, c * 128:(c + 1) * 128]:
                 SY.dma_start(out=d, in_=s, transpose=True),
                 reads=r1q_reads, writes=(f"r1qg{c}",), dma=True)
        for t in range(al0 // 128, min(NT, (a1 + 127) // 128) if real else 0):
            tl0, tl1 = TILES[t]
            uw_t = tl1 - tl0
            p.op("PE", lambda t=t, o0=tl0 - al0, e_=uw_t: PE.transpose(
                ps_bc[0:1, o0:o0 + e_], dqcols[0:e_, t:t + 1],
                s_idf[0:e_, 0:e_]),
                 reads=("dqcols", "s_idf"), writes=("psBC",), group=tg + "dqt")
        if real:
            p.op("DVE", lambda o0=a0 - al0: V.tensor_scalar(
                rowf[:, 0:aw], ps_bc[0:1, o0:o0 + aw], s_w1mean[0:1, :], None,
                OP.mult),
                 reads=("psBC", "s_w1mean"), writes=("rowf",))
            p.op("PE", lambda: PE.matmul(ps_bc[0:128, 0:aw], s_ones1[:],
                                         rowf[:, 0:aw], start=True, stop=True),
                 reads=("s_ones1", "rowf"), writes=("psBC",), group=tg + "bc")
            p.op("DVE", lambda: V.tensor_copy(dqs[:, 0:aw], ps_bc[0:128, 0:aw]),
                 reads=("psBC",), writes=("dqs",))
        inline_conv = not (debug and gi == 1)
        m2q = []
        m2_nsub = 0
        if m2g is not None:
            f0m, f1m = FG[m2g]
            m2_nsub = (f1m - f0m + 127) // 128
            for si in range(m2_nsub):
                m2q.extend(ffn_m2_mm_list(m2g, si))
            ffn_r1g_load(m2g, 0)
        m2i = 0
        if SAFE_ROUND:
            m2pump = (len(m2q) + 31) // 32 if m2q else 0
            for m_ in range(33):
                if m_ < 32:
                    if qprev is not None:
                        ffn_quant_chunk(qprev, m_)
                    for _ in range(m2pump):
                        if m2i < len(m2q):
                            if m2i == 64 and m2_nsub > 1:
                                for _x in ffn_m2_tail(m2g, 0):
                                    pass
                                ffn_r1g_load(m2g, 1)
                            m2q[m2i]()
                            m2i += 1
                    od = ps_m1[0:128, (m_ % 3) * 512:(m_ % 3) * 512 + aw]
                    for k in range(8):
                        mm(tg + f"m{m_}", od,
                           w1q[:, k * DH + m_ * 128:k * DH + m_ * 128 + 128],
                           r1qg[:, k * RQW + off:k * RQW + off + aw],
                           reads=(f"w1q{k}", f"r1qg{k}"),
                           wr=f"psM{m_ % 3}", start=(k == 0), stop=(k == 7))
                    p.op("DVE",
                         lambda o=hbuf[:, m_ * (GW + 2):m_ * (GW + 2) + aw],
                         i=od, d=dqs[:, 0:aw], mm_=m_:
                         V.scalar_tensor_tensor(o, i, 1.0, d, OP.mult, OP.mult),
                         reads=(f"psM{m_ % 3}", "dqs"), writes=(HB + str(m_),))
                    if inline_conv:
                        ffn_conv_a(gi, m_)
                        ffn_conv_cd(gi, m_)
            if m2g is not None:
                for _x in ffn_m2_tail(m2g, m2_nsub - 1):
                    pass
            if qprev is not None:
                ffn_hq_cast(qprev)
            if not inline_conv:
                store(dbg["d_h"][:], hbuf[:],
                      tuple(HB + str(i) for i in range(32)), "d_hx")
                for m_ in range(32):
                    ffn_conv_chunk(gi, m_)
            return

        a2hold = [ffn_A2(qprev) if qprev is not None else None]
        tails = []
        on_tail_done = []

        def pump_tails(n):
            for tgen in tails[:]:
                for _ in range(n):
                    try:
                        next(tgen)
                    except StopIteration:
                        tails.remove(tgen)
                        if on_tail_done:
                            on_tail_done.pop(0)()
                        break
        PRE = 9
        NIT = 33 + PRE
        m2pump = (len(m2q) + NIT - 2) // (NIT - 1) if m2q else 0
        for it in range(NIT):
            m_ = it - PRE
            if a2hold[0] is not None:
                for _ in range(4):
                    try:
                        next(a2hold[0])
                    except StopIteration:
                        a2hold[0] = None
                        break
            q_m = it - PRE + 1
            if qprev is not None and 0 <= q_m < 32:
                ffn_quant_chunk(qprev, q_m)
            for _ in range(m2pump):
                if m2i < len(m2q):
                    if m2i == 64 and m2_nsub > 1:
                        t0 = ffn_m2_tail(m2g, 0)
                        try:
                            next(t0)
                            tails.append(t0)
                            on_tail_done.append(
                                lambda: ffn_r1g_load(m2g, 1))
                        except StopIteration:
                            ffn_r1g_load(m2g, 1)
                    m2q[m2i]()
                    m2i += 1
                    if m2i == len(m2q):
                        t1 = ffn_m2_tail(m2g, m2_nsub - 1)
                        try:
                            next(t1)
                            tails.append(t1)
                        except StopIteration:
                            pass
            pump_tails(2)
            if real and 0 <= m_ < 32:
                od = ps_m1[0:128, (m_ % 3) * 512:(m_ % 3) * 512 + aw]
                for k in range(8):
                    mm(tg + f"m{m_}", od,
                       w1q[:, k * DH + m_ * 128:k * DH + m_ * 128 + 128],
                       r1qg[:, k * RQW + off:k * RQW + off + aw],
                       reads=(f"w1q{k}", f"r1qg{k}"),
                       wr=f"psM{m_ % 3}", start=(k == 0), stop=(k == 7))
                p.op("DVE", lambda o=hbuf[:, m_ * (GW + 2):m_ * (GW + 2) + aw],
                     i=od, d=dqs[:, 0:aw], mm_=m_:
                     V.scalar_tensor_tensor(o, i, 1.0, d, OP.mult, OP.mult),
                     reads=(f"psM{m_ % 3}", "dqs"), writes=(HB + str(m_),))
                if inline_conv:
                    ffn_conv_a(gi, m_)
            if real and inline_conv and m_ >= 1:
                ffn_conv_cd(gi, m_ - 1)
        while a2hold[0] is not None:
            try:
                next(a2hold[0])
            except StopIteration:
                a2hold[0] = None
        pump_tails(10 ** 9)
        if qprev is not None:
            ffn_hq_cast(qprev)
        if not inline_conv:
            store(dbg["d_h"][:], hbuf[:],
                  tuple(HB + str(i) for i in range(32)), "d_hx")
            for m_ in range(32):
                ffn_conv_chunk(gi, m_)

    def ffn_A2(gi):
        colf = colfs[gi % 2]
        CF = f"colf{gi % 2}"
        f0, f1, fw, a0, a1, aw = ffn_pars(gi)
        tg = f"F{gi}"
        co = f0 - a0
        p.op("DVE", lambda e=fw, c0=co: V.tensor_reduce(
            pmax[:, 0:e].rearrange("p (t o) -> p t o", o=1),
            hbuf[:, :].rearrange("p (c t) -> p t c", c=32)[:, c0:c0 + e, :],
            AX, OP.max, apply_absolute_value=True),
             reads=tuple(HB + str(i) for i in range(32)), writes=("pmax",))
        yield
        nsub = (fw + 127) // 128
        for si in range(nsub):
            u0 = f0 + si * 128
            u1 = min(f1, u0 + 128)
            uw = u1 - u0
            sg = tg + f"s{si}"
            p.op("PE", lambda e=uw, s0=si * 128: PE.transpose(
                ps_bc[0:e, 0:128], pmax[:, s0:s0 + e], s_idf[:]),
                 reads=("pmax", "s_idf"), writes=("psBC",), group=sg + "tp")
            yield
            p.op("DVE", lambda e=uw: V.tensor_reduce(colf[0:e, 0:1],
                                                     ps_bc[0:e, 0:128], AX,
                                                     OP.max),
                 reads=("psBC",), writes=(CF,))
            yield
            p.op("DVE", lambda e=uw: V.tensor_scalar_max(colf[0:e, 0:1],
                                                         colf[0:e, 0:1], 1e-5),
                 reads=(CF,), writes=(CF,))
            yield
            p.op("DVE", lambda e=uw: V.reciprocal(colf[0:e, 1:2], colf[0:e, 0:1]),
                 reads=(CF,), writes=(CF,))
            yield
            p.op("DVE", lambda e=uw: V.tensor_scalar(colf[0:e, 2:3],
                                                     colf[0:e, 1:2], 127.0,
                                                     None, OP.mult),
                 reads=(CF,), writes=(CF,))
            yield
            p.op("DVE", lambda e=uw, si=si: V.tensor_scalar(
                colf[0:e, 8 + si:9 + si], colf[0:e, 0:1], s_w2mean[0:e, :],
                1.0 / 127.0, OP.mult, OP.mult),
                 reads=(CF, "s_w2mean"), writes=(CF,))
            yield
            p.op("PE", lambda e=uw: PE.transpose(ps_bc[0:1, 0:e],
                                                 colf[0:e, 2:3],
                                                 s_idf[0:e, 0:e]),
                 reads=(CF, "s_idf"), writes=("psBC",), group=sg + "t2")
            yield
            p.op("DVE", lambda e=uw, s0=si * 128: V.tensor_copy(
                rowf[:, s0:s0 + e], ps_bc[0:1, 0:e]),
                 reads=("psBC",), writes=("rowf",))
            yield
        p.op("PE", lambda e=fw: PE.matmul(ps_bc[0:128, 0:e], s_ones1[:],
                                          rowf[:, 0:e], start=True, stop=True),
             reads=("s_ones1", "rowf"), writes=("psBC",), group=tg + "b2")
        yield
        p.op("DVE", lambda e=fw: V.tensor_copy(s2xs[:, 0:e], ps_bc[0:128, 0:e]),
             reads=("psBC",), writes=("s2xs",))
        yield

    def ffn_m2_sub(gi, si):
        f0, f1 = FG[gi]
        u0 = f0 + si * 128
        u1 = min(f1, u0 + 128)
        uw = u1 - u0
        out0 = max(2, u0)
        out1 = min(EXT - 2, u1)
        return u0, u1, uw, out0, out1

    def ffn_m2_mm_list(gi, si):
        tg = f"F{gi}"
        u0, u1, uw, out0, out1 = ffn_m2_sub(gi, si)
        if out1 <= out0:
            return []
        sg = tg + f"S{si}"
        ems = []
        for nh in range(2):
            od = ps_o[0:uw, nh * 512:(nh + 1) * 512]
            for c in range(32):
                sta = hq_b[:, c * GW + si * 128:c * GW + si * 128 + uw]
                mv = w2q[:, c * H + nh * 512:c * H + nh * 512 + 512]
                ems.append(lambda od=od, sta=sta, mv=mv, c=c, nh=nh:
                           mm(sg + f"o{nh}", od, sta, mv,
                              reads=(HQ, f"w2q{c}"), wr=f"psO{nh}",
                              start=(c == 0), stop=(c == 31)))
        return ems

    def ffn_r1g_load(gi, si):
        u0, u1, uw, out0, out1 = ffn_m2_sub(gi, si)
        if out1 <= out0:
            return
        load(r1g[0:uw, :], r1_hbm[u0:u1, :], "o_t", q="ACT",
             reads=(f"r1_{u0 // 128}",))

    def ffn_m2_tail(gi, si):
        colf = colfs[gi % 2]
        CF = f"colf{gi % 2}"
        u0, u1, uw, out0, out1 = ffn_m2_sub(gi, si)
        if out1 <= out0:
            return
        if True:
            p.op("DVE", lambda e=uw, si=si: V.scalar_tensor_tensor(
                s2[0:e, :], ps_o[0:e, :], colf[0:e, 8 + si:9 + si], r1g[0:e, :],
                OP.mult, OP.add, accum_out=colf[0:e, 4:5]),
                 reads=("psO0", "psO1", CF, "o_t"), writes=("s2", CF))
            yield
            p.op("ACT", lambda e=uw: A.activation(s2q[0:e, :], s2[0:e, :],
                                                  AF.Square,
                                                  accum_out=colf[0:e, 5:6]),
                 reads=("s2",), writes=("o_t", CF))
            yield
            p.op("DVE", lambda e=uw: V.tensor_scalar(colf[0:e, 4:5],
                                                     colf[0:e, 4:5], 1.0 / H,
                                                     None, OP.mult),
                 reads=(CF,), writes=(CF,))
            yield
            p.op("DVE", lambda e=uw: V.tensor_mul(colf[0:e, 6:7], colf[0:e, 4:5],
                                                  colf[0:e, 4:5]),
                 reads=(CF,), writes=(CF,))
            yield
            p.op("DVE", lambda e=uw: V.scalar_tensor_tensor(
                colf[0:e, 6:7], colf[0:e, 5:6], 1.0 / H, colf[0:e, 6:7],
                OP.mult, OP.subtract),
                 reads=(CF,), writes=(CF,))
            yield
            p.op("DVE", lambda e=uw: V.tensor_scalar(colf[0:e, 6:7],
                                                     colf[0:e, 6:7], EPS, None,
                                                     OP.add),
                 reads=(CF,), writes=(CF,))
            yield
            p.op("ACT", lambda e=uw: A.activation(colf[0:e, 7:8], colf[0:e, 6:7],
                                                  AF.Sqrt),
                 reads=(CF,), writes=(CF,))
            yield
            p.op("DVE", lambda e=uw: V.reciprocal(colf[0:e, 7:8], colf[0:e, 7:8]),
                 reads=(CF,), writes=(CF,))
            yield
            p.op("DVE", lambda e=uw: V.tensor_scalar(o_t[0:e, :], s2[0:e, :],
                                                     colf[0:e, 4:5],
                                                     colf[0:e, 7:8],
                                                     OP.subtract, OP.mult),
                 reads=("s2", CF), writes=("o_t",))
            yield
            store(out_tm[out0 - 2:out1 - 2, :], o_t[out0 - u0:out1 - u0, :],
                  ("o_t",), f"out{u0}")

    def ffn_m2_out(gi):
        f0, f1 = FG[gi]
        nsub = (f1 - f0 + 127) // 128
        for si in range(nsub):
            for em in ffn_m2_mm_list(gi, si):
                em()
            ffn_r1g_load(gi, si)
            for _ in ffn_m2_tail(gi, si):
                pass

    NG = len(FG)
    load_w2q_once = [load_w2q]
    NLOOP = NG if SAFE_ROUND else NG + 2
    for gi in range(NLOOP):
        qprev = gi - 1 if (gi >= 1 and gi - 1 < NG) else None
        m2g = gi - 2 if (gi >= 2 and not SAFE_ROUND) else None
        ffn_A1(gi, qprev, m2g)
        if load_w2q_once:
            load_w2q_once.pop()()
        if SAFE_ROUND:
            if gi >= 1:
                ffn_m2_out(gi - 1)
            for _ in ffn_A2(gi):
                pass
    if SAFE_ROUND:
        for m_ in range(32):
            ffn_quant_chunk(NG - 1, m_)
        ffn_hq_cast(NG - 1)
        ffn_m2_out(NG - 1)

    stC.close()
    root.close()
    n = p.finalize_and_emit()
    return nc, n, dbg


_NC_CACHE = {}


def _get_nc(debug=False):
    key = (debug, SAFE_ROUND)
    if key not in _NC_CACHE:
        _NC_CACHE[key] = build_nc(debug)
    return _NC_CACHE[key]


def make_in_maps(inputs, debug=False):
    f32 = np.float32
    K_EXT = EXT
    x = np.asarray(inputs["L0_attn_x"], f32)
    W_in = np.ascontiguousarray(inputs["L0_attn_W_in"], f32)
    W_B = np.ascontiguousarray(inputs["L0_attn_W_B"], f32)
    W_D = np.ascontiguousarray(inputs["L0_attn_W_D"], f32)
    W_out = np.ascontiguousarray(inputs["L0_attn_W_out"], f32)
    B_mat = np.ascontiguousarray(inputs["L0_attn_B_mat"], f32)
    C_mat = np.ascontiguousarray(inputs["L0_attn_C_mat"], f32)
    W_ig = np.ascontiguousarray(inputs["L0_attn_W_ig"], f32)
    W_fg = np.ascontiguousarray(inputs["L0_attn_W_fg"], f32)
    b_in = np.asarray(inputs["L0_attn_b_in"], f32)
    b_B = np.asarray(inputs["L0_attn_b_B"], f32)
    b_D = np.asarray(inputs["L0_attn_b_D"], f32)
    b_out = np.asarray(inputs["L0_attn_b_out"], f32)
    b_ig = np.asarray(inputs["L0_attn_b_ig"], f32)
    b_fg = np.asarray(inputs["L0_attn_b_fg"], f32)
    gamma = np.asarray(inputs["L0_attn_gamma"], f32)
    beta = np.asarray(inputs["L0_attn_beta"], f32)
    w1 = np.asarray(inputs["L0_ffn_w1"], f32)
    conv_w = np.asarray(inputs["L0_ffn_conv_w"], f32)
    conv_b = np.asarray(inputs["L0_ffn_conv_b"], f32)
    w2 = np.asarray(inputs["L0_ffn_w2"], f32)

    shared = {
        "W_in": W_in, "W_D": W_D, "W_out": W_out,
        "W_in_T": np.ascontiguousarray(W_in.T),
        "W_B_T": np.ascontiguousarray(W_B.T),
        "B_mat": B_mat, "C_mat": C_mat, "W_ig": W_ig, "W_fg": W_fg,
        "b_in_cols": np.ascontiguousarray(b_in.reshape(8, 128).T),
        "b_B_cols": np.ascontiguousarray(b_B.reshape(8, 128).T),
        "b_out_cols": np.ascontiguousarray(b_out.reshape(8, 128).T),
        "b_ig_col": np.ascontiguousarray(b_ig.reshape(S, 1)),
        "b_fg_col": np.ascontiguousarray(b_fg.reshape(S, 1)),
        "gamma_b": np.ascontiguousarray(np.broadcast_to(gamma[None, :], (128, H))),
        "beta_b": np.ascontiguousarray(np.broadcast_to(beta[None, :], (128, H))),
        "bD_row": np.ascontiguousarray(b_D.reshape(1, H)),
        "convw": np.ascontiguousarray(
            conv_w[:, 0, :].reshape(32, 128, 3).transpose(1, 2, 0).reshape(128, 96)),
        "convb": np.ascontiguousarray(conv_b.reshape(32, 128).T),
        "w1t_f": np.ascontiguousarray(w1.T),
        "w2t_f": np.ascontiguousarray(w2.T),
        "ones1": np.ones((1, 128), f32),
        "ones_col": np.ones((128, 1), f32),
        "id_f32": np.eye(128, dtype=f32),
        "onesrow_ext": np.ones((1, K_EXT), ml_dtypes.bfloat16),
        "zrow_ext": np.zeros((1, K_EXT), ml_dtypes.bfloat16),
    }
    in_maps = []
    for c in range(8):
        b, half = c // 2, c % 2
        s = half * 2048
        lo, hi = s - 2, s + 2048 + 2
        xe = np.zeros((EXT, H), f32)
        a, bb = max(0, lo), min(T, hi)
        xe[a - lo:bb - lo] = x[b, a:bb]
        qm = np.zeros((EXTP,), f32)
        qm[a - lo:bb - lo] = 1.0
        m = dict(shared)
        m["x_tm"] = xe
        m["x_fm"] = np.ascontiguousarray(xe.T)
        m["xg_fm"] = np.ascontiguousarray(x[b].T)
        m["qmask"] = np.ascontiguousarray(qm.reshape(NT, 128).T)
        m["hsel0"] = np.full((S, 1), 1.0 - half, f32)
        m["hsel1"] = np.full((S, 1), float(half), f32)
        in_maps.append(m)
    return in_maps


def kernel(**inputs):
    nc, _, _ = _get_nc(False)
    in_maps = make_in_maps(inputs)
    res = run_bass_kernel_spmd(nc, in_maps, core_ids=list(range(8)))
    out = np.empty((B, T, H), np.float32)
    for c in range(8):
        b, half = c // 2, c % 2
        out[b, half * 2048:(half + 1) * 2048] = res.results[c]["out_tm"]
    return out


# version 41
# speedup vs baseline: 1.0888x; 1.0000x over previous
import contextlib

import numpy as np
import ml_dtypes

import concourse.bass as bass
import concourse.mybir as mybir
from concourse.bass_isa import ReduceOp
from concourse.bass_utils import run_bass_kernel_spmd

F32 = mybir.dt.float32
BF16 = mybir.dt.bfloat16
I32 = mybir.dt.int32
I16 = mybir.dt.int16
I8 = mybir.dt.int8
AF = mybir.ActivationFunctionType
OP = mybir.AluOpType
AX = mybir.AxisListType.X

B, T, H, S, DH = 4, 4096, 1024, 16, 4096
EXT = T // 2 + 4
EXTP = 2176
EPS = 1e-5
SAFE_ROUND = False

NT = (EXT + 127) // 128
TILES = [(i * 128, min(EXT, (i + 1) * 128)) for i in range(NT)]
AW = 256
AGROUPS = [(i * AW, (i + 1) * AW) for i in range(7)] + [(7 * AW, EXT)]
AWM = EXT - 7 * AW
GW = 256
FG = [(a, min(a + GW, EXT)) for a in range(0, EXT, GW)]

ENGINES = ("SP", "PE", "DVE", "ACT", "GP")
_ENG_ATTR = {"SP": "sync", "PE": "tensor", "DVE": "vector", "ACT": "scalar",
             "GP": "gpsimd"}


class Prog:

    def __init__(self, nc):
        self.nc = nc
        self.ops = []

    def op(self, eng, fn, reads=(), writes=(), dma=False, group=None):
        self.ops.append(dict(eng=eng, fn=fn, reads=tuple(reads),
                             writes=tuple(writes), dma=dma, group=group,
                             inc=None, sem=None, barrier=False))

    def barrier(self):
        self.ops.append(dict(eng="SP", fn=None, reads=(), writes=(), dma=False,
                             group=None, inc=None, sem=None, barrier=True))

    def finalize_and_emit(self):
        nc, ops = self.nc, self.ops
        last_w, readers = {}, {}
        deps_l = []
        last_per_eng = {}
        last_dma_sem = {}
        dma_i = 0
        NDMA_ = 48
        last_barrier = None
        for i, o in enumerate(ops):
            deps = set()
            if o["barrier"]:
                deps.update(v for v in last_per_eng.values() if v is not None)
                deps.update(v for v in last_dma_sem.values() if v is not None)
                if last_barrier is not None:
                    deps.add(last_barrier)
                deps.discard(i)
                deps_l.append(deps)
                last_barrier = i
                last_per_eng = {}
                last_dma_sem = {}
                continue
            if last_barrier is not None:
                deps.add(last_barrier)
            if o["dma"]:
                last_dma_sem[dma_i % NDMA_] = i
                dma_i += 1
            last_per_eng[(o["eng"], o["dma"])] = i
            for r in o["reads"]:
                if r in last_w:
                    deps.add(last_w[r][0])
            for w in o["writes"]:
                if w in last_w:
                    wi, wg = last_w[w]
                    if not (o["group"] is not None and o["group"] == wg):
                        deps.add(wi)
                deps.update(readers.get(w, ()))
            deps.discard(i)
            deps_l.append(deps)
            for w in o["writes"]:
                last_w[w] = (i, o["group"])
                readers[w] = []
            for r in o["reads"]:
                readers.setdefault(r, []).append(i)

        need_inc = [False] * len(ops)
        for ds in deps_l:
            for d in ds:
                need_inc[d] = True

        es = contextlib.ExitStack()
        chain = {e: es.enter_context(nc.semaphore(f"ch_{e}")) for e in ENGINES}
        NDMA = 48
        dpool = [es.enter_context(nc.semaphore(f"dma{i}")) for i in range(NDMA)]
        cnt = {e: 0 for e in ENGINES}
        dcnt = [0] * NDMA
        dix = 0
        for i, o in enumerate(ops):
            if o["dma"]:
                j = dix % NDMA
                dix += 1
                o["pre_wait"] = (dpool[j], dcnt[j]) if dcnt[j] else None
                dcnt[j] += 16
                o["sem"] = (f"D{j}", dpool[j])
                o["inc"] = dcnt[j]
            elif need_inc[i]:
                cnt[o["eng"]] += 1
                o["sem"] = ("C" + o["eng"], chain[o["eng"]])
                o["inc"] = cnt[o["eng"]]

        know = [None] * len(ops)
        know_is = [None] * len(ops)
        eng_last = {e: None for e in ENGINES}
        waits_l = [None] * len(ops)
        for i, o in enumerate(ops):
            k = (dict(know_is[eng_last[o["eng"]]])
                 if eng_last[o["eng"]] is not None else {})
            waits = []
            for d in sorted(deps_l[i]):
                do = ops[d]
                skey, shandle = do["sem"]
                val = do["inc"]
                if k.get(skey, 0) < val:
                    waits.append((shandle, val))
                    k[skey] = val
                    for kk, vv in know[d].items():
                        if k.get(kk, 0) < vv:
                            k[kk] = vv
            if o.get("pre_wait") is not None:
                sh, val = o["pre_wait"]
                skey = o["sem"][0]
                if k.get(skey, 0) < val:
                    waits.append((sh, val))
                    k[skey] = val
            if o["sem"] is not None:
                know[i] = dict(k)
                skey = o["sem"][0]
                know[i][skey] = max(know[i].get(skey, 0), o["inc"])
            else:
                know[i] = k
            know_is[i] = k
            eng_last[o["eng"]] = i
            waits_l[i] = waits

        with nc.Block() as block:
            for eng in ENGINES:
                idxs = [i for i, o in enumerate(ops) if o["eng"] == eng]
                if not idxs and eng != "SP":
                    continue

                def body(handle, eng=eng, idxs=idxs):
                    for i in idxs:
                        o = ops[i]
                        for sh, val in waits_l[i]:
                            handle.wait_ge(sh, val)
                        instr = o["fn"]() if o["fn"] is not None else handle.nop()
                        if o["sem"] is not None:
                            instr.then_inc(o["sem"][1], 16 if o["dma"] else 1)
                    if eng == "SP":
                        for e2 in ENGINES:
                            if cnt.get(e2, 0):
                                handle.wait_ge(chain[e2], cnt[e2])
                        for j in range(NDMA):
                            if dcnt[j]:
                                handle.wait_ge(dpool[j], dcnt[j])

                getattr(block, _ENG_ATTR[eng])(body)
        self._sem_stack = es
        return len(ops)


def build_nc(debug=False):
    nc = bass.Bass("TRN2", target_bir_lowering=False, debug=False)
    p = Prog(nc)
    V, A, PE, GP, SY = nc.vector, nc.scalar, nc.tensor, nc.gpsimd, nc.sync
    QENG = {"SP": SY, "ACT": A, "GP": GP}

    def din(name, shape, dt=F32):
        return nc.dram_tensor(name, shape, dt, kind="ExternalInput").ap()

    x_tm = din("x_tm", [EXT, H])
    x_fm = din("x_fm", [H, EXT])
    xg_fm = din("xg_fm", [H, T])
    W_in = din("W_in", [H, H]); W_D = din("W_D", [H, H]); W_out = din("W_out", [H, H])
    W_in_T = din("W_in_T", [H, H]); W_B_T = din("W_B_T", [H, H])
    B_mat = din("B_mat", [H, S]); C_mat = din("C_mat", [S, H])
    W_ig = din("W_ig", [H, S]); W_fg = din("W_fg", [H, S])
    b_in_cols = din("b_in_cols", [128, 8]); b_B_cols = din("b_B_cols", [128, 8])
    b_out_cols = din("b_out_cols", [128, 8])
    b_ig_col = din("b_ig_col", [S, 1]); b_fg_col = din("b_fg_col", [S, 1])
    gamma_b = din("gamma_b", [128, H]); beta_b = din("beta_b", [128, H])
    bD_row = din("bD_row", [1, H])
    convw = din("convw", [128, 3 * 32])
    convb = din("convb", [128, 32])
    w1t_f = din("w1t_f", [H, DH])
    w2t_f = din("w2t_f", [DH, H])
    qmask = din("qmask", [128, NT])
    hsel0 = din("hsel0", [S, 1]); hsel1 = din("hsel1", [S, 1])
    ones1 = din("ones1", [1, 128])
    ones_col = din("ones_col", [128, 1])
    id_f32 = din("id_f32", [128, 128])
    onesrow_ext = din("onesrow_ext", [1, EXT], BF16)
    zrow_ext = din("zrow_ext", [1, EXT], BF16)
    out_tm = nc.dram_tensor("out_tm", [T // 2, H], F32, kind="ExternalOutput").ap()
    w1q_hbm = nc.dram_tensor("w1q_hbm", [H, DH], BF16, kind="Internal").ap()
    w2q_hbm = nc.dram_tensor("w2q_hbm", [DH, H], BF16, kind="Internal").ap()
    r1_hbm = nc.dram_tensor("r1_hbm", [EXT, H], F32, kind="Internal").ap()
    r1q_hbm = nc.dram_tensor("r1q_hbm", [EXTP, H], BF16, kind="Internal").ap()
    dbg = {}
    if debug:
        for nm, shape, dt in [("d_gates", [48, T], F32), ("d_hs", [S, EXT], F32),
                              ("d_xp", [128, 8 * 512], F32),
                              ("d_y", [128, 8 * 512], F32),
                              ("d_z", [128, 8 * 512], F32),
                              ("d_r1", [EXT, H], F32),
                              ("d_h", [128, 32 * (GW + 2)], F32),
                              ("d_wg", [128, 8 * 48], F32),
                              ("d_dq", [1, EXT], F32)]:
            dbg[nm] = nc.dram_tensor(nm, shape, dt, kind="ExternalOutput").ap()

    root = contextlib.ExitStack()

    def SB(name, shape, dt=F32, stack=None):
        return (stack or root).enter_context(nc.sbuf_tensor(name, shape, dt))

    def PSUM(name, shape, dt=F32, stack=None):
        return (stack or root).enter_context(nc.psum_tensor(name, shape, dt))

    def load(dst_ap, src_ap, region, q="SP", reads=()):
        p.op(q, lambda d=dst_ap, s=src_ap, e=QENG[q]: e.dma_start(out=d, in_=s),
             reads=reads, writes=(region,), dma=True)

    def store(dst_ap, src_ap, reads, region, q="ACT"):
        p.op(q, lambda d=dst_ap, s=src_ap, e=QENG[q]: e.dma_start(out=d, in_=s),
             reads=reads, writes=(region,), dma=True)

    def load_fm(dst2d, width, src_slice, region, q="SP", stride=None):
        st = stride if stride is not None else width
        nch = src_slice.shape[0] // 128
        d3 = dst2d[:, 0:nch * st].rearrange("p (c t) -> p c t",
                                            c=nch)[:, :, 0:width]
        s3 = src_slice.rearrange("(c p) t -> p c t", p=128)
        p.op(q, lambda d=d3, s=s3, e=QENG[q]: e.dma_start(out=d, in_=s),
             reads=(), writes=(region,), dma=True)

    def mm(group, out_ap, lhsT_ap, rhs_ap, reads, wr, start, stop):
        p.op("PE", lambda o=out_ap, l=lhsT_ap, r=rhs_ap, st=start, sp=stop:
             PE.matmul(o, l, r, start=st, stop=sp),
             reads=reads, writes=(wr,), group=group)

    stAB = contextlib.ExitStack()
    s_qmask = SB("s_qmask", [128, NT])
    s_hsel0 = SB("s_hsel0", [S, 1]); s_hsel1 = SB("s_hsel1", [S, 1])
    s_ones1 = SB("s_ones1", [1, 128])
    s_onesc = SB("s_onesc", [128, 1])
    s_idf = SB("s_idf", [128, 128])
    s_binc = SB("s_binc", [128, 8]); s_boutc = SB("s_boutc", [128, 8])
    s_w1mean = SB("s_w1mean", [128, 1]); s_w2mean = SB("s_w2mean", [128, 1])
    dqcols = SB("dqcols", [128, NT])
    s_convw = SB("s_convw", [128, 3 * 32]); s_convb = SB("s_convb", [128, 32])
    s_gamb = SB("s_gamb", [128, H], stack=stAB)
    s_betb = SB("s_betb", [128, H], stack=stAB)
    hsA_hi = SB("hsA_hi", [S + 1, EXT], BF16, stack=stAB)
    hsA_lo = SB("hsA_lo", [S + 1, EXT], BF16, stack=stAB)
    CA_hi = SB("CA_hi", [S + 1, H], BF16, stack=stAB)
    CA_lo = SB("CA_lo", [S + 1, H], BF16, stack=stAB)

    TW = 1024 if SAFE_ROUND else 2048
    NTB = 4

    def alloc_tern(tstack):
        twfs = tuple(SB(f"twf{i}", [128, TW], stack=tstack) for i in range(NTB))
        _tu = tuple(SB(f"tu{i}", [128, TW], stack=tstack) for i in range(2))
        tus = (_tu[0], _tu[1], _tu[0], _tu[1])
        _tk16 = tuple(SB(f"tk16_{i}", [128, TW], I16, stack=tstack)
                      for i in range(2))
        tk16s = (_tk16[0], _tk16[1], _tk16[0], _tk16[1])
        _tb = tuple(SB(f"tb{i}", [128, TW], BF16, stack=tstack) for i in range(2))
        tbs = (_tb[0], _tb[1], _tb[0], _tb[1])
        if SAFE_ROUND:
            tk = SB("tk", [128, TW], I32, stack=tstack)
            trr = SB("trr", [128, TW], stack=tstack)
            tcp = SB("tcp", [128, TW], stack=tstack)
        else:
            tk = trr = tcp = None
        tcol = SB("tcol", [128, 10], stack=tstack)
        return twfs, tus, tk16s, tbs, tk, trr, tcp, tcol

    for dst, src, rg in [(s_gamb, gamma_b, "s_gamb"), (s_betb, beta_b, "s_betb"),
                         (s_qmask, qmask, "s_qmask"), (s_hsel0, hsel0, "s_hsel0"),
                         (s_hsel1, hsel1, "s_hsel1"), (s_ones1, ones1, "s_ones1"), (s_onesc, ones_col, "s_onesc"),
                         (s_idf, id_f32, "s_idf"), (s_binc, b_in_cols, "s_binc"),
                         (s_boutc, b_out_cols, "s_boutc"),
                         (s_convw, convw, "s_convw"), (s_convb, convb, "s_convb")]:
        load(dst[:], src[:], rg)

    stA = contextlib.ExitStack()
    g80 = SB("g80", [80, T], stack=stA)
    hs_buf = SB("hs_buf", [S, T + 4], stack=stA)
    GM = 80
    G = SB("G", [128, 8 * GM], stack=stA)
    WGf = SB("WGf", [128, 8 * GM], stack=stA)
    WG_hi = SB("WG_hi", [128, 8 * GM], BF16, stack=stA)
    WG_lo = SB("WG_lo", [128, 8 * GM], BF16, stack=stA)
    bG = SB("bG", [GM, 1], stack=stA)
    wtrs = (SB("wtr0", [128, H], stack=stA),
            SB("wtr1", [128, H], stack=stA))
    bmt = SB("bmt", [128, 8 * S], stack=stA)
    wigt = SB("wigt", [128, 8 * S], stack=stA)
    wfgt = SB("wfgt", [128, 8 * S], stack=stA)
    bic = SB("bic", [128, 8], stack=stA)
    bbc = SB("bbc", [128, 8], stack=stA)
    ps_pre = PSUM("ps_pre", [128, 1024], stack=stA)
    ps_g = PSUM("ps_g", [GM, 512], stack=stA)

    load_fm(bmt, S, B_mat[:], "bmt")
    load(bic[:], b_in_cols[:], "bic")
    load(bbc[:], b_B_cols[:], "bbc")
    load_fm(wigt, S, W_ig[:], "wigt")
    load_fm(wfgt, S, W_fg[:], "wfgt")

    p.op("DVE", lambda: V.memset(G[:, :], 0.0), writes=("Ginit",))
    for fc in range(8):
        wtr, wr_ = wtrs[fc % 2], f"wtr{fc % 2}"
        po = (fc % 2) * 512
        load_fm(wtr, 128, W_B_T[:, fc * 128:(fc + 1) * 128], wr_)
        for j in range(8):
            mm(f"P{fc}", ps_pre[:, po:po + 16], wtr[:, j * 128:(j + 1) * 128],
               bmt[:, j * S:(j + 1) * S], reads=(wr_, "bmt"),
               wr=f"psPre{fc % 2}", start=(j == 0), stop=(j == 7))
        p.op("DVE", lambda o=G[:, fc * GM:fc * GM + 16],
             i=ps_pre[:, po:po + 16], fc=fc:
             V.tensor_copy(o, i), reads=(f"psPre{fc % 2}", "Ginit"),
             writes=(f"G{fc}",))
        p.op("DVE", lambda o=G[:, fc * GM + 32:fc * GM + 48],
             i=wigt[:, fc * S:(fc + 1) * S]: V.tensor_copy(o, i),
             reads=("wigt", "Ginit"), writes=(f"G{fc}",))
        p.op("DVE", lambda o=G[:, fc * GM + 64:fc * GM + 80],
             i=wfgt[:, fc * S:(fc + 1) * S]: V.tensor_copy(o, i),
             reads=("wfgt", "Ginit"), writes=(f"G{fc}",))

    for fin in range(8):
        wtr, wr_ = wtrs[fin % 2], f"wtr{fin % 2}"
        po = (fin % 2) * 512
        load_fm(wtr, 128, W_in_T[:, fin * 128:(fin + 1) * 128], wr_)
        for fc in range(8):
            mm(f"WG{fin}", ps_pre[:, po:po + GM],
               wtr[:, fc * 128:(fc + 1) * 128],
               G[:, fc * GM:(fc + 1) * GM], reads=(wr_, f"G{fc}"),
               wr=f"psPre{fin % 2}", start=(fc == 0), stop=(fc == 7))
        p.op("DVE", lambda o=WGf[:, fin * GM:(fin + 1) * GM],
             i=ps_pre[:, po:po + GM], fin=fin:
             V.tensor_copy(o, i), reads=(f"psPre{fin % 2}",),
             writes=(f"WGf{fin}",))
        p.op("ACT", lambda o=WG_hi[:, fin * GM:(fin + 1) * GM],
             i=WGf[:, fin * GM:(fin + 1) * GM]: A.activation(o, i, AF.Copy),
             reads=(f"WGf{fin}",), writes=(f"WGhi{fin}",))
        p.op("DVE", lambda o=WG_lo[:, fin * GM:(fin + 1) * GM],
             a=WGf[:, fin * GM:(fin + 1) * GM],
             b=WG_hi[:, fin * GM:(fin + 1) * GM]:
             V.tensor_sub(o, a, b), reads=(f"WGf{fin}", f"WGhi{fin}"),
             writes=(f"WGlo{fin}",))
    if debug:
        store(dbg["d_wg"][:], WGf[:, 0:8 * 48], tuple(f"WGf{i}" for i in range(8)),
              "d_wg")

    t1 = SB("t1", [128, 8], stack=stA)
    for j in range(8):
        wtr, wr_ = wtrs[j % 2], f"wtr{j % 2}"
        po = (j % 2) * 512
        load_fm(wtr, 128, W_B_T[:, j * 128:(j + 1) * 128], wr_)
        for fc in range(8):
            mm(f"t1{j}", ps_pre[:, po:po + 1], wtr[:, fc * 128:(fc + 1) * 128],
               bic[:, fc:fc + 1], reads=(wr_, "bic"),
               wr=f"psPre{j % 2}", start=(fc == 0), stop=(fc == 7))
        p.op("DVE", lambda o=t1[:, j:j + 1], i=ps_pre[:, po:po + 1],
             b=bbc[:, j:j + 1], j=j:
             V.tensor_add(o, i, b), reads=(f"psPre{j % 2}", "bbc"),
             writes=("t1",))
    for j in range(8):
        mm("bP", ps_pre[0:16, 0:1], bmt[:, j * S:(j + 1) * S], t1[:, j:j + 1],
           reads=("bmt", "t1"), wr="psPre0", start=(j == 0), stop=(j == 7))
    btmp = SB("btmp", [S, 3], stack=stA)
    p.op("DVE", lambda: V.tensor_copy(btmp[:, 0:1], ps_pre[0:16, 0:1]),
         reads=("psPre0",), writes=("btmp0",))
    for which, wt, wrg, bcol, col in (("ig", wigt, "wigt", b_ig_col, 1),
                                      ("fg", wfgt, "wfgt", b_fg_col, 2)):
        for fc in range(8):
            mm(f"b{which}", ps_pre[0:16, 1:2], wt[:, fc * S:(fc + 1) * S],
               bic[:, fc:fc + 1], reads=(wrg, "bic"),
               wr="psPre0", start=(fc == 0), stop=(fc == 7))
        bcs = SB(f"bc_{which}", [S, 1], stack=stA)
        load(bcs[:], bcol[:], f"bc_{which}")
        p.op("DVE", lambda o=btmp[:, col:col + 1], i=ps_pre[0:16, 1:2], b=bcs[:]:
             V.tensor_add(o, i, b), reads=("psPre0", f"bc_{which}"),
             writes=(f"btmp{col}",))
    store(bG[0:16, :], btmp[:, 0:1], ("btmp0",), "bG")
    store(bG[32:48, :], btmp[:, 1:2], ("btmp1",), "bG")
    store(bG[64:80, :], btmp[:, 2:3], ("btmp2",), "bG")

    ctmp = SB("ctmp", [S, H], stack=stA)
    bdtmp = SB("bdtmp", [1, H], stack=stA)
    load(ctmp[:], C_mat[:], "ctmp")
    load(bdtmp[:], bD_row[:], "bdtmp")
    p.op("ACT", lambda: A.activation(CA_hi[0:S, :], ctmp[:], AF.Copy),
         reads=("ctmp",), writes=("CAhi",))
    p.op("DVE", lambda: V.tensor_sub(CA_lo[0:S, :], ctmp[:], CA_hi[0:S, :]),
         reads=("ctmp", "CAhi"), writes=("CAlo",))
    bdh = SB("bdh", [1, H], BF16, stack=stA)
    bdl = SB("bdl", [1, H], BF16, stack=stA)
    p.op("ACT", lambda: A.activation(bdh[:], bdtmp[:], AF.Copy),
         reads=("bdtmp",), writes=("bdh",))
    p.op("DVE", lambda: V.tensor_sub(bdl[:], bdtmp[:], bdh[:]),
         reads=("bdtmp", "bdh"), writes=("bdl",))
    store(CA_hi[S:S + 1, :], bdh[:], ("bdh",), "CAhi", q="SP")
    store(CA_lo[S:S + 1, :], bdl[:], ("bdl",), "CAlo", q="SP")

    twfs, tus, tk16s, tbs, tk, trr, tcp, tcol = alloc_tern(stA)

    def tern_slices(nrows, width):
        return [(rc, c0, min(width, c0 + TW))
                for rc in range(nrows) for c0 in range(0, width, TW)]

    def tern_pass1(src, nrows, width, mean_out, mean_rg, tg,
                   ps_hbuf, ps_rg, scol):
        stag = f"tcol{scol}"
        slices = tern_slices(nrows, width)
        for i, (rc, c0, c1) in enumerate(slices):
            wv = c1 - c0
            pr = i % NTB
            twf, tu = twfs[pr], tus[pr]
            load(twf[:, 0:wv], src[rc * 128:(rc + 1) * 128, c0:c1], f"twf{pr}")
            if i % 2 == 0:
                p.op("ACT", lambda twf=twf, tu=tu, e=wv: A.activation(
                    tu[:, 0:e], twf[:, 0:e], AF.Abs, accum_out=tcol[:, 0:1]),
                     reads=(f"twf{pr}",), writes=(f"tu{pr % 2}", "tcol0"))
                acc, arg = "tcol0", 0
            else:
                p.op("DVE", lambda twf=twf, e=wv: V.tensor_reduce(
                    tcol[:, 3:4].rearrange("p (t o) -> p t o", o=1),
                    twf[:, 0:e].rearrange("p (t e) -> p t e", t=1),
                    AX, OP.add, apply_absolute_value=True),
                     reads=(f"twf{pr}",), writes=("tcol3",))
                acc, arg = "tcol3", 3
            ac = 6 + (i % 4)
            if i < 4:
                p.op("DVE", lambda ac=ac, a=arg: V.tensor_copy(
                    tcol[:, ac:ac + 1], tcol[:, a:a + 1]),
                     reads=(acc,), writes=(f"tcol{ac}",))
            else:
                p.op("DVE", lambda a=arg, ac=ac: V.tensor_add(
                    tcol[:, ac:ac + 1], tcol[:, ac:ac + 1], tcol[:, a:a + 1]),
                     reads=(acc, f"tcol{ac}"), writes=(f"tcol{ac}",))
            yield
        nun = len(slices)
        p.op("DVE", lambda: V.tensor_add(tcol[:, 6:7], tcol[:, 6:7],
                                         tcol[:, 7:8]),
             reads=("tcol6", "tcol7"), writes=("tcol6",))
        if nun > 2:
            p.op("DVE", lambda: V.tensor_add(tcol[:, 8:9], tcol[:, 8:9],
                                             tcol[:, 9:10]),
                 reads=("tcol8", "tcol9"), writes=("tcol8",))
            p.op("DVE", lambda: V.tensor_add(tcol[:, 6:7], tcol[:, 6:7],
                                             tcol[:, 8:9]),
                 reads=("tcol6", "tcol8"), writes=("tcol6",))
        p.op("DVE", lambda: V.tensor_scalar(tcol[:, 1:2], tcol[:, 6:7],
                                            1.0 / (nrows * 128 * width), None,
                                            OP.mult),
             reads=("tcol6",), writes=("tcol1",))
        mm("parsum" + tg, ps_hbuf[0:1, 0:1], s_onesc[:], tcol[:, 1:2],
           reads=("s_onesc", "tcol1"), wr=ps_rg, start=True, stop=True)
        p.op("DVE", lambda: V.tensor_copy(tcol[0:1, 1:2], ps_hbuf[0:1, 0:1]),
             reads=(ps_rg,), writes=("tcol1",))
        mm("parbc" + tg, ps_hbuf[0:128, 0:1], s_ones1[:], tcol[0:1, 1:2],
           reads=("s_ones1", "tcol1"), wr=ps_rg, start=True, stop=True)
        p.op("DVE", lambda: V.tensor_copy(tcol[:, 1:2], ps_hbuf[0:128, 0:1]),
             reads=(ps_rg,), writes=("tcol1",))
        p.op("DVE", lambda: V.tensor_scalar_max(mean_out[:], tcol[:, 1:2], 1e-5),
             reads=("tcol1",), writes=(mean_rg,))
        p.op("DVE", lambda mo=mean_out, sc=scol: V.reciprocal(
            tcol[:, sc:sc + 1], mo[:]),
             reads=(mean_rg,), writes=(stag,))
        yield

    def tern_pass2(src, dst_hbm, nrows, width, tg, scol):
        stag = f"tcol{scol}"
        slices = tern_slices(nrows, width)
        for i, (rc, c0, c1) in enumerate(slices):
            wv = c1 - c0
            pr = i % NTB
            twf, tu, tk16, tb = twfs[pr], tus[pr], tk16s[pr], tbs[pr]
            load(twf[:, 0:wv], src[rc * 128:(rc + 1) * 128, c0:c1], f"twf{pr}")
            if SAFE_ROUND:
                p.op("DVE", lambda twf=twf, tu=tu, e=wv: V.tensor_scalar(
                    tu[:, 0:e], twf[:, 0:e], tcol[:, scol:scol + 1], None,
                    OP.mult),
                     reads=(f"twf{pr}", stag), writes=(f"tu{pr % 2}",))
                p.op("DVE", lambda tu=tu, e=wv: V.tensor_copy(tk[:, 0:e],
                                                              tu[:, 0:e]),
                     reads=(f"tu{pr % 2}",), writes=("tk",))
                p.op("DVE", lambda tu=tu, e=wv: V.tensor_sub(trr[:, 0:e],
                                                             tu[:, 0:e],
                                                             tk[:, 0:e]),
                     reads=(f"tu{pr % 2}", "tk"), writes=("trr",))
                p.op("DVE", lambda e=wv: V.tensor_scalar(tcp[:, 0:e], trr[:, 0:e],
                                                         0.5, None, OP.is_gt),
                     reads=("trr",), writes=("tcp",))
                p.op("DVE", lambda tu=tu, e=wv: V.scalar_tensor_tensor(
                    tu[:, 0:e], trr[:, 0:e], -0.5, tcp[:, 0:e], OP.is_lt,
                    OP.subtract),
                     reads=("trr", "tcp"), writes=(f"tu{pr % 2}",))
                p.op("DVE", lambda tu=tu, e=wv: V.tensor_sub(tu[:, 0:e],
                                                             tk[:, 0:e],
                                                             tu[:, 0:e]),
                     reads=(f"tu{pr % 2}", "tk"), writes=(f"tu{pr % 2}",))
                p.op("DVE", lambda tu=tu, tb=tb, e=wv: V.tensor_scalar(
                    tb[:, 0:e], tu[:, 0:e], 1.0, -1.0, OP.min, OP.max),
                     reads=(f"tu{pr % 2}",), writes=(f"tb{pr % 2}",))
            elif i % 2 == 0:
                p.op("ACT", lambda twf=twf, tu=tu, e=wv: A.activation(
                    tu[:, 0:e], twf[:, 0:e], AF.Identity,
                    scale=tcol[:, scol:scol + 1]),
                     reads=(f"twf{pr}", stag), writes=(f"tu{pr % 2}",))
                p.op("DVE", lambda tu=tu, tk16=tk16, e=wv: V.tensor_scalar(
                    tk16[:, 0:e], tu[:, 0:e], 1.49, -1.49, OP.min, OP.max),
                     reads=(f"tu{pr % 2}",), writes=(f"tk16{pr % 2}",))
                p.op("ACT", lambda tk16=tk16, tb=tb, e=wv: A.activation(
                    tb[:, 0:e], tk16[:, 0:e], AF.Copy),
                     reads=(f"tk16{pr % 2}",), writes=(f"tb{pr % 2}",))
            else:
                p.op("DVE", lambda twf=twf, tu=tu, e=wv: V.tensor_scalar(
                    tu[:, 0:e], twf[:, 0:e], tcol[:, scol:scol + 1], 1.49,
                    OP.mult, OP.min),
                     reads=(f"twf{pr}", stag), writes=(f"tu{pr % 2}",))
                p.op("DVE", lambda tu=tu, tk16=tk16, e=wv: V.tensor_scalar(
                    tk16[:, 0:e], tu[:, 0:e], -1.49, None, OP.max),
                     reads=(f"tu{pr % 2}",), writes=(f"tk16{pr % 2}",))
                p.op("ACT", lambda tk16=tk16, tb=tb, e=wv: A.activation(
                    tb[:, 0:e], tk16[:, 0:e], AF.Copy),
                     reads=(f"tk16{pr % 2}",), writes=(f"tb{pr % 2}",))
            store(dst_hbm[rc * 128:(rc + 1) * 128, c0:c1], tb[:, 0:wv],
                  (f"tb{pr % 2}",), f"q{tg}{rc}_{c0}")
            yield

    def tern_p1_all(ps_hbuf, ps_rg):
        yield from tern_pass1(w1t_f, 8, DH, s_w1mean, "s_w1mean", "w1",
                              ps_hbuf, ps_rg, 2)
        yield from tern_pass1(w2t_f, 32, H, s_w2mean, "s_w2mean", "w2",
                              ps_hbuf, ps_rg, 4)

    def tern_p2_all():
        yield from tern_pass2(w1t_f, w1q_hbm, 8, DH, "w1", 2)
        yield from tern_pass2(w2t_f, w2q_hbm, 32, H, "w2", 4)

    def tern_all():
        yield from tern_p1_all(ps_pre, "psPre0")
        yield from tern_p2_all()

    tern_p1 = tern_all()

    def pump_p1(n):
        for _ in range(n):
            try:
                next(tern_p1)
            except StopIteration:
                break

    GSL = 256
    xgfs = (SB("xgf0", [128, 8 * GSL], stack=stA),
            SB("xgf1", [128, 8 * GSL], stack=stA))
    xghs = (SB("xgh0", [128, 8 * GSL], BF16, stack=stA),
            SB("xgh1", [128, 8 * GSL], BF16, stack=stA))
    xgls = (SB("xgl0", [128, 8 * GSL], BF16, stack=stA),
            SB("xgl1", [128, 8 * GSL], BF16, stack=stA))
    for sl in range(T // GSL):
        c0 = sl * GSL
        pr = sl % 2
        xgf, xgh, xgl = xgfs[pr], xghs[pr], xgls[pr]
        go = pr * GSL
        load_fm(xgf, GSL, xg_fm[:, c0:c0 + GSL], f"xgf{pr}")
        p.op("ACT", lambda xgf=xgf, xgh=xgh: A.activation(xgh[:], xgf[:],
                                                          AF.Copy),
             reads=(f"xgf{pr}",), writes=(f"xgh{pr}",))
        p.op("DVE", lambda xgf=xgf, xgh=xgh, xgl=xgl:
             V.tensor_sub(xgl[:], xgf[:], xgh[:]),
             reads=(f"xgf{pr}", f"xgh{pr}"), writes=(f"xgl{pr}",))
        for c in range(8):
            sh = WG_hi[:, c * GM:(c + 1) * GM]
            slo = WG_lo[:, c * GM:(c + 1) * GM]
            mh = xgh[:, c * GSL:(c + 1) * GSL]
            ml = xgl[:, c * GSL:(c + 1) * GSL]
            mm(f"g{sl}", ps_g[:, go:go + GSL], sh, mh,
               reads=(f"WGhi{c}", f"xgh{pr}"),
               wr=f"psG{pr}", start=(c == 0), stop=False)
            mm(f"g{sl}", ps_g[:, go:go + GSL], sh, ml,
               reads=(f"WGhi{c}", f"xgl{pr}"),
               wr=f"psG{pr}", start=False, stop=False)
            mm(f"g{sl}", ps_g[:, go:go + GSL], slo, mh,
               reads=(f"WGlo{c}", f"xgh{pr}"),
               wr=f"psG{pr}", start=False, stop=(c == 7))
        p.op("ACT", lambda o=g80[0:16, c0:c0 + GSL], go=go:
             A.activation(o, ps_g[0:16, go:go + GSL], AF.Identity,
                          bias=bG[0:16, :]),
             reads=(f"psG{pr}", "bG"), writes=(f"gB{sl % 8}",))
        p.op("ACT", lambda o=g80[32:48, c0:c0 + GSL], go=go:
             A.activation(o, ps_g[32:48, go:go + GSL], AF.Sigmoid,
                          bias=bG[32:48, :]),
             reads=(f"psG{pr}", "bG"), writes=(f"gI{sl % 8}",))
        p.op("ACT", lambda o=g80[64:80, c0:c0 + GSL], go=go:
             A.activation(o, ps_g[64:80, go:go + GSL], AF.Sigmoid,
                          bias=bG[64:80, :]),
             reads=(f"psG{pr}", "bG"), writes=(f"gF{sl % 8}",))
        pump_p1(9)
    gr = tuple(f"gB{i}" for i in range(8)) + tuple(f"gI{i}" for i in range(8)) + tuple(f"gF{i}" for i in range(8))
    if debug:
        store(dbg["d_gates"][0:16, :], g80[0:16, :], gr, "d_gates")
        store(dbg["d_gates"][16:32, :], g80[32:48, :], gr, "d_gates")
        store(dbg["d_gates"][32:48, :], g80[64:80, :], gr, "d_gates")

    hs_e = SB("hs_e", [S, EXT], stack=stA)
    TH = T // 2
    uu = g80[0:16, :]
    store(xgfs[0][0:16, :TH], g80[32:48, 0:TH], gr, "igc", q="SP")
    store(xgfs[1][0:16, :TH], g80[32:48, TH:T], gr, "igc2", q="SP")
    p.op("DVE", lambda: V.tensor_mul(uu[:, 0:TH], xgfs[0][0:16, :TH],
                                     g80[0:16, 0:TH]),
         reads=gr + ("igc",), writes=("uu",) + tuple(f"gB{i}" for i in range(8)))
    p.op("DVE", lambda: V.tensor_mul(uu[:, TH:T], xgfs[1][0:16, :TH],
                                     g80[0:16, TH:T]),
         reads=gr + ("igc2", "uu"), writes=("uu",))
    p.op("DVE", lambda: V.memset(hs_buf[:, 0:2], 0.0), writes=("hs",))
    p.op("DVE", lambda: V.memset(hs_buf[:, T + 2:T + 4], 0.0), writes=("hs",))
    store(xgfs[0][0:16, :TH], g80[64:80, 0:TH], gr + ("uu", "igc"), "fgc",
          q="SP")
    store(xgfs[1][0:16, :TH], g80[64:80, TH:T], gr + ("uu", "igc2"), "fgc2",
          q="SP")
    p.op("DVE", lambda: V.tensor_tensor_scan(hs_buf[:, 2:TH + 2],
                                             xgfs[0][0:16, :TH],
                                             uu[:, 0:TH], 0.0,
                                             OP.mult, OP.add),
         reads=gr + ("uu", "fgc"), writes=("hs",))
    p.op("DVE", lambda: V.tensor_tensor_scan(hs_buf[:, TH + 2:T + 2],
                                             xgfs[1][0:16, :TH],
                                             uu[:, TH:T],
                                             hs_buf[:, TH + 1:TH + 2],
                                             OP.mult, OP.add),
         reads=gr + ("uu", "fgc2", "hs"), writes=("hs",))
    p.op("DVE", lambda: V.tensor_scalar(hs_e[:], hs_buf[:, 0:EXT], s_hsel0[:],
                                        None, OP.mult),
         reads=("hs", "s_hsel0"), writes=("hs_e",))
    p.op("DVE", lambda: V.scalar_tensor_tensor(hs_e[:], hs_buf[:, 2048:2048 + EXT],
                                               s_hsel1[:], hs_e[:], OP.mult, OP.add),
         reads=("hs", "s_hsel1", "hs_e"), writes=("hs_e",))
    if debug:
        store(dbg["d_hs"][:], hs_e[:], ("hs_e",), "d_hs")
    p.op("ACT", lambda: A.activation(hsA_hi[0:S, :], hs_e[:], AF.Copy),
         reads=("hs_e",), writes=("hsAhi",))
    p.op("DVE", lambda: V.tensor_sub(hsA_lo[0:S, :], hs_e[:], hsA_hi[0:S, :]),
         reads=("hs_e", "hsAhi"), writes=("hsAlo",))
    load(hsA_hi[S:S + 1, :], onesrow_ext[:], "hsAhi")
    load(hsA_lo[S:S + 1, :], zrow_ext[:], "hsAlo")

    pump_p1(10 ** 9)

    stA.close()
    p.barrier()

    stB = contextlib.ExitStack()
    Wq = {}
    for nm in ("Win", "WD", "Wout"):
        Wq[nm] = (SB(nm + "_hi", [128, 8 * H], BF16, stack=stB),
                  SB(nm + "_lo", [128, 8 * H], BF16, stack=stB))
    xtm = SB("xtm", [128, H], stack=stB)
    xb0 = SB("xb0", [128, 8 * AWM], stack=stB)
    xb1 = SB("xb1", [128, 8 * AWM], stack=stB)
    zS2 = (SB("zS0", [128, 8 * AWM], stack=stB),
           SB("zS1", [128, 8 * AWM], stack=stB))
    xp_f = SB("xp_f", [128, 8 * AWM], stack=stB)
    yf_c = SB("yf_c", [128, AWM], stack=stB)
    wi = 0
    for nm, src in (("Win", W_in), ("WD", W_D), ("Wout", W_out)):
        hi, lo = Wq[nm]
        for c in range(8):
            pr = wi % 2
            wi += 1
            wtmp = (xb0, xb1)[pr][:, 0:H]
            load(wtmp, src[c * 128:(c + 1) * 128, :], f"xb{pr}")
            if pr == 0:
                p.op("ACT", lambda o=hi[:, c * H:(c + 1) * H], a=wtmp:
                     A.activation(o, a, AF.Copy),
                     reads=(f"xb{pr}",), writes=(f"{nm}hi{c}",))
            else:
                p.op("DVE", lambda o=hi[:, c * H:(c + 1) * H], a=wtmp:
                     V.tensor_copy(o, a),
                     reads=(f"xb{pr}",), writes=(f"{nm}hi{c}",))
            p.op("GP", lambda o=lo[:, c * H:(c + 1) * H], a=wtmp,
                 b=hi[:, c * H:(c + 1) * H]: GP.tensor_sub(o, a, b),
                 reads=(f"xb{pr}", f"{nm}hi{c}"), writes=(f"{nm}lo{c}",))

    pA_h = SB("pA_h", [128, 8 * AWM], BF16, stack=stB)
    pA_l = SB("pA_l", [128, 8 * AWM], BF16, stack=stB)
    pB_h = SB("pB_h", [128, 8 * AWM], BF16, stack=stB)
    pB_l = SB("pB_l", [128, 8 * AWM], BF16, stack=stB)
    z_tm = SB("z_tm", [128, H], stack=stB)
    zsq = SB("zsq", [128, H], stack=stB)
    s_s = z_tm
    sc_t = SB("sc_t", [128, H], stack=stB)
    r1t = z_tm
    q_i = SB("q_i", [128, H], I16, stack=stB)
    q_b = SB("q_b", [128, H], BF16, stack=stB)
    colz = SB("colz", [128, 16], stack=stB)
    if SAFE_ROUND:
        rr_t = SB("rr_t", [128, H], stack=stB)
        cp_t = SB("cp_t", [128, H], stack=stB)
    ps_a = PSUM("ps_a", [128, 2048], stack=stB)
    ps_tr = PSUM("ps_tr", [128, 2048], stack=stB)

    lgen = [None]

    def lpump(n):
        if lgen[0] is None:
            return
        for _ in range(n):
            try:
                next(lgen[0])
            except StopIteration:
                lgen[0] = None
                break

    def attn_mm(gi):
        g0, g1 = AGROUPS[gi]
        w = g1 - g0
        tg = f"A{gi}"
        zS = zS2[gi % 2]
        ZR = f"zS{gi % 2}"
        xb = (xb0, xb1)[gi % 2]
        xbr = f"xb{gi % 2}"
        load_fm(xb, w, x_fm[:, g0:g1], xbr, stride=AWM)
        for k in range(8):
            p.op("ACT", lambda k=k: A.activation(
                pA_h[:, k * AWM:k * AWM + w], xb[:, k * AWM:k * AWM + w],
                AF.Copy),
                 reads=(xbr,), writes=(f"pAh{k}",))
            p.op("DVE", lambda k=k: V.tensor_sub(
                pA_l[:, k * AWM:k * AWM + w], xb[:, k * AWM:k * AWM + w],
                pA_h[:, k * AWM:k * AWM + w]),
                 reads=(xbr, f"pAh{k}"), writes=(f"pAl{k}",))

        for m in range(8):
            od = ps_a[0:128, (m % 4) * 512:(m % 4) * 512 + w]
            for k in range(8):
                sh = Wq["Win"][0][:, k * H + m * 128:k * H + m * 128 + 128]
                sl = Wq["Win"][1][:, k * H + m * 128:k * H + m * 128 + 128]
                mh = pA_h[:, k * AWM:k * AWM + w]
                ml = pA_l[:, k * AWM:k * AWM + w]
                mm(tg + f"xp{m}", od, sh, mh, reads=(f"Winhi{k}", f"pAh{k}"),
                   wr=f"psA{m % 4}", start=(k == 0), stop=False)
                mm(tg + f"xp{m}", od, sh, ml, reads=(f"Winhi{k}", f"pAl{k}"),
                   wr=f"psA{m % 4}", start=False, stop=False)
                mm(tg + f"xp{m}", od, sl, mh, reads=(f"Winlo{k}", f"pAh{k}"),
                   wr=f"psA{m % 4}", start=False, stop=(k == 7))
            p.op("DVE", lambda o=xp_f[:, m * AWM:m * AWM + w], i=od,
                 bc=s_binc[:, m:m + 1], mm_=m:
                 V.tensor_scalar(o, i, bc, None, OP.add),
                 reads=(f"psA{m % 4}", "s_binc"), writes=(f"xpf{m}",))
            p.op("ACT", lambda m=m: A.activation(
                pB_h[:, m * AWM:m * AWM + w], xp_f[:, m * AWM:m * AWM + w],
                AF.Copy),
                 reads=(f"xpf{m}",), writes=(f"pBh{m}",))
            p.op("DVE", lambda m=m: V.tensor_sub(
                pB_l[:, m * AWM:m * AWM + w], xp_f[:, m * AWM:m * AWM + w],
                pB_h[:, m * AWM:m * AWM + w]),
                 reads=(f"xpf{m}", f"pBh{m}"), writes=(f"pBl{m}",))
            lpump(1)
        if debug and gi == 0:
            for m in range(8):
                store(dbg["d_xp"][:, m * w:(m + 1) * w],
                      xp_f[:, m * AWM:m * AWM + w], (f"xpf{m}",), "d_xp")

        for m in range(8):
            od = ps_a[0:128, (m % 4) * 512:(m % 4) * 512 + w]
            for k in range(8):
                sh = Wq["WD"][0][:, k * H + m * 128:k * H + m * 128 + 128]
                sl = Wq["WD"][1][:, k * H + m * 128:k * H + m * 128 + 128]
                mh = pB_h[:, k * AWM:k * AWM + w]
                ml = pB_l[:, k * AWM:k * AWM + w]
                mm(tg + f"y{m}", od, sh, mh, reads=(f"WDhi{k}", f"pBh{k}"),
                   wr=f"psA{m % 4}", start=(k == 0), stop=False)
                mm(tg + f"y{m}", od, sh, ml, reads=(f"WDhi{k}", f"pBl{k}"),
                   wr=f"psA{m % 4}", start=False, stop=False)
                mm(tg + f"y{m}", od, sl, mh, reads=(f"WDlo{k}", f"pBh{k}"),
                   wr=f"psA{m % 4}", start=False, stop=False)
            mm(tg + f"y{m}", od, CA_hi[:, m * 128:(m + 1) * 128],
               hsA_hi[:, g0:g1], reads=("CAhi", "hsAhi"),
               wr=f"psA{m % 4}", start=False, stop=False)
            mm(tg + f"y{m}", od, CA_hi[:, m * 128:(m + 1) * 128],
               hsA_lo[:, g0:g1], reads=("CAhi", "hsAlo"),
               wr=f"psA{m % 4}", start=False, stop=False)
            mm(tg + f"y{m}", od, CA_lo[:, m * 128:(m + 1) * 128],
               hsA_hi[:, g0:g1], reads=("CAlo", "hsAhi"),
               wr=f"psA{m % 4}", start=False, stop=True)
            p.op("ACT", lambda i=od, mm_=m: A.activation(yf_c[:, 0:w], i, AF.Tanh),
                 reads=(f"psA{m % 4}",), writes=("yf_c",))
            if debug and gi == 0:
                store(dbg["d_y"][:, m * w:(m + 1) * w], yf_c[:, 0:w],
                      ("yf_c",), "d_y")
            p.op("ACT", lambda m=m: A.activation(
                pA_h[:, m * AWM:m * AWM + w], yf_c[:, 0:w], AF.Copy),
                 reads=("yf_c",), writes=(f"pAh{m}",))
            p.op("DVE", lambda m=m: V.tensor_sub(
                pA_l[:, m * AWM:m * AWM + w], yf_c[:, 0:w],
                pA_h[:, m * AWM:m * AWM + w]),
                 reads=("yf_c", f"pAh{m}"), writes=(f"pAl{m}",))

        for m in range(8):
            od = ps_a[0:128, (m % 4) * 512:(m % 4) * 512 + w]
            for k in range(8):
                sh = Wq["Wout"][0][:, k * H + m * 128:k * H + m * 128 + 128]
                sl = Wq["Wout"][1][:, k * H + m * 128:k * H + m * 128 + 128]
                mh = pA_h[:, k * AWM:k * AWM + w]
                ml = pA_l[:, k * AWM:k * AWM + w]
                mm(tg + f"z{m}", od, sh, mh, reads=(f"Wouthi{k}", f"pAh{k}"),
                   wr=f"psA{m % 4}", start=(k == 0), stop=False)
                mm(tg + f"z{m}", od, sh, ml, reads=(f"Wouthi{k}", f"pAl{k}"),
                   wr=f"psA{m % 4}", start=False, stop=False)
                mm(tg + f"z{m}", od, sl, mh, reads=(f"Woutlo{k}", f"pAh{k}"),
                   wr=f"psA{m % 4}", start=False, stop=(k == 7))
            p.op("DVE", lambda o=zS[:, m * AWM:m * AWM + w], i=od,
                 x=xp_f[:, m * AWM:m * AWM + w], bc=s_boutc[:, m:m + 1], mm_=m:
                 V.scalar_tensor_tensor(o, i, bc, x, OP.add, OP.add),
                 reads=(f"psA{m % 4}", f"xpf{m}", "s_boutc"), writes=(ZR,))
            lpump(1)
        if debug and gi == 0:
            for m in range(8):
                store(dbg["d_z"][:, m * w:(m + 1) * w],
                      zS[:, m * AWM:m * AWM + w], (ZR,), "d_z")

    def attn_ladder(gi):
        g0, g1 = AGROUPS[gi]
        w = g1 - g0
        tg = f"A{gi}"
        zS = zS2[gi % 2]
        ZR = f"zS{gi % 2}"
        nsub = (w + 127) // 128
        for si in range(nsub):
            u0 = g0 + si * 128
            u1 = min(g1, u0 + 128)
            uw = u1 - u0
            ti = u0 // 128
            sg = f"{tg}s{si}"
            to = (si % 2) * 1024
            for m_ in range(8):
                p.op("PE", lambda o=ps_tr[0:uw, to + m_ * 128:to + (m_ + 1) * 128],
                     i=zS[:, m_ * AWM + si * 128:m_ * AWM + si * 128 + uw]:
                     PE.transpose(o, i, s_idf[:]),
                     reads=(ZR, "s_idf"), writes=(f"psTR{si % 2}",),
                     group=sg + "tr")
            p.op("DVE", lambda e=uw, to=to: V.tensor_scalar(
                z_tm[0:e, :], ps_tr[0:e, to:to + 1024], 1.0, None, OP.mult,
                OP.add, accum_out=colz[0:e, 0:1]),
                 reads=(f"psTR{si % 2}",), writes=("z_tm", "colz"))
            p.op("ACT", lambda e=uw: A.activation(zsq[0:e, :], z_tm[0:e, :],
                                                  AF.Square,
                                                  accum_out=colz[0:e, 1:2]),
                 reads=("z_tm",), writes=("zsq", "colz"))
            p.op("DVE", lambda e=uw: V.tensor_scalar(colz[0:e, 2:3],
                                                     colz[0:e, 0:1], 1.0 / H,
                                                     None, OP.mult),
                 reads=("colz",), writes=("colz",))
            p.op("DVE", lambda e=uw: V.tensor_mul(colz[0:e, 3:4], colz[0:e, 2:3],
                                                  colz[0:e, 2:3]),
                 reads=("colz",), writes=("colz",))
            p.op("DVE", lambda e=uw: V.scalar_tensor_tensor(
                colz[0:e, 4:5], colz[0:e, 1:2], 1.0 / H, colz[0:e, 3:4],
                OP.mult, OP.subtract),
                 reads=("colz",), writes=("colz",))
            p.op("DVE", lambda e=uw: V.tensor_scalar(colz[0:e, 4:5],
                                                     colz[0:e, 4:5], EPS, None,
                                                     OP.add),
                 reads=("colz",), writes=("colz",))
            p.op("ACT", lambda e=uw: A.activation(colz[0:e, 5:6], colz[0:e, 4:5],
                                                  AF.Sqrt),
                 reads=("colz",), writes=("colz",))
            p.op("DVE", lambda e=uw: V.reciprocal(colz[0:e, 6:7], colz[0:e, 5:6]),
                 reads=("colz",), writes=("colz",))
            yield
            p.op("DVE", lambda e=uw: V.tensor_scalar(sc_t[0:e, :], z_tm[0:e, :],
                                                     colz[0:e, 2:3],
                                                     colz[0:e, 6:7],
                                                     OP.subtract, OP.mult),
                 reads=("z_tm", "colz"), writes=("sc_t",))
            load(xtm[0:uw, :], x_tm[u0:u1, :], "xtm", q="ACT")
            p.op("GP", lambda e=uw: GP.tensor_mul(sc_t[0:e, :], sc_t[0:e, :],
                                                  s_gamb[0:e, :]),
                 reads=("sc_t", "s_gamb"), writes=("sc_t",))
            p.op("GP", lambda e=uw: GP.tensor_add(xtm[0:e, :], xtm[0:e, :],
                                                  s_betb[0:e, :]),
                 reads=("xtm", "s_betb"), writes=("xtm",))
            p.op("DVE", lambda e=uw: V.tensor_add(s_s[0:e, :], sc_t[0:e, :],
                                                  xtm[0:e, :]),
                 reads=("sc_t", "xtm"), writes=("z_tm",))
            yield
            p.op("DVE", lambda e=uw: V.tensor_reduce(colz[0:e, 7:8], s_s[0:e, :],
                                                     AX, OP.add),
                 reads=("z_tm",), writes=("colz",))
            p.op("ACT", lambda e=uw: A.activation(zsq[0:e, :], s_s[0:e, :],
                                                  AF.Square,
                                                  accum_out=colz[0:e, 8:9]),
                 reads=("z_tm",), writes=("zsq", "colz"))
            p.op("DVE", lambda e=uw: V.tensor_scalar(colz[0:e, 9:10],
                                                     colz[0:e, 7:8], 1.0 / H,
                                                     None, OP.mult),
                 reads=("colz",), writes=("colz",))
            p.op("DVE", lambda e=uw: V.tensor_mul(colz[0:e, 10:11],
                                                  colz[0:e, 9:10],
                                                  colz[0:e, 9:10]),
                 reads=("colz",), writes=("colz",))
            p.op("DVE", lambda e=uw: V.scalar_tensor_tensor(
                colz[0:e, 10:11], colz[0:e, 8:9], 1.0 / H, colz[0:e, 10:11],
                OP.mult, OP.subtract),
                 reads=("colz",), writes=("colz",))
            p.op("DVE", lambda e=uw: V.tensor_scalar(colz[0:e, 10:11],
                                                     colz[0:e, 10:11], EPS,
                                                     None, OP.add),
                 reads=("colz",), writes=("colz",))
            p.op("ACT", lambda e=uw: A.activation(colz[0:e, 11:12],
                                                  colz[0:e, 10:11], AF.Sqrt),
                 reads=("colz",), writes=("colz",))
            p.op("DVE", lambda e=uw: V.reciprocal(colz[0:e, 12:13],
                                                  colz[0:e, 11:12]),
                 reads=("colz",), writes=("colz",))
            yield
            p.op("DVE", lambda e=uw: V.tensor_scalar(sc_t[0:e, :], s_s[0:e, :],
                                                     colz[0:e, 9:10], None,
                                                     OP.subtract),
                 reads=("z_tm", "colz"), writes=("sc_t",))
            p.op("DVE", lambda e=uw: V.tensor_reduce(colz[0:e, 13:14],
                                                     sc_t[0:e, :], AX, OP.max,
                                                     apply_absolute_value=True),
                 reads=("sc_t",), writes=("colz",))
            p.op("DVE", lambda e=uw: V.tensor_mul(colz[0:e, 13:14],
                                                  colz[0:e, 13:14],
                                                  colz[0:e, 12:13]),
                 reads=("colz",), writes=("colz",))
            p.op("DVE", lambda e=uw: V.tensor_scalar_max(colz[0:e, 13:14],
                                                         colz[0:e, 13:14], 1e-5),
                 reads=("colz",), writes=("colz",))
            p.op("DVE", lambda e=uw: V.tensor_scalar(colz[0:e, 14:15],
                                                     colz[0:e, 13:14],
                                                     1.0 / 127.0, None,
                                                     OP.mult),
                 reads=("colz",), writes=("colz",))
            p.op("DVE", lambda e=uw: V.reciprocal(colz[0:e, 15:16],
                                                  colz[0:e, 13:14]),
                 reads=("colz",), writes=("colz",))
            p.op("DVE", lambda e=uw: V.tensor_mul(colz[0:e, 15:16],
                                                  colz[0:e, 15:16],
                                                  colz[0:e, 12:13]),
                 reads=("colz",), writes=("colz",))
            p.op("DVE", lambda e=uw, t=ti: V.tensor_scalar(
                colz[0:e, 15:16], colz[0:e, 15:16], 127.0,
                s_qmask[0:e, t:t + 1], OP.mult, OP.mult),
                 reads=("colz", "s_qmask"), writes=("colz",))
            p.op("DVE", lambda e=uw: V.tensor_scalar(r1t[0:e, :], sc_t[0:e, :],
                                                     colz[0:e, 12:13], None,
                                                     OP.mult),
                 reads=("sc_t", "colz"), writes=("z_tm",))
            store(r1_hbm[u0:u1, :], r1t[0:uw, :], ("z_tm",), f"r1_{ti}")
            if debug:
                store(dbg["d_r1"][u0:u1, :], r1t[0:uw, :], ("z_tm",), "d_r1x")
            yield
            p.op("DVE", lambda e=uw: V.tensor_scalar(q_i[0:e, :], sc_t[0:e, :],
                                                     colz[0:e, 15:16], None,
                                                     OP.mult),
                 reads=("sc_t", "colz"), writes=("q_i",))
            if SAFE_ROUND:
                p.op("DVE", lambda e=uw: V.tensor_scalar(zsq[0:e, :],
                                                         sc_t[0:e, :],
                                                         colz[0:e, 15:16],
                                                         None, OP.mult),
                     reads=("sc_t", "colz"), writes=("zsq",))
                p.op("DVE", lambda e=uw: V.tensor_sub(rr_t[0:e, :], zsq[0:e, :],
                                                      q_i[0:e, :]),
                     reads=("zsq", "q_i"), writes=("rr_t",))
                p.op("DVE", lambda e=uw: V.tensor_scalar(cp_t[0:e, :],
                                                         rr_t[0:e, :], 0.5,
                                                         None, OP.is_gt),
                     reads=("rr_t",), writes=("cp_t",))
                p.op("DVE", lambda e=uw: V.scalar_tensor_tensor(
                    rr_t[0:e, :], rr_t[0:e, :], -0.5, cp_t[0:e, :], OP.is_lt,
                    OP.subtract),
                     reads=("rr_t", "cp_t"), writes=("rr_t",))
                p.op("DVE", lambda e=uw: V.tensor_sub(q_b[0:e, :], q_i[0:e, :],
                                                      rr_t[0:e, :]),
                     reads=("q_i", "rr_t"), writes=("q_b",))
            else:
                p.op("ACT", lambda e=uw: A.activation(q_b[0:e, :], q_i[0:e, :],
                                                      AF.Copy),
                     reads=("q_i",), writes=("q_b",))
            store(r1q_hbm[u0:u1, :], q_b[0:uw, :], ("q_b",), f"r1q_{ti}")
            p.op("DVE", lambda e=uw, t=ti: V.tensor_copy(dqcols[0:e, t:t + 1],
                                                         colz[0:e, 14:15]),
                 reads=("colz",), writes=("dqcols",))
            yield

    for gi in range(len(AGROUPS)):
        attn_mm(gi)
        lpump(10 ** 9)
        lgen[0] = attn_ladder(gi)
    lpump(10 ** 9)

    p.op("DVE", lambda: V.memset(q_b[:], 0.0), reads=(), writes=("q_b",))
    store(r1q_hbm[EXT:EXTP, :], q_b[0:EXTP - EXT, :], ("q_b",), "r1qpad")

    stB.close()
    stAB.close()
    p.barrier()

    stC = contextlib.ExitStack()
    w1q = SB("w1q", [128, 8 * DH], BF16, stack=stC)
    w2q = SB("w2q", [128, 32 * H], BF16, stack=stC)
    for c in range(8):
        load(w1q[:, c * DH:(c + 1) * DH], w1q_hbm[c * 128:(c + 1) * 128, :],
             f"w1q{c}", reads=(f"qw1{c}_0", f"qw1{c}_2048"))
    def load_w2q():
        for c in range(32):
            load(w2q[:, c * H:(c + 1) * H], w2q_hbm[c * 128:(c + 1) * 128, :],
                 f"w2q{c}", reads=(f"qw2{c}_0",))

    hbuf = SB("hbuf", [128, 32 * (GW + 2)], stack=stC)
    RQW = 512
    r1qg = SB("r1qg", [128, 8 * RQW], BF16, stack=stC)
    s2 = PSUM("ps_s2", [128, H], stack=stC)
    o_t = SB("o_t", [128, H], stack=stC)
    s2q = o_t
    r1g = o_t
    dqs = SB("dqs", [128, GW + 2], stack=stC)
    s2xs = SB("s2xs", [128, GW], stack=stC)
    pmax = SB("pmax", [128, GW], stack=stC)
    colfs = (SB("colfA", [128, 10], stack=stC),
             SB("colfB", [128, 10], stack=stC))
    rowf = SB("rowf", [1, GW + 2], stack=stC)
    if SAFE_ROUND:
        _sa = SB("sc_a0", [128, GW], stack=stC)
        _sd = SB("sc_d0", [128, GW], stack=stC)
        sc_as, sc_ds = (_sa, _sa), (_sd, _sd)

        def satag(m):
            return "sca0"

        def sdtag(m):
            return "scd0"
    else:
        sc_as = (SB("sc_a0", [128, GW], stack=stC),
                 SB("sc_a1", [128, GW], stack=stC))
        sc_ds = (SB("sc_d0", [128, GW], stack=stC),
                 SB("sc_d1", [128, GW], stack=stC))

        def satag(m):
            return f"sca{m % 2}"

        def sdtag(m):
            return f"scd{m % 2}"
    hq_i = SB("hq_i", [128, 32 * GW], I8, stack=stC)
    hq_b = SB("hq_b", [128, 32 * GW], BF16, stack=stC)
    if SAFE_ROUND:
        hrr = SB("hrr", [128, GW], stack=stC)
        hcp = SB("hcp", [128, GW], stack=stC)
    ps_m1 = PSUM("ps_m1", [128, 1536], stack=stC)
    ps_bc = PSUM("ps_bc", [128, 512], stack=stC)
    ps_o = PSUM("ps_o", [128, 1024], stack=stC)

    HB = "hb"
    HQ = "hq_b"

    def ffn_pars(gi):
        f0, f1 = FG[gi]
        fw = f1 - f0
        a0 = max(0, f0 - 1)
        a1 = min(EXT, f1 + 1)
        return f0, f1, fw, a0, a1, a1 - a0

    def ffn_conv_a(gi, m_):
        f0, f1, fw, a0, a1, aw = ffn_pars(gi)
        c_lo = max(f0, 1)
        c_hi = min(f1, EXT - 1)
        ce = c_hi - c_lo
        cs = c_lo - a0
        hm = hbuf[:, m_ * (GW + 2):m_ * (GW + 2) + aw]
        p.op("ACT", lambda h=hm, e=ce, c0=cs, mm_=m_:
             A.activation(sc_as[mm_ % 2][:, 0:e], h[:, c0:c0 + e],
                          AF.Identity, bias=s_convb[:, mm_:mm_ + 1],
                          scale=s_convw[:, 32 + mm_:33 + mm_]),
             reads=(HB + str(m_), "s_convw", "s_convb"),
             writes=(satag(m_),))
        p.op("ACT", lambda h=hm, e=ce, c0=cs, mm_=m_:
             A.activation(sc_ds[mm_ % 2][:, 0:e], h[:, c0 + 1:c0 + 1 + e],
                          AF.Identity, scale=s_convw[:, 64 + mm_:65 + mm_]),
             reads=(HB + str(m_), "s_convw"), writes=(sdtag(m_),))

    def ffn_conv_cd(gi, m_):
        f0, f1, fw, a0, a1, aw = ffn_pars(gi)
        c_lo = max(f0, 1)
        c_hi = min(f1, EXT - 1)
        ce = c_hi - c_lo
        cs = c_lo - a0
        hm = hbuf[:, m_ * (GW + 2):m_ * (GW + 2) + aw]
        p.op("DVE", lambda h=hm, e=ce, c0=cs, mm_=m_:
             V.scalar_tensor_tensor(sc_as[mm_ % 2][:, 0:e],
                                    h[:, c0 - 1:c0 - 1 + e],
                                    s_convw[:, mm_:mm_ + 1],
                                    sc_as[mm_ % 2][:, 0:e],
                                    OP.mult, OP.add),
             reads=(HB + str(m_), "s_convw", satag(m_)),
             writes=(satag(m_),))
        p.op("DVE", lambda h=hm, e=ce, c0=cs, mm_=m_:
             V.tensor_add(sc_ds[mm_ % 2][:, 0:e], sc_ds[mm_ % 2][:, 0:e],
                          sc_as[mm_ % 2][:, 0:e]),
             reads=(satag(m_), sdtag(m_)), writes=(sdtag(m_),))
        p.op("ACT", lambda h=hm, e=ce, c0=cs, mm_=m_:
             A.activation(h[:, c0:c0 + e], sc_ds[mm_ % 2][:, 0:e], AF.Silu),
             reads=(sdtag(m_),), writes=(HB + str(m_),))

    def ffn_conv_chunk(gi, m_):
        ffn_conv_a(gi, m_)
        ffn_conv_cd(gi, m_)

    def ffn_quant_chunk(gi, m_):
        colf = colfs[gi % 2]
        f0, f1, fw, a0, a1, aw = ffn_pars(gi)
        co = f0 - a0
        hm = hbuf[:, m_ * (GW + 2) + co:m_ * (GW + 2) + co + fw]
        if SAFE_ROUND:
            p.op("DVE", lambda h=hm, e=fw, mm_=m_:
                 V.scalar_tensor_tensor(sc_as[mm_ % 2][:, 0:e], h, 1.0,
                                        s2xs[:, 0:e], OP.mult, OP.mult),
                 reads=(HB + str(m_), "s2xs"), writes=(satag(m_),))
            p.op("DVE", lambda e=fw, mm_=m_: V.tensor_copy(
                hq_i[:, mm_ * GW:mm_ * GW + e], sc_as[mm_ % 2][:, 0:e]),
                 reads=(satag(m_),), writes=("hq_i",))
            p.op("DVE", lambda e=fw, mm_=m_: V.tensor_sub(
                hrr[:, 0:e], sc_as[mm_ % 2][:, 0:e],
                hq_i[:, mm_ * GW:mm_ * GW + e]),
                 reads=(satag(m_), "hq_i"), writes=("hrr",))
            p.op("DVE", lambda e=fw: V.tensor_scalar(hcp[:, 0:e], hrr[:, 0:e],
                                                     0.5, None, OP.is_gt),
                 reads=("hrr",), writes=("hcp",))
            p.op("DVE", lambda e=fw: V.scalar_tensor_tensor(
                hrr[:, 0:e], hrr[:, 0:e], -0.5, hcp[:, 0:e], OP.is_lt,
                OP.subtract),
                 reads=("hrr", "hcp"), writes=("hrr",))
            p.op("DVE", lambda e=fw, mm_=m_: V.tensor_sub(
                hq_b[:, mm_ * GW:mm_ * GW + e],
                hq_i[:, mm_ * GW:mm_ * GW + e], hrr[:, 0:e]),
                 reads=("hq_i", "hrr"), writes=(HQ,))
        else:
            p.op("DVE", lambda h=hm, e=fw, mm_=m_:
                 V.scalar_tensor_tensor(hq_i[:, mm_ * GW:mm_ * GW + e], h,
                                        1.0, s2xs[:, 0:e], OP.mult, OP.mult),
                 reads=(HB + str(m_), "s2xs"), writes=("hq_i",))

    def ffn_hq_cast(gi):
        if not SAFE_ROUND:
            p.op("ACT", lambda: A.activation(hq_b[:, :], hq_i[:, :], AF.Copy),
                 reads=("hq_i",), writes=(HQ,))

    def ffn_A1(gi, qprev, m2g=None):
        real = gi < len(FG)
        colf = colfs[gi % 2]
        CF = f"colf{gi % 2}"
        f0, f1, fw, a0, a1, aw = ffn_pars(min(gi, len(FG) - 1))
        tg = f"F{gi}"
        al0 = (a0 // 128) * 128
        al1 = min(EXTP, ((a1 + 127) // 128) * 128)
        alw = al1 - al0
        off = a0 - al0
        r1q_reads = tuple(f"r1q_{t}" for t in range(al0 // 128,
                                                    min(NT, (al1 + 127) // 128)))
        if al1 > EXT:
            r1q_reads = r1q_reads + ("r1qpad",)
        for c in range(8 if real else 0):
            p.op("SP", lambda d=r1qg[:, c * RQW:c * RQW + alw],
                 s=r1q_hbm[al0:al1, c * 128:(c + 1) * 128]:
                 SY.dma_start(out=d, in_=s, transpose=True),
                 reads=r1q_reads, writes=(f"r1qg{c}",), dma=True)
        for t in range(al0 // 128, min(NT, (a1 + 127) // 128) if real else 0):
            tl0, tl1 = TILES[t]
            uw_t = tl1 - tl0
            p.op("PE", lambda t=t, o0=tl0 - al0, e_=uw_t: PE.transpose(
                ps_bc[0:1, o0:o0 + e_], dqcols[0:e_, t:t + 1],
                s_idf[0:e_, 0:e_]),
                 reads=("dqcols", "s_idf"), writes=("psBC",), group=tg + "dqt")
        if real:
            p.op("DVE", lambda o0=a0 - al0: V.tensor_scalar(
                rowf[:, 0:aw], ps_bc[0:1, o0:o0 + aw], s_w1mean[0:1, :], None,
                OP.mult),
                 reads=("psBC", "s_w1mean"), writes=("rowf",))
            p.op("PE", lambda: PE.matmul(ps_bc[0:128, 0:aw], s_ones1[:],
                                         rowf[:, 0:aw], start=True, stop=True),
                 reads=("s_ones1", "rowf"), writes=("psBC",), group=tg + "bc")
            p.op("DVE", lambda: V.tensor_copy(dqs[:, 0:aw], ps_bc[0:128, 0:aw]),
                 reads=("psBC",), writes=("dqs",))
        inline_conv = not (debug and gi == 1)
        m2q = []
        m2_nsub = 0
        if m2g is not None:
            f0m, f1m = FG[m2g]
            m2_nsub = (f1m - f0m + 127) // 128
            for si in range(m2_nsub):
                m2q.extend(ffn_m2_mm_list(m2g, si))
            ffn_r1g_load(m2g, 0)
        m2i = 0
        if SAFE_ROUND:
            m2pump = (len(m2q) + 31) // 32 if m2q else 0
            for m_ in range(33):
                if m_ < 32:
                    if qprev is not None:
                        ffn_quant_chunk(qprev, m_)
                    for _ in range(m2pump):
                        if m2i < len(m2q):
                            if m2i == 64 and m2_nsub > 1:
                                for _x in ffn_m2_tail(m2g, 0):
                                    pass
                                ffn_r1g_load(m2g, 1)
                            m2q[m2i]()
                            m2i += 1
                    od = ps_m1[0:128, (m_ % 3) * 512:(m_ % 3) * 512 + aw]
                    for k in range(8):
                        mm(tg + f"m{m_}", od,
                           w1q[:, k * DH + m_ * 128:k * DH + m_ * 128 + 128],
                           r1qg[:, k * RQW + off:k * RQW + off + aw],
                           reads=(f"w1q{k}", f"r1qg{k}"),
                           wr=f"psM{m_ % 3}", start=(k == 0), stop=(k == 7))
                    p.op("DVE",
                         lambda o=hbuf[:, m_ * (GW + 2):m_ * (GW + 2) + aw],
                         i=od, d=dqs[:, 0:aw], mm_=m_:
                         V.scalar_tensor_tensor(o, i, 1.0, d, OP.mult, OP.mult),
                         reads=(f"psM{m_ % 3}", "dqs"), writes=(HB + str(m_),))
                    if inline_conv:
                        ffn_conv_a(gi, m_)
                        ffn_conv_cd(gi, m_)
            if m2g is not None:
                for _x in ffn_m2_tail(m2g, m2_nsub - 1):
                    pass
            if qprev is not None:
                ffn_hq_cast(qprev)
            if not inline_conv:
                store(dbg["d_h"][:], hbuf[:],
                      tuple(HB + str(i) for i in range(32)), "d_hx")
                for m_ in range(32):
                    ffn_conv_chunk(gi, m_)
            return

        a2hold = [ffn_A2(qprev) if qprev is not None else None]
        tails = []
        on_tail_done = []

        def pump_tails(n):
            for tgen in tails[:]:
                for _ in range(n):
                    try:
                        next(tgen)
                    except StopIteration:
                        tails.remove(tgen)
                        if on_tail_done:
                            on_tail_done.pop(0)()
                        break
        PRE = 9
        NIT = 33 + PRE
        m2pump = (len(m2q) + NIT - 2) // (NIT - 1) if m2q else 0
        for it in range(NIT):
            m_ = it - PRE
            if a2hold[0] is not None:
                for _ in range(4):
                    try:
                        next(a2hold[0])
                    except StopIteration:
                        a2hold[0] = None
                        break
            q_m = it - PRE + 1
            if qprev is not None and 0 <= q_m < 32:
                ffn_quant_chunk(qprev, q_m)
            for _ in range(m2pump):
                if m2i < len(m2q):
                    if m2i == 64 and m2_nsub > 1:
                        t0 = ffn_m2_tail(m2g, 0)
                        try:
                            next(t0)
                            tails.append(t0)
                            on_tail_done.append(
                                lambda: ffn_r1g_load(m2g, 1))
                        except StopIteration:
                            ffn_r1g_load(m2g, 1)
                    m2q[m2i]()
                    m2i += 1
                    if m2i == len(m2q):
                        t1 = ffn_m2_tail(m2g, m2_nsub - 1)
                        try:
                            next(t1)
                            tails.append(t1)
                        except StopIteration:
                            pass
            pump_tails(2)
            if real and 0 <= m_ < 32:
                od = ps_m1[0:128, (m_ % 3) * 512:(m_ % 3) * 512 + aw]
                for k in range(8):
                    mm(tg + f"m{m_}", od,
                       w1q[:, k * DH + m_ * 128:k * DH + m_ * 128 + 128],
                       r1qg[:, k * RQW + off:k * RQW + off + aw],
                       reads=(f"w1q{k}", f"r1qg{k}"),
                       wr=f"psM{m_ % 3}", start=(k == 0), stop=(k == 7))
                p.op("DVE", lambda o=hbuf[:, m_ * (GW + 2):m_ * (GW + 2) + aw],
                     i=od, d=dqs[:, 0:aw], mm_=m_:
                     V.scalar_tensor_tensor(o, i, 1.0, d, OP.mult, OP.mult),
                     reads=(f"psM{m_ % 3}", "dqs"), writes=(HB + str(m_),))
                if inline_conv:
                    ffn_conv_a(gi, m_)
            if real and inline_conv and m_ >= 1:
                ffn_conv_cd(gi, m_ - 1)
        while a2hold[0] is not None:
            try:
                next(a2hold[0])
            except StopIteration:
                a2hold[0] = None
        pump_tails(10 ** 9)
        if qprev is not None:
            ffn_hq_cast(qprev)
        if not inline_conv:
            store(dbg["d_h"][:], hbuf[:],
                  tuple(HB + str(i) for i in range(32)), "d_hx")
            for m_ in range(32):
                ffn_conv_chunk(gi, m_)

    def ffn_A2(gi):
        colf = colfs[gi % 2]
        CF = f"colf{gi % 2}"
        f0, f1, fw, a0, a1, aw = ffn_pars(gi)
        tg = f"F{gi}"
        co = f0 - a0
        p.op("DVE", lambda e=fw, c0=co: V.tensor_reduce(
            pmax[:, 0:e].rearrange("p (t o) -> p t o", o=1),
            hbuf[:, :].rearrange("p (c t) -> p t c", c=32)[:, c0:c0 + e, :],
            AX, OP.max, apply_absolute_value=True),
             reads=tuple(HB + str(i) for i in range(32)), writes=("pmax",))
        yield
        nsub = (fw + 127) // 128
        for si in range(nsub):
            u0 = f0 + si * 128
            u1 = min(f1, u0 + 128)
            uw = u1 - u0
            sg = tg + f"s{si}"
            p.op("PE", lambda e=uw, s0=si * 128: PE.transpose(
                ps_bc[0:e, 0:128], pmax[:, s0:s0 + e], s_idf[:]),
                 reads=("pmax", "s_idf"), writes=("psBC",), group=sg + "tp")
            yield
            p.op("DVE", lambda e=uw: V.tensor_reduce(colf[0:e, 0:1],
                                                     ps_bc[0:e, 0:128], AX,
                                                     OP.max),
                 reads=("psBC",), writes=(CF,))
            yield
            p.op("DVE", lambda e=uw: V.tensor_scalar_max(colf[0:e, 0:1],
                                                         colf[0:e, 0:1], 1e-5),
                 reads=(CF,), writes=(CF,))
            yield
            p.op("DVE", lambda e=uw: V.reciprocal(colf[0:e, 1:2], colf[0:e, 0:1]),
                 reads=(CF,), writes=(CF,))
            yield
            p.op("DVE", lambda e=uw: V.tensor_scalar(colf[0:e, 2:3],
                                                     colf[0:e, 1:2], 127.0,
                                                     None, OP.mult),
                 reads=(CF,), writes=(CF,))
            yield
            p.op("DVE", lambda e=uw, si=si: V.tensor_scalar(
                colf[0:e, 8 + si:9 + si], colf[0:e, 0:1], s_w2mean[0:e, :],
                1.0 / 127.0, OP.mult, OP.mult),
                 reads=(CF, "s_w2mean"), writes=(CF,))
            yield
            p.op("PE", lambda e=uw: PE.transpose(ps_bc[0:1, 0:e],
                                                 colf[0:e, 2:3],
                                                 s_idf[0:e, 0:e]),
                 reads=(CF, "s_idf"), writes=("psBC",), group=sg + "t2")
            yield
            p.op("DVE", lambda e=uw, s0=si * 128: V.tensor_copy(
                rowf[:, s0:s0 + e], ps_bc[0:1, 0:e]),
                 reads=("psBC",), writes=("rowf",))
            yield
        p.op("PE", lambda e=fw: PE.matmul(ps_bc[0:128, 0:e], s_ones1[:],
                                          rowf[:, 0:e], start=True, stop=True),
             reads=("s_ones1", "rowf"), writes=("psBC",), group=tg + "b2")
        yield
        p.op("DVE", lambda e=fw: V.tensor_copy(s2xs[:, 0:e], ps_bc[0:128, 0:e]),
             reads=("psBC",), writes=("s2xs",))
        yield

    def ffn_m2_sub(gi, si):
        f0, f1 = FG[gi]
        u0 = f0 + si * 128
        u1 = min(f1, u0 + 128)
        uw = u1 - u0
        out0 = max(2, u0)
        out1 = min(EXT - 2, u1)
        return u0, u1, uw, out0, out1

    def ffn_m2_mm_list(gi, si):
        tg = f"F{gi}"
        u0, u1, uw, out0, out1 = ffn_m2_sub(gi, si)
        if out1 <= out0:
            return []
        sg = tg + f"S{si}"
        ems = []
        for nh in range(2):
            od = ps_o[0:uw, nh * 512:(nh + 1) * 512]
            for c in range(32):
                sta = hq_b[:, c * GW + si * 128:c * GW + si * 128 + uw]
                mv = w2q[:, c * H + nh * 512:c * H + nh * 512 + 512]
                ems.append(lambda od=od, sta=sta, mv=mv, c=c, nh=nh:
                           mm(sg + f"o{nh}", od, sta, mv,
                              reads=(HQ, f"w2q{c}"), wr=f"psO{nh}",
                              start=(c == 0), stop=(c == 31)))
        return ems

    def ffn_r1g_load(gi, si):
        u0, u1, uw, out0, out1 = ffn_m2_sub(gi, si)
        if out1 <= out0:
            return
        load(r1g[0:uw, :], r1_hbm[u0:u1, :], "o_t", q="ACT",
             reads=(f"r1_{u0 // 128}",))

    def ffn_m2_tail(gi, si):
        colf = colfs[gi % 2]
        CF = f"colf{gi % 2}"
        u0, u1, uw, out0, out1 = ffn_m2_sub(gi, si)
        if out1 <= out0:
            return
        if True:
            p.op("DVE", lambda e=uw, si=si: V.scalar_tensor_tensor(
                s2[0:e, :], ps_o[0:e, :], colf[0:e, 8 + si:9 + si], r1g[0:e, :],
                OP.mult, OP.add, accum_out=colf[0:e, 4:5]),
                 reads=("psO0", "psO1", CF, "o_t"), writes=("s2", CF))
            yield
            p.op("ACT", lambda e=uw: A.activation(s2q[0:e, :], s2[0:e, :],
                                                  AF.Square,
                                                  accum_out=colf[0:e, 5:6]),
                 reads=("s2",), writes=("o_t", CF))
            yield
            p.op("DVE", lambda e=uw: V.tensor_scalar(colf[0:e, 4:5],
                                                     colf[0:e, 4:5], 1.0 / H,
                                                     None, OP.mult),
                 reads=(CF,), writes=(CF,))
            yield
            p.op("DVE", lambda e=uw: V.tensor_mul(colf[0:e, 6:7], colf[0:e, 4:5],
                                                  colf[0:e, 4:5]),
                 reads=(CF,), writes=(CF,))
            yield
            p.op("DVE", lambda e=uw: V.scalar_tensor_tensor(
                colf[0:e, 6:7], colf[0:e, 5:6], 1.0 / H, colf[0:e, 6:7],
                OP.mult, OP.subtract),
                 reads=(CF,), writes=(CF,))
            yield
            p.op("DVE", lambda e=uw: V.tensor_scalar(colf[0:e, 6:7],
                                                     colf[0:e, 6:7], EPS, None,
                                                     OP.add),
                 reads=(CF,), writes=(CF,))
            yield
            p.op("ACT", lambda e=uw: A.activation(colf[0:e, 7:8], colf[0:e, 6:7],
                                                  AF.Sqrt),
                 reads=(CF,), writes=(CF,))
            yield
            p.op("DVE", lambda e=uw: V.reciprocal(colf[0:e, 7:8], colf[0:e, 7:8]),
                 reads=(CF,), writes=(CF,))
            yield
            p.op("DVE", lambda e=uw: V.tensor_scalar(o_t[0:e, :], s2[0:e, :],
                                                     colf[0:e, 4:5],
                                                     colf[0:e, 7:8],
                                                     OP.subtract, OP.mult),
                 reads=("s2", CF), writes=("o_t",))
            yield
            store(out_tm[out0 - 2:out1 - 2, :], o_t[out0 - u0:out1 - u0, :],
                  ("o_t",), f"out{u0}")

    def ffn_m2_out(gi):
        f0, f1 = FG[gi]
        nsub = (f1 - f0 + 127) // 128
        for si in range(nsub):
            for em in ffn_m2_mm_list(gi, si):
                em()
            ffn_r1g_load(gi, si)
            for _ in ffn_m2_tail(gi, si):
                pass

    NG = len(FG)
    load_w2q_once = [load_w2q]
    NLOOP = NG if SAFE_ROUND else NG + 2
    for gi in range(NLOOP):
        qprev = gi - 1 if (gi >= 1 and gi - 1 < NG) else None
        m2g = gi - 2 if (gi >= 2 and not SAFE_ROUND) else None
        ffn_A1(gi, qprev, m2g)
        if load_w2q_once:
            load_w2q_once.pop()()
        if SAFE_ROUND:
            if gi >= 1:
                ffn_m2_out(gi - 1)
            for _ in ffn_A2(gi):
                pass
    if SAFE_ROUND:
        for m_ in range(32):
            ffn_quant_chunk(NG - 1, m_)
        ffn_hq_cast(NG - 1)
        ffn_m2_out(NG - 1)

    stC.close()
    root.close()
    n = p.finalize_and_emit()
    return nc, n, dbg


_NC_CACHE = {}


def _get_nc(debug=False):
    key = (debug, SAFE_ROUND)
    if key not in _NC_CACHE:
        _NC_CACHE[key] = build_nc(debug)
    return _NC_CACHE[key]


def make_in_maps(inputs, debug=False):
    f32 = np.float32
    K_EXT = EXT
    x = np.asarray(inputs["L0_attn_x"], f32)
    W_in = np.ascontiguousarray(inputs["L0_attn_W_in"], f32)
    W_B = np.ascontiguousarray(inputs["L0_attn_W_B"], f32)
    W_D = np.ascontiguousarray(inputs["L0_attn_W_D"], f32)
    W_out = np.ascontiguousarray(inputs["L0_attn_W_out"], f32)
    B_mat = np.ascontiguousarray(inputs["L0_attn_B_mat"], f32)
    C_mat = np.ascontiguousarray(inputs["L0_attn_C_mat"], f32)
    W_ig = np.ascontiguousarray(inputs["L0_attn_W_ig"], f32)
    W_fg = np.ascontiguousarray(inputs["L0_attn_W_fg"], f32)
    b_in = np.asarray(inputs["L0_attn_b_in"], f32)
    b_B = np.asarray(inputs["L0_attn_b_B"], f32)
    b_D = np.asarray(inputs["L0_attn_b_D"], f32)
    b_out = np.asarray(inputs["L0_attn_b_out"], f32)
    b_ig = np.asarray(inputs["L0_attn_b_ig"], f32)
    b_fg = np.asarray(inputs["L0_attn_b_fg"], f32)
    gamma = np.asarray(inputs["L0_attn_gamma"], f32)
    beta = np.asarray(inputs["L0_attn_beta"], f32)
    w1 = np.asarray(inputs["L0_ffn_w1"], f32)
    conv_w = np.asarray(inputs["L0_ffn_conv_w"], f32)
    conv_b = np.asarray(inputs["L0_ffn_conv_b"], f32)
    w2 = np.asarray(inputs["L0_ffn_w2"], f32)

    shared = {
        "W_in": W_in, "W_D": W_D, "W_out": W_out,
        "W_in_T": np.ascontiguousarray(W_in.T),
        "W_B_T": np.ascontiguousarray(W_B.T),
        "B_mat": B_mat, "C_mat": C_mat, "W_ig": W_ig, "W_fg": W_fg,
        "b_in_cols": np.ascontiguousarray(b_in.reshape(8, 128).T),
        "b_B_cols": np.ascontiguousarray(b_B.reshape(8, 128).T),
        "b_out_cols": np.ascontiguousarray(b_out.reshape(8, 128).T),
        "b_ig_col": np.ascontiguousarray(b_ig.reshape(S, 1)),
        "b_fg_col": np.ascontiguousarray(b_fg.reshape(S, 1)),
        "gamma_b": np.ascontiguousarray(np.broadcast_to(gamma[None, :], (128, H))),
        "beta_b": np.ascontiguousarray(np.broadcast_to(beta[None, :], (128, H))),
        "bD_row": np.ascontiguousarray(b_D.reshape(1, H)),
        "convw": np.ascontiguousarray(
            conv_w[:, 0, :].reshape(32, 128, 3).transpose(1, 2, 0).reshape(128, 96)),
        "convb": np.ascontiguousarray(conv_b.reshape(32, 128).T),
        "w1t_f": np.ascontiguousarray(w1.T),
        "w2t_f": np.ascontiguousarray(w2.T),
        "ones1": np.ones((1, 128), f32),
        "ones_col": np.ones((128, 1), f32),
        "id_f32": np.eye(128, dtype=f32),
        "onesrow_ext": np.ones((1, K_EXT), ml_dtypes.bfloat16),
        "zrow_ext": np.zeros((1, K_EXT), ml_dtypes.bfloat16),
    }
    in_maps = []
    for c in range(8):
        b, half = c // 2, c % 2
        s = half * 2048
        lo, hi = s - 2, s + 2048 + 2
        xe = np.zeros((EXT, H), f32)
        a, bb = max(0, lo), min(T, hi)
        xe[a - lo:bb - lo] = x[b, a:bb]
        qm = np.zeros((EXTP,), f32)
        qm[a - lo:bb - lo] = 1.0
        m = dict(shared)
        m["x_tm"] = xe
        m["x_fm"] = np.ascontiguousarray(xe.T)
        m["xg_fm"] = np.ascontiguousarray(x[b].T)
        m["qmask"] = np.ascontiguousarray(qm.reshape(NT, 128).T)
        m["hsel0"] = np.full((S, 1), 1.0 - half, f32)
        m["hsel1"] = np.full((S, 1), float(half), f32)
        in_maps.append(m)
    return in_maps


def kernel(**inputs):
    nc, _, _ = _get_nc(False)
    in_maps = make_in_maps(inputs)
    res = run_bass_kernel_spmd(nc, in_maps, core_ids=list(range(8)))
    out = np.empty((B, T, H), np.float32)
    for c in range(8):
        b, half = c // 2, c % 2
        out[b, half * 2048:(half + 1) * 2048] = res.results[c]["out_tm"]
    return out


# version 42 (final)
# speedup vs baseline: 1.0916x; 1.0026x over previous
import contextlib

import numpy as np
import ml_dtypes

import concourse.bass as bass
import concourse.mybir as mybir
from concourse.bass_isa import ReduceOp
from concourse.bass_utils import run_bass_kernel_spmd

F32 = mybir.dt.float32
BF16 = mybir.dt.bfloat16
I32 = mybir.dt.int32
I16 = mybir.dt.int16
I8 = mybir.dt.int8
AF = mybir.ActivationFunctionType
OP = mybir.AluOpType
AX = mybir.AxisListType.X

B, T, H, S, DH = 4, 4096, 1024, 16, 4096
EXT = T // 2 + 4
EXTP = 2176
EPS = 1e-5
SAFE_ROUND = False

NT = (EXT + 127) // 128
TILES = [(i * 128, min(EXT, (i + 1) * 128)) for i in range(NT)]
AW = 256
AGROUPS = [(i * AW, (i + 1) * AW) for i in range(7)] + [(7 * AW, EXT)]
AWM = EXT - 7 * AW
GW = 256
FG = [(a, min(a + GW, EXT)) for a in range(0, EXT, GW)]

ENGINES = ("SP", "PE", "DVE", "ACT", "GP")
_ENG_ATTR = {"SP": "sync", "PE": "tensor", "DVE": "vector", "ACT": "scalar",
             "GP": "gpsimd"}


class Prog:

    def __init__(self, nc):
        self.nc = nc
        self.ops = []

    def op(self, eng, fn, reads=(), writes=(), dma=False, group=None):
        self.ops.append(dict(eng=eng, fn=fn, reads=tuple(reads),
                             writes=tuple(writes), dma=dma, group=group,
                             inc=None, sem=None, barrier=False))

    def barrier(self):
        self.ops.append(dict(eng="SP", fn=None, reads=(), writes=(), dma=False,
                             group=None, inc=None, sem=None, barrier=True))

    def finalize_and_emit(self):
        nc, ops = self.nc, self.ops
        last_w, readers = {}, {}
        deps_l = []
        last_per_eng = {}
        last_dma_sem = {}
        dma_i = 0
        NDMA_ = 48
        last_barrier = None
        for i, o in enumerate(ops):
            deps = set()
            if o["barrier"]:
                deps.update(v for v in last_per_eng.values() if v is not None)
                deps.update(v for v in last_dma_sem.values() if v is not None)
                if last_barrier is not None:
                    deps.add(last_barrier)
                deps.discard(i)
                deps_l.append(deps)
                last_barrier = i
                last_per_eng = {}
                last_dma_sem = {}
                continue
            if last_barrier is not None:
                deps.add(last_barrier)
            if o["dma"]:
                last_dma_sem[dma_i % NDMA_] = i
                dma_i += 1
            last_per_eng[(o["eng"], o["dma"])] = i
            for r in o["reads"]:
                if r in last_w:
                    deps.add(last_w[r][0])
            for w in o["writes"]:
                if w in last_w:
                    wi, wg = last_w[w]
                    if not (o["group"] is not None and o["group"] == wg):
                        deps.add(wi)
                deps.update(readers.get(w, ()))
            deps.discard(i)
            deps_l.append(deps)
            for w in o["writes"]:
                last_w[w] = (i, o["group"])
                readers[w] = []
            for r in o["reads"]:
                readers.setdefault(r, []).append(i)

        need_inc = [False] * len(ops)
        for ds in deps_l:
            for d in ds:
                need_inc[d] = True

        es = contextlib.ExitStack()
        chain = {e: es.enter_context(nc.semaphore(f"ch_{e}")) for e in ENGINES}
        NDMA = 48
        dpool = [es.enter_context(nc.semaphore(f"dma{i}")) for i in range(NDMA)]
        cnt = {e: 0 for e in ENGINES}
        dcnt = [0] * NDMA
        dix = 0
        for i, o in enumerate(ops):
            if o["dma"]:
                j = dix % NDMA
                dix += 1
                o["pre_wait"] = (dpool[j], dcnt[j]) if dcnt[j] else None
                dcnt[j] += 16
                o["sem"] = (f"D{j}", dpool[j])
                o["inc"] = dcnt[j]
            elif need_inc[i]:
                cnt[o["eng"]] += 1
                o["sem"] = ("C" + o["eng"], chain[o["eng"]])
                o["inc"] = cnt[o["eng"]]

        know = [None] * len(ops)
        know_is = [None] * len(ops)
        eng_last = {e: None for e in ENGINES}
        waits_l = [None] * len(ops)
        for i, o in enumerate(ops):
            k = (dict(know_is[eng_last[o["eng"]]])
                 if eng_last[o["eng"]] is not None else {})
            waits = []
            for d in sorted(deps_l[i]):
                do = ops[d]
                skey, shandle = do["sem"]
                val = do["inc"]
                if k.get(skey, 0) < val:
                    waits.append((shandle, val))
                    k[skey] = val
                    for kk, vv in know[d].items():
                        if k.get(kk, 0) < vv:
                            k[kk] = vv
            if o.get("pre_wait") is not None:
                sh, val = o["pre_wait"]
                skey = o["sem"][0]
                if k.get(skey, 0) < val:
                    waits.append((sh, val))
                    k[skey] = val
            if o["sem"] is not None:
                know[i] = dict(k)
                skey = o["sem"][0]
                know[i][skey] = max(know[i].get(skey, 0), o["inc"])
            else:
                know[i] = k
            know_is[i] = k
            eng_last[o["eng"]] = i
            waits_l[i] = waits

        with nc.Block() as block:
            for eng in ENGINES:
                idxs = [i for i, o in enumerate(ops) if o["eng"] == eng]
                if not idxs and eng != "SP":
                    continue

                def body(handle, eng=eng, idxs=idxs):
                    for i in idxs:
                        o = ops[i]
                        for sh, val in waits_l[i]:
                            handle.wait_ge(sh, val)
                        instr = o["fn"]() if o["fn"] is not None else handle.nop()
                        if o["sem"] is not None:
                            instr.then_inc(o["sem"][1], 16 if o["dma"] else 1)
                    if eng == "SP":
                        for e2 in ENGINES:
                            if cnt.get(e2, 0):
                                handle.wait_ge(chain[e2], cnt[e2])
                        for j in range(NDMA):
                            if dcnt[j]:
                                handle.wait_ge(dpool[j], dcnt[j])

                getattr(block, _ENG_ATTR[eng])(body)
        self._sem_stack = es
        return len(ops)


def build_nc(debug=False):
    nc = bass.Bass("TRN2", target_bir_lowering=False, debug=False)
    p = Prog(nc)
    V, A, PE, GP, SY = nc.vector, nc.scalar, nc.tensor, nc.gpsimd, nc.sync
    QENG = {"SP": SY, "ACT": A, "GP": GP}

    def din(name, shape, dt=F32):
        return nc.dram_tensor(name, shape, dt, kind="ExternalInput").ap()

    x_tm = din("x_tm", [EXT, H])
    x_fm = din("x_fm", [H, EXT])
    xg_fm = din("xg_fm", [H, T])
    W_in = din("W_in", [H, H]); W_D = din("W_D", [H, H]); W_out = din("W_out", [H, H])
    W_in_T = din("W_in_T", [H, H]); W_B_T = din("W_B_T", [H, H])
    B_mat = din("B_mat", [H, S]); C_mat = din("C_mat", [S, H])
    W_ig = din("W_ig", [H, S]); W_fg = din("W_fg", [H, S])
    b_in_cols = din("b_in_cols", [128, 8]); b_B_cols = din("b_B_cols", [128, 8])
    b_out_cols = din("b_out_cols", [128, 8])
    b_ig_col = din("b_ig_col", [S, 1]); b_fg_col = din("b_fg_col", [S, 1])
    gamma_b = din("gamma_b", [128, H]); beta_b = din("beta_b", [128, H])
    bD_row = din("bD_row", [1, H])
    convw = din("convw", [128, 3 * 32])
    convb = din("convb", [128, 32])
    w1t_f = din("w1t_f", [H, DH])
    w2t_f = din("w2t_f", [DH, H])
    qmask = din("qmask", [128, NT])
    hsel0 = din("hsel0", [S, 1]); hsel1 = din("hsel1", [S, 1])
    ones1 = din("ones1", [1, 128])
    ones_col = din("ones_col", [128, 1])
    id_f32 = din("id_f32", [128, 128])
    onesrow_ext = din("onesrow_ext", [1, EXT], BF16)
    zrow_ext = din("zrow_ext", [1, EXT], BF16)
    out_tm = nc.dram_tensor("out_tm", [T // 2, H], F32, kind="ExternalOutput").ap()
    w1q_hbm = nc.dram_tensor("w1q_hbm", [H, DH], BF16, kind="Internal").ap()
    w2q_hbm = nc.dram_tensor("w2q_hbm", [DH, H], BF16, kind="Internal").ap()
    r1_hbm = nc.dram_tensor("r1_hbm", [EXT, H], F32, kind="Internal").ap()
    r1q_hbm = nc.dram_tensor("r1q_hbm", [EXTP, H], BF16, kind="Internal").ap()
    dbg = {}
    if debug:
        for nm, shape, dt in [("d_gates", [48, T], F32), ("d_hs", [S, EXT], F32),
                              ("d_xp", [128, 8 * 512], F32),
                              ("d_y", [128, 8 * 512], F32),
                              ("d_z", [128, 8 * 512], F32),
                              ("d_r1", [EXT, H], F32),
                              ("d_h", [128, 32 * (GW + 2)], F32),
                              ("d_wg", [128, 8 * 48], F32),
                              ("d_dq", [1, EXT], F32)]:
            dbg[nm] = nc.dram_tensor(nm, shape, dt, kind="ExternalOutput").ap()

    root = contextlib.ExitStack()

    def SB(name, shape, dt=F32, stack=None):
        return (stack or root).enter_context(nc.sbuf_tensor(name, shape, dt))

    def PSUM(name, shape, dt=F32, stack=None):
        return (stack or root).enter_context(nc.psum_tensor(name, shape, dt))

    def load(dst_ap, src_ap, region, q="SP", reads=()):
        p.op(q, lambda d=dst_ap, s=src_ap, e=QENG[q]: e.dma_start(out=d, in_=s),
             reads=reads, writes=(region,), dma=True)

    def store(dst_ap, src_ap, reads, region, q="ACT"):
        p.op(q, lambda d=dst_ap, s=src_ap, e=QENG[q]: e.dma_start(out=d, in_=s),
             reads=reads, writes=(region,), dma=True)

    def load_fm(dst2d, width, src_slice, region, q="SP", stride=None):
        st = stride if stride is not None else width
        nch = src_slice.shape[0] // 128
        d3 = dst2d[:, 0:nch * st].rearrange("p (c t) -> p c t",
                                            c=nch)[:, :, 0:width]
        s3 = src_slice.rearrange("(c p) t -> p c t", p=128)
        p.op(q, lambda d=d3, s=s3, e=QENG[q]: e.dma_start(out=d, in_=s),
             reads=(), writes=(region,), dma=True)

    def mm(group, out_ap, lhsT_ap, rhs_ap, reads, wr, start, stop):
        p.op("PE", lambda o=out_ap, l=lhsT_ap, r=rhs_ap, st=start, sp=stop:
             PE.matmul(o, l, r, start=st, stop=sp),
             reads=reads, writes=(wr,), group=group)

    stAB = contextlib.ExitStack()
    s_qmask = SB("s_qmask", [128, NT])
    s_hsel0 = SB("s_hsel0", [S, 1]); s_hsel1 = SB("s_hsel1", [S, 1])
    s_ones1 = SB("s_ones1", [1, 128])
    s_onesc = SB("s_onesc", [128, 1])
    s_idf = SB("s_idf", [128, 128])
    s_binc = SB("s_binc", [128, 8]); s_boutc = SB("s_boutc", [128, 8])
    s_w1mean = SB("s_w1mean", [128, 1]); s_w2mean = SB("s_w2mean", [128, 1])
    dqcols = SB("dqcols", [128, NT])
    s_convw = SB("s_convw", [128, 3 * 32]); s_convb = SB("s_convb", [128, 32])
    s_gamb = SB("s_gamb", [128, H], stack=stAB)
    s_betb = SB("s_betb", [128, H], stack=stAB)
    hsA_hi = SB("hsA_hi", [S + 1, EXT], BF16, stack=stAB)
    hsA_lo = SB("hsA_lo", [S + 1, EXT], BF16, stack=stAB)
    CA_hi = SB("CA_hi", [S + 1, H], BF16, stack=stAB)
    CA_lo = SB("CA_lo", [S + 1, H], BF16, stack=stAB)

    TW = 1024 if SAFE_ROUND else 2048
    NTB = 4

    def alloc_tern(tstack):
        twfs = tuple(SB(f"twf{i}", [128, TW], stack=tstack) for i in range(NTB))
        _tu = tuple(SB(f"tu{i}", [128, TW], stack=tstack) for i in range(2))
        tus = (_tu[0], _tu[1], _tu[0], _tu[1])
        _tk16 = tuple(SB(f"tk16_{i}", [128, TW], I16, stack=tstack)
                      for i in range(2))
        tk16s = (_tk16[0], _tk16[1], _tk16[0], _tk16[1])
        _tb = tuple(SB(f"tb{i}", [128, TW], BF16, stack=tstack) for i in range(2))
        tbs = (_tb[0], _tb[1], _tb[0], _tb[1])
        if SAFE_ROUND:
            tk = SB("tk", [128, TW], I32, stack=tstack)
            trr = SB("trr", [128, TW], stack=tstack)
            tcp = SB("tcp", [128, TW], stack=tstack)
        else:
            tk = trr = tcp = None
        tcol = SB("tcol", [128, 10], stack=tstack)
        return twfs, tus, tk16s, tbs, tk, trr, tcp, tcol

    for dst, src, rg in [(s_gamb, gamma_b, "s_gamb"), (s_betb, beta_b, "s_betb"),
                         (s_qmask, qmask, "s_qmask"), (s_hsel0, hsel0, "s_hsel0"),
                         (s_hsel1, hsel1, "s_hsel1"), (s_ones1, ones1, "s_ones1"), (s_onesc, ones_col, "s_onesc"),
                         (s_idf, id_f32, "s_idf"), (s_binc, b_in_cols, "s_binc"),
                         (s_boutc, b_out_cols, "s_boutc"),
                         (s_convw, convw, "s_convw"), (s_convb, convb, "s_convb")]:
        load(dst[:], src[:], rg)

    stA = contextlib.ExitStack()
    g80 = SB("g80", [80, T], stack=stA)
    hs_buf = SB("hs_buf", [S, T + 4], stack=stA)
    GM = 80
    G = SB("G", [128, 8 * GM], stack=stA)
    WGf = SB("WGf", [128, 8 * GM], stack=stA)
    WG_hi = SB("WG_hi", [128, 8 * GM], BF16, stack=stA)
    WG_lo = SB("WG_lo", [128, 8 * GM], BF16, stack=stA)
    bG = SB("bG", [GM, 1], stack=stA)
    wtrs = (SB("wtr0", [128, H], stack=stA),
            SB("wtr1", [128, H], stack=stA))
    bmt = SB("bmt", [128, 8 * S], stack=stA)
    wigt = SB("wigt", [128, 8 * S], stack=stA)
    wfgt = SB("wfgt", [128, 8 * S], stack=stA)
    bic = SB("bic", [128, 8], stack=stA)
    bbc = SB("bbc", [128, 8], stack=stA)
    ps_pre = PSUM("ps_pre", [128, 1024], stack=stA)
    ps_g = PSUM("ps_g", [GM, 512], stack=stA)

    load_fm(bmt, S, B_mat[:], "bmt")
    load(bic[:], b_in_cols[:], "bic")
    load(bbc[:], b_B_cols[:], "bbc")
    load_fm(wigt, S, W_ig[:], "wigt")
    load_fm(wfgt, S, W_fg[:], "wfgt")

    p.op("DVE", lambda: V.memset(G[:, :], 0.0), writes=("Ginit",))
    for fc in range(8):
        wtr, wr_ = wtrs[fc % 2], f"wtr{fc % 2}"
        po = (fc % 2) * 512
        load_fm(wtr, 128, W_B_T[:, fc * 128:(fc + 1) * 128], wr_)
        for j in range(8):
            mm(f"P{fc}", ps_pre[:, po:po + 16], wtr[:, j * 128:(j + 1) * 128],
               bmt[:, j * S:(j + 1) * S], reads=(wr_, "bmt"),
               wr=f"psPre{fc % 2}", start=(j == 0), stop=(j == 7))
        p.op("DVE", lambda o=G[:, fc * GM:fc * GM + 16],
             i=ps_pre[:, po:po + 16], fc=fc:
             V.tensor_copy(o, i), reads=(f"psPre{fc % 2}", "Ginit"),
             writes=(f"G{fc}",))
        p.op("DVE", lambda o=G[:, fc * GM + 32:fc * GM + 48],
             i=wigt[:, fc * S:(fc + 1) * S]: V.tensor_copy(o, i),
             reads=("wigt", "Ginit"), writes=(f"G{fc}",))
        p.op("DVE", lambda o=G[:, fc * GM + 64:fc * GM + 80],
             i=wfgt[:, fc * S:(fc + 1) * S]: V.tensor_copy(o, i),
             reads=("wfgt", "Ginit"), writes=(f"G{fc}",))

    for fin in range(8):
        wtr, wr_ = wtrs[fin % 2], f"wtr{fin % 2}"
        po = (fin % 2) * 512
        load_fm(wtr, 128, W_in_T[:, fin * 128:(fin + 1) * 128], wr_)
        for fc in range(8):
            mm(f"WG{fin}", ps_pre[:, po:po + GM],
               wtr[:, fc * 128:(fc + 1) * 128],
               G[:, fc * GM:(fc + 1) * GM], reads=(wr_, f"G{fc}"),
               wr=f"psPre{fin % 2}", start=(fc == 0), stop=(fc == 7))
        p.op("DVE", lambda o=WGf[:, fin * GM:(fin + 1) * GM],
             i=ps_pre[:, po:po + GM], fin=fin:
             V.tensor_copy(o, i), reads=(f"psPre{fin % 2}",),
             writes=(f"WGf{fin}",))
        p.op("ACT", lambda o=WG_hi[:, fin * GM:(fin + 1) * GM],
             i=WGf[:, fin * GM:(fin + 1) * GM]: A.activation(o, i, AF.Copy),
             reads=(f"WGf{fin}",), writes=(f"WGhi{fin}",))
        p.op("DVE", lambda o=WG_lo[:, fin * GM:(fin + 1) * GM],
             a=WGf[:, fin * GM:(fin + 1) * GM],
             b=WG_hi[:, fin * GM:(fin + 1) * GM]:
             V.tensor_sub(o, a, b), reads=(f"WGf{fin}", f"WGhi{fin}"),
             writes=(f"WGlo{fin}",))
    if debug:
        store(dbg["d_wg"][:], WGf[:, 0:8 * 48], tuple(f"WGf{i}" for i in range(8)),
              "d_wg")

    t1 = SB("t1", [128, 8], stack=stA)
    for j in range(8):
        wtr, wr_ = wtrs[j % 2], f"wtr{j % 2}"
        po = (j % 2) * 512
        load_fm(wtr, 128, W_B_T[:, j * 128:(j + 1) * 128], wr_)
        for fc in range(8):
            mm(f"t1{j}", ps_pre[:, po:po + 1], wtr[:, fc * 128:(fc + 1) * 128],
               bic[:, fc:fc + 1], reads=(wr_, "bic"),
               wr=f"psPre{j % 2}", start=(fc == 0), stop=(fc == 7))
        p.op("DVE", lambda o=t1[:, j:j + 1], i=ps_pre[:, po:po + 1],
             b=bbc[:, j:j + 1], j=j:
             V.tensor_add(o, i, b), reads=(f"psPre{j % 2}", "bbc"),
             writes=("t1",))
    for j in range(8):
        mm("bP", ps_pre[0:16, 0:1], bmt[:, j * S:(j + 1) * S], t1[:, j:j + 1],
           reads=("bmt", "t1"), wr="psPre0", start=(j == 0), stop=(j == 7))
    btmp = SB("btmp", [S, 3], stack=stA)
    p.op("DVE", lambda: V.tensor_copy(btmp[:, 0:1], ps_pre[0:16, 0:1]),
         reads=("psPre0",), writes=("btmp0",))
    for which, wt, wrg, bcol, col in (("ig", wigt, "wigt", b_ig_col, 1),
                                      ("fg", wfgt, "wfgt", b_fg_col, 2)):
        for fc in range(8):
            mm(f"b{which}", ps_pre[0:16, 1:2], wt[:, fc * S:(fc + 1) * S],
               bic[:, fc:fc + 1], reads=(wrg, "bic"),
               wr="psPre0", start=(fc == 0), stop=(fc == 7))
        bcs = SB(f"bc_{which}", [S, 1], stack=stA)
        load(bcs[:], bcol[:], f"bc_{which}")
        p.op("DVE", lambda o=btmp[:, col:col + 1], i=ps_pre[0:16, 1:2], b=bcs[:]:
             V.tensor_add(o, i, b), reads=("psPre0", f"bc_{which}"),
             writes=(f"btmp{col}",))
    store(bG[0:16, :], btmp[:, 0:1], ("btmp0",), "bG")
    store(bG[32:48, :], btmp[:, 1:2], ("btmp1",), "bG")
    store(bG[64:80, :], btmp[:, 2:3], ("btmp2",), "bG")

    ctmp = SB("ctmp", [S, H], stack=stA)
    bdtmp = SB("bdtmp", [1, H], stack=stA)
    load(ctmp[:], C_mat[:], "ctmp")
    load(bdtmp[:], bD_row[:], "bdtmp")
    p.op("ACT", lambda: A.activation(CA_hi[0:S, :], ctmp[:], AF.Copy),
         reads=("ctmp",), writes=("CAhi",))
    p.op("DVE", lambda: V.tensor_sub(CA_lo[0:S, :], ctmp[:], CA_hi[0:S, :]),
         reads=("ctmp", "CAhi"), writes=("CAlo",))
    bdh = SB("bdh", [1, H], BF16, stack=stA)
    bdl = SB("bdl", [1, H], BF16, stack=stA)
    p.op("ACT", lambda: A.activation(bdh[:], bdtmp[:], AF.Copy),
         reads=("bdtmp",), writes=("bdh",))
    p.op("DVE", lambda: V.tensor_sub(bdl[:], bdtmp[:], bdh[:]),
         reads=("bdtmp", "bdh"), writes=("bdl",))
    store(CA_hi[S:S + 1, :], bdh[:], ("bdh",), "CAhi", q="SP")
    store(CA_lo[S:S + 1, :], bdl[:], ("bdl",), "CAlo", q="SP")

    twfs, tus, tk16s, tbs, tk, trr, tcp, tcol = alloc_tern(stA)

    def tern_slices(nrows, width):
        return [(rc, c0, min(width, c0 + TW))
                for rc in range(nrows) for c0 in range(0, width, TW)]

    def tern_pass1(src, nrows, width, mean_out, mean_rg, tg,
                   ps_hbuf, ps_rg, scol):
        stag = f"tcol{scol}"
        slices = tern_slices(nrows, width)
        for i, (rc, c0, c1) in enumerate(slices):
            wv = c1 - c0
            pr = i % NTB
            twf, tu = twfs[pr], tus[pr]
            load(twf[:, 0:wv], src[rc * 128:(rc + 1) * 128, c0:c1], f"twf{pr}")
            if i % 2 == 0:
                p.op("ACT", lambda twf=twf, tu=tu, e=wv: A.activation(
                    tu[:, 0:e], twf[:, 0:e], AF.Abs, accum_out=tcol[:, 0:1]),
                     reads=(f"twf{pr}",), writes=(f"tu{pr % 2}", "tcol0"))
                acc, arg = "tcol0", 0
            else:
                p.op("DVE", lambda twf=twf, e=wv: V.tensor_reduce(
                    tcol[:, 3:4].rearrange("p (t o) -> p t o", o=1),
                    twf[:, 0:e].rearrange("p (t e) -> p t e", t=1),
                    AX, OP.add, apply_absolute_value=True),
                     reads=(f"twf{pr}",), writes=("tcol3",))
                acc, arg = "tcol3", 3
            ac = 6 + (i % 4)
            if i < 4:
                p.op("DVE", lambda ac=ac, a=arg: V.tensor_copy(
                    tcol[:, ac:ac + 1], tcol[:, a:a + 1]),
                     reads=(acc,), writes=(f"tcol{ac}",))
            else:
                p.op("DVE", lambda a=arg, ac=ac: V.tensor_add(
                    tcol[:, ac:ac + 1], tcol[:, ac:ac + 1], tcol[:, a:a + 1]),
                     reads=(acc, f"tcol{ac}"), writes=(f"tcol{ac}",))
            yield
        nun = len(slices)
        p.op("DVE", lambda: V.tensor_add(tcol[:, 6:7], tcol[:, 6:7],
                                         tcol[:, 7:8]),
             reads=("tcol6", "tcol7"), writes=("tcol6",))
        if nun > 2:
            p.op("DVE", lambda: V.tensor_add(tcol[:, 8:9], tcol[:, 8:9],
                                             tcol[:, 9:10]),
                 reads=("tcol8", "tcol9"), writes=("tcol8",))
            p.op("DVE", lambda: V.tensor_add(tcol[:, 6:7], tcol[:, 6:7],
                                             tcol[:, 8:9]),
                 reads=("tcol6", "tcol8"), writes=("tcol6",))
        p.op("DVE", lambda: V.tensor_scalar(tcol[:, 1:2], tcol[:, 6:7],
                                            1.0 / (nrows * 128 * width), None,
                                            OP.mult),
             reads=("tcol6",), writes=("tcol1",))
        mm("parsum" + tg, ps_hbuf[0:1, 0:1], s_onesc[:], tcol[:, 1:2],
           reads=("s_onesc", "tcol1"), wr=ps_rg, start=True, stop=True)
        p.op("DVE", lambda: V.tensor_copy(tcol[0:1, 1:2], ps_hbuf[0:1, 0:1]),
             reads=(ps_rg,), writes=("tcol1",))
        mm("parbc" + tg, ps_hbuf[0:128, 0:1], s_ones1[:], tcol[0:1, 1:2],
           reads=("s_ones1", "tcol1"), wr=ps_rg, start=True, stop=True)
        p.op("DVE", lambda: V.tensor_copy(tcol[:, 1:2], ps_hbuf[0:128, 0:1]),
             reads=(ps_rg,), writes=("tcol1",))
        p.op("DVE", lambda: V.tensor_scalar_max(mean_out[:], tcol[:, 1:2], 1e-5),
             reads=("tcol1",), writes=(mean_rg,))
        p.op("DVE", lambda mo=mean_out, sc=scol: V.reciprocal(
            tcol[:, sc:sc + 1], mo[:]),
             reads=(mean_rg,), writes=(stag,))
        yield

    def tern_pass2(src, dst_hbm, nrows, width, tg, scol):
        stag = f"tcol{scol}"
        slices = tern_slices(nrows, width)
        for i, (rc, c0, c1) in enumerate(slices):
            wv = c1 - c0
            pr = i % NTB
            twf, tu, tk16, tb = twfs[pr], tus[pr], tk16s[pr], tbs[pr]
            load(twf[:, 0:wv], src[rc * 128:(rc + 1) * 128, c0:c1], f"twf{pr}")
            if SAFE_ROUND:
                p.op("DVE", lambda twf=twf, tu=tu, e=wv: V.tensor_scalar(
                    tu[:, 0:e], twf[:, 0:e], tcol[:, scol:scol + 1], None,
                    OP.mult),
                     reads=(f"twf{pr}", stag), writes=(f"tu{pr % 2}",))
                p.op("DVE", lambda tu=tu, e=wv: V.tensor_copy(tk[:, 0:e],
                                                              tu[:, 0:e]),
                     reads=(f"tu{pr % 2}",), writes=("tk",))
                p.op("DVE", lambda tu=tu, e=wv: V.tensor_sub(trr[:, 0:e],
                                                             tu[:, 0:e],
                                                             tk[:, 0:e]),
                     reads=(f"tu{pr % 2}", "tk"), writes=("trr",))
                p.op("DVE", lambda e=wv: V.tensor_scalar(tcp[:, 0:e], trr[:, 0:e],
                                                         0.5, None, OP.is_gt),
                     reads=("trr",), writes=("tcp",))
                p.op("DVE", lambda tu=tu, e=wv: V.scalar_tensor_tensor(
                    tu[:, 0:e], trr[:, 0:e], -0.5, tcp[:, 0:e], OP.is_lt,
                    OP.subtract),
                     reads=("trr", "tcp"), writes=(f"tu{pr % 2}",))
                p.op("DVE", lambda tu=tu, e=wv: V.tensor_sub(tu[:, 0:e],
                                                             tk[:, 0:e],
                                                             tu[:, 0:e]),
                     reads=(f"tu{pr % 2}", "tk"), writes=(f"tu{pr % 2}",))
                p.op("DVE", lambda tu=tu, tb=tb, e=wv: V.tensor_scalar(
                    tb[:, 0:e], tu[:, 0:e], 1.0, -1.0, OP.min, OP.max),
                     reads=(f"tu{pr % 2}",), writes=(f"tb{pr % 2}",))
            elif i % 2 == 0:
                p.op("ACT", lambda twf=twf, tu=tu, e=wv: A.activation(
                    tu[:, 0:e], twf[:, 0:e], AF.Identity,
                    scale=tcol[:, scol:scol + 1]),
                     reads=(f"twf{pr}", stag), writes=(f"tu{pr % 2}",))
                p.op("DVE", lambda tu=tu, tk16=tk16, e=wv: V.tensor_scalar(
                    tk16[:, 0:e], tu[:, 0:e], 1.49, -1.49, OP.min, OP.max),
                     reads=(f"tu{pr % 2}",), writes=(f"tk16{pr % 2}",))
                p.op("ACT", lambda tk16=tk16, tb=tb, e=wv: A.activation(
                    tb[:, 0:e], tk16[:, 0:e], AF.Copy),
                     reads=(f"tk16{pr % 2}",), writes=(f"tb{pr % 2}",))
            else:
                p.op("DVE", lambda twf=twf, tu=tu, e=wv: V.tensor_scalar(
                    tu[:, 0:e], twf[:, 0:e], tcol[:, scol:scol + 1], 1.49,
                    OP.mult, OP.min),
                     reads=(f"twf{pr}", stag), writes=(f"tu{pr % 2}",))
                p.op("DVE", lambda tu=tu, tk16=tk16, e=wv: V.tensor_scalar(
                    tk16[:, 0:e], tu[:, 0:e], -1.49, None, OP.max),
                     reads=(f"tu{pr % 2}",), writes=(f"tk16{pr % 2}",))
                p.op("ACT", lambda tk16=tk16, tb=tb, e=wv: A.activation(
                    tb[:, 0:e], tk16[:, 0:e], AF.Copy),
                     reads=(f"tk16{pr % 2}",), writes=(f"tb{pr % 2}",))
            store(dst_hbm[rc * 128:(rc + 1) * 128, c0:c1], tb[:, 0:wv],
                  (f"tb{pr % 2}",), f"q{tg}{rc}_{c0}")
            yield

    def tern_p1_all(ps_hbuf, ps_rg):
        yield from tern_pass1(w1t_f, 8, DH, s_w1mean, "s_w1mean", "w1",
                              ps_hbuf, ps_rg, 2)
        yield from tern_pass1(w2t_f, 32, H, s_w2mean, "s_w2mean", "w2",
                              ps_hbuf, ps_rg, 4)

    def tern_p2_all():
        yield from tern_pass2(w1t_f, w1q_hbm, 8, DH, "w1", 2)
        yield from tern_pass2(w2t_f, w2q_hbm, 32, H, "w2", 4)

    def tern_all():
        yield from tern_p1_all(ps_pre, "psPre0")
        yield from tern_p2_all()

    tern_p1 = tern_all()

    def pump_p1(n):
        for _ in range(n):
            try:
                next(tern_p1)
            except StopIteration:
                break

    GSL = 256
    xgfs = (SB("xgf0", [128, 8 * GSL], stack=stA),
            SB("xgf1", [128, 8 * GSL], stack=stA))
    xghs = (SB("xgh0", [128, 8 * GSL], BF16, stack=stA),
            SB("xgh1", [128, 8 * GSL], BF16, stack=stA))
    xgls = (SB("xgl0", [128, 8 * GSL], BF16, stack=stA),
            SB("xgl1", [128, 8 * GSL], BF16, stack=stA))
    for sl in range(T // GSL):
        c0 = sl * GSL
        pr = sl % 2
        xgf, xgh, xgl = xgfs[pr], xghs[pr], xgls[pr]
        go = pr * GSL
        load_fm(xgf, GSL, xg_fm[:, c0:c0 + GSL], f"xgf{pr}")
        p.op("ACT", lambda xgf=xgf, xgh=xgh: A.activation(xgh[:], xgf[:],
                                                          AF.Copy),
             reads=(f"xgf{pr}",), writes=(f"xgh{pr}",))
        p.op("DVE", lambda xgf=xgf, xgh=xgh, xgl=xgl:
             V.tensor_sub(xgl[:], xgf[:], xgh[:]),
             reads=(f"xgf{pr}", f"xgh{pr}"), writes=(f"xgl{pr}",))
        for c in range(8):
            sh = WG_hi[:, c * GM:(c + 1) * GM]
            slo = WG_lo[:, c * GM:(c + 1) * GM]
            mh = xgh[:, c * GSL:(c + 1) * GSL]
            ml = xgl[:, c * GSL:(c + 1) * GSL]
            mm(f"g{sl}", ps_g[:, go:go + GSL], sh, mh,
               reads=(f"WGhi{c}", f"xgh{pr}"),
               wr=f"psG{pr}", start=(c == 0), stop=False)
            mm(f"g{sl}", ps_g[:, go:go + GSL], sh, ml,
               reads=(f"WGhi{c}", f"xgl{pr}"),
               wr=f"psG{pr}", start=False, stop=False)
            mm(f"g{sl}", ps_g[:, go:go + GSL], slo, mh,
               reads=(f"WGlo{c}", f"xgh{pr}"),
               wr=f"psG{pr}", start=False, stop=(c == 7))
        p.op("ACT", lambda o=g80[0:16, c0:c0 + GSL], go=go:
             A.activation(o, ps_g[0:16, go:go + GSL], AF.Identity,
                          bias=bG[0:16, :]),
             reads=(f"psG{pr}", "bG"), writes=(f"gB{sl % 8}",))
        p.op("ACT", lambda o=g80[32:48, c0:c0 + GSL], go=go:
             A.activation(o, ps_g[32:48, go:go + GSL], AF.Sigmoid,
                          bias=bG[32:48, :]),
             reads=(f"psG{pr}", "bG"), writes=(f"gI{sl % 8}",))
        p.op("ACT", lambda o=g80[64:80, c0:c0 + GSL], go=go:
             A.activation(o, ps_g[64:80, go:go + GSL], AF.Sigmoid,
                          bias=bG[64:80, :]),
             reads=(f"psG{pr}", "bG"), writes=(f"gF{sl % 8}",))
        pump_p1(9)
    gr = tuple(f"gB{i}" for i in range(8)) + tuple(f"gI{i}" for i in range(8)) + tuple(f"gF{i}" for i in range(8))
    if debug:
        store(dbg["d_gates"][0:16, :], g80[0:16, :], gr, "d_gates")
        store(dbg["d_gates"][16:32, :], g80[32:48, :], gr, "d_gates")
        store(dbg["d_gates"][32:48, :], g80[64:80, :], gr, "d_gates")

    hs_e = SB("hs_e", [S, EXT], stack=stA)
    TH = T // 2
    uu = g80[0:16, :]
    store(xgfs[0][0:16, :TH], g80[32:48, 0:TH], gr, "igc", q="SP")
    store(xgfs[1][0:16, :TH], g80[32:48, TH:T], gr, "igc2", q="SP")
    p.op("DVE", lambda: V.tensor_mul(uu[:, 0:TH], xgfs[0][0:16, :TH],
                                     g80[0:16, 0:TH]),
         reads=gr + ("igc",), writes=("uu",) + tuple(f"gB{i}" for i in range(8)))
    p.op("DVE", lambda: V.tensor_mul(uu[:, TH:T], xgfs[1][0:16, :TH],
                                     g80[0:16, TH:T]),
         reads=gr + ("igc2", "uu"), writes=("uu",))
    p.op("DVE", lambda: V.memset(hs_buf[:, 0:2], 0.0), writes=("hs",))
    p.op("DVE", lambda: V.memset(hs_buf[:, T + 2:T + 4], 0.0), writes=("hs",))
    store(xgfs[0][0:16, :TH], g80[64:80, 0:TH], gr + ("uu", "igc"), "fgc",
          q="SP")
    store(xgfs[1][0:16, :TH], g80[64:80, TH:T], gr + ("uu", "igc2"), "fgc2",
          q="SP")
    p.op("DVE", lambda: V.tensor_tensor_scan(hs_buf[:, 2:TH + 2],
                                             xgfs[0][0:16, :TH],
                                             uu[:, 0:TH], 0.0,
                                             OP.mult, OP.add),
         reads=gr + ("uu", "fgc"), writes=("hs",))
    p.op("DVE", lambda: V.tensor_tensor_scan(hs_buf[:, TH + 2:T + 2],
                                             xgfs[1][0:16, :TH],
                                             uu[:, TH:T],
                                             hs_buf[:, TH + 1:TH + 2],
                                             OP.mult, OP.add),
         reads=gr + ("uu", "fgc2", "hs"), writes=("hs",))
    p.op("DVE", lambda: V.tensor_scalar(hs_e[:], hs_buf[:, 0:EXT], s_hsel0[:],
                                        None, OP.mult),
         reads=("hs", "s_hsel0"), writes=("hs_e",))
    p.op("DVE", lambda: V.scalar_tensor_tensor(hs_e[:], hs_buf[:, 2048:2048 + EXT],
                                               s_hsel1[:], hs_e[:], OP.mult, OP.add),
         reads=("hs", "s_hsel1", "hs_e"), writes=("hs_e",))
    if debug:
        store(dbg["d_hs"][:], hs_e[:], ("hs_e",), "d_hs")
    p.op("ACT", lambda: A.activation(hsA_hi[0:S, :], hs_e[:], AF.Copy),
         reads=("hs_e",), writes=("hsAhi",))
    p.op("DVE", lambda: V.tensor_sub(hsA_lo[0:S, :], hs_e[:], hsA_hi[0:S, :]),
         reads=("hs_e", "hsAhi"), writes=("hsAlo",))
    load(hsA_hi[S:S + 1, :], onesrow_ext[:], "hsAhi")
    load(hsA_lo[S:S + 1, :], zrow_ext[:], "hsAlo")

    pump_p1(10 ** 9)

    stA.close()
    p.barrier()

    stB = contextlib.ExitStack()
    Wq = {}
    for nm in ("Win", "WD", "Wout"):
        Wq[nm] = (SB(nm + "_hi", [128, 8 * H], BF16, stack=stB),
                  SB(nm + "_lo", [128, 8 * H], BF16, stack=stB))
    xtm = SB("xtm", [128, H], stack=stB)
    xb0 = SB("xb0", [128, 8 * AWM], stack=stB)
    xb1 = SB("xb1", [128, 8 * AWM], stack=stB)
    zS2 = (SB("zS0", [128, 8 * AWM], stack=stB),
           SB("zS1", [128, 8 * AWM], stack=stB))
    xp_f = SB("xp_f", [128, 8 * AWM], stack=stB)
    yf_c = SB("yf_c", [128, AWM], stack=stB)
    wi = 0
    for nm, src in (("Win", W_in), ("WD", W_D), ("Wout", W_out)):
        hi, lo = Wq[nm]
        for c in range(8):
            pr = wi % 2
            wi += 1
            wtmp = (xb0, xb1)[pr][:, 0:H]
            load(wtmp, src[c * 128:(c + 1) * 128, :], f"xb{pr}")
            if pr == 0:
                p.op("ACT", lambda o=hi[:, c * H:(c + 1) * H], a=wtmp:
                     A.activation(o, a, AF.Copy),
                     reads=(f"xb{pr}",), writes=(f"{nm}hi{c}",))
            else:
                p.op("DVE", lambda o=hi[:, c * H:(c + 1) * H], a=wtmp:
                     V.tensor_copy(o, a),
                     reads=(f"xb{pr}",), writes=(f"{nm}hi{c}",))
            p.op("GP", lambda o=lo[:, c * H:(c + 1) * H], a=wtmp,
                 b=hi[:, c * H:(c + 1) * H]: GP.tensor_sub(o, a, b),
                 reads=(f"xb{pr}", f"{nm}hi{c}"), writes=(f"{nm}lo{c}",))

    pA_h = SB("pA_h", [128, 8 * AWM], BF16, stack=stB)
    pA_l = SB("pA_l", [128, 8 * AWM], BF16, stack=stB)
    pB_h = SB("pB_h", [128, 8 * AWM], BF16, stack=stB)
    pB_l = SB("pB_l", [128, 8 * AWM], BF16, stack=stB)
    z_tm = SB("z_tm", [128, H], stack=stB)
    zsq = SB("zsq", [128, H], stack=stB)
    s_s = z_tm
    sc_t = SB("sc_t", [128, H], stack=stB)
    r1t = z_tm
    q_i = SB("q_i", [128, H], I16, stack=stB)
    q_b = SB("q_b", [128, H], BF16, stack=stB)
    colz = SB("colz", [128, 16], stack=stB)
    if SAFE_ROUND:
        rr_t = SB("rr_t", [128, H], stack=stB)
        cp_t = SB("cp_t", [128, H], stack=stB)
    ps_a = PSUM("ps_a", [128, 2048], stack=stB)
    ps_tr = PSUM("ps_tr", [128, 1024], stack=stB)

    lgen = [None]

    def lpump(n):
        if lgen[0] is None:
            return
        for _ in range(n):
            try:
                next(lgen[0])
            except StopIteration:
                lgen[0] = None
                break

    def attn_mm(gi):
        g0, g1 = AGROUPS[gi]
        w = g1 - g0
        tg = f"A{gi}"
        zS = zS2[gi % 2]
        ZR = f"zS{gi % 2}"
        xb = (xb0, xb1)[gi % 2]
        xbr = f"xb{gi % 2}"
        load_fm(xb, w, x_fm[:, g0:g1], xbr, stride=AWM)
        for k in range(8):
            p.op("ACT", lambda k=k: A.activation(
                pA_h[:, k * AWM:k * AWM + w], xb[:, k * AWM:k * AWM + w],
                AF.Copy),
                 reads=(xbr,), writes=(f"pAh{k}",))
            p.op("DVE", lambda k=k: V.tensor_sub(
                pA_l[:, k * AWM:k * AWM + w], xb[:, k * AWM:k * AWM + w],
                pA_h[:, k * AWM:k * AWM + w]),
                 reads=(xbr, f"pAh{k}"), writes=(f"pAl{k}",))

        for m in range(8):
            od = ps_a[0:128, (m % 4) * 512:(m % 4) * 512 + w]
            for k in range(8):
                sh = Wq["Win"][0][:, k * H + m * 128:k * H + m * 128 + 128]
                sl = Wq["Win"][1][:, k * H + m * 128:k * H + m * 128 + 128]
                mh = pA_h[:, k * AWM:k * AWM + w]
                ml = pA_l[:, k * AWM:k * AWM + w]
                mm(tg + f"xp{m}", od, sh, mh, reads=(f"Winhi{k}", f"pAh{k}"),
                   wr=f"psA{m % 4}", start=(k == 0), stop=False)
                mm(tg + f"xp{m}", od, sh, ml, reads=(f"Winhi{k}", f"pAl{k}"),
                   wr=f"psA{m % 4}", start=False, stop=False)
                mm(tg + f"xp{m}", od, sl, mh, reads=(f"Winlo{k}", f"pAh{k}"),
                   wr=f"psA{m % 4}", start=False, stop=(k == 7))
            p.op("DVE", lambda o=xp_f[:, m * AWM:m * AWM + w], i=od,
                 bc=s_binc[:, m:m + 1], mm_=m:
                 V.tensor_scalar(o, i, bc, None, OP.add),
                 reads=(f"psA{m % 4}", "s_binc"), writes=(f"xpf{m}",))
            p.op("ACT", lambda m=m: A.activation(
                pB_h[:, m * AWM:m * AWM + w], xp_f[:, m * AWM:m * AWM + w],
                AF.Copy),
                 reads=(f"xpf{m}",), writes=(f"pBh{m}",))
            p.op("DVE", lambda m=m: V.tensor_sub(
                pB_l[:, m * AWM:m * AWM + w], xp_f[:, m * AWM:m * AWM + w],
                pB_h[:, m * AWM:m * AWM + w]),
                 reads=(f"xpf{m}", f"pBh{m}"), writes=(f"pBl{m}",))
            lpump(1)
        if debug and gi == 0:
            for m in range(8):
                store(dbg["d_xp"][:, m * w:(m + 1) * w],
                      xp_f[:, m * AWM:m * AWM + w], (f"xpf{m}",), "d_xp")

        for m in range(8):
            od = ps_a[0:128, (m % 4) * 512:(m % 4) * 512 + w]
            for k in range(8):
                sh = Wq["WD"][0][:, k * H + m * 128:k * H + m * 128 + 128]
                sl = Wq["WD"][1][:, k * H + m * 128:k * H + m * 128 + 128]
                mh = pB_h[:, k * AWM:k * AWM + w]
                ml = pB_l[:, k * AWM:k * AWM + w]
                mm(tg + f"y{m}", od, sh, mh, reads=(f"WDhi{k}", f"pBh{k}"),
                   wr=f"psA{m % 4}", start=(k == 0), stop=False)
                mm(tg + f"y{m}", od, sh, ml, reads=(f"WDhi{k}", f"pBl{k}"),
                   wr=f"psA{m % 4}", start=False, stop=False)
                mm(tg + f"y{m}", od, sl, mh, reads=(f"WDlo{k}", f"pBh{k}"),
                   wr=f"psA{m % 4}", start=False, stop=False)
            mm(tg + f"y{m}", od, CA_hi[:, m * 128:(m + 1) * 128],
               hsA_hi[:, g0:g1], reads=("CAhi", "hsAhi"),
               wr=f"psA{m % 4}", start=False, stop=False)
            mm(tg + f"y{m}", od, CA_hi[:, m * 128:(m + 1) * 128],
               hsA_lo[:, g0:g1], reads=("CAhi", "hsAlo"),
               wr=f"psA{m % 4}", start=False, stop=False)
            mm(tg + f"y{m}", od, CA_lo[:, m * 128:(m + 1) * 128],
               hsA_hi[:, g0:g1], reads=("CAlo", "hsAhi"),
               wr=f"psA{m % 4}", start=False, stop=True)
            p.op("ACT", lambda i=od, mm_=m: A.activation(yf_c[:, 0:w], i, AF.Tanh),
                 reads=(f"psA{m % 4}",), writes=("yf_c",))
            if debug and gi == 0:
                store(dbg["d_y"][:, m * w:(m + 1) * w], yf_c[:, 0:w],
                      ("yf_c",), "d_y")
            p.op("ACT", lambda m=m: A.activation(
                pA_h[:, m * AWM:m * AWM + w], yf_c[:, 0:w], AF.Copy),
                 reads=("yf_c",), writes=(f"pAh{m}",))
            p.op("DVE", lambda m=m: V.tensor_sub(
                pA_l[:, m * AWM:m * AWM + w], yf_c[:, 0:w],
                pA_h[:, m * AWM:m * AWM + w]),
                 reads=("yf_c", f"pAh{m}"), writes=(f"pAl{m}",))

        for m in range(8):
            od = ps_a[0:128, (m % 4) * 512:(m % 4) * 512 + w]
            for k in range(8):
                sh = Wq["Wout"][0][:, k * H + m * 128:k * H + m * 128 + 128]
                sl = Wq["Wout"][1][:, k * H + m * 128:k * H + m * 128 + 128]
                mh = pA_h[:, k * AWM:k * AWM + w]
                ml = pA_l[:, k * AWM:k * AWM + w]
                mm(tg + f"z{m}", od, sh, mh, reads=(f"Wouthi{k}", f"pAh{k}"),
                   wr=f"psA{m % 4}", start=(k == 0), stop=False)
                mm(tg + f"z{m}", od, sh, ml, reads=(f"Wouthi{k}", f"pAl{k}"),
                   wr=f"psA{m % 4}", start=False, stop=False)
                mm(tg + f"z{m}", od, sl, mh, reads=(f"Woutlo{k}", f"pAh{k}"),
                   wr=f"psA{m % 4}", start=False, stop=(k == 7))
            p.op("DVE", lambda o=zS[:, m * AWM:m * AWM + w], i=od,
                 x=xp_f[:, m * AWM:m * AWM + w], bc=s_boutc[:, m:m + 1], mm_=m:
                 V.scalar_tensor_tensor(o, i, bc, x, OP.add, OP.add),
                 reads=(f"psA{m % 4}", f"xpf{m}", "s_boutc"), writes=(ZR,))
            lpump(1)
        if debug and gi == 0:
            for m in range(8):
                store(dbg["d_z"][:, m * w:(m + 1) * w],
                      zS[:, m * AWM:m * AWM + w], (ZR,), "d_z")

    def attn_ladder(gi):
        g0, g1 = AGROUPS[gi]
        w = g1 - g0
        tg = f"A{gi}"
        zS = zS2[gi % 2]
        ZR = f"zS{gi % 2}"
        nsub = (w + 127) // 128
        for si in range(nsub):
            u0 = g0 + si * 128
            u1 = min(g1, u0 + 128)
            uw = u1 - u0
            ti = u0 // 128
            sg = f"{tg}s{si}"
            for m_ in range(8):
                p.op("PE", lambda o=ps_tr[0:uw, m_ * 128:(m_ + 1) * 128],
                     i=zS[:, m_ * AWM + si * 128:m_ * AWM + si * 128 + uw]:
                     PE.transpose(o, i, s_idf[:]),
                     reads=(ZR, "s_idf"), writes=("psTR",), group=sg + "tr")
            p.op("DVE", lambda e=uw: V.tensor_scalar(
                z_tm[0:e, :], ps_tr[0:e, :], 1.0, None, OP.mult, OP.add,
                accum_out=colz[0:e, 0:1]),
                 reads=("psTR",), writes=("z_tm", "colz"))
            p.op("ACT", lambda e=uw: A.activation(zsq[0:e, :], z_tm[0:e, :],
                                                  AF.Square,
                                                  accum_out=colz[0:e, 1:2]),
                 reads=("z_tm",), writes=("zsq", "colz"))
            p.op("DVE", lambda e=uw: V.tensor_scalar(colz[0:e, 2:3],
                                                     colz[0:e, 0:1], 1.0 / H,
                                                     None, OP.mult),
                 reads=("colz",), writes=("colz",))
            p.op("DVE", lambda e=uw: V.tensor_mul(colz[0:e, 3:4], colz[0:e, 2:3],
                                                  colz[0:e, 2:3]),
                 reads=("colz",), writes=("colz",))
            p.op("DVE", lambda e=uw: V.scalar_tensor_tensor(
                colz[0:e, 4:5], colz[0:e, 1:2], 1.0 / H, colz[0:e, 3:4],
                OP.mult, OP.subtract),
                 reads=("colz",), writes=("colz",))
            p.op("DVE", lambda e=uw: V.tensor_scalar(colz[0:e, 4:5],
                                                     colz[0:e, 4:5], EPS, None,
                                                     OP.add),
                 reads=("colz",), writes=("colz",))
            p.op("ACT", lambda e=uw: A.activation(colz[0:e, 5:6], colz[0:e, 4:5],
                                                  AF.Sqrt),
                 reads=("colz",), writes=("colz",))
            p.op("DVE", lambda e=uw: V.reciprocal(colz[0:e, 6:7], colz[0:e, 5:6]),
                 reads=("colz",), writes=("colz",))
            yield
            p.op("DVE", lambda e=uw: V.tensor_scalar(sc_t[0:e, :], z_tm[0:e, :],
                                                     colz[0:e, 2:3],
                                                     colz[0:e, 6:7],
                                                     OP.subtract, OP.mult),
                 reads=("z_tm", "colz"), writes=("sc_t",))
            load(xtm[0:uw, :], x_tm[u0:u1, :], "xtm", q="ACT")
            p.op("GP", lambda e=uw: GP.tensor_mul(sc_t[0:e, :], sc_t[0:e, :],
                                                  s_gamb[0:e, :]),
                 reads=("sc_t", "s_gamb"), writes=("sc_t",))
            p.op("GP", lambda e=uw: GP.tensor_add(xtm[0:e, :], xtm[0:e, :],
                                                  s_betb[0:e, :]),
                 reads=("xtm", "s_betb"), writes=("xtm",))
            p.op("DVE", lambda e=uw: V.tensor_add(s_s[0:e, :], sc_t[0:e, :],
                                                  xtm[0:e, :]),
                 reads=("sc_t", "xtm"), writes=("z_tm",))
            yield
            p.op("DVE", lambda e=uw: V.tensor_reduce(colz[0:e, 7:8], s_s[0:e, :],
                                                     AX, OP.add),
                 reads=("z_tm",), writes=("colz",))
            p.op("ACT", lambda e=uw: A.activation(zsq[0:e, :], s_s[0:e, :],
                                                  AF.Square,
                                                  accum_out=colz[0:e, 8:9]),
                 reads=("z_tm",), writes=("zsq", "colz"))
            p.op("DVE", lambda e=uw: V.tensor_scalar(colz[0:e, 9:10],
                                                     colz[0:e, 7:8], 1.0 / H,
                                                     None, OP.mult),
                 reads=("colz",), writes=("colz",))
            p.op("DVE", lambda e=uw: V.tensor_mul(colz[0:e, 10:11],
                                                  colz[0:e, 9:10],
                                                  colz[0:e, 9:10]),
                 reads=("colz",), writes=("colz",))
            p.op("DVE", lambda e=uw: V.scalar_tensor_tensor(
                colz[0:e, 10:11], colz[0:e, 8:9], 1.0 / H, colz[0:e, 10:11],
                OP.mult, OP.subtract),
                 reads=("colz",), writes=("colz",))
            p.op("DVE", lambda e=uw: V.tensor_scalar(colz[0:e, 10:11],
                                                     colz[0:e, 10:11], EPS,
                                                     None, OP.add),
                 reads=("colz",), writes=("colz",))
            p.op("ACT", lambda e=uw: A.activation(colz[0:e, 11:12],
                                                  colz[0:e, 10:11], AF.Sqrt),
                 reads=("colz",), writes=("colz",))
            p.op("DVE", lambda e=uw: V.reciprocal(colz[0:e, 12:13],
                                                  colz[0:e, 11:12]),
                 reads=("colz",), writes=("colz",))
            yield
            p.op("DVE", lambda e=uw: V.tensor_scalar(sc_t[0:e, :], s_s[0:e, :],
                                                     colz[0:e, 9:10], None,
                                                     OP.subtract),
                 reads=("z_tm", "colz"), writes=("sc_t",))
            p.op("DVE", lambda e=uw: V.tensor_reduce(colz[0:e, 13:14],
                                                     sc_t[0:e, :], AX, OP.max,
                                                     apply_absolute_value=True),
                 reads=("sc_t",), writes=("colz",))
            p.op("DVE", lambda e=uw: V.tensor_mul(colz[0:e, 13:14],
                                                  colz[0:e, 13:14],
                                                  colz[0:e, 12:13]),
                 reads=("colz",), writes=("colz",))
            p.op("DVE", lambda e=uw: V.tensor_scalar_max(colz[0:e, 13:14],
                                                         colz[0:e, 13:14], 1e-5),
                 reads=("colz",), writes=("colz",))
            p.op("DVE", lambda e=uw: V.tensor_scalar(colz[0:e, 14:15],
                                                     colz[0:e, 13:14],
                                                     1.0 / 127.0, None,
                                                     OP.mult),
                 reads=("colz",), writes=("colz",))
            p.op("DVE", lambda e=uw: V.reciprocal(colz[0:e, 15:16],
                                                  colz[0:e, 13:14]),
                 reads=("colz",), writes=("colz",))
            p.op("DVE", lambda e=uw: V.tensor_mul(colz[0:e, 15:16],
                                                  colz[0:e, 15:16],
                                                  colz[0:e, 12:13]),
                 reads=("colz",), writes=("colz",))
            p.op("DVE", lambda e=uw, t=ti: V.tensor_scalar(
                colz[0:e, 15:16], colz[0:e, 15:16], 127.0,
                s_qmask[0:e, t:t + 1], OP.mult, OP.mult),
                 reads=("colz", "s_qmask"), writes=("colz",))
            p.op("DVE", lambda e=uw: V.tensor_scalar(r1t[0:e, :], sc_t[0:e, :],
                                                     colz[0:e, 12:13], None,
                                                     OP.mult),
                 reads=("sc_t", "colz"), writes=("z_tm",))
            store(r1_hbm[u0:u1, :], r1t[0:uw, :], ("z_tm",), f"r1_{ti}")
            if debug:
                store(dbg["d_r1"][u0:u1, :], r1t[0:uw, :], ("z_tm",), "d_r1x")
            yield
            p.op("DVE", lambda e=uw: V.tensor_scalar(q_i[0:e, :], sc_t[0:e, :],
                                                     colz[0:e, 15:16], None,
                                                     OP.mult),
                 reads=("sc_t", "colz"), writes=("q_i",))
            if SAFE_ROUND:
                p.op("DVE", lambda e=uw: V.tensor_scalar(zsq[0:e, :],
                                                         sc_t[0:e, :],
                                                         colz[0:e, 15:16],
                                                         None, OP.mult),
                     reads=("sc_t", "colz"), writes=("zsq",))
                p.op("DVE", lambda e=uw: V.tensor_sub(rr_t[0:e, :], zsq[0:e, :],
                                                      q_i[0:e, :]),
                     reads=("zsq", "q_i"), writes=("rr_t",))
                p.op("DVE", lambda e=uw: V.tensor_scalar(cp_t[0:e, :],
                                                         rr_t[0:e, :], 0.5,
                                                         None, OP.is_gt),
                     reads=("rr_t",), writes=("cp_t",))
                p.op("DVE", lambda e=uw: V.scalar_tensor_tensor(
                    rr_t[0:e, :], rr_t[0:e, :], -0.5, cp_t[0:e, :], OP.is_lt,
                    OP.subtract),
                     reads=("rr_t", "cp_t"), writes=("rr_t",))
                p.op("DVE", lambda e=uw: V.tensor_sub(q_b[0:e, :], q_i[0:e, :],
                                                      rr_t[0:e, :]),
                     reads=("q_i", "rr_t"), writes=("q_b",))
            else:
                p.op("ACT", lambda e=uw: A.activation(q_b[0:e, :], q_i[0:e, :],
                                                      AF.Copy),
                     reads=("q_i",), writes=("q_b",))
            store(r1q_hbm[u0:u1, :], q_b[0:uw, :], ("q_b",), f"r1q_{ti}")
            p.op("DVE", lambda e=uw, t=ti: V.tensor_copy(dqcols[0:e, t:t + 1],
                                                         colz[0:e, 14:15]),
                 reads=("colz",), writes=("dqcols",))
            yield

    for gi in range(len(AGROUPS)):
        attn_mm(gi)
        lpump(10 ** 9)
        lgen[0] = attn_ladder(gi)
    lpump(10 ** 9)

    p.op("DVE", lambda: V.memset(q_b[:], 0.0), reads=(), writes=("q_b",))
    store(r1q_hbm[EXT:EXTP, :], q_b[0:EXTP - EXT, :], ("q_b",), "r1qpad")

    stB.close()
    stAB.close()
    p.barrier()

    stC = contextlib.ExitStack()
    w1q = SB("w1q", [128, 8 * DH], BF16, stack=stC)
    w2q = SB("w2q", [128, 32 * H], BF16, stack=stC)
    for c in range(8):
        load(w1q[:, c * DH:(c + 1) * DH], w1q_hbm[c * 128:(c + 1) * 128, :],
             f"w1q{c}", reads=(f"qw1{c}_0", f"qw1{c}_2048"))
    def load_w2q():
        for c in range(32):
            load(w2q[:, c * H:(c + 1) * H], w2q_hbm[c * 128:(c + 1) * 128, :],
                 f"w2q{c}", reads=(f"qw2{c}_0",))

    hbuf = SB("hbuf", [128, 32 * (GW + 2)], stack=stC)
    RQW = 512
    r1qg = SB("r1qg", [128, 8 * RQW], BF16, stack=stC)
    s2 = PSUM("ps_s2", [128, H], stack=stC)
    o_t = SB("o_t", [128, H], stack=stC)
    s2q = o_t
    r1g = o_t
    dqs = SB("dqs", [128, GW + 2], stack=stC)
    s2xs = SB("s2xs", [128, GW], stack=stC)
    pmax = SB("pmax", [128, GW], stack=stC)
    colfs = (SB("colfA", [128, 10], stack=stC),
             SB("colfB", [128, 10], stack=stC))
    rowf = SB("rowf", [1, GW + 2], stack=stC)
    if SAFE_ROUND:
        _sa = SB("sc_a0", [128, GW], stack=stC)
        _sd = SB("sc_d0", [128, GW], stack=stC)
        sc_as, sc_ds = (_sa, _sa), (_sd, _sd)

        def satag(m):
            return "sca0"

        def sdtag(m):
            return "scd0"
    else:
        sc_as = (SB("sc_a0", [128, GW], stack=stC),
                 SB("sc_a1", [128, GW], stack=stC))
        sc_ds = (SB("sc_d0", [128, GW], stack=stC),
                 SB("sc_d1", [128, GW], stack=stC))

        def satag(m):
            return f"sca{m % 2}"

        def sdtag(m):
            return f"scd{m % 2}"
    hq_i = SB("hq_i", [128, 32 * GW], I8, stack=stC)
    hq_b = SB("hq_b", [128, 32 * GW], BF16, stack=stC)
    if SAFE_ROUND:
        hrr = SB("hrr", [128, GW], stack=stC)
        hcp = SB("hcp", [128, GW], stack=stC)
    ps_m1 = PSUM("ps_m1", [128, 1536], stack=stC)
    ps_bc = PSUM("ps_bc", [128, 512], stack=stC)
    ps_o = PSUM("ps_o", [128, 1024], stack=stC)

    HB = "hb"
    HQ = "hq_b"

    def ffn_pars(gi):
        f0, f1 = FG[gi]
        fw = f1 - f0
        a0 = max(0, f0 - 1)
        a1 = min(EXT, f1 + 1)
        return f0, f1, fw, a0, a1, a1 - a0

    def ffn_conv_a(gi, m_):
        f0, f1, fw, a0, a1, aw = ffn_pars(gi)
        c_lo = max(f0, 1)
        c_hi = min(f1, EXT - 1)
        ce = c_hi - c_lo
        cs = c_lo - a0
        hm = hbuf[:, m_ * (GW + 2):m_ * (GW + 2) + aw]
        p.op("ACT", lambda h=hm, e=ce, c0=cs, mm_=m_:
             A.activation(sc_as[mm_ % 2][:, 0:e], h[:, c0:c0 + e],
                          AF.Identity, bias=s_convb[:, mm_:mm_ + 1],
                          scale=s_convw[:, 32 + mm_:33 + mm_]),
             reads=(HB + str(m_), "s_convw", "s_convb"),
             writes=(satag(m_),))
        p.op("ACT", lambda h=hm, e=ce, c0=cs, mm_=m_:
             A.activation(sc_ds[mm_ % 2][:, 0:e], h[:, c0 + 1:c0 + 1 + e],
                          AF.Identity, scale=s_convw[:, 64 + mm_:65 + mm_]),
             reads=(HB + str(m_), "s_convw"), writes=(sdtag(m_),))

    def ffn_conv_cd(gi, m_):
        f0, f1, fw, a0, a1, aw = ffn_pars(gi)
        c_lo = max(f0, 1)
        c_hi = min(f1, EXT - 1)
        ce = c_hi - c_lo
        cs = c_lo - a0
        hm = hbuf[:, m_ * (GW + 2):m_ * (GW + 2) + aw]
        p.op("DVE", lambda h=hm, e=ce, c0=cs, mm_=m_:
             V.scalar_tensor_tensor(sc_as[mm_ % 2][:, 0:e],
                                    h[:, c0 - 1:c0 - 1 + e],
                                    s_convw[:, mm_:mm_ + 1],
                                    sc_as[mm_ % 2][:, 0:e],
                                    OP.mult, OP.add),
             reads=(HB + str(m_), "s_convw", satag(m_)),
             writes=(satag(m_),))
        p.op("DVE", lambda h=hm, e=ce, c0=cs, mm_=m_:
             V.tensor_add(sc_ds[mm_ % 2][:, 0:e], sc_ds[mm_ % 2][:, 0:e],
                          sc_as[mm_ % 2][:, 0:e]),
             reads=(satag(m_), sdtag(m_)), writes=(sdtag(m_),))
        p.op("ACT", lambda h=hm, e=ce, c0=cs, mm_=m_:
             A.activation(h[:, c0:c0 + e], sc_ds[mm_ % 2][:, 0:e], AF.Silu),
             reads=(sdtag(m_),), writes=(HB + str(m_),))

    def ffn_conv_chunk(gi, m_):
        ffn_conv_a(gi, m_)
        ffn_conv_cd(gi, m_)

    def ffn_quant_chunk(gi, m_):
        colf = colfs[gi % 2]
        f0, f1, fw, a0, a1, aw = ffn_pars(gi)
        co = f0 - a0
        hm = hbuf[:, m_ * (GW + 2) + co:m_ * (GW + 2) + co + fw]
        if SAFE_ROUND:
            p.op("DVE", lambda h=hm, e=fw, mm_=m_:
                 V.scalar_tensor_tensor(sc_as[mm_ % 2][:, 0:e], h, 1.0,
                                        s2xs[:, 0:e], OP.mult, OP.mult),
                 reads=(HB + str(m_), "s2xs"), writes=(satag(m_),))
            p.op("DVE", lambda e=fw, mm_=m_: V.tensor_copy(
                hq_i[:, mm_ * GW:mm_ * GW + e], sc_as[mm_ % 2][:, 0:e]),
                 reads=(satag(m_),), writes=("hq_i",))
            p.op("DVE", lambda e=fw, mm_=m_: V.tensor_sub(
                hrr[:, 0:e], sc_as[mm_ % 2][:, 0:e],
                hq_i[:, mm_ * GW:mm_ * GW + e]),
                 reads=(satag(m_), "hq_i"), writes=("hrr",))
            p.op("DVE", lambda e=fw: V.tensor_scalar(hcp[:, 0:e], hrr[:, 0:e],
                                                     0.5, None, OP.is_gt),
                 reads=("hrr",), writes=("hcp",))
            p.op("DVE", lambda e=fw: V.scalar_tensor_tensor(
                hrr[:, 0:e], hrr[:, 0:e], -0.5, hcp[:, 0:e], OP.is_lt,
                OP.subtract),
                 reads=("hrr", "hcp"), writes=("hrr",))
            p.op("DVE", lambda e=fw, mm_=m_: V.tensor_sub(
                hq_b[:, mm_ * GW:mm_ * GW + e],
                hq_i[:, mm_ * GW:mm_ * GW + e], hrr[:, 0:e]),
                 reads=("hq_i", "hrr"), writes=(HQ,))
        else:
            p.op("DVE", lambda h=hm, e=fw, mm_=m_:
                 V.scalar_tensor_tensor(hq_i[:, mm_ * GW:mm_ * GW + e], h,
                                        1.0, s2xs[:, 0:e], OP.mult, OP.mult),
                 reads=(HB + str(m_), "s2xs"), writes=("hq_i",))

    def ffn_hq_cast(gi):
        if not SAFE_ROUND:
            p.op("ACT", lambda: A.activation(hq_b[:, :], hq_i[:, :], AF.Copy),
                 reads=("hq_i",), writes=(HQ,))

    def ffn_A1(gi, qprev, m2g=None):
        real = gi < len(FG)
        colf = colfs[gi % 2]
        CF = f"colf{gi % 2}"
        f0, f1, fw, a0, a1, aw = ffn_pars(min(gi, len(FG) - 1))
        tg = f"F{gi}"
        al0 = (a0 // 128) * 128
        al1 = min(EXTP, ((a1 + 127) // 128) * 128)
        alw = al1 - al0
        off = a0 - al0
        r1q_reads = tuple(f"r1q_{t}" for t in range(al0 // 128,
                                                    min(NT, (al1 + 127) // 128)))
        if al1 > EXT:
            r1q_reads = r1q_reads + ("r1qpad",)
        for c in range(8 if real else 0):
            p.op("SP", lambda d=r1qg[:, c * RQW:c * RQW + alw],
                 s=r1q_hbm[al0:al1, c * 128:(c + 1) * 128]:
                 SY.dma_start(out=d, in_=s, transpose=True),
                 reads=r1q_reads, writes=(f"r1qg{c}",), dma=True)
        for t in range(al0 // 128, min(NT, (a1 + 127) // 128) if real else 0):
            tl0, tl1 = TILES[t]
            uw_t = tl1 - tl0
            p.op("PE", lambda t=t, o0=tl0 - al0, e_=uw_t: PE.transpose(
                ps_bc[0:1, o0:o0 + e_], dqcols[0:e_, t:t + 1],
                s_idf[0:e_, 0:e_]),
                 reads=("dqcols", "s_idf"), writes=("psBC",), group=tg + "dqt")
        if real:
            p.op("DVE", lambda o0=a0 - al0: V.tensor_scalar(
                rowf[:, 0:aw], ps_bc[0:1, o0:o0 + aw], s_w1mean[0:1, :], None,
                OP.mult),
                 reads=("psBC", "s_w1mean"), writes=("rowf",))
            p.op("PE", lambda: PE.matmul(ps_bc[0:128, 0:aw], s_ones1[:],
                                         rowf[:, 0:aw], start=True, stop=True),
                 reads=("s_ones1", "rowf"), writes=("psBC",), group=tg + "bc")
            p.op("DVE", lambda: V.tensor_copy(dqs[:, 0:aw], ps_bc[0:128, 0:aw]),
                 reads=("psBC",), writes=("dqs",))
        inline_conv = not (debug and gi == 1)
        m2q = []
        m2_nsub = 0
        if m2g is not None:
            f0m, f1m = FG[m2g]
            m2_nsub = (f1m - f0m + 127) // 128
            for si in range(m2_nsub):
                m2q.extend(ffn_m2_mm_list(m2g, si))
            ffn_r1g_load(m2g, 0)
        m2i = 0
        if SAFE_ROUND:
            m2pump = (len(m2q) + 31) // 32 if m2q else 0
            for m_ in range(33):
                if m_ < 32:
                    if qprev is not None:
                        ffn_quant_chunk(qprev, m_)
                    for _ in range(m2pump):
                        if m2i < len(m2q):
                            if m2i == 64 and m2_nsub > 1:
                                for _x in ffn_m2_tail(m2g, 0):
                                    pass
                                ffn_r1g_load(m2g, 1)
                            m2q[m2i]()
                            m2i += 1
                    od = ps_m1[0:128, (m_ % 3) * 512:(m_ % 3) * 512 + aw]
                    for k in range(8):
                        mm(tg + f"m{m_}", od,
                           w1q[:, k * DH + m_ * 128:k * DH + m_ * 128 + 128],
                           r1qg[:, k * RQW + off:k * RQW + off + aw],
                           reads=(f"w1q{k}", f"r1qg{k}"),
                           wr=f"psM{m_ % 3}", start=(k == 0), stop=(k == 7))
                    p.op("DVE",
                         lambda o=hbuf[:, m_ * (GW + 2):m_ * (GW + 2) + aw],
                         i=od, d=dqs[:, 0:aw], mm_=m_:
                         V.scalar_tensor_tensor(o, i, 1.0, d, OP.mult, OP.mult),
                         reads=(f"psM{m_ % 3}", "dqs"), writes=(HB + str(m_),))
                    if inline_conv:
                        ffn_conv_a(gi, m_)
                        ffn_conv_cd(gi, m_)
            if m2g is not None:
                for _x in ffn_m2_tail(m2g, m2_nsub - 1):
                    pass
            if qprev is not None:
                ffn_hq_cast(qprev)
            if not inline_conv:
                store(dbg["d_h"][:], hbuf[:],
                      tuple(HB + str(i) for i in range(32)), "d_hx")
                for m_ in range(32):
                    ffn_conv_chunk(gi, m_)
            return

        a2hold = [ffn_A2(qprev) if qprev is not None else None]
        tails = []
        on_tail_done = []

        def pump_tails(n):
            for tgen in tails[:]:
                for _ in range(n):
                    try:
                        next(tgen)
                    except StopIteration:
                        tails.remove(tgen)
                        if on_tail_done:
                            on_tail_done.pop(0)()
                        break
        PRE = 6
        NIT = 33 + PRE
        m2pump = (len(m2q) + NIT - 2) // (NIT - 1) if m2q else 0
        for it in range(NIT):
            m_ = it - PRE
            if a2hold[0] is not None:
                for _ in range(4):
                    try:
                        next(a2hold[0])
                    except StopIteration:
                        a2hold[0] = None
                        break
            q_m = it - PRE + 1
            if qprev is not None and 0 <= q_m < 32:
                ffn_quant_chunk(qprev, q_m)
            for _ in range(m2pump):
                if m2i < len(m2q):
                    if m2i == 64 and m2_nsub > 1:
                        t0 = ffn_m2_tail(m2g, 0)
                        try:
                            next(t0)
                            tails.append(t0)
                            on_tail_done.append(
                                lambda: ffn_r1g_load(m2g, 1))
                        except StopIteration:
                            ffn_r1g_load(m2g, 1)
                    m2q[m2i]()
                    m2i += 1
                    if m2i == len(m2q):
                        t1 = ffn_m2_tail(m2g, m2_nsub - 1)
                        try:
                            next(t1)
                            tails.append(t1)
                        except StopIteration:
                            pass
            pump_tails(2)
            if real and 0 <= m_ < 32:
                od = ps_m1[0:128, (m_ % 3) * 512:(m_ % 3) * 512 + aw]
                for k in range(8):
                    mm(tg + f"m{m_}", od,
                       w1q[:, k * DH + m_ * 128:k * DH + m_ * 128 + 128],
                       r1qg[:, k * RQW + off:k * RQW + off + aw],
                       reads=(f"w1q{k}", f"r1qg{k}"),
                       wr=f"psM{m_ % 3}", start=(k == 0), stop=(k == 7))
                p.op("DVE", lambda o=hbuf[:, m_ * (GW + 2):m_ * (GW + 2) + aw],
                     i=od, d=dqs[:, 0:aw], mm_=m_:
                     V.scalar_tensor_tensor(o, i, 1.0, d, OP.mult, OP.mult),
                     reads=(f"psM{m_ % 3}", "dqs"), writes=(HB + str(m_),))
                if inline_conv:
                    ffn_conv_a(gi, m_)
            if real and inline_conv and m_ >= 1:
                ffn_conv_cd(gi, m_ - 1)
        while a2hold[0] is not None:
            try:
                next(a2hold[0])
            except StopIteration:
                a2hold[0] = None
        pump_tails(10 ** 9)
        if qprev is not None:
            ffn_hq_cast(qprev)
        if not inline_conv:
            store(dbg["d_h"][:], hbuf[:],
                  tuple(HB + str(i) for i in range(32)), "d_hx")
            for m_ in range(32):
                ffn_conv_chunk(gi, m_)

    def ffn_A2(gi):
        colf = colfs[gi % 2]
        CF = f"colf{gi % 2}"
        f0, f1, fw, a0, a1, aw = ffn_pars(gi)
        tg = f"F{gi}"
        co = f0 - a0
        p.op("DVE", lambda e=fw, c0=co: V.tensor_reduce(
            pmax[:, 0:e].rearrange("p (t o) -> p t o", o=1),
            hbuf[:, :].rearrange("p (c t) -> p t c", c=32)[:, c0:c0 + e, :],
            AX, OP.max, apply_absolute_value=True),
             reads=tuple(HB + str(i) for i in range(32)), writes=("pmax",))
        yield
        nsub = (fw + 127) // 128
        for si in range(nsub):
            u0 = f0 + si * 128
            u1 = min(f1, u0 + 128)
            uw = u1 - u0
            sg = tg + f"s{si}"
            p.op("PE", lambda e=uw, s0=si * 128: PE.transpose(
                ps_bc[0:e, 0:128], pmax[:, s0:s0 + e], s_idf[:]),
                 reads=("pmax", "s_idf"), writes=("psBC",), group=sg + "tp")
            yield
            p.op("DVE", lambda e=uw: V.tensor_reduce(colf[0:e, 0:1],
                                                     ps_bc[0:e, 0:128], AX,
                                                     OP.max),
                 reads=("psBC",), writes=(CF,))
            yield
            p.op("DVE", lambda e=uw: V.tensor_scalar_max(colf[0:e, 0:1],
                                                         colf[0:e, 0:1], 1e-5),
                 reads=(CF,), writes=(CF,))
            yield
            p.op("DVE", lambda e=uw: V.reciprocal(colf[0:e, 1:2], colf[0:e, 0:1]),
                 reads=(CF,), writes=(CF,))
            yield
            p.op("DVE", lambda e=uw: V.tensor_scalar(colf[0:e, 2:3],
                                                     colf[0:e, 1:2], 127.0,
                                                     None, OP.mult),
                 reads=(CF,), writes=(CF,))
            yield
            p.op("DVE", lambda e=uw, si=si: V.tensor_scalar(
                colf[0:e, 8 + si:9 + si], colf[0:e, 0:1], s_w2mean[0:e, :],
                1.0 / 127.0, OP.mult, OP.mult),
                 reads=(CF, "s_w2mean"), writes=(CF,))
            yield
            p.op("PE", lambda e=uw: PE.transpose(ps_bc[0:1, 0:e],
                                                 colf[0:e, 2:3],
                                                 s_idf[0:e, 0:e]),
                 reads=(CF, "s_idf"), writes=("psBC",), group=sg + "t2")
            yield
            p.op("DVE", lambda e=uw, s0=si * 128: V.tensor_copy(
                rowf[:, s0:s0 + e], ps_bc[0:1, 0:e]),
                 reads=("psBC",), writes=("rowf",))
            yield
        p.op("PE", lambda e=fw: PE.matmul(ps_bc[0:128, 0:e], s_ones1[:],
                                          rowf[:, 0:e], start=True, stop=True),
             reads=("s_ones1", "rowf"), writes=("psBC",), group=tg + "b2")
        yield
        p.op("DVE", lambda e=fw: V.tensor_copy(s2xs[:, 0:e], ps_bc[0:128, 0:e]),
             reads=("psBC",), writes=("s2xs",))
        yield

    def ffn_m2_sub(gi, si):
        f0, f1 = FG[gi]
        u0 = f0 + si * 128
        u1 = min(f1, u0 + 128)
        uw = u1 - u0
        out0 = max(2, u0)
        out1 = min(EXT - 2, u1)
        return u0, u1, uw, out0, out1

    def ffn_m2_mm_list(gi, si):
        tg = f"F{gi}"
        u0, u1, uw, out0, out1 = ffn_m2_sub(gi, si)
        if out1 <= out0:
            return []
        sg = tg + f"S{si}"
        ems = []
        for nh in range(2):
            od = ps_o[0:uw, nh * 512:(nh + 1) * 512]
            for c in range(32):
                sta = hq_b[:, c * GW + si * 128:c * GW + si * 128 + uw]
                mv = w2q[:, c * H + nh * 512:c * H + nh * 512 + 512]
                ems.append(lambda od=od, sta=sta, mv=mv, c=c, nh=nh:
                           mm(sg + f"o{nh}", od, sta, mv,
                              reads=(HQ, f"w2q{c}"), wr=f"psO{nh}",
                              start=(c == 0), stop=(c == 31)))
        return ems

    def ffn_r1g_load(gi, si):
        u0, u1, uw, out0, out1 = ffn_m2_sub(gi, si)
        if out1 <= out0:
            return
        load(r1g[0:uw, :], r1_hbm[u0:u1, :], "o_t", q="ACT",
             reads=(f"r1_{u0 // 128}",))

    def ffn_m2_tail(gi, si):
        colf = colfs[gi % 2]
        CF = f"colf{gi % 2}"
        u0, u1, uw, out0, out1 = ffn_m2_sub(gi, si)
        if out1 <= out0:
            return
        if True:
            p.op("DVE", lambda e=uw, si=si: V.scalar_tensor_tensor(
                s2[0:e, :], ps_o[0:e, :], colf[0:e, 8 + si:9 + si], r1g[0:e, :],
                OP.mult, OP.add, accum_out=colf[0:e, 4:5]),
                 reads=("psO0", "psO1", CF, "o_t"), writes=("s2", CF))
            yield
            p.op("ACT", lambda e=uw: A.activation(s2q[0:e, :], s2[0:e, :],
                                                  AF.Square,
                                                  accum_out=colf[0:e, 5:6]),
                 reads=("s2",), writes=("o_t", CF))
            yield
            p.op("DVE", lambda e=uw: V.tensor_scalar(colf[0:e, 4:5],
                                                     colf[0:e, 4:5], 1.0 / H,
                                                     None, OP.mult),
                 reads=(CF,), writes=(CF,))
            yield
            p.op("DVE", lambda e=uw: V.tensor_mul(colf[0:e, 6:7], colf[0:e, 4:5],
                                                  colf[0:e, 4:5]),
                 reads=(CF,), writes=(CF,))
            yield
            p.op("DVE", lambda e=uw: V.scalar_tensor_tensor(
                colf[0:e, 6:7], colf[0:e, 5:6], 1.0 / H, colf[0:e, 6:7],
                OP.mult, OP.subtract),
                 reads=(CF,), writes=(CF,))
            yield
            p.op("DVE", lambda e=uw: V.tensor_scalar(colf[0:e, 6:7],
                                                     colf[0:e, 6:7], EPS, None,
                                                     OP.add),
                 reads=(CF,), writes=(CF,))
            yield
            p.op("ACT", lambda e=uw: A.activation(colf[0:e, 7:8], colf[0:e, 6:7],
                                                  AF.Sqrt),
                 reads=(CF,), writes=(CF,))
            yield
            p.op("DVE", lambda e=uw: V.reciprocal(colf[0:e, 7:8], colf[0:e, 7:8]),
                 reads=(CF,), writes=(CF,))
            yield
            p.op("DVE", lambda e=uw: V.tensor_scalar(o_t[0:e, :], s2[0:e, :],
                                                     colf[0:e, 4:5],
                                                     colf[0:e, 7:8],
                                                     OP.subtract, OP.mult),
                 reads=("s2", CF), writes=("o_t",))
            yield
            store(out_tm[out0 - 2:out1 - 2, :], o_t[out0 - u0:out1 - u0, :],
                  ("o_t",), f"out{u0}")

    def ffn_m2_out(gi):
        f0, f1 = FG[gi]
        nsub = (f1 - f0 + 127) // 128
        for si in range(nsub):
            for em in ffn_m2_mm_list(gi, si):
                em()
            ffn_r1g_load(gi, si)
            for _ in ffn_m2_tail(gi, si):
                pass

    NG = len(FG)
    load_w2q_once = [load_w2q]
    NLOOP = NG if SAFE_ROUND else NG + 2
    for gi in range(NLOOP):
        qprev = gi - 1 if (gi >= 1 and gi - 1 < NG) else None
        m2g = gi - 2 if (gi >= 2 and not SAFE_ROUND) else None
        ffn_A1(gi, qprev, m2g)
        if load_w2q_once:
            load_w2q_once.pop()()
        if SAFE_ROUND:
            if gi >= 1:
                ffn_m2_out(gi - 1)
            for _ in ffn_A2(gi):
                pass
    if SAFE_ROUND:
        for m_ in range(32):
            ffn_quant_chunk(NG - 1, m_)
        ffn_hq_cast(NG - 1)
        ffn_m2_out(NG - 1)

    stC.close()
    root.close()
    n = p.finalize_and_emit()
    return nc, n, dbg


_NC_CACHE = {}


def _get_nc(debug=False):
    key = (debug, SAFE_ROUND)
    if key not in _NC_CACHE:
        _NC_CACHE[key] = build_nc(debug)
    return _NC_CACHE[key]


def make_in_maps(inputs, debug=False):
    f32 = np.float32
    K_EXT = EXT
    x = np.asarray(inputs["L0_attn_x"], f32)
    W_in = np.ascontiguousarray(inputs["L0_attn_W_in"], f32)
    W_B = np.ascontiguousarray(inputs["L0_attn_W_B"], f32)
    W_D = np.ascontiguousarray(inputs["L0_attn_W_D"], f32)
    W_out = np.ascontiguousarray(inputs["L0_attn_W_out"], f32)
    B_mat = np.ascontiguousarray(inputs["L0_attn_B_mat"], f32)
    C_mat = np.ascontiguousarray(inputs["L0_attn_C_mat"], f32)
    W_ig = np.ascontiguousarray(inputs["L0_attn_W_ig"], f32)
    W_fg = np.ascontiguousarray(inputs["L0_attn_W_fg"], f32)
    b_in = np.asarray(inputs["L0_attn_b_in"], f32)
    b_B = np.asarray(inputs["L0_attn_b_B"], f32)
    b_D = np.asarray(inputs["L0_attn_b_D"], f32)
    b_out = np.asarray(inputs["L0_attn_b_out"], f32)
    b_ig = np.asarray(inputs["L0_attn_b_ig"], f32)
    b_fg = np.asarray(inputs["L0_attn_b_fg"], f32)
    gamma = np.asarray(inputs["L0_attn_gamma"], f32)
    beta = np.asarray(inputs["L0_attn_beta"], f32)
    w1 = np.asarray(inputs["L0_ffn_w1"], f32)
    conv_w = np.asarray(inputs["L0_ffn_conv_w"], f32)
    conv_b = np.asarray(inputs["L0_ffn_conv_b"], f32)
    w2 = np.asarray(inputs["L0_ffn_w2"], f32)

    shared = {
        "W_in": W_in, "W_D": W_D, "W_out": W_out,
        "W_in_T": np.ascontiguousarray(W_in.T),
        "W_B_T": np.ascontiguousarray(W_B.T),
        "B_mat": B_mat, "C_mat": C_mat, "W_ig": W_ig, "W_fg": W_fg,
        "b_in_cols": np.ascontiguousarray(b_in.reshape(8, 128).T),
        "b_B_cols": np.ascontiguousarray(b_B.reshape(8, 128).T),
        "b_out_cols": np.ascontiguousarray(b_out.reshape(8, 128).T),
        "b_ig_col": np.ascontiguousarray(b_ig.reshape(S, 1)),
        "b_fg_col": np.ascontiguousarray(b_fg.reshape(S, 1)),
        "gamma_b": np.ascontiguousarray(np.broadcast_to(gamma[None, :], (128, H))),
        "beta_b": np.ascontiguousarray(np.broadcast_to(beta[None, :], (128, H))),
        "bD_row": np.ascontiguousarray(b_D.reshape(1, H)),
        "convw": np.ascontiguousarray(
            conv_w[:, 0, :].reshape(32, 128, 3).transpose(1, 2, 0).reshape(128, 96)),
        "convb": np.ascontiguousarray(conv_b.reshape(32, 128).T),
        "w1t_f": np.ascontiguousarray(w1.T),
        "w2t_f": np.ascontiguousarray(w2.T),
        "ones1": np.ones((1, 128), f32),
        "ones_col": np.ones((128, 1), f32),
        "id_f32": np.eye(128, dtype=f32),
        "onesrow_ext": np.ones((1, K_EXT), ml_dtypes.bfloat16),
        "zrow_ext": np.zeros((1, K_EXT), ml_dtypes.bfloat16),
    }
    in_maps = []
    for c in range(8):
        b, half = c // 2, c % 2
        s = half * 2048
        lo, hi = s - 2, s + 2048 + 2
        xe = np.zeros((EXT, H), f32)
        a, bb = max(0, lo), min(T, hi)
        xe[a - lo:bb - lo] = x[b, a:bb]
        qm = np.zeros((EXTP,), f32)
        qm[a - lo:bb - lo] = 1.0
        m = dict(shared)
        m["x_tm"] = xe
        m["x_fm"] = np.ascontiguousarray(xe.T)
        m["xg_fm"] = np.ascontiguousarray(x[b].T)
        m["qmask"] = np.ascontiguousarray(qm.reshape(NT, 128).T)
        m["hsel0"] = np.full((S, 1), 1.0 - half, f32)
        m["hsel1"] = np.full((S, 1), float(half), f32)
        in_maps.append(m)
    return in_maps


def kernel(**inputs):
    nc, _, _ = _get_nc(False)
    in_maps = make_in_maps(inputs)
    res = run_bass_kernel_spmd(nc, in_maps, core_ids=list(range(8)))
    out = np.empty((B, T, H), np.float32)
    for c in range(8):
        b, half = c // 2, c % 2
        out[b, half * 2048:(half + 1) * 2048] = res.results[c]["out_tm"]
    return out
